# version 07
# baseline (speedup 1.0000x reference)
import numpy as np
from contextlib import ExitStack

import concourse.bass as bass
import concourse.mybir as mybir
from concourse.bass_utils import run_bass_kernel_spmd

F32 = mybir.dt.float32
BF16 = mybir.dt.bfloat16
AF = mybir.ActivationFunctionType
ALU = mybir.AluOpType
AX = mybir.AxisListType.X

MAGIC = 12582912.0
D = 1024
NH = 16
HD = 64
FF = 2816
NTOK = 4096
TILE = 512
NT = NTOK // TILE
NG = TILE // 128
NCF = D // 128
NCI = FF // 128
LN_EPS = 1e-5
ATTN_EPS = 1e-6
VAGW = NH * 65 + 15

ENGINES = ("sync", "tensor", "vector", "scalar", "gpsimd")
SELF_SERIAL = {"vector": "ve", "scalar": "ac", "gpsimd": "gp"}


NPOOL = 8


class Prog:

    def __init__(self):
        self.items = {e: [] for e in ENGINES}
        self.cnt = {}
        self.pool_rr = {"sx": 0, "sg": 0}
        self.pool_last = {}
        self.last_sc = None

    def _flat(self, after):
        out = []
        for t in after:
            if t is None:
                continue
            if isinstance(t, list):
                out.extend(x for x in t if x is not None)
            else:
                out.append(t)
        return out

    def op(self, eng, fn, after=(), sig=None, amt=1):
        after = self._flat(after)
        if sig in ("sx", "sg"):
            i = self.pool_rr[sig] % NPOOL
            self.pool_rr[sig] += 1
            sem = f"{sig}{i}"
            prev = self.pool_last.get(sem)
            if prev is not None:
                after = after + [prev]
            self.cnt[sem] = self.cnt.get(sem, 0) + amt
            tok = (sem, self.cnt[sem])
            self.pool_last[sem] = tok
            self.items[eng].append((after, fn, sem, amt, None))
            return tok
        if sig == "sc":
            if self.last_sc is not None:
                after = after + [self.last_sc]
        own = SELF_SERIAL.get(eng)
        if own is not None and sig is None:
            sig = own
        self_wait = None
        if own is not None:
            self_wait = (own, self.cnt.get(own, 0))
        tok = None
        if sig is not None:
            self.cnt[sig] = self.cnt.get(sig, 0) + amt
            tok = (sig, self.cnt[sig])
        if sig == "sc":
            self.last_sc = tok
        self.items[eng].append((after, fn, sig, amt, self_wait))
        return tok

    def now(self, sig):
        if sig in ("sx", "sg"):
            return [self.pool_last[s] for s in
                    (f"{sig}{i}" for i in range(NPOOL)) if s in self.pool_last]
        return (sig, self.cnt.get(sig, 0))

    def emit(self, eng_name, eng, sems):
        waited = {}
        for waits, fn, sig, amt, self_wait in self.items[eng_name]:
            allw = list(waits)
            if self_wait is not None and self_wait[1] > 0:
                allw.append(self_wait)
            for (s, v) in allw:
                if waited.get(s, 0) >= v:
                    continue
                eng.wait_ge(sems[s], v)
                waited[s] = v
            inst = fn(eng)
            if sig is not None:
                inst.then_inc(sems[sig], amt)


def build_nc(dbg=False):
    nc = bass.Bass("TRN2", target_bir_lowering=False, detect_race_conditions=True)
    dkind = "ExternalOutput" if dbg else "Internal"

    x_in = nc.dram_tensor("x", [NTOK, D], F32, kind="ExternalInput")
    Wq = nc.dram_tensor("Wq", [D, D], F32, kind="ExternalInput")
    Wk = nc.dram_tensor("Wk", [D, D], F32, kind="ExternalInput")
    Wv = nc.dram_tensor("Wv", [D, D], F32, kind="ExternalInput")
    Wo = nc.dram_tensor("Wo", [D, D], F32, kind="ExternalInput")
    bqd = nc.dram_tensor("bq", [1, D], F32, kind="ExternalInput")
    bkd = nc.dram_tensor("bk", [1, D], F32, kind="ExternalInput")
    bvd = nc.dram_tensor("bv", [1, D], F32, kind="ExternalInput")
    bod = nc.dram_tensor("bo", [1, D], F32, kind="ExternalInput")
    wgd = nc.dram_tensor("wg", [FF, D], F32, kind="ExternalInput")
    wud = nc.dram_tensor("wu", [FF, D], F32, kind="ExternalInput")
    wdd = nc.dram_tensor("wd", [D, FF], F32, kind="ExternalInput")
    clipx = nc.dram_tensor("clipx", [1, 1], F32, kind="ExternalInput")
    clipd = nc.dram_tensor("clipd", [1, 1], F32, kind="ExternalInput")
    ident_in = nc.dram_tensor("ident", [128, 128], BF16, kind="ExternalInput")
    ones_in = nc.dram_tensor("onesc", [1, TILE], BF16, kind="ExternalInput")
    epair_in = nc.dram_tensor("epair", [16, NCF * 128], BF16, kind="ExternalInput")
    smask_in = nc.dram_tensor("smask", [128, 4], F32, kind="ExternalInput")

    y_out = nc.dram_tensor("y", [NTOK, D], F32, kind="ExternalOutput")

    w_bf = {n: nc.dram_tensor(f"{n}_bf", [D, D], BF16, kind="Internal")
            for n in ("wq", "wk", "wv", "wo")}
    wg_q = nc.dram_tensor("wg_q", [FF, D], BF16, kind="Internal")
    wu_q = nc.dram_tensor("wu_q", [FF, D], BF16, kind="Internal")
    wd_q = nc.dram_tensor("wd_q", [D, FF], BF16, kind="Internal")
    x1_hbm = nc.dram_tensor("x1_hbm", [NTOK, D], F32, kind=dkind)
    h_hbm = nc.dram_tensor("h_hbm", [NCI, 128, NTOK], F32, kind="Internal")
    cc_kv_in = nc.dram_tensor("cc_kv_in", [128, 4 * 520], F32, kind="Internal")
    cc_kv_out = nc.dram_tensor("cc_kv_out", [128, 4 * 520], F32, kind="Internal")
    cc_m1_in = nc.dram_tensor("cc_m1_in", [1, 128], F32, kind="Internal")
    cc_m1_out = nc.dram_tensor("cc_m1_out", [1, 128], F32, kind="Internal")
    cc_m2_in = nc.dram_tensor("cc_m2_in", [1, 128], F32, kind="Internal")
    cc_m2_out = nc.dram_tensor("cc_m2_out", [1, 128], F32, kind="Internal")
    if dbg:
        dbg_kv = nc.dram_tensor("dbg_kv", [128, 4 * 520], F32, kind="ExternalOutput")
        dbg_m1 = nc.dram_tensor("dbg_m1", [1, 128], F32, kind="ExternalOutput")
        dbg_m2 = nc.dram_tensor("dbg_m2", [1, 128], F32, kind="ExternalOutput")
        dbg_kvloc = nc.dram_tensor("dbg_kvloc", [128, 520], F32, kind="ExternalOutput")
        dbg_phiq = nc.dram_tensor("dbg_phiq", [128, NCF * NTOK], BF16, kind="ExternalOutput")
        dbg_ktm = nc.dram_tensor("dbg_ktm", [128, 2 * D], BF16, kind="ExternalOutput")
        dbg_vag = nc.dram_tensor("dbg_vag", [128, 2 * VAGW], BF16, kind="ExternalOutput")
        dbg_xnfm = nc.dram_tensor("dbg_xnfm", [128, NCF * TILE], BF16, kind="ExternalOutput")
        dbg_wkt = nc.dram_tensor("dbg_wkt", [128, NCF * D], BF16, kind="ExternalOutput")
        dbg_brow = nc.dram_tensor("dbg_brow", [1, 4 * D], BF16, kind="ExternalOutput")
        dbg_vag0 = nc.dram_tensor("dbg_vag0", [128, 2 * VAGW], BF16, kind="ExternalOutput")
        dbg_xntm = nc.dram_tensor("dbg_xntm", [128, NG * D], BF16, kind="ExternalOutput")
        dbg_rstd = nc.dram_tensor("dbg_rstd", [128, NT * NG], F32, kind="ExternalOutput")
        dbg_mv1 = nc.dram_tensor("dbg_mv1", [128, 2 * NT * NG], F32, kind="ExternalOutput")

    p = Prog()
    es = ExitStack()
    sem_names = ["sc", "pe", "ve", "ac", "gp"] + \
        [f"sx{i}" for i in range(NPOOL)] + [f"sg{i}" for i in range(NPOOL)]
    sems = {n: es.enter_context(nc.semaphore(n)) for n in sem_names}

    ident = es.enter_context(nc.sbuf_tensor("identt", [128, 128], BF16))
    onesc = es.enter_context(nc.sbuf_tensor("onesct", [1, TILE], BF16))
    brow = {n: es.enter_context(nc.sbuf_tensor(f"brow_{n}", [1, D], BF16))
            for n in ("q", "k", "v", "o")}
    mv1 = es.enter_context(nc.sbuf_tensor("mv1", [128, 2 * NT * NG], F32))
    rstd1 = es.enter_context(nc.sbuf_tensor("rstd1", [128, NT * NG], F32))
    mb1 = es.enter_context(nc.sbuf_tensor("mb1", [128, NT * NG], F32))
    mv2 = es.enter_context(nc.sbuf_tensor("mv2", [128, 2 * NT * NG], F32))
    xm2 = es.enter_context(nc.sbuf_tensor("xm2", [128, NT * NG], F32))
    a2c = es.enter_context(nc.sbuf_tensor("a2c", [128, NT * NG], F32))
    b2c = es.enter_context(nc.sbuf_tensor("b2c", [128, NT * NG], F32))
    wgs = es.enter_context(nc.sbuf_tensor("wgs", [128, NCI], F32))
    wus = es.enter_context(nc.sbuf_tensor("wus", [128, NCI], F32))
    wds = es.enter_context(nc.sbuf_tensor("wds", [128, NCF], F32))
    sg_all = es.enter_context(nc.sbuf_tensor("sg_all", [128, NCI], F32))
    su_all = es.enter_context(nc.sbuf_tensor("su_all", [128, NCI], F32))
    scl = es.enter_context(nc.sbuf_tensor("scl", [128, 4], F32))
    gmaxb = es.enter_context(nc.sbuf_tensor("gmaxb", [128, 128], F32))
    kv_sb = es.enter_context(nc.sbuf_tensor("kv_sb", [128, 520], F32))
    habs = es.enter_context(nc.sbuf_tensor("habs", [128, NCI], F32))
    tiny = es.enter_context(nc.sbuf_tensor("tiny", [128, 8], F32))
    tin2 = es.enter_context(nc.sbuf_tensor("tin2", [128, 8], F32))
    smask = es.enter_context(nc.sbuf_tensor("smaskt", [128, 4], F32))
    zrow = es.enter_context(nc.sbuf_tensor("zrow", [1, 512], BF16))
    x_t = es.enter_context(nc.sbuf_tensor("x_t", [128, NG * D], F32))
    scr = es.enter_context(nc.sbuf_tensor("scr", [128, 2560], F32))
    gsc = es.enter_context(nc.sbuf_tensor("gsc", [128, 1024], F32))
    bqfm = es.enter_context(nc.sbuf_tensor("bqfm", [128, NCF], F32))
    bkb = es.enter_context(nc.sbuf_tensor("bkb", [128, D], BF16))
    bvb = es.enter_context(nc.sbuf_tensor("bvb", [128, D], BF16))

    kps = es.enter_context(nc.psum_tensor("kps", [128, 1024], F32))
    vps = es.enter_context(nc.psum_tensor("vps", [128, 1024], F32))
    pb4t = es.enter_context(nc.psum_tensor("pb4t", [128, 512], F32))
    pb5t = es.enter_context(nc.psum_tensor("pb5t", [128, 512], F32))
    pb6t = es.enter_context(nc.psum_tensor("pb6t", [128, 512], F32))
    pbf = es.enter_context(nc.psum_tensor("pbf", [128, 1024], BF16))
    pb = [kps[:, 0:512], kps[:, 512:1024], vps[:, 0:512], vps[:, 512:1024],
          pb4t[:, :], pb5t[:, :], pb6t[:, :]]

    def ts(tensor, c0, w):
        return tensor[:, c0:c0 + w]

    def tok_ap(dram, t):
        return bass.AP(dram, t * TILE * D, [[D, 128], [128 * D, NG], [1, D]])

    def sb3(sb):
        return sb.ap().rearrange("p (g f) -> p g f", f=D)

    cast_tok = {}
    for n, W in (("wk", Wk), ("wv", Wv), ("wq", Wq), ("wo", Wo)):
        cast_tok[n] = p.op("gpsimd",
                           lambda g, a=w_bf[n].ap(), b=W.ap(): g.dma_start(out=a, in_=b),
                           sig="sg", amt=16)
    for n, B in (("q", bqd), ("k", bkd), ("v", bvd), ("o", bod)):
        p.op("gpsimd", lambda g, a=brow[n][:, :], b=B.ap(): g.dma_start(out=a, in_=b),
             sig="sg", amt=16)
    brow_tok = p.now("sg")
    bq_tok = p.op("sync", lambda s: s.dma_start(
        out=bqfm[:, :], in_=bass.AP(bqd, 0, [[1, 128], [128, NCF]]),
        allow_slow_non_contiguous=True), sig="sx", amt=16)

    p.op("sync", lambda s: s.dma_start(out=ident[:, :], in_=ident_in.ap()), sig="sx", amt=16)
    onesc_tok = p.op("sync", lambda s: s.dma_start(out=onesc[:, :], in_=ones_in.ap()),
                     sig="sx", amt=16)
    smask_tok = p.op("sync", lambda s: s.dma_start(out=smask[:, :], in_=smask_in.ap()),
                     sig="sx", amt=16)

    esA = ExitStack()
    WkT = esA.enter_context(nc.sbuf_tensor("WkT", [128, NCF * D], BF16))
    WvT = esA.enter_context(nc.sbuf_tensor("WvT", [128, NCF * D], BF16))
    WqT = esA.enter_context(nc.sbuf_tensor("WqT", [128, NCF * D], BF16))
    xn_tm = esA.enter_context(nc.sbuf_tensor("xn_tm", [128, NG * D], BF16))
    xn_fm = esA.enter_context(nc.sbuf_tensor("xn_fm", [128, NCF * TILE], BF16))
    k_tm = esA.enter_context(nc.sbuf_tensor("k_tm", [128, 2 * D], BF16))
    v_ag = esA.enter_context(nc.sbuf_tensor("v_ag", [128, 2 * VAGW], BF16))
    wstg = esA.enter_context(nc.sbuf_tensor("wstg", [128, D], F32))
    wqnt = esA.enter_context(nc.sbuf_tensor("wqnt", [128, D], BF16))
    esWo = ExitStack()
    WoT = esWo.enter_context(nc.sbuf_tensor("WoT", [128, NCF * D], BF16, side="right"))
    esPhi = ExitStack()
    phiq = esPhi.enter_context(nc.sbuf_tensor("phiq", [128, NCF * NTOK], BF16, side="right"))

    wT_tok = {}
    for n, t_ in (("wk", WkT), ("wv", WvT), ("wq", WqT), ("wo", WoT)):
        for c in range(NCF):
            wT_tok[n] = p.op(
                "sync",
                lambda s, o=t_[:, c * D:(c + 1) * D],
                       i=w_bf[n].ap()[:, c * 128:(c + 1) * 128]:
                    s.dma_start_transpose(o, i),
                after=[cast_tok[n]], sig="sx", amt=16)

    p.op("vector", lambda v: v.memset(v_ag[:, :], 0.0), sig="ve")
    for sl in range(2):
        for h in range(NH):
            cpos = sl * VAGW + h * 65 + 64
            p.op("vector", lambda v, c=cpos: v.memset(v_ag[:, c:c + 1], 1.0), sig="ve")
    p.op("vector", lambda v: v.memset(habs[:, :], 0.0), sig="ve")

    prep_jobs = [(wgd, wg_q, wgs, i) for i in range(NCI)] + \
                [(wud, wu_q, wus, i) for i in range(NCI)]
    prep = {"ld": None, "wstg_free": None, "wqnt_free": None, "stores": []}

    def emit_prep(j):
        src, dst, wsc, i = prep_jobs[j]
        ld = p.op("sync", lambda s, o=wstg[:, :], a=src.ap()[i * 128:(i + 1) * 128, :]:
                  s.dma_start(out=o, in_=a), after=[prep["wstg_free"]], sig="sx", amt=16)
        rd = p.op("vector", lambda v, o=wsc[:, i:i + 1], a=wstg[:, :]:
                  v.reduce_max(o, a, axis=AX, apply_absolute_value=True),
                  after=[ld], sig="ve")
        sc = p.op("vector", lambda v, o=wsc[:, i:i + 1], a=wsc[:, i:i + 1]:
                  v.tensor_scalar(o, a, 1.0 / 127.0, None, op0=ALU.mult),
                  after=[rd], sig="ve")
        rc = p.op("vector", lambda v, o=tin2[:, 6:7], a=wsc[:, i:i + 1]:
                  v.reciprocal(o, a), after=[sc], sig="ve")
        t1 = p.op("gpsimd", lambda g, o=wstg[:, :], a=wstg[:, :], sa=tin2[:, 6:7]:
                  g.tensor_scalar(o, a, sa, None, op0=ALU.mult),
                  after=[rc], sig="gp")
        t2 = p.op("gpsimd", lambda g, o=wstg[:, :], a=wstg[:, :]:
                  g.tensor_scalar(o, a, MAGIC, -MAGIC, op0=ALU.add, op1=ALU.add),
                  after=[t1], sig="gp")
        t3 = p.op("gpsimd", lambda g, o=wqnt[:, :], a=wstg[:, :]:
                  g.tensor_scalar(o, a, -128.0, 127.0, op0=ALU.max, op1=ALU.min),
                  after=[t2, prep["wqnt_free"]], sig="gp")
        st = p.op("gpsimd", lambda g, o=dst.ap()[i * 128:(i + 1) * 128, :], a=wqnt[:, :]:
                  g.dma_start(out=o, in_=a), after=[t3], sig="sg", amt=16)
        prep["wstg_free"] = t3
        prep["wqnt_free"] = st
        prep["stores"].append(st)

    def load_act_set(eng, set_id):
        return eng.add_instruction(mybir.InstLoadActFuncSet(
            name=eng.bass.get_next_instruction_name(),
            act_func_set_id=set_id, ins=[], outs=[]))
    p.op("scalar", lambda s: load_act_set(s, 6), sig="ac")
    zr0 = p.op("vector", lambda v: v.memset(zrow[:, :], 0.0), sig="ve")
    for bank in (pb[5], pb[6]):
        p.op("tensor", lambda te, o=bank[:, :], l=zrow[0:1, 0:128], r=zrow[0:1, 0:512]:
             te.matmul(o, l, r, start=True, stop=False, skip_group_check=True),
             after=[zr0], sig="pe")
    al = {}
    phiq_ps_tok = {}
    prep_done = 0
    phiq_tile_tok = {}
    lnst = {}

    bias_build_toks = []
    for row, dst in ((brow["k"], bkb), (brow["v"], bvb)):
        mmt = None
        for n in range(2):
            mmt = p.op("tensor", lambda te, o=kps[:, n * 512:(n + 1) * 512],
                       l=onesc[0:1, 0:128], r=row[0:1, n * 512:(n + 1) * 512]:
                       te.matmul(o, l, r, start=True, stop=True, skip_group_check=True),
                       after=[brow_tok, zr0, onesc_tok,
                              bias_build_toks[-1] if bias_build_toks else None],
                       sig="pe")
        bias_build_toks.append(p.op("vector", lambda v, o=dst[:, :], a=kps[:, 0:1024]:
                                    v.tensor_copy(o, a), after=[mmt], sig="ve"))
    bias_built = bias_build_toks[-1]

    def emit_ln(t):
        ldx = p.op("sync", lambda s, o=sb3(x_t), a=tok_ap(x_in, t):
                   s.dma_start(out=o, in_=a), after=[lnst.get("xn_last")],
                   sig="sx", amt=16)
        aggr_last = None
        for g in range(NG):
            col = 2 * (t * NG + g)
            sqt = p.op("gpsimd", lambda g_, o=gsc[:, :], a=ts(x_t, g * D, D):
                       g_.tensor_mul(o, a, a), after=[ldx, lnst.get("gsc_free")],
                       sig="gp")
            p.op("vector", lambda v, o=tin2[:, 0:1], a=ts(x_t, g * D, D):
                 v.reduce_sum(o, a, axis=AX), after=[ldx], sig="ve")
            lnst["gsc_free"] = p.op("vector", lambda v, o=tin2[:, 1:2], a=gsc[:, :]:
                                    v.reduce_sum(o, a, axis=AX), after=[sqt], sig="ve")
            p.op("vector", lambda v, o=mv1[:, col:col + 1], a=tin2[:, 0:1]:
                 v.tensor_scalar(o, a, 1.0 / D, None, op0=ALU.mult), sig="ve")
            p.op("vector", lambda v, o=tin2[:, 2:3], a=tin2[:, 1:2]:
                 v.tensor_scalar(o, a, 1.0 / D, None, op0=ALU.mult), sig="ve")
            p.op("vector", lambda v, o=tin2[:, 3:4], a=mv1[:, col:col + 1]:
                 v.tensor_mul(o, a, a), sig="ve")
            aggr_last = p.op("vector", lambda v, o=mv1[:, col + 1:col + 2],
                             a=tin2[:, 2:3], b=tin2[:, 3:4]:
                             v.tensor_sub(o, a, b), sig="ve")
        c0 = t * NG
        t_eps = p.op("vector", lambda v, o=ts(tiny, 0, 4),
                     a=mv1[:, 2 * c0 + 1:2 * (c0 + NG):2]:
                     v.tensor_scalar(o, a, LN_EPS, None, op0=ALU.add),
                     after=[aggr_last, al.get("lnfree")], sig="ve")
        t_ln = p.op("scalar", lambda s, o=ts(tiny, 4, 4), a=ts(tiny, 0, 4):
                    s.activation(o, a, AF.Ln), after=[t_eps], sig="ac")
        al["lnfree"] = t_ln
        t_rs = p.op("scalar", lambda s, o=rstd1[:, c0:c0 + NG], a=ts(tiny, 4, 4):
                    s.activation(o, a, AF.Exp, bias=0.0, scale=-0.5), sig="ac")
        t_mb = p.op("vector", lambda v, o=mb1[:, c0:c0 + NG],
                    a=mv1[:, 2 * c0:2 * (c0 + NG):2], b=rstd1[:, c0:c0 + NG]:
                    v.scalar_tensor_tensor(o, a, -1.0, b, op0=ALU.mult, op1=ALU.mult),
                    after=[t_rs], sig="ve")
        xn_last = None
        for g in range(NG):
            gc = c0 + g
            xn_last = p.op(
                "scalar",
                lambda s, o=ts(xn_tm, g * D, D), a=ts(x_t, g * D, D),
                       sa=rstd1[:, gc:gc + 1], ba=mb1[:, gc:gc + 1]:
                    s.activation(o, a, AF.Identity, bias=ba, scale=sa),
                after=[t_mb, al.get("xn_tm_free")], sig="ac")
        lnst["xn_last"] = xn_last
        lnst[("xn_done", t)] = xn_last

    emit_ln(0)
    for t in range(NT):
        xn_done_t = lnst[("xn_done", t)]

        evt = []
        for c in range(NCF):
            half = c % 2
            tl = None
            for g in range(NG):
                tl = p.op(
                    "tensor",
                    lambda te, o=pbf[:, half * 512 + g * 128:half * 512 + (g + 1) * 128],
                           a=ts(xn_tm, g * D + c * 128, 128):
                        te.transpose(o, a, ident[:, :]),
                    after=[xn_done_t, evt[c - 1] if c >= 1 else al.get("pbf_free")],
                    sig="pe")
            evt.append(p.op(
                "vector",
                lambda v, o=ts(xn_fm, c * TILE, TILE),
                       a=pbf[:, half * 512:half * 512 + 512]:
                    v.tensor_copy(o, a), after=[tl], sig="ve"))
        al["pbf_free"] = evt[-1]
        al["xn_tm_free"] = p.now("pe")
        xn_fm_done = evt[-1]

        for g in range(NG):
            ksl = g % 2
            for n in range(2):
                for c in range(NCF):
                    p.op("tensor",
                         lambda te, o=kps[:, n * 512:(n + 1) * 512],
                                l=ts(xn_fm, c * TILE + g * 128, 128),
                                r=ts(WkT, c * D + n * 512, 512),
                                st=(c == 0), sp=(c == NCF - 1):
                             te.matmul(o, l, r, start=st, stop=sp,
                                       skip_group_check=True),
                         after=[evt[c]], sig="pe")
            k_done = p.now("pe")
            for n in range(2):
                for c in range(NCF):
                    p.op("tensor",
                         lambda te, o=vps[:, n * 512:(n + 1) * 512],
                                l=ts(xn_fm, c * TILE + g * 128, 128),
                                r=ts(WvT, c * D + n * 512, 512),
                                st=(c == 0), sp=(c == NCF - 1):
                             te.matmul(o, l, r, start=st, stop=sp,
                                       skip_group_check=True),
                         after=[evt[c], phiq_ps_tok.get(("q", 1))], sig="pe")
            v_done = p.now("pe")
            cq = 2 * g
            for cp in range(NCF):
                p.op("tensor",
                     lambda te, o=pb4t[:, :], l=ts(WqT, cp * D + cq * 128, 128),
                            r=ts(xn_fm, cp * TILE, TILE), st=(cp == 0),
                            sp=(cp == NCF - 1):
                         te.matmul(o, l, r, start=st, stop=sp,
                                   skip_group_check=True),
                     after=[xn_fm_done, phiq_ps_tok.get(("q", 0)), bias_built],
                     sig="pe")
            q_even = p.now("pe")
            tt = p.op("vector", lambda v, o=ts(scr, 0, D), a=kps[:, 0:1024], b=bkb[:, :]:
                      v.scalar_tensor_tensor(o, a, 1.0, b, op0=ALU.mult, op1=ALU.add),
                      after=[k_done, bias_built], sig="ve")
            p.op("vector", lambda v, o=ts(scr, D, D), a=ts(scr, 0, D):
                 v.tensor_scalar(o, a, 0.0, None, op0=ALU.max), sig="ve")
            m_t = p.op("vector", lambda v, o=ts(scr, 0, D), a=ts(scr, 0, D):
                       v.tensor_scalar(o, a, 0.0, None, op0=ALU.min), sig="ve")
            e_t = p.op("scalar", lambda s, o=ts(scr, 0, D), a=ts(scr, 0, D):
                       s.activation(o, a, AF.Exp), after=[m_t], sig="ac")
            kk = k_tm[:, ksl * D:(ksl + 1) * D]
            kphi = p.op("vector", lambda v, o=kk, a=ts(scr, D, D), b=ts(scr, 0, D):
                        v.tensor_add(o, a, b), after=[e_t], sig="ve")
            vsl = v_ag[:, ksl * VAGW:ksl * VAGW + NH * 65]
            vev = p.op(
                "vector",
                lambda v, o=vsl.rearrange("p (h e) -> p h e", e=65)[:, :, 0:64],
                       a=vps[:, 0:1024].rearrange("p (h e) -> p h e", e=64),
                       b=bvb[:, :].rearrange("p (h e) -> p h e", e=64):
                    v.scalar_tensor_tensor(o, a, 1.0, b, op0=ALU.mult, op1=ALU.add),
                after=[v_done, bias_built], sig="ve")
            cq = 2 * g + 1
            for cp in range(NCF):
                p.op("tensor",
                     lambda te, o=vps[:, 0:512], l=ts(WqT, cp * D + cq * 128, 128),
                            r=ts(xn_fm, cp * TILE, TILE), st=(cp == 0),
                            sp=(cp == NCF - 1):
                         te.matmul(o, l, r, start=st, stop=sp,
                                   skip_group_check=True),
                     after=[vev, phiq_ps_tok.get(("q", 1))], sig="pe")
            q_odd = p.now("pe")
            for qi, (qbank, q_tok, cqq) in enumerate(
                    (((pb4t[:, :]), q_even, 2 * g), ((vps[:, 0:512]), q_odd, 2 * g + 1))):
                p.op("vector", lambda v, o=ts(scr, 1024, 512), a=qbank,
                     sa=bqfm[:, cqq:cqq + 1]:
                     v.tensor_scalar(o, a, sa, 0.0, op0=ALU.add, op1=ALU.max),
                     after=[q_tok, bq_tok], sig="ve")
                m2_ = p.op("vector", lambda v, o=ts(scr, 1536, 512), a=qbank,
                           sa=bqfm[:, cqq:cqq + 1]:
                           v.tensor_scalar(o, a, sa, 0.0, op0=ALU.add, op1=ALU.min),
                           sig="ve")
                e2_ = p.op("scalar", lambda s, o=ts(scr, 1536, 512), a=ts(scr, 1536, 512):
                           s.activation(o, a, AF.Exp), after=[m2_], sig="ac")
                ph = p.op("vector", lambda v, o=ts(phiq, cqq * NTOK + t * TILE, TILE),
                          a=ts(scr, 1024, 512), b=ts(scr, 1536, 512):
                          v.tensor_add(o, a, b), after=[e2_], sig="ve")
                phiq_ps_tok[("q", qi)] = ph
            for h in range(NH):
                bank = pb5t if h < 8 else pb6t
                q_ = (h % 8) // 2
                par = (h % 2) * 64
                p.op("tensor",
                     lambda te, o=bank[par:par + 64, q_ * 65:q_ * 65 + 65],
                            l=kk[:, h * HD:(h + 1) * HD],
                            r=vsl[:, h * 65:(h + 1) * 65],
                            sp=(t == NT - 1 and g == NG - 1),
                            tp=(0, par):
                         te.matmul(o, l, r, start=False, stop=sp, tile_position=tp,
                                   skip_group_check=True),
                     after=[kphi, vev], sig="pe")
            if g == 1 and t + 1 < NT:
                emit_ln(t + 1)
        phiq_tile_tok[t] = p.now("ve")

        n_jobs = len(prep_jobs) - prep_done if t == NT - 1 else min(6, len(prep_jobs) - prep_done)
        for _ in range(n_jobs):
            emit_prep(prep_done)
            prep_done += 1

    kv_done = p.now("pe")
    ev1 = p.op("vector", lambda v: v.tensor_copy(kv_sb[:, 0:260], pb[5][:, 0:260]),
               after=[kv_done], sig="ve")
    ev2 = p.op("vector", lambda v: v.tensor_copy(kv_sb[:, 260:520], pb[6][:, 0:260]),
               sig="ve")
    if dbg:
        p.op("gpsimd", lambda g: g.dma_start(out=dbg_kvloc.ap(), in_=kv_sb[:, :]),
             after=[ev1, ev2], sig="sg", amt=16)
        p.op("gpsimd", lambda g: g.dma_start(out=dbg_phiq.ap(), in_=phiq[:, :]),
             after=[phiq_tile_tok[NT - 1]], sig="sg", amt=16)
        p.op("gpsimd", lambda g: g.dma_start(out=dbg_ktm.ap(), in_=k_tm[:, :]),
             after=[kv_done], sig="sg", amt=16)
        p.op("gpsimd", lambda g: g.dma_start(out=dbg_vag.ap(), in_=v_ag[:, :]),
             after=[kv_done], sig="sg", amt=16)
        p.op("gpsimd", lambda g: g.dma_start(out=dbg_xnfm.ap(), in_=xn_fm[:, :]),
             after=[kv_done], sig="sg", amt=16)
        p.op("gpsimd", lambda g: g.dma_start(out=dbg_xntm.ap(), in_=xn_tm[:, :]),
             after=[kv_done], sig="sg", amt=16)
        p.op("gpsimd", lambda g: g.dma_start(out=dbg_rstd.ap(), in_=rstd1[:, :]),
             after=[kv_done], sig="sg", amt=16)
        p.op("gpsimd", lambda g: g.dma_start(out=dbg_mv1.ap(), in_=mv1[:, :]),
             after=[kv_done], sig="sg", amt=16)
        for bi, bn in enumerate(("q", "k", "v", "o")):
            p.op("gpsimd", lambda g, o=dbg_brow.ap()[0:1, bi * D:(bi + 1) * D],
                 a=brow[bn][:, :]: g.dma_start(out=o, in_=a),
                 after=[kv_done], sig="sg", amt=16)
        p.op("gpsimd", lambda g: g.dma_start(out=dbg_wkt.ap(), in_=WkT[:, :]),
             after=[kv_done], sig="sg", amt=16)
    dbg_dump_tok = p.now("sg")
    mtoks = []
    for s_ in range(4):
        mtoks.append(p.op(
            "vector", lambda v, o=ts(scr, s_ * 520, 520), m=smask[:, s_:s_ + 1]:
                v.tensor_scalar(o, kv_sb[:, :], m, None, op0=ALU.mult),
            after=[ev1, ev2, smask_tok], sig="ve"))
    d1 = p.op("gpsimd", lambda g: g.dma_start(out=cc_kv_in.ap(), in_=ts(scr, 0, 2080)),
              after=[mtoks[-1]], sig="sg", amt=16)
    cc1 = p.op("gpsimd", lambda g: g.collective_compute(
        "AllReduce", ALU.add, replica_groups=[list(range(8))],
        ins=[cc_kv_in.ap()], outs=[cc_kv_out.ap()]), after=[d1], sig="sc")
    d2a = p.op("gpsimd", lambda g: g.dma_start(out=ts(scr, 0, 2080), in_=cc_kv_out.ap()),
               after=[cc1], sig="sg", amt=16)
    d2 = p.op("vector", lambda v: v.tensor_scalar(kv_sb[:, :], ts(scr, 0, 520),
                                                  smask[:, 0:1], None, op0=ALU.mult),
              after=[d2a], sig="ve")
    for s_ in range(1, 4):
        d2 = p.op("vector", lambda v, a=ts(scr, s_ * 520, 520), m=smask[:, s_:s_ + 1]:
                  v.scalar_tensor_tensor(kv_sb[:, :], a, m, kv_sb[:, :],
                                         op0=ALU.mult, op1=ALU.add),
                  after=[d2], sig="ve")

    esA.close()
    esB = ExitStack()
    epair = esB.enter_context(nc.sbuf_tensor("epairt", [16, NCF * 128], BF16))
    kv_bd = esB.enter_context(nc.sbuf_tensor("kv_bd", [128, 8 * 128], BF16))
    ksum2 = esB.enter_context(nc.sbuf_tensor("ksum2", [128, NCF * 16], BF16))
    nrm_sb = esB.enter_context(nc.sbuf_tensor("nrm_sb", [16, TILE], F32))
    rec_sb = esB.enter_context(nc.sbuf_tensor("rec_sb", [16, TILE], F32))
    rec_bf = esB.enter_context(nc.sbuf_tensor("rec_bf", [16, TILE], BF16))
    rbc_sb = esB.enter_context(nc.sbuf_tensor("rbc_sb", [128, TILE], F32))
    qkv_fm_sb = esB.enter_context(nc.sbuf_tensor("qkv_fm", [128, NCF * TILE], BF16))
    x1_sb = esB.enter_context(nc.sbuf_tensor("x1_sb", [128, NG * D], F32))
    epair_tok = p.op("sync", lambda s: s.dma_start(out=epair[:, :], in_=epair_in.ap()),
                     after=[kv_done, prep["stores"][-1], d2, dbg_dump_tok], sig="sx", amt=16)

    if dbg:
        p.op("gpsimd", lambda g: g.dma_start(out=dbg_kv.ap(), in_=cc_kv_out.ap()),
             after=[cc1], sig="sg", amt=16)
    bd0 = p.op("vector", lambda v: v.memset(kv_bd[:, :], 0.0),
               after=[d2, kv_done, prep["stores"][-1], dbg_dump_tok], sig="ve")
    p.op("vector", lambda v: v.memset(ksum2[:, :], 0.0), sig="ve")
    kvbd_done = None
    for par in (0, 64):
        for b in range(2):
            p.op("vector",
                 lambda v, o=kv_bd[par:par + 64, 512 * b:512 * b + 512]
                          .rearrange("p (q e) -> p q e", e=128)[:, :, par:par + 64],
                        a=kv_sb[par:par + 64, 260 * b:260 * b + 260]
                          .rearrange("p (q e) -> p q e", e=65)[:, :, 0:64]:
                     v.tensor_copy(o, a), after=[bd0], sig="ve")
    for par, off in ((0, 0), (64, 1)):
        for b in range(2):
            kvbd_done = p.op(
                "vector",
                lambda v, o=bass.AP(ksum2, 72 * b + off, [[NCF * 16, 64], [18, 4]],
                                    base_partition=par)
                       if False else
                       ksum2[par:par + 64, 72 * b + off:72 * b + off + 55:18],
                       a=kv_sb[par:par + 64, 260 * b + 64:260 * b + 260:65]:
                    v.tensor_copy(o, a), sig="ve")

    bl = {"xfree": al.get("xfree"), "gsc_free": lnst.get("gsc_free")}
    for t in range(NT):
        xin = tok_ap(x_in, t)
        ldx = p.op("sync", lambda s, o=sb3(x_t), a=xin: s.dma_start(out=o, in_=a),
                   after=[bl.get("xfree")], sig="sx", amt=16)

        for c in range(NCF):
            p.op("tensor", lambda te, o=pb[4][0:16, :], l=ts(ksum2, c * 16, 16),
                 r=ts(phiq, c * NTOK + t * TILE, TILE), st=(c == 0), sp=(c == NCF - 1):
                 te.matmul(o, l, r, start=st, stop=sp, skip_group_check=True),
                 after=[kvbd_done, bl.get("nrm_ps"), phiq_tile_tok[t],
                        phiq_ps_tok.get(id(pb[4])) if t == 0 else None], sig="pe")
        nm = p.now("pe")
        n1 = p.op("vector", lambda v: v.tensor_scalar(nrm_sb[:, :], pb[4][0:16, :],
                                                      ATTN_EPS, None, op0=ALU.max),
                  after=[nm], sig="ve")
        bl["nrm_ps"] = n1
        n2 = p.op("vector", lambda v: v.reciprocal(rec_sb[:, :], nrm_sb[:, :]),
                  after=[n1, bl.get("rec_free")], sig="ve")
        n2 = p.op("vector", lambda v: v.tensor_copy(rec_bf[:, :], rec_sb[:, :]),
                  after=[n2], sig="ve")

        mm2 = None
        for pr in range(NCF):
            bank = pb[pr % 3]
            mm1 = p.op("tensor", lambda te, o=bank[:, :], l=ts(kv_bd, pr * 128, 128),
                       r=ts(phiq, pr * NTOK + t * TILE, TILE):
                       te.matmul(o, l, r, start=True, stop=True, skip_group_check=True),
                       after=[kvbd_done, bl.get(("qps", pr % 3)),
                              phiq_ps_tok.get(id(pb[0])) if t == 0 and pr % 3 == 0 else None],
                       sig="pe")
            mm2 = p.op("tensor", lambda te, o=pb[3][:, :], l=ts(epair, pr * 128, 128),
                       r=rec_bf[:, :]:
                       te.matmul(o, l, r, start=True, stop=True, skip_group_check=True),
                       after=[n2, bl.get("bc_ps"),
                              epair_tok if t == 0 and pr == 0 else None], sig="pe")
            bc = p.op("scalar", lambda s, o=rbc_sb[:, :], a=pb[3][:, :]:
                      s.activation(o, a, AF.Copy),
                      after=[mm2, bl.get("rbc_free")], sig="ac")
            bl["bc_ps"] = bc
            ev = p.op("vector", lambda v, o=ts(qkv_fm_sb, pr * TILE, TILE), a=bank[:, :],
                      b=rbc_sb[:, :]:
                      v.scalar_tensor_tensor(o, a, 1.0, b, op0=ALU.mult, op1=ALU.mult),
                      after=[mm1, bc], sig="ve")
            bl[("qps", pr % 3)] = ev
            bl["rbc_free"] = ev
        bl["rec_free"] = mm2
        qkv_done = p.now("ve")

        evbank = {}
        for g in range(NG):
            for n in range(2):
                bank = pb[(2 * g + n) % 4]
                for c in range(NCF):
                    p.op("tensor", lambda te, o=bank[:, :],
                         l=ts(qkv_fm_sb, c * TILE + g * 128, 128),
                         r=ts(WoT, c * D + n * 512, 512), st=(c == 0):
                         te.matmul(o, l, r, start=st, stop=False, skip_group_check=True),
                         after=[qkv_done, wT_tok["wo"], bl.get(("ops", (2 * g + n) % 4))],
                         sig="pe")
                om = p.op("tensor", lambda te, o=bank[:, :], l=onesc[0:1, 0:128],
                          r=brow["o"][0:1, n * 512:(n + 1) * 512]:
                          te.matmul(o, l, r, start=False, stop=True, skip_group_check=True),
                          after=[brow_tok], sig="pe")
                ev = p.op("vector", lambda v, o=ts(x1_sb, g * D + n * 512, 512),
                          a=bank[:, :], b=ts(x_t, g * D + n * 512, 512):
                          v.scalar_tensor_tensor(o, a, 1.0, b, op0=ALU.mult, op1=ALU.add),
                          after=[om, ldx, bl.get("x1free")], sig="ve")
                bl[("ops", (2 * g + n) % 4)] = ev
                evbank[(g, n)] = ev
        bl["xfree"] = p.now("ve")
        for b in range(3):
            bl[("qps", b)] = p.now("ve")

        for g in range(NG):
            col = 2 * (t * NG + g)
            sqt = p.op("gpsimd", lambda g, o=gsc[:, :], a=ts(x1_sb, g * D, D):
                       g.tensor_mul(o, a, a),
                       after=[evbank[(g, 0)], evbank[(g, 1)], bl.get("gsc_free")],
                       sig="gp")
            p.op("vector", lambda v, o=tin2[:, 0:1], a=ts(x1_sb, g * D, D):
                 v.reduce_sum(o, a, axis=AX), after=[evbank[(g, 0)], evbank[(g, 1)]],
                 sig="ve")
            bl["gsc_free"] = p.op("vector", lambda v, o=tin2[:, 1:2], a=gsc[:, :]:
                                  v.reduce_sum(o, a, axis=AX), after=[sqt], sig="ve")
            p.op("vector", lambda v, o=mv2[:, col:col + 1], a=tin2[:, 0:1]:
                 v.tensor_scalar(o, a, 1.0 / D, None, op0=ALU.mult), sig="ve")
            p.op("vector", lambda v, o=tin2[:, 2:3], a=tin2[:, 1:2]:
                 v.tensor_scalar(o, a, 1.0 / D, None, op0=ALU.mult), sig="ve")
            p.op("vector", lambda v, o=tin2[:, 3:4], a=mv2[:, col:col + 1]:
                 v.tensor_mul(o, a, a), sig="ve")
            aggr_last = p.op("vector", lambda v, o=mv2[:, col + 1:col + 2],
                             a=tin2[:, 2:3], b=tin2[:, 3:4]:
                 v.tensor_sub(o, a, b), sig="ve")
            ag = aggr_last
            sb_ = p.op("gpsimd", lambda g, o=ts(scr, 0, D), a=ts(x1_sb, g * D, D),
                       sa=mv2[:, col:col + 1]:
                       g.tensor_scalar(o, a, sa, None, op0=ALU.subtract),
                       after=[ag], sig="gp")
            p.op("vector", lambda v, o=xm2[:, t * NG + g:t * NG + g + 1],
                 a=ts(scr, 0, D):
                 v.reduce_max(o, a, axis=AX, apply_absolute_value=True),
                 after=[sb_], sig="ve")
        xm_done = p.now("ve")
        st = p.op("gpsimd", lambda g, o=tok_ap(x1_hbm, t), a=sb3(x1_sb):
                  g.dma_start(out=o, in_=a), after=[xm_done], sig="sg", amt=16)
        bl["x1free"] = st
        bl[("x1st", t)] = st

    t_eps = p.op("vector", lambda v, o=ts(scr, 0, 32), a=mv2[:, 1:64:2]:
                 v.tensor_scalar(o, a, LN_EPS, None, op0=ALU.add), sig="ve")
    t_ln = p.op("scalar", lambda s, o=ts(scr, 32, 32), a=ts(scr, 0, 32):
                s.activation(o, a, AF.Ln), after=[t_eps], sig="ac")
    t_rs2 = p.op("scalar", lambda s, o=ts(scr, 64, 32), a=ts(scr, 32, 32):
                 s.activation(o, a, AF.Exp, bias=0.0, scale=-0.5), sig="ac")
    t_xm = p.op("vector", lambda v, o=ts(scr, 96, 32), a=xm2[:, 0:32], b=ts(scr, 64, 32):
                v.tensor_mul(o, a, b), after=[t_rs2], sig="ve")
    t_mx = p.op("vector", lambda v, o=ts(tiny, 0, 1), a=ts(scr, 96, 32):
                v.reduce_max(o, a, axis=AX), after=[t_xm], sig="ve")

    d3 = p.op("gpsimd", lambda g: g.dma_start(out=cc_m1_in.ap()[0:1, :], in_=ts(tiny, 0, 1)),
              after=[t_mx], sig="sg", amt=16)
    cc2 = p.op("gpsimd", lambda g: g.collective_compute(
        "AllReduce", ALU.max, replica_groups=[list(range(8))],
        ins=[cc_m1_in.ap()], outs=[cc_m1_out.ap()]), after=[d3], sig="sc")
    bc1 = p.op("gpsimd", lambda g: g.dma_start(
        out=gmaxb[:, :], in_=bass.AP(cc_m1_out, 0, [[0, 128], [1, 128]])),
        after=[cc2], sig="sg", amt=16)
    gm1 = p.op("vector", lambda v: v.reduce_max(ts(tiny, 0, 1), gmaxb[:, :], axis=AX),
               after=[bc1], sig="ve")
    if dbg:
        p.op("gpsimd", lambda g: g.dma_start(out=dbg_m1.ap(), in_=cc_m1_out.ap()),
             after=[cc2], sig="sg", amt=16)
    bcx = p.op("gpsimd", lambda g: g.dma_start(
        out=ts(tiny, 1, 1), in_=bass.AP(clipx, 0, [[0, 128], [1, 1]])),
        sig="sg", amt=16)
    e1 = p.op("scalar", lambda s: s.activation(ts(tiny, 2, 1), ts(tiny, 1, 1),
                                               AF.Exp, bias=0.0, scale=-1.0),
              after=[bcx], sig="ac")
    e2 = p.op("vector", lambda v: v.tensor_scalar(ts(tiny, 2, 1), ts(tiny, 2, 1),
                                                  1.0, 127.0, op0=ALU.add, op1=ALU.mult),
              after=[e1], sig="ve")
    e3 = p.op("vector", lambda v: v.reciprocal(ts(tiny, 3, 1), ts(tiny, 0, 1)),
              after=[gm1], sig="ve")
    e4 = p.op("vector", lambda v: v.tensor_mul(scl[:, 0:1], ts(tiny, 2, 1), ts(tiny, 3, 1)),
              after=[e2, e3], sig="ve")
    e5 = p.op("vector", lambda v: v.reciprocal(ts(tiny, 3, 1), ts(tiny, 2, 1)), sig="ve")
    e6 = p.op("vector", lambda v: v.tensor_mul(scl[:, 1:2], ts(tiny, 0, 1), ts(tiny, 3, 1)),
              sig="ve")
    e7 = p.op("vector", lambda v: v.tensor_scalar(a2c[:, 0:32], ts(scr, 64, 32),
                                                  scl[:, 0:1], None, op0=ALU.mult),
              after=[e4], sig="ve")
    e8 = p.op("vector", lambda v: v.scalar_tensor_tensor(b2c[:, 0:32], mv2[:, 0:64:2],
                                                         -1.0, a2c[:, 0:32],
                                                         op0=ALU.mult, op1=ALU.mult),
              after=[e7], sig="ve")
    e9 = p.op("vector", lambda v: v.tensor_scalar(sg_all[:, :], wgs[:, :], scl[:, 1:2],
                                                  None, op0=ALU.mult),
              after=[e6, prep["stores"][NCI - 1]], sig="ve")
    scales_done = p.op("vector", lambda v: v.tensor_scalar(su_all[:, :], wus[:, :],
                                                           scl[:, 1:2], None, op0=ALU.mult),
                       after=[prep["stores"][-1]], sig="ve")

    b_pe_done = p.now("pe")
    esB.close()
    esPhi.close()
    esWo.close()
    esC = ExitStack()
    wgT = esC.enter_context(nc.sbuf_tensor("wgT", [128, NCF * FF], BF16))
    wuT = esC.enter_context(nc.sbuf_tensor("wuT", [128, NCF * FF], BF16))
    xq_tm = esC.enter_context(nc.sbuf_tensor("xq_tm", [128, NG * D], BF16))
    xq_fm = esC.enter_context(nc.sbuf_tensor("xq_fm", [128, NCF * TILE], BF16))
    g_sb = esC.enter_context(nc.sbuf_tensor("g_sb", [128, 512], F32))
    u_sb = esC.enter_context(nc.sbuf_tensor("u_sb", [128, 512], F32))
    h_sb = [esC.enter_context(nc.sbuf_tensor(f"h_sb{i}", [128, 512], F32)) for i in range(2)]
    wdstg = esC.enter_context(nc.sbuf_tensor("wdstg", [128, FF], F32))
    wdq_sb = esC.enter_context(nc.sbuf_tensor("wdq_sb", [128, FF], BF16))
    dum = esC.enter_context(nc.sbuf_tensor("dum", [128, 512], F32))

    wgT_tok = []
    wuT_tok = []
    for c in range(NCF):
        wgT_tok.append(p.op(
            "sync", lambda s, o=ts(wgT, c * FF, FF), i=wg_q.ap()[:, c * 128:(c + 1) * 128]:
                s.dma_start_transpose(o, i),
            after=[prep["stores"][NCI - 1], b_pe_done, bl[("x1st", NT - 1)],
                   scales_done], sig="sx", amt=16))
        wuT_tok.append(p.op(
            "sync", lambda s, o=ts(wuT, c * FF, FF), i=wu_q.ap()[:, c * 128:(c + 1) * 128]:
                s.dma_start_transpose(o, i),
            after=[prep["stores"][-1], b_pe_done, scales_done], sig="sx", amt=16))

    wdp = {"wstg_free": None, "wq_free": None, "stores": [],
           "extra": [b_pe_done, scales_done, bl[("x1st", NT - 1)]]}

    def emit_wd_prep(i):
        ld = p.op("sync", lambda s, o=wdstg[:, :], a=wdd.ap()[i * 128:(i + 1) * 128, :]:
                  s.dma_start(out=o, in_=a),
                  after=[wdp["wstg_free"]] + wdp.pop("extra", []), sig="sx", amt=16)
        rd = p.op("vector", lambda v, o=wds[:, i:i + 1], a=wdstg[:, :]:
                  v.reduce_max(o, a, axis=AX, apply_absolute_value=True),
                  after=[ld], sig="ve")
        sc = p.op("vector", lambda v, o=wds[:, i:i + 1], a=wds[:, i:i + 1]:
                  v.tensor_scalar(o, a, 1.0 / 127.0, None, op0=ALU.mult),
                  after=[rd], sig="ve")
        rc = p.op("vector", lambda v, o=tin2[:, 7:8], a=wds[:, i:i + 1]:
                  v.reciprocal(o, a), after=[sc], sig="ve")
        t1 = p.op("gpsimd", lambda g, o=wdstg[:, :], a=wdstg[:, :], sa=tin2[:, 7:8]:
                  g.tensor_scalar(o, a, sa, None, op0=ALU.mult), after=[rc], sig="gp")
        t2 = p.op("gpsimd", lambda g, o=wdstg[:, :], a=wdstg[:, :]:
                  g.tensor_scalar(o, a, MAGIC, -MAGIC, op0=ALU.add, op1=ALU.add),
                  after=[t1], sig="gp")
        t3 = p.op("gpsimd", lambda g, o=wdstg[:, :], a=wdstg[:, :]:
                  g.tensor_scalar(o, a, -128.0, 127.0, op0=ALU.max, op1=ALU.min),
                  after=[t2], sig="gp")
        t5 = p.op("gpsimd", lambda g, o=wdq_sb[:, :], a=wdstg[:, :], sa=wds[:, i:i + 1]:
                  g.tensor_scalar(o, a, sa, None, op0=ALU.mult),
                  after=[t3, wdp["wq_free"]], sig="gp")
        st = p.op("gpsimd", lambda g, o=wd_q.ap()[i * 128:(i + 1) * 128, :], a=wdq_sb[:, :]:
                  g.dma_start(out=o, in_=a), after=[t5], sig="sg", amt=16)
        wdp["wstg_free"] = t5
        wdp["wq_free"] = st
        wdp["stores"].append(st)

    p.op("scalar", lambda s: load_act_set(s, 18), sig="ac")
    cl = {"xfree": bl["xfree"]}
    hst_tok = {}
    for t in range(NT):
        xin = tok_ap(x1_hbm, t)
        ldx = p.op("sync", lambda s, o=sb3(x_t), a=xin: s.dma_start(out=o, in_=a),
                   after=[cl.get("xfree"), bl[("x1st", t)]], sig="sx", amt=16)

        xq_toks = []
        for g in range(NG):
            gc = t * NG + g
            q1 = p.op("scalar", lambda s, o=ts(scr, 0, D), a=ts(x_t, g * D, D),
                      sa=a2c[:, gc:gc + 1], ba=b2c[:, gc:gc + 1]:
                      s.activation(o, a, AF.Identity, bias=ba, scale=sa),
                      after=[ldx, e8, xq_toks[-1] if g else cl.get("scr_free"),
                             cl.get("xq_tm_free"),
                             b_pe_done if t == 0 and g == 0 else None], sig="ac")
            q2 = p.op("vector", lambda v, o=ts(scr, 0, D), a=ts(scr, 0, D):
                      v.tensor_scalar(o, a, -128.49999, 127.49998,
                                      op0=ALU.max, op1=ALU.min),
                      after=[q1], sig="ve")
            xq_toks.append(p.op("gpsimd", lambda g, o=ts(xq_tm, g * D, D), a=ts(scr, 0, D):
                                g.tensor_scalar(o, a, MAGIC, -MAGIC,
                                                op0=ALU.add, op1=ALU.add),
                                after=[q2], sig="gp"))
        cl["xfree"] = xq_toks[-1]
        cl["scr_free"] = xq_toks[-1]

        evt = []
        for c in range(NCF):
            half = c % 2
            tl = None
            for g in range(NG):
                tl = p.op(
                    "tensor",
                    lambda te, o=pbf[:, half * 512 + g * 128:half * 512 + (g + 1) * 128],
                           a=ts(xq_tm, g * D + c * 128, 128):
                        te.transpose(o, a, ident[:, :]),
                    after=[xq_toks[-1], evt[c - 1] if c >= 1 else cl.get("pbf_free")],
                    sig="pe")
            evt.append(p.op(
                "vector",
                lambda v, o=ts(xq_fm, c * TILE, TILE),
                       a=pbf[:, half * 512:half * 512 + 512]:
                    v.tensor_copy(o, a), after=[tl], sig="ve"))
        cl["pbf_free"] = evt[-1]
        cl["xq_tm_free"] = p.now("pe")

        for i in range(NCI):
            bg = pb[2 * (i % 3)]
            bu = pb[2 * (i % 3) + 1]
            for c in range(NCF):
                p.op("tensor", lambda te, o=bg[:, :], l=ts(wgT, c * FF + i * 128, 128),
                     r=ts(xq_fm, c * TILE, TILE), st=(c == 0), sp=(c == NCF - 1):
                     te.matmul(o, l, r, start=st, stop=sp, skip_group_check=True),
                     after=[evt[c], wgT_tok[c], cl.get(("gps", i % 3))], sig="pe")
            gmm = p.now("pe")
            for c in range(NCF):
                p.op("tensor", lambda te, o=bu[:, :], l=ts(wuT, c * FF + i * 128, 128),
                     r=ts(xq_fm, c * TILE, TILE), st=(c == 0), sp=(c == NCF - 1):
                     te.matmul(o, l, r, start=st, stop=sp, skip_group_check=True),
                     after=[wuT_tok[c]], sig="pe")
            umm = p.now("pe")
            hb = h_sb[i % 2]
            gv = p.op("scalar", lambda s, o=g_sb[:, :], a=bg[:, :], sa=sg_all[:, i:i + 1]:
                      s.activation(o, a, AF.Silu, bias=0.0, scale=sa),
                      after=[gmm, e9, scales_done, cl.get("g_free")], sig="ac")
            uv = p.op("scalar", lambda s, o=u_sb[:, :], a=bu[:, :], sa=su_all[:, i:i + 1]:
                      s.activation(o, a, AF.Copy, bias=0.0, scale=sa),
                      after=[umm], sig="ac")
            hm = p.op("gpsimd", lambda g, o=hb[:, :], a=g_sb[:, :], b=u_sb[:, :]:
                      g.tensor_mul(o, a, b), after=[gv, uv, cl.get(("hfree", i % 2))],
                      sig="gp")
            cl[("gps", i % 3)] = hm
            cl["g_free"] = hm
            ht = p.op("vector", lambda v, o=tiny[:, 6:7], a=hb[:, :]:
                      v.reduce_max(o, a, axis=AX, apply_absolute_value=True),
                      after=[hm], sig="ve")
            p.op("vector", lambda v, o=habs[:, i:i + 1], a=habs[:, i:i + 1], b=tiny[:, 6:7]:
                 v.tensor_max(o, a, b), after=[ht], sig="ve")
            st = p.op("gpsimd", lambda g, o=h_hbm.ap()[i, :, t * TILE:(t + 1) * TILE],
                      a=hb[:, :]:
                      g.dma_start(out=o, in_=a), after=[hm], sig="sg", amt=16)
            cl[("hfree", i % 2)] = [st, ht]
            hst_tok[(i, t)] = st
        if t < NCF:
            emit_wd_prep(t)

    hmx = p.op("vector", lambda v: v.reduce_max(ts(tiny, 0, 1), habs[:, :], axis=AX),
               sig="ve")
    d4 = p.op("gpsimd", lambda g: g.dma_start(out=cc_m2_in.ap()[0:1, :], in_=ts(tiny, 0, 1)),
              after=[hmx], sig="sg", amt=16)
    cc3 = p.op("gpsimd", lambda g: g.collective_compute(
        "AllReduce", ALU.max, replica_groups=[list(range(8))],
        ins=[cc_m2_in.ap()], outs=[cc_m2_out.ap()]), after=[d4], sig="sc")
    bc2 = p.op("gpsimd", lambda g: g.dma_start(
        out=gmaxb[:, :], in_=bass.AP(cc_m2_out, 0, [[0, 128], [1, 128]])),
        after=[cc3], sig="sg", amt=16)
    gm2 = p.op("vector", lambda v: v.reduce_max(ts(tiny, 0, 1), gmaxb[:, :], axis=AX),
               after=[bc2], sig="ve")
    if dbg:
        p.op("gpsimd", lambda g: g.dma_start(out=dbg_m2.ap(), in_=cc_m2_out.ap()),
             after=[cc3], sig="sg", amt=16)
    bcd = p.op("gpsimd", lambda g: g.dma_start(
        out=ts(tiny, 1, 1), in_=bass.AP(clipd, 0, [[0, 128], [1, 1]])),
        sig="sg", amt=16)
    p.op("scalar", lambda s: load_act_set(s, 6), sig="ac")
    f1 = p.op("scalar", lambda s: s.activation(ts(tiny, 2, 1), ts(tiny, 1, 1),
                                               AF.Exp, bias=0.0, scale=-1.0),
              after=[bcd], sig="ac")
    f2 = p.op("vector", lambda v: v.tensor_scalar(ts(tiny, 2, 1), ts(tiny, 2, 1),
                                                  1.0, 127.0, op0=ALU.add, op1=ALU.mult),
              after=[f1], sig="ve")
    f3 = p.op("vector", lambda v: v.reciprocal(ts(tiny, 3, 1), ts(tiny, 0, 1)),
              after=[gm2], sig="ve")
    p.op("vector", lambda v: v.tensor_mul(scl[:, 2:3], ts(tiny, 2, 1), ts(tiny, 3, 1)),
         after=[f2, f3], sig="ve")
    f5 = p.op("vector", lambda v: v.reciprocal(ts(tiny, 3, 1), ts(tiny, 2, 1)), sig="ve")
    s2_done = p.op("vector", lambda v: v.tensor_mul(scl[:, 3:4], ts(tiny, 0, 1),
                                                    ts(tiny, 3, 1)),
                   after=[f5], sig="ve")

    c_pe_done = p.now("pe")
    c_ac_done = p.now("ac")
    esC.close()
    esD = ExitStack()
    wdT = esD.enter_context(nc.sbuf_tensor("wdT", [128, NCI * D], BF16))
    hq_sb = esD.enter_context(nc.sbuf_tensor("hq_sb", [128, NCI * 512], BF16))
    hstg = [esD.enter_context(nc.sbuf_tensor(f"hstg{i}", [128, 512], F32)) for i in range(2)]
    out_sb = [esD.enter_context(nc.sbuf_tensor(f"out_sb{i}", [128, NG * D], F32))
              for i in range(2)]

    wdT_tok = []
    for c in range(NCI):
        wdT_tok.append(p.op(
            "sync", lambda s, o=ts(wdT, c * D, D), i=wd_q.ap()[:, c * 128:(c + 1) * 128]:
                s.dma_start_transpose(o, i),
            after=[wdp["stores"][-1], c_pe_done, c_ac_done, s2_done,
                   hst_tok[(NCI - 1, NT - 1)]], sig="sx", amt=16))

    dl = {"xfree": cl["xfree"], "psum_free": c_ac_done}
    for t in range(NT):
        xin = tok_ap(x1_hbm, t)
        ldx1 = p.op("sync", lambda s, o=sb3(x_t), a=xin: s.dma_start(out=o, in_=a),
                    after=[dl.get("xfree")], sig="sx", amt=16)
        hq_toks = []
        for i in range(NCI):
            hb = hstg[i % 2]
            ld = p.op("sync", lambda s, o=hb[:, :],
                      a=h_hbm.ap()[i, :, t * TILE:(t + 1) * TILE]:
                      s.dma_start(out=o, in_=a),
                      after=[dl.get(("hfree", i % 2)), hst_tok[(i, t)],
                             c_pe_done if t == 0 and i == 0 else None],
                      sig="sx", amt=16)
            q1 = p.op("scalar", lambda s, o=ts(scr, 0, 512), a=hb[:, :], sa=scl[:, 2:3]:
                      s.activation(o, a, AF.Copy, bias=0.0, scale=sa),
                      after=[ld, s2_done,
                             hq_toks[-1] if i else dl.get("scr_free"),
                             dl.get("hq_free")], sig="ac")
            dl[("hfree", i % 2)] = q1
            q2 = p.op("vector", lambda v, o=ts(scr, 0, 512), a=ts(scr, 0, 512):
                      v.tensor_scalar(o, a, -128.49999, 127.49998,
                                      op0=ALU.max, op1=ALU.min),
                      after=[q1], sig="ve")
            hq_toks.append(p.op("gpsimd", lambda g, o=ts(hq_sb, i * 512, 512),
                                a=ts(scr, 0, 512):
                                g.tensor_scalar(o, a, MAGIC, -MAGIC,
                                                op0=ALU.add, op1=ALU.add),
                                after=[q2], sig="gp"))
        dl["scr_free"] = hq_toks[-1]

        osb = out_sb[t % 2]
        wave_banks = ((pb[0], pb[1], pb[2], pb[3]), (pb[4], pb[5], pb[6], pb[3]))
        for w, groups in enumerate(((0, 1), (2, 3))):
            for i in range(NCI):
                for gi, g in enumerate(groups):
                    for n in range(2):
                        bank = wave_banks[w][2 * gi + n]
                        p.op("tensor", lambda te, o=bank[:, :],
                             l=ts(hq_sb, i * 512 + g * 128, 128),
                             r=ts(wdT, i * D + n * 512, 512),
                             st=(i == 0), sp=(i == NCI - 1):
                             te.matmul(o, l, r, start=st, stop=sp, skip_group_check=True),
                             after=[hq_toks[i], wdT_tok[i],
                                    dl.get(("dps", id(bank))), dl.get("psum_free")],
                             sig="pe")
            wave_done = p.now("pe")
            for gi, g in enumerate(groups):
                for n in range(2):
                    bank = wave_banks[w][2 * gi + n]
                    ev = p.op("vector", lambda v, o=ts(osb, g * D + n * 512, 512),
                              a=bank[:, :], b=ts(x_t, g * D + n * 512, 512):
                              v.scalar_tensor_tensor(o, a, scl[:, 3:4], b,
                                                     op0=ALU.mult, op1=ALU.add),
                              after=[wave_done, ldx1, s2_done,
                                     dl.get(("outfree", t % 2))], sig="ve")
                    dl[("dps", id(bank))] = ev
        dl["psum_free"] = None
        ev_done = p.now("ve")
        dl["xfree"] = ev_done
        dl["hq_free"] = wave_done
        yst = p.op("gpsimd", lambda g, o=tok_ap(y_out, t), a=sb3(osb):
                   g.dma_start(out=o, in_=a), after=[ev_done], sig="sg", amt=16)
        dl[("outfree", t % 2)] = yst
        dl[("yst", t)] = yst

    p.op("gpsimd", lambda g: g.nop(), after=[dl[("yst", NT - 1)]])

    with nc.Block() as block:
        @block.sync
        def _(eng):
            p.emit("sync", eng, sems)

        @block.tensor
        def _(eng):
            p.emit("tensor", eng, sems)

        @block.vector
        def _(eng):
            p.emit("vector", eng, sems)

        @block.scalar
        def _(eng):
            p.emit("scalar", eng, sems)

        @block.gpsimd
        def _(eng):
            p.emit("gpsimd", eng, sems)

    esD.close()
    es.close()
    return nc



def make_inputs(inputs):
    import ml_dtypes
    x = np.asarray(inputs["L0_attn_x"])
    flat = np.ascontiguousarray(x.reshape(8, NTOK, D))
    ident = np.eye(128, dtype=ml_dtypes.bfloat16)
    ones = np.ones((1, TILE), dtype=ml_dtypes.bfloat16)
    epair = np.zeros((16, NCF * 128), dtype=ml_dtypes.bfloat16)
    for pr in range(NCF):
        for j in range(128):
            epair[2 * pr + (j // 64), pr * 128 + j] = 1.0
    common = {
        "Wq": np.ascontiguousarray(inputs["L0_attn_Wq"], dtype=np.float32),
        "Wk": np.ascontiguousarray(inputs["L0_attn_Wk"], dtype=np.float32),
        "Wv": np.ascontiguousarray(inputs["L0_attn_Wv"], dtype=np.float32),
        "Wo": np.ascontiguousarray(inputs["L0_attn_Wo"], dtype=np.float32),
        "bq": np.ascontiguousarray(inputs["L0_attn_bq"], dtype=np.float32).reshape(1, D),
        "bk": np.ascontiguousarray(inputs["L0_attn_bk"], dtype=np.float32).reshape(1, D),
        "bv": np.ascontiguousarray(inputs["L0_attn_bv"], dtype=np.float32).reshape(1, D),
        "bo": np.ascontiguousarray(inputs["L0_attn_bo"], dtype=np.float32).reshape(1, D),
        "wg": np.ascontiguousarray(inputs["L0_ffn_w_gate"], dtype=np.float32),
        "wu": np.ascontiguousarray(inputs["L0_ffn_w_up"], dtype=np.float32),
        "wd": np.ascontiguousarray(inputs["L0_ffn_w_down"], dtype=np.float32),
        "clipx": np.ascontiguousarray(inputs["L0_ffn_clip_x"], dtype=np.float32).reshape(1, 1),
        "clipd": np.ascontiguousarray(inputs["L0_ffn_clip_down"], dtype=np.float32).reshape(1, 1),
        "ident": ident, "onesc": ones, "epair": epair,
    }
    out = []
    for c in range(8):
        sm = np.zeros((128, 4), dtype=np.float32)
        sm[:, c // 2] = 1.0
        out.append(dict(common, x=np.ascontiguousarray(flat[c]), smask=sm))
    return out


def kernel(**inputs):
    in_maps = make_inputs(inputs)
    nc = build_nc(dbg=False)
    res = run_bass_kernel_spmd(nc, in_maps, core_ids=list(range(8)))
    y = np.concatenate([np.asarray(r["y"]) for r in res.results], axis=0)
    return y.reshape(4, 8192, 1024)


# version 08
# speedup vs baseline: 2.1921x; 2.1921x over previous
import numpy as np
from contextlib import ExitStack

import concourse.bass as bass
import concourse.mybir as mybir
from concourse.bass_utils import run_bass_kernel_spmd

F32 = mybir.dt.float32
BF16 = mybir.dt.bfloat16
AF = mybir.ActivationFunctionType
ALU = mybir.AluOpType
AX = mybir.AxisListType.X

MAGIC = 12582912.0
D = 1024
NH = 16
HD = 64
FF = 2816
NTOK = 4096
TILE = 512
NT = NTOK // TILE
NG = TILE // 128
NCF = D // 128
NCI = FF // 128
LN_EPS = 1e-5
ATTN_EPS = 1e-6
VAGW = NH * 65 + 15

ENGINES = ("sync", "tensor", "vector", "scalar", "gpsimd")
SELF_SERIAL = {"vector": "ve", "scalar": "ac", "gpsimd": "gp"}


NPOOL = 8


class Prog:

    def __init__(self):
        self.items = {e: [] for e in ENGINES}
        self.cnt = {}
        self.pool_rr = {"sx": 0, "sg": 0}
        self.pool_last = {}
        self.last_sc = None

    def _flat(self, after):
        out = []
        for t in after:
            if t is None:
                continue
            if isinstance(t, list):
                out.extend(x for x in t if x is not None)
            else:
                out.append(t)
        return out

    def op(self, eng, fn, after=(), sig=None, amt=1):
        after = self._flat(after)
        if sig in ("sx", "sg"):
            i = self.pool_rr[sig] % NPOOL
            self.pool_rr[sig] += 1
            sem = f"{sig}{i}"
            prev = self.pool_last.get(sem)
            if prev is not None:
                after = after + [prev]
            self.cnt[sem] = self.cnt.get(sem, 0) + amt
            tok = (sem, self.cnt[sem])
            self.pool_last[sem] = tok
            self.items[eng].append((after, fn, sem, amt, None))
            return tok
        if sig == "sc":
            if self.last_sc is not None:
                after = after + [self.last_sc]
        own = SELF_SERIAL.get(eng)
        if own is not None and sig is None:
            sig = own
        self_wait = None
        if own is not None:
            self_wait = (own, self.cnt.get(own, 0))
        tok = None
        if sig is not None:
            self.cnt[sig] = self.cnt.get(sig, 0) + amt
            tok = (sig, self.cnt[sig])
        if sig == "sc":
            self.last_sc = tok
        self.items[eng].append((after, fn, sig, amt, self_wait))
        return tok

    def now(self, sig):
        if sig in ("sx", "sg"):
            return [self.pool_last[s] for s in
                    (f"{sig}{i}" for i in range(NPOOL)) if s in self.pool_last]
        return (sig, self.cnt.get(sig, 0))

    def emit(self, eng_name, eng, sems):
        waited = {}
        for waits, fn, sig, amt, self_wait in self.items[eng_name]:
            allw = list(waits)
            if self_wait is not None and self_wait[1] > 0:
                allw.append(self_wait)
            for (s, v) in allw:
                if waited.get(s, 0) >= v:
                    continue
                eng.wait_ge(sems[s], v)
                waited[s] = v
            inst = fn(eng)
            if sig is not None:
                inst.then_inc(sems[sig], amt)


def build_nc(dbg=False):
    nc = bass.Bass("TRN2", target_bir_lowering=False, detect_race_conditions=True)
    dkind = "ExternalOutput" if dbg else "Internal"

    x_in = nc.dram_tensor("x", [NTOK, D], F32, kind="ExternalInput")
    Wq = nc.dram_tensor("Wq", [D, D], F32, kind="ExternalInput")
    Wk = nc.dram_tensor("Wk", [D, D], F32, kind="ExternalInput")
    Wv = nc.dram_tensor("Wv", [D, D], F32, kind="ExternalInput")
    Wo = nc.dram_tensor("Wo", [D, D], F32, kind="ExternalInput")
    bqd = nc.dram_tensor("bq", [1, D], F32, kind="ExternalInput")
    bkd = nc.dram_tensor("bk", [1, D], F32, kind="ExternalInput")
    bvd = nc.dram_tensor("bv", [1, D], F32, kind="ExternalInput")
    bod = nc.dram_tensor("bo", [1, D], F32, kind="ExternalInput")
    wgd = nc.dram_tensor("wg", [FF, D], F32, kind="ExternalInput")
    wud = nc.dram_tensor("wu", [FF, D], F32, kind="ExternalInput")
    wdd = nc.dram_tensor("wd", [D, FF], F32, kind="ExternalInput")
    clipx = nc.dram_tensor("clipx", [1, 1], F32, kind="ExternalInput")
    clipd = nc.dram_tensor("clipd", [1, 1], F32, kind="ExternalInput")
    ident_in = nc.dram_tensor("ident", [128, 128], BF16, kind="ExternalInput")
    ones_in = nc.dram_tensor("onesc", [1, TILE], BF16, kind="ExternalInput")
    epair_in = nc.dram_tensor("epair", [16, NCF * 128], BF16, kind="ExternalInput")
    smask_in = nc.dram_tensor("smask", [128, 4], F32, kind="ExternalInput")

    y_out = nc.dram_tensor("y", [NTOK, D], F32, kind="ExternalOutput")

    w_bf = {n: nc.dram_tensor(f"{n}_bf", [D, D], BF16, kind="Internal")
            for n in ("wq", "wk", "wv", "wo")}
    wg_q = nc.dram_tensor("wg_q", [FF, D], BF16, kind="Internal")
    wu_q = nc.dram_tensor("wu_q", [FF, D], BF16, kind="Internal")
    wd_q = nc.dram_tensor("wd_q", [D, FF], BF16, kind="Internal")
    x1_hbm = nc.dram_tensor("x1_hbm", [NTOK, D], F32, kind=dkind)
    h_hbm = nc.dram_tensor("h_hbm", [NCI, 128, NTOK], F32, kind="Internal")
    cc_kv_in = nc.dram_tensor("cc_kv_in", [128, 4 * 520], F32, kind="Internal")
    cc_kv_out = nc.dram_tensor("cc_kv_out", [128, 4 * 520], F32, kind="Internal")
    cc_m1_in = nc.dram_tensor("cc_m1_in", [1, 128], F32, kind="Internal")
    cc_m1_out = nc.dram_tensor("cc_m1_out", [1, 128], F32, kind="Internal")
    cc_m2_in = nc.dram_tensor("cc_m2_in", [1, 128], F32, kind="Internal")
    cc_m2_out = nc.dram_tensor("cc_m2_out", [1, 128], F32, kind="Internal")
    if dbg:
        dbg_kv = nc.dram_tensor("dbg_kv", [128, 4 * 520], F32, kind="ExternalOutput")
        dbg_m1 = nc.dram_tensor("dbg_m1", [1, 128], F32, kind="ExternalOutput")
        dbg_m2 = nc.dram_tensor("dbg_m2", [1, 128], F32, kind="ExternalOutput")
        dbg_kvloc = nc.dram_tensor("dbg_kvloc", [128, 520], F32, kind="ExternalOutput")
        dbg_phiq = nc.dram_tensor("dbg_phiq", [128, NCF * NTOK], BF16, kind="ExternalOutput")
        dbg_ktm = nc.dram_tensor("dbg_ktm", [128, 2 * D], BF16, kind="ExternalOutput")
        dbg_vag = nc.dram_tensor("dbg_vag", [128, 2 * VAGW], BF16, kind="ExternalOutput")
        dbg_xnfm = nc.dram_tensor("dbg_xnfm", [128, NCF * TILE], BF16, kind="ExternalOutput")
        dbg_wkt = nc.dram_tensor("dbg_wkt", [128, NCF * D], BF16, kind="ExternalOutput")
        dbg_brow = nc.dram_tensor("dbg_brow", [1, 4 * D], BF16, kind="ExternalOutput")
        dbg_vag0 = nc.dram_tensor("dbg_vag0", [128, 2 * VAGW], BF16, kind="ExternalOutput")
        dbg_xntm = nc.dram_tensor("dbg_xntm", [128, NG * D], BF16, kind="ExternalOutput")
        dbg_rstd = nc.dram_tensor("dbg_rstd", [128, NT * NG], F32, kind="ExternalOutput")
        dbg_mv1 = nc.dram_tensor("dbg_mv1", [128, 2 * NT * NG], F32, kind="ExternalOutput")

    p = Prog()
    es = ExitStack()
    sem_names = ["sc", "pe", "ve", "ac", "gp"] + \
        [f"sx{i}" for i in range(NPOOL)] + [f"sg{i}" for i in range(NPOOL)]
    sems = {n: es.enter_context(nc.semaphore(n)) for n in sem_names}

    ident = es.enter_context(nc.sbuf_tensor("identt", [128, 128], BF16))
    onesc = es.enter_context(nc.sbuf_tensor("onesct", [1, TILE], BF16))
    brow = {n: es.enter_context(nc.sbuf_tensor(f"brow_{n}", [1, D], BF16))
            for n in ("q", "k", "v", "o")}
    mv1 = es.enter_context(nc.sbuf_tensor("mv1", [128, 2 * NT * NG], F32))
    rstd1 = es.enter_context(nc.sbuf_tensor("rstd1", [128, NT * NG], F32))
    mb1 = es.enter_context(nc.sbuf_tensor("mb1", [128, NT * NG], F32))
    mv2 = es.enter_context(nc.sbuf_tensor("mv2", [128, 2 * NT * NG], F32))
    xm2 = es.enter_context(nc.sbuf_tensor("xm2", [128, NT * NG], F32))
    a2c = es.enter_context(nc.sbuf_tensor("a2c", [128, NT * NG], F32))
    b2c = es.enter_context(nc.sbuf_tensor("b2c", [128, NT * NG], F32))
    wgs = es.enter_context(nc.sbuf_tensor("wgs", [128, NCI], F32))
    wus = es.enter_context(nc.sbuf_tensor("wus", [128, NCI], F32))
    wds = es.enter_context(nc.sbuf_tensor("wds", [128, NCF], F32))
    sg_all = es.enter_context(nc.sbuf_tensor("sg_all", [128, NCI], F32))
    su_all = es.enter_context(nc.sbuf_tensor("su_all", [128, NCI], F32))
    scl = es.enter_context(nc.sbuf_tensor("scl", [128, 4], F32))
    gmaxb = es.enter_context(nc.sbuf_tensor("gmaxb", [128, 128], F32))
    kv_sb = es.enter_context(nc.sbuf_tensor("kv_sb", [128, 520], F32))
    habs = es.enter_context(nc.sbuf_tensor("habs", [128, NCI], F32))
    tiny = es.enter_context(nc.sbuf_tensor("tiny", [128, 8], F32))
    tin2 = es.enter_context(nc.sbuf_tensor("tin2", [128, 8], F32))
    smask = es.enter_context(nc.sbuf_tensor("smaskt", [128, 4], F32))
    zrow = es.enter_context(nc.sbuf_tensor("zrow", [1, 512], BF16))
    x_t = es.enter_context(nc.sbuf_tensor("x_t", [128, NG * D], F32))
    scr = es.enter_context(nc.sbuf_tensor("scr", [128, 2560], F32))
    gsc = es.enter_context(nc.sbuf_tensor("gsc", [128, 1024], F32))
    bqfm = es.enter_context(nc.sbuf_tensor("bqfm", [128, NCF], F32))
    bkb = es.enter_context(nc.sbuf_tensor("bkb", [128, D], BF16))
    bvb = es.enter_context(nc.sbuf_tensor("bvb", [128, D], BF16))

    kps = es.enter_context(nc.psum_tensor("kps", [128, 1024], F32))
    vps = es.enter_context(nc.psum_tensor("vps", [128, 1024], F32))
    pb4t = es.enter_context(nc.psum_tensor("pb4t", [128, 512], F32))
    pb5t = es.enter_context(nc.psum_tensor("pb5t", [128, 512], F32))
    pb6t = es.enter_context(nc.psum_tensor("pb6t", [128, 512], F32))
    pbf = es.enter_context(nc.psum_tensor("pbf", [128, 1024], BF16))
    pb = [kps[:, 0:512], kps[:, 512:1024], vps[:, 0:512], vps[:, 512:1024],
          pb4t[:, :], pb5t[:, :], pb6t[:, :]]

    def ts(tensor, c0, w):
        return tensor[:, c0:c0 + w]

    def tok_ap(dram, t):
        return bass.AP(dram, t * TILE * D, [[D, 128], [128 * D, NG], [1, D]])

    def sb3(sb):
        return sb.ap().rearrange("p (g f) -> p g f", f=D)

    cast_tok = {}
    for n, W in (("wk", Wk), ("wv", Wv), ("wq", Wq), ("wo", Wo)):
        cast_tok[n] = p.op("gpsimd",
                           lambda g, a=w_bf[n].ap(), b=W.ap(): g.dma_start(out=a, in_=b),
                           sig="sg", amt=16)
    for n, B in (("q", bqd), ("k", bkd), ("v", bvd), ("o", bod)):
        p.op("gpsimd", lambda g, a=brow[n][:, :], b=B.ap(): g.dma_start(out=a, in_=b),
             sig="sg", amt=16)
    brow_tok = p.now("sg")
    bq_tok = p.op("sync", lambda s: s.dma_start(
        out=bqfm[:, :], in_=bass.AP(bqd, 0, [[1, 128], [128, NCF]]),
        allow_slow_non_contiguous=True), sig="sx", amt=16)

    p.op("sync", lambda s: s.dma_start(out=ident[:, :], in_=ident_in.ap()), sig="sx", amt=16)
    onesc_tok = p.op("sync", lambda s: s.dma_start(out=onesc[:, :], in_=ones_in.ap()),
                     sig="sx", amt=16)
    smask_tok = p.op("sync", lambda s: s.dma_start(out=smask[:, :], in_=smask_in.ap()),
                     sig="sx", amt=16)

    esA = ExitStack()
    WkT = esA.enter_context(nc.sbuf_tensor("WkT", [128, NCF * D], BF16))
    WvT = esA.enter_context(nc.sbuf_tensor("WvT", [128, NCF * D], BF16))
    WqT = esA.enter_context(nc.sbuf_tensor("WqT", [128, NCF * D], BF16))
    xn_tm = esA.enter_context(nc.sbuf_tensor("xn_tm", [128, NG * D], BF16))
    xn_fm = esA.enter_context(nc.sbuf_tensor("xn_fm", [128, NCF * TILE], BF16))
    k_tm = esA.enter_context(nc.sbuf_tensor("k_tm", [128, 2 * D], BF16))
    v_ag = esA.enter_context(nc.sbuf_tensor("v_ag", [128, 2 * VAGW], BF16))
    wstg = esA.enter_context(nc.sbuf_tensor("wstg", [128, D], F32))
    wqnt = esA.enter_context(nc.sbuf_tensor("wqnt", [128, D], BF16))
    esWo = ExitStack()
    WoT = esWo.enter_context(nc.sbuf_tensor("WoT", [128, NCF * D], BF16, side="right"))
    esPhi = ExitStack()
    phiq = esPhi.enter_context(nc.sbuf_tensor("phiq", [128, NCF * NTOK], BF16, side="right"))

    wT_tok = {}
    for n, t_ in (("wk", WkT), ("wv", WvT), ("wq", WqT), ("wo", WoT)):
        for c in range(NCF):
            wT_tok[n] = p.op(
                "sync",
                lambda s, o=t_[:, c * D:(c + 1) * D],
                       i=w_bf[n].ap()[:, c * 128:(c + 1) * 128]:
                    s.dma_start_transpose(o, i),
                after=[cast_tok[n]], sig="sx", amt=16)

    p.op("vector", lambda v: v.memset(v_ag[:, :], 0.0), sig="ve")
    for sl in range(2):
        for h in range(NH):
            cpos = sl * VAGW + h * 65 + 64
            p.op("vector", lambda v, c=cpos: v.memset(v_ag[:, c:c + 1], 1.0), sig="ve")
    p.op("vector", lambda v: v.memset(habs[:, :], 0.0), sig="ve")

    prep_jobs = [(wgd, wg_q, wgs, i) for i in range(NCI)] + \
                [(wud, wu_q, wus, i) for i in range(NCI)]
    prep = {"ld": None, "wstg_free": None, "wqnt_free": None, "stores": []}

    def emit_prep(j):
        src, dst, wsc, i = prep_jobs[j]
        ld = p.op("sync", lambda s, o=wstg[:, :], a=src.ap()[i * 128:(i + 1) * 128, :]:
                  s.dma_start(out=o, in_=a), after=[prep["wstg_free"]], sig="sx", amt=16)
        rd = p.op("vector", lambda v, o=wsc[:, i:i + 1], a=wstg[:, :]:
                  v.reduce_max(o, a, axis=AX, apply_absolute_value=True),
                  after=[ld], sig="ve")
        sc = p.op("vector", lambda v, o=wsc[:, i:i + 1], a=wsc[:, i:i + 1]:
                  v.tensor_scalar(o, a, 1.0 / 127.0, None, op0=ALU.mult),
                  after=[rd], sig="ve")
        rc = p.op("vector", lambda v, o=tin2[:, 6:7], a=wsc[:, i:i + 1]:
                  v.reciprocal(o, a), after=[sc], sig="ve")
        t1 = p.op("scalar", lambda s, o=wstg[:, :], a=wstg[:, :], sa=tin2[:, 6:7]:
                  s.activation(o, a, AF.Copy, bias=0.0, scale=sa),
                  after=[rc], sig="ac")
        t2a = p.op("scalar", lambda s, o=wstg[:, :], a=wstg[:, :]:
                   s.activation(o, a, AF.Copy, bias=MAGIC, scale=1.0),
                   after=[t1], sig="ac")
        t2 = p.op("scalar", lambda s, o=wstg[:, :], a=wstg[:, :]:
                  s.activation(o, a, AF.Copy, bias=-MAGIC, scale=1.0),
                  after=[t2a], sig="ac")
        t3 = p.op("vector", lambda v, o=wqnt[:, :], a=wstg[:, :]:
                  v.tensor_scalar(o, a, -128.0, 127.0, op0=ALU.max, op1=ALU.min),
                  after=[t2, prep["wqnt_free"]], sig="ve")
        st = p.op("gpsimd", lambda g, o=dst.ap()[i * 128:(i + 1) * 128, :], a=wqnt[:, :]:
                  g.dma_start(out=o, in_=a), after=[t3], sig="sg", amt=16)
        prep["wstg_free"] = t3
        prep["wqnt_free"] = st
        prep["stores"].append(st)

    def load_act_set(eng, set_id):
        return eng.add_instruction(mybir.InstLoadActFuncSet(
            name=eng.bass.get_next_instruction_name(),
            act_func_set_id=set_id, ins=[], outs=[]))
    p.op("scalar", lambda s: load_act_set(s, 6), sig="ac")
    zr0 = p.op("vector", lambda v: v.memset(zrow[:, :], 0.0), sig="ve")
    for bank in (pb[5], pb[6]):
        p.op("tensor", lambda te, o=bank[:, :], l=zrow[0:1, 0:128], r=zrow[0:1, 0:512]:
             te.matmul(o, l, r, start=True, stop=False, skip_group_check=True),
             after=[zr0], sig="pe")
    al = {}
    phiq_ps_tok = {}
    prep_done = 0
    phiq_tile_tok = {}
    lnst = {}

    bias_build_toks = []
    for row, dst in ((brow["k"], bkb), (brow["v"], bvb)):
        mmt = None
        for n in range(2):
            mmt = p.op("tensor", lambda te, o=kps[:, n * 512:(n + 1) * 512],
                       l=onesc[0:1, 0:128], r=row[0:1, n * 512:(n + 1) * 512]:
                       te.matmul(o, l, r, start=True, stop=True, skip_group_check=True),
                       after=[brow_tok, zr0, onesc_tok,
                              bias_build_toks[-1] if bias_build_toks else None],
                       sig="pe")
        bias_build_toks.append(p.op("vector", lambda v, o=dst[:, :], a=kps[:, 0:1024]:
                                    v.tensor_copy(o, a), after=[mmt], sig="ve"))
    bias_built = bias_build_toks[-1]

    def emit_ln(t):
        ldx = p.op("sync", lambda s, o=sb3(x_t), a=tok_ap(x_in, t):
                   s.dma_start(out=o, in_=a), after=[lnst.get("xn_last")],
                   sig="sx", amt=16)
        aggr_last = None
        for g in range(NG):
            col = 2 * (t * NG + g)
            sqt = p.op("gpsimd", lambda g_, o=gsc[:, :], a=ts(x_t, g * D, D):
                       g_.tensor_mul(o, a, a), after=[ldx, lnst.get("gsc_free")],
                       sig="gp")
            p.op("vector", lambda v, o=tin2[:, 0:1], a=ts(x_t, g * D, D):
                 v.reduce_sum(o, a, axis=AX), after=[ldx], sig="ve")
            lnst["gsc_free"] = p.op("vector", lambda v, o=tin2[:, 1:2], a=gsc[:, :]:
                                    v.reduce_sum(o, a, axis=AX), after=[sqt], sig="ve")
            p.op("vector", lambda v, o=mv1[:, col:col + 1], a=tin2[:, 0:1]:
                 v.tensor_scalar(o, a, 1.0 / D, None, op0=ALU.mult), sig="ve")
            p.op("vector", lambda v, o=tin2[:, 2:3], a=tin2[:, 1:2]:
                 v.tensor_scalar(o, a, 1.0 / D, None, op0=ALU.mult), sig="ve")
            p.op("vector", lambda v, o=tin2[:, 3:4], a=mv1[:, col:col + 1]:
                 v.tensor_mul(o, a, a), sig="ve")
            aggr_last = p.op("vector", lambda v, o=mv1[:, col + 1:col + 2],
                             a=tin2[:, 2:3], b=tin2[:, 3:4]:
                             v.tensor_sub(o, a, b), sig="ve")
        c0 = t * NG
        t_eps = p.op("vector", lambda v, o=ts(tiny, 0, 4),
                     a=mv1[:, 2 * c0 + 1:2 * (c0 + NG):2]:
                     v.tensor_scalar(o, a, LN_EPS, None, op0=ALU.add),
                     after=[aggr_last, al.get("lnfree")], sig="ve")
        t_ln = p.op("scalar", lambda s, o=ts(tiny, 4, 4), a=ts(tiny, 0, 4):
                    s.activation(o, a, AF.Ln), after=[t_eps], sig="ac")
        al["lnfree"] = t_ln
        t_rs = p.op("scalar", lambda s, o=rstd1[:, c0:c0 + NG], a=ts(tiny, 4, 4):
                    s.activation(o, a, AF.Exp, bias=0.0, scale=-0.5), sig="ac")
        t_mb = p.op("vector", lambda v, o=mb1[:, c0:c0 + NG],
                    a=mv1[:, 2 * c0:2 * (c0 + NG):2], b=rstd1[:, c0:c0 + NG]:
                    v.scalar_tensor_tensor(o, a, -1.0, b, op0=ALU.mult, op1=ALU.mult),
                    after=[t_rs], sig="ve")
        xn_last = None
        for g in range(NG):
            gc = c0 + g
            xn_last = p.op(
                "scalar",
                lambda s, o=ts(xn_tm, g * D, D), a=ts(x_t, g * D, D),
                       sa=rstd1[:, gc:gc + 1], ba=mb1[:, gc:gc + 1]:
                    s.activation(o, a, AF.Identity, bias=ba, scale=sa),
                after=[t_mb, al.get("xn_tm_free")], sig="ac")
        lnst["xn_last"] = xn_last
        lnst[("xn_done", t)] = xn_last

    emit_ln(0)
    for t in range(NT):
        xn_done_t = lnst[("xn_done", t)]

        evt = []
        for c in range(NCF):
            half = c % 2
            tl = None
            for g in range(NG):
                tl = p.op(
                    "tensor",
                    lambda te, o=pbf[:, half * 512 + g * 128:half * 512 + (g + 1) * 128],
                           a=ts(xn_tm, g * D + c * 128, 128):
                        te.transpose(o, a, ident[:, :]),
                    after=[xn_done_t, evt[c - 1] if c >= 1 else al.get("pbf_free")],
                    sig="pe")
            evt.append(p.op(
                "vector",
                lambda v, o=ts(xn_fm, c * TILE, TILE),
                       a=pbf[:, half * 512:half * 512 + 512]:
                    v.tensor_copy(o, a), after=[tl], sig="ve"))
        al["pbf_free"] = evt[-1]
        al["xn_tm_free"] = p.now("pe")
        xn_fm_done = evt[-1]

        for g in range(NG):
            ksl = g % 2
            for n in range(2):
                for c in range(NCF):
                    p.op("tensor",
                         lambda te, o=kps[:, n * 512:(n + 1) * 512],
                                l=ts(xn_fm, c * TILE + g * 128, 128),
                                r=ts(WkT, c * D + n * 512, 512),
                                st=(c == 0), sp=(c == NCF - 1):
                             te.matmul(o, l, r, start=st, stop=sp,
                                       skip_group_check=True),
                         after=[evt[c]], sig="pe")
            k_done = p.now("pe")
            for n in range(2):
                for c in range(NCF):
                    p.op("tensor",
                         lambda te, o=vps[:, n * 512:(n + 1) * 512],
                                l=ts(xn_fm, c * TILE + g * 128, 128),
                                r=ts(WvT, c * D + n * 512, 512),
                                st=(c == 0), sp=(c == NCF - 1):
                             te.matmul(o, l, r, start=st, stop=sp,
                                       skip_group_check=True),
                         after=[evt[c], phiq_ps_tok.get(("q", 1))], sig="pe")
            v_done = p.now("pe")
            cq = 2 * g
            for cp in range(NCF):
                p.op("tensor",
                     lambda te, o=pb4t[:, :], l=ts(WqT, cp * D + cq * 128, 128),
                            r=ts(xn_fm, cp * TILE, TILE), st=(cp == 0),
                            sp=(cp == NCF - 1):
                         te.matmul(o, l, r, start=st, stop=sp,
                                   skip_group_check=True),
                     after=[xn_fm_done, phiq_ps_tok.get(("q", 0)), bias_built],
                     sig="pe")
            q_even = p.now("pe")
            tt = p.op("vector", lambda v, o=ts(scr, 0, D), a=kps[:, 0:1024], b=bkb[:, :]:
                      v.scalar_tensor_tensor(o, a, 1.0, b, op0=ALU.mult, op1=ALU.add),
                      after=[k_done, bias_built], sig="ve")
            p.op("vector", lambda v, o=ts(scr, D, D), a=ts(scr, 0, D):
                 v.tensor_scalar(o, a, 0.0, None, op0=ALU.max), sig="ve")
            m_t = p.op("vector", lambda v, o=ts(scr, 0, D), a=ts(scr, 0, D):
                       v.tensor_scalar(o, a, 0.0, None, op0=ALU.min), sig="ve")
            e_t = p.op("scalar", lambda s, o=ts(scr, 0, D), a=ts(scr, 0, D):
                       s.activation(o, a, AF.Exp), after=[m_t], sig="ac")
            kk = k_tm[:, ksl * D:(ksl + 1) * D]
            kphi = p.op("vector", lambda v, o=kk, a=ts(scr, D, D), b=ts(scr, 0, D):
                        v.tensor_add(o, a, b), after=[e_t], sig="ve")
            vsl = v_ag[:, ksl * VAGW:ksl * VAGW + NH * 65]
            vev = p.op(
                "vector",
                lambda v, o=vsl.rearrange("p (h e) -> p h e", e=65)[:, :, 0:64],
                       a=vps[:, 0:1024].rearrange("p (h e) -> p h e", e=64),
                       b=bvb[:, :].rearrange("p (h e) -> p h e", e=64):
                    v.scalar_tensor_tensor(o, a, 1.0, b, op0=ALU.mult, op1=ALU.add),
                after=[v_done, bias_built], sig="ve")
            cq = 2 * g + 1
            for cp in range(NCF):
                p.op("tensor",
                     lambda te, o=vps[:, 0:512], l=ts(WqT, cp * D + cq * 128, 128),
                            r=ts(xn_fm, cp * TILE, TILE), st=(cp == 0),
                            sp=(cp == NCF - 1):
                         te.matmul(o, l, r, start=st, stop=sp,
                                   skip_group_check=True),
                     after=[vev, phiq_ps_tok.get(("q", 1))], sig="pe")
            q_odd = p.now("pe")
            for qi, (qbank, q_tok, cqq) in enumerate(
                    (((pb4t[:, :]), q_even, 2 * g), ((vps[:, 0:512]), q_odd, 2 * g + 1))):
                p.op("vector", lambda v, o=ts(scr, 1024, 512), a=qbank,
                     sa=bqfm[:, cqq:cqq + 1]:
                     v.tensor_scalar(o, a, sa, 0.0, op0=ALU.add, op1=ALU.max),
                     after=[q_tok, bq_tok], sig="ve")
                m2_ = p.op("vector", lambda v, o=ts(scr, 1536, 512), a=qbank,
                           sa=bqfm[:, cqq:cqq + 1]:
                           v.tensor_scalar(o, a, sa, 0.0, op0=ALU.add, op1=ALU.min),
                           sig="ve")
                e2_ = p.op("scalar", lambda s, o=ts(scr, 1536, 512), a=ts(scr, 1536, 512):
                           s.activation(o, a, AF.Exp), after=[m2_], sig="ac")
                ph = p.op("vector", lambda v, o=ts(phiq, cqq * NTOK + t * TILE, TILE),
                          a=ts(scr, 1024, 512), b=ts(scr, 1536, 512):
                          v.tensor_add(o, a, b), after=[e2_], sig="ve")
                phiq_ps_tok[("q", qi)] = ph
            for h in range(NH):
                bank = pb5t if h < 8 else pb6t
                q_ = (h % 8) // 2
                par = (h % 2) * 64
                p.op("tensor",
                     lambda te, o=bank[par:par + 64, q_ * 65:q_ * 65 + 65],
                            l=kk[:, h * HD:(h + 1) * HD],
                            r=vsl[:, h * 65:(h + 1) * 65],
                            sp=(t == NT - 1 and g == NG - 1),
                            tp=(0, par):
                         te.matmul(o, l, r, start=False, stop=sp, tile_position=tp,
                                   skip_group_check=True),
                     after=[kphi, vev], sig="pe")
            if g == 1 and t + 1 < NT:
                emit_ln(t + 1)
        phiq_tile_tok[t] = p.now("ve")

        n_jobs = len(prep_jobs) - prep_done if t == NT - 1 else min(6, len(prep_jobs) - prep_done)
        for _ in range(n_jobs):
            emit_prep(prep_done)
            prep_done += 1

    kv_done = p.now("pe")
    ev1 = p.op("vector", lambda v: v.tensor_copy(kv_sb[:, 0:260], pb[5][:, 0:260]),
               after=[kv_done], sig="ve")
    ev2 = p.op("vector", lambda v: v.tensor_copy(kv_sb[:, 260:520], pb[6][:, 0:260]),
               sig="ve")
    if dbg:
        p.op("gpsimd", lambda g: g.dma_start(out=dbg_kvloc.ap(), in_=kv_sb[:, :]),
             after=[ev1, ev2], sig="sg", amt=16)
        p.op("gpsimd", lambda g: g.dma_start(out=dbg_phiq.ap(), in_=phiq[:, :]),
             after=[phiq_tile_tok[NT - 1]], sig="sg", amt=16)
        p.op("gpsimd", lambda g: g.dma_start(out=dbg_ktm.ap(), in_=k_tm[:, :]),
             after=[kv_done], sig="sg", amt=16)
        p.op("gpsimd", lambda g: g.dma_start(out=dbg_vag.ap(), in_=v_ag[:, :]),
             after=[kv_done], sig="sg", amt=16)
        p.op("gpsimd", lambda g: g.dma_start(out=dbg_xnfm.ap(), in_=xn_fm[:, :]),
             after=[kv_done], sig="sg", amt=16)
        p.op("gpsimd", lambda g: g.dma_start(out=dbg_xntm.ap(), in_=xn_tm[:, :]),
             after=[kv_done], sig="sg", amt=16)
        p.op("gpsimd", lambda g: g.dma_start(out=dbg_rstd.ap(), in_=rstd1[:, :]),
             after=[kv_done], sig="sg", amt=16)
        p.op("gpsimd", lambda g: g.dma_start(out=dbg_mv1.ap(), in_=mv1[:, :]),
             after=[kv_done], sig="sg", amt=16)
        for bi, bn in enumerate(("q", "k", "v", "o")):
            p.op("gpsimd", lambda g, o=dbg_brow.ap()[0:1, bi * D:(bi + 1) * D],
                 a=brow[bn][:, :]: g.dma_start(out=o, in_=a),
                 after=[kv_done], sig="sg", amt=16)
        p.op("gpsimd", lambda g: g.dma_start(out=dbg_wkt.ap(), in_=WkT[:, :]),
             after=[kv_done], sig="sg", amt=16)
    dbg_dump_tok = p.now("sg")
    mtoks = []
    for s_ in range(4):
        mtoks.append(p.op(
            "vector", lambda v, o=ts(scr, s_ * 520, 520), m=smask[:, s_:s_ + 1]:
                v.tensor_scalar(o, kv_sb[:, :], m, None, op0=ALU.mult),
            after=[ev1, ev2, smask_tok], sig="ve"))
    d1 = p.op("gpsimd", lambda g: g.dma_start(out=cc_kv_in.ap(), in_=ts(scr, 0, 2080)),
              after=[mtoks[-1]], sig="sg", amt=16)
    cc1 = p.op("gpsimd", lambda g: g.collective_compute(
        "AllReduce", ALU.add, replica_groups=[list(range(8))],
        ins=[cc_kv_in.ap()], outs=[cc_kv_out.ap()]), after=[d1], sig="sc")
    d2a = p.op("gpsimd", lambda g: g.dma_start(out=ts(scr, 0, 2080), in_=cc_kv_out.ap()),
               after=[cc1], sig="sg", amt=16)
    d2 = p.op("vector", lambda v: v.tensor_scalar(kv_sb[:, :], ts(scr, 0, 520),
                                                  smask[:, 0:1], None, op0=ALU.mult),
              after=[d2a], sig="ve")
    for s_ in range(1, 4):
        d2 = p.op("vector", lambda v, a=ts(scr, s_ * 520, 520), m=smask[:, s_:s_ + 1]:
                  v.scalar_tensor_tensor(kv_sb[:, :], a, m, kv_sb[:, :],
                                         op0=ALU.mult, op1=ALU.add),
                  after=[d2], sig="ve")

    esA.close()
    esB = ExitStack()
    epair = esB.enter_context(nc.sbuf_tensor("epairt", [16, NCF * 128], BF16))
    kv_bd = esB.enter_context(nc.sbuf_tensor("kv_bd", [128, 8 * 128], BF16))
    ksum2 = esB.enter_context(nc.sbuf_tensor("ksum2", [128, NCF * 16], BF16))
    nrm_sb = esB.enter_context(nc.sbuf_tensor("nrm_sb", [16, TILE], F32))
    rec_sb = esB.enter_context(nc.sbuf_tensor("rec_sb", [16, TILE], F32))
    rec_bf = esB.enter_context(nc.sbuf_tensor("rec_bf", [16, TILE], BF16))
    rbc_sb = esB.enter_context(nc.sbuf_tensor("rbc_sb", [128, TILE], F32))
    qkv_fm_sb = esB.enter_context(nc.sbuf_tensor("qkv_fm", [128, NCF * TILE], BF16))
    x1_sb = esB.enter_context(nc.sbuf_tensor("x1_sb", [128, NG * D], F32))
    epair_tok = p.op("sync", lambda s: s.dma_start(out=epair[:, :], in_=epair_in.ap()),
                     after=[kv_done, prep["stores"][-1], d2, dbg_dump_tok], sig="sx", amt=16)

    if dbg:
        p.op("gpsimd", lambda g: g.dma_start(out=dbg_kv.ap(), in_=cc_kv_out.ap()),
             after=[cc1], sig="sg", amt=16)
    bd0 = p.op("vector", lambda v: v.memset(kv_bd[:, :], 0.0),
               after=[d2, kv_done, prep["stores"][-1], dbg_dump_tok], sig="ve")
    p.op("vector", lambda v: v.memset(ksum2[:, :], 0.0), sig="ve")
    kvbd_done = None
    for par in (0, 64):
        for b in range(2):
            p.op("vector",
                 lambda v, o=kv_bd[par:par + 64, 512 * b:512 * b + 512]
                          .rearrange("p (q e) -> p q e", e=128)[:, :, par:par + 64],
                        a=kv_sb[par:par + 64, 260 * b:260 * b + 260]
                          .rearrange("p (q e) -> p q e", e=65)[:, :, 0:64]:
                     v.tensor_copy(o, a), after=[bd0], sig="ve")
    for par, off in ((0, 0), (64, 1)):
        for b in range(2):
            kvbd_done = p.op(
                "vector",
                lambda v, o=bass.AP(ksum2, 72 * b + off, [[NCF * 16, 64], [18, 4]],
                                    base_partition=par)
                       if False else
                       ksum2[par:par + 64, 72 * b + off:72 * b + off + 55:18],
                       a=kv_sb[par:par + 64, 260 * b + 64:260 * b + 260:65]:
                    v.tensor_copy(o, a), sig="ve")

    bl = {"xfree": al.get("xfree"), "gsc_free": lnst.get("gsc_free")}
    for t in range(NT):
        xin = tok_ap(x_in, t)
        ldx = p.op("sync", lambda s, o=sb3(x_t), a=xin: s.dma_start(out=o, in_=a),
                   after=[bl.get("xfree")], sig="sx", amt=16)

        for c in range(NCF):
            p.op("tensor", lambda te, o=pb[4][0:16, :], l=ts(ksum2, c * 16, 16),
                 r=ts(phiq, c * NTOK + t * TILE, TILE), st=(c == 0), sp=(c == NCF - 1):
                 te.matmul(o, l, r, start=st, stop=sp, skip_group_check=True),
                 after=[kvbd_done, bl.get("nrm_ps"), phiq_tile_tok[t],
                        phiq_ps_tok.get(id(pb[4])) if t == 0 else None], sig="pe")
        nm = p.now("pe")
        n1 = p.op("vector", lambda v: v.tensor_scalar(nrm_sb[:, :], pb[4][0:16, :],
                                                      ATTN_EPS, None, op0=ALU.max),
                  after=[nm], sig="ve")
        bl["nrm_ps"] = n1
        n2 = p.op("vector", lambda v: v.reciprocal(rec_sb[:, :], nrm_sb[:, :]),
                  after=[n1, bl.get("rec_free")], sig="ve")
        n2 = p.op("vector", lambda v: v.tensor_copy(rec_bf[:, :], rec_sb[:, :]),
                  after=[n2], sig="ve")

        mm2 = None
        for pr in range(NCF):
            bank = pb[pr % 3]
            mm1 = p.op("tensor", lambda te, o=bank[:, :], l=ts(kv_bd, pr * 128, 128),
                       r=ts(phiq, pr * NTOK + t * TILE, TILE):
                       te.matmul(o, l, r, start=True, stop=True, skip_group_check=True),
                       after=[kvbd_done, bl.get(("qps", pr % 3)),
                              phiq_ps_tok.get(id(pb[0])) if t == 0 and pr % 3 == 0 else None],
                       sig="pe")
            mm2 = p.op("tensor", lambda te, o=pb[3][:, :], l=ts(epair, pr * 128, 128),
                       r=rec_bf[:, :]:
                       te.matmul(o, l, r, start=True, stop=True, skip_group_check=True),
                       after=[n2, bl.get("bc_ps"),
                              epair_tok if t == 0 and pr == 0 else None], sig="pe")
            bc = p.op("scalar", lambda s, o=rbc_sb[:, :], a=pb[3][:, :]:
                      s.activation(o, a, AF.Copy),
                      after=[mm2, bl.get("rbc_free")], sig="ac")
            bl["bc_ps"] = bc
            ev = p.op("vector", lambda v, o=ts(qkv_fm_sb, pr * TILE, TILE), a=bank[:, :],
                      b=rbc_sb[:, :]:
                      v.scalar_tensor_tensor(o, a, 1.0, b, op0=ALU.mult, op1=ALU.mult),
                      after=[mm1, bc], sig="ve")
            bl[("qps", pr % 3)] = ev
            bl["rbc_free"] = ev
        bl["rec_free"] = mm2
        qkv_done = p.now("ve")

        evbank = {}
        for g in range(NG):
            for n in range(2):
                bank = pb[(2 * g + n) % 4]
                for c in range(NCF):
                    p.op("tensor", lambda te, o=bank[:, :],
                         l=ts(qkv_fm_sb, c * TILE + g * 128, 128),
                         r=ts(WoT, c * D + n * 512, 512), st=(c == 0):
                         te.matmul(o, l, r, start=st, stop=False, skip_group_check=True),
                         after=[qkv_done, wT_tok["wo"], bl.get(("ops", (2 * g + n) % 4))],
                         sig="pe")
                om = p.op("tensor", lambda te, o=bank[:, :], l=onesc[0:1, 0:128],
                          r=brow["o"][0:1, n * 512:(n + 1) * 512]:
                          te.matmul(o, l, r, start=False, stop=True, skip_group_check=True),
                          after=[brow_tok], sig="pe")
                ev = p.op("vector", lambda v, o=ts(x1_sb, g * D + n * 512, 512),
                          a=bank[:, :], b=ts(x_t, g * D + n * 512, 512):
                          v.scalar_tensor_tensor(o, a, 1.0, b, op0=ALU.mult, op1=ALU.add),
                          after=[om, ldx, bl.get("x1free")], sig="ve")
                bl[("ops", (2 * g + n) % 4)] = ev
                evbank[(g, n)] = ev
        bl["xfree"] = p.now("ve")
        for b in range(3):
            bl[("qps", b)] = p.now("ve")

        for g in range(NG):
            col = 2 * (t * NG + g)
            sqt = p.op("gpsimd", lambda g, o=gsc[:, :], a=ts(x1_sb, g * D, D):
                       g.tensor_mul(o, a, a),
                       after=[evbank[(g, 0)], evbank[(g, 1)], bl.get("gsc_free")],
                       sig="gp")
            p.op("vector", lambda v, o=tin2[:, 0:1], a=ts(x1_sb, g * D, D):
                 v.reduce_sum(o, a, axis=AX), after=[evbank[(g, 0)], evbank[(g, 1)]],
                 sig="ve")
            bl["gsc_free"] = p.op("vector", lambda v, o=tin2[:, 1:2], a=gsc[:, :]:
                                  v.reduce_sum(o, a, axis=AX), after=[sqt], sig="ve")
            p.op("vector", lambda v, o=mv2[:, col:col + 1], a=tin2[:, 0:1]:
                 v.tensor_scalar(o, a, 1.0 / D, None, op0=ALU.mult), sig="ve")
            p.op("vector", lambda v, o=tin2[:, 2:3], a=tin2[:, 1:2]:
                 v.tensor_scalar(o, a, 1.0 / D, None, op0=ALU.mult), sig="ve")
            p.op("vector", lambda v, o=tin2[:, 3:4], a=mv2[:, col:col + 1]:
                 v.tensor_mul(o, a, a), sig="ve")
            aggr_last = p.op("vector", lambda v, o=mv2[:, col + 1:col + 2],
                             a=tin2[:, 2:3], b=tin2[:, 3:4]:
                 v.tensor_sub(o, a, b), sig="ve")
            ag = aggr_last
            sb_ = p.op("vector", lambda v, o=ts(scr, 0, D), a=ts(x1_sb, g * D, D),
                       sa=mv2[:, col:col + 1]:
                       v.tensor_scalar(o, a, sa, None, op0=ALU.subtract),
                       after=[ag], sig="ve")
            p.op("vector", lambda v, o=xm2[:, t * NG + g:t * NG + g + 1],
                 a=ts(scr, 0, D):
                 v.reduce_max(o, a, axis=AX, apply_absolute_value=True),
                 after=[sb_], sig="ve")
        xm_done = p.now("ve")
        st = p.op("gpsimd", lambda g, o=tok_ap(x1_hbm, t), a=sb3(x1_sb):
                  g.dma_start(out=o, in_=a), after=[xm_done], sig="sg", amt=16)
        bl["x1free"] = st
        bl[("x1st", t)] = st

    t_eps = p.op("vector", lambda v, o=ts(scr, 0, 32), a=mv2[:, 1:64:2]:
                 v.tensor_scalar(o, a, LN_EPS, None, op0=ALU.add), sig="ve")
    t_ln = p.op("scalar", lambda s, o=ts(scr, 32, 32), a=ts(scr, 0, 32):
                s.activation(o, a, AF.Ln), after=[t_eps], sig="ac")
    t_rs2 = p.op("scalar", lambda s, o=ts(scr, 64, 32), a=ts(scr, 32, 32):
                 s.activation(o, a, AF.Exp, bias=0.0, scale=-0.5), sig="ac")
    t_xm = p.op("vector", lambda v, o=ts(scr, 96, 32), a=xm2[:, 0:32], b=ts(scr, 64, 32):
                v.tensor_mul(o, a, b), after=[t_rs2], sig="ve")
    t_mx = p.op("vector", lambda v, o=ts(tiny, 0, 1), a=ts(scr, 96, 32):
                v.reduce_max(o, a, axis=AX), after=[t_xm], sig="ve")

    d3 = p.op("gpsimd", lambda g: g.dma_start(out=cc_m1_in.ap()[0:1, :], in_=ts(tiny, 0, 1)),
              after=[t_mx], sig="sg", amt=16)
    cc2 = p.op("gpsimd", lambda g: g.collective_compute(
        "AllReduce", ALU.max, replica_groups=[list(range(8))],
        ins=[cc_m1_in.ap()], outs=[cc_m1_out.ap()]), after=[d3], sig="sc")
    bc1 = p.op("gpsimd", lambda g: g.dma_start(
        out=gmaxb[:, :], in_=bass.AP(cc_m1_out, 0, [[0, 128], [1, 128]])),
        after=[cc2], sig="sg", amt=16)
    gm1 = p.op("vector", lambda v: v.reduce_max(ts(tiny, 0, 1), gmaxb[:, :], axis=AX),
               after=[bc1], sig="ve")
    if dbg:
        p.op("gpsimd", lambda g: g.dma_start(out=dbg_m1.ap(), in_=cc_m1_out.ap()),
             after=[cc2], sig="sg", amt=16)
    bcx = p.op("gpsimd", lambda g: g.dma_start(
        out=ts(tiny, 1, 1), in_=bass.AP(clipx, 0, [[0, 128], [1, 1]])),
        sig="sg", amt=16)
    e1 = p.op("scalar", lambda s: s.activation(ts(tiny, 2, 1), ts(tiny, 1, 1),
                                               AF.Exp, bias=0.0, scale=-1.0),
              after=[bcx], sig="ac")
    e2 = p.op("vector", lambda v: v.tensor_scalar(ts(tiny, 2, 1), ts(tiny, 2, 1),
                                                  1.0, 127.0, op0=ALU.add, op1=ALU.mult),
              after=[e1], sig="ve")
    e3 = p.op("vector", lambda v: v.reciprocal(ts(tiny, 3, 1), ts(tiny, 0, 1)),
              after=[gm1], sig="ve")
    e4 = p.op("vector", lambda v: v.tensor_mul(scl[:, 0:1], ts(tiny, 2, 1), ts(tiny, 3, 1)),
              after=[e2, e3], sig="ve")
    e5 = p.op("vector", lambda v: v.reciprocal(ts(tiny, 3, 1), ts(tiny, 2, 1)), sig="ve")
    e6 = p.op("vector", lambda v: v.tensor_mul(scl[:, 1:2], ts(tiny, 0, 1), ts(tiny, 3, 1)),
              sig="ve")
    e7 = p.op("vector", lambda v: v.tensor_scalar(a2c[:, 0:32], ts(scr, 64, 32),
                                                  scl[:, 0:1], None, op0=ALU.mult),
              after=[e4], sig="ve")
    e8 = p.op("vector", lambda v: v.scalar_tensor_tensor(b2c[:, 0:32], mv2[:, 0:64:2],
                                                         -1.0, a2c[:, 0:32],
                                                         op0=ALU.mult, op1=ALU.mult),
              after=[e7], sig="ve")
    e9 = p.op("vector", lambda v: v.tensor_scalar(sg_all[:, :], wgs[:, :], scl[:, 1:2],
                                                  None, op0=ALU.mult),
              after=[e6, prep["stores"][NCI - 1]], sig="ve")
    scales_done = p.op("vector", lambda v: v.tensor_scalar(su_all[:, :], wus[:, :],
                                                           scl[:, 1:2], None, op0=ALU.mult),
                       after=[prep["stores"][-1]], sig="ve")

    b_pe_done = p.now("pe")
    esB.close()
    esPhi.close()
    esWo.close()
    esC = ExitStack()
    wgT = esC.enter_context(nc.sbuf_tensor("wgT", [128, NCF * FF], BF16))
    wuT = esC.enter_context(nc.sbuf_tensor("wuT", [128, NCF * FF], BF16))
    xq_tm = esC.enter_context(nc.sbuf_tensor("xq_tm", [128, NG * D], BF16))
    xq_fm = esC.enter_context(nc.sbuf_tensor("xq_fm", [128, NCF * TILE], BF16))
    g_sb = esC.enter_context(nc.sbuf_tensor("g_sb", [128, 512], F32))
    u_sb = esC.enter_context(nc.sbuf_tensor("u_sb", [128, 512], F32))
    h_sb = [esC.enter_context(nc.sbuf_tensor(f"h_sb{i}", [128, 512], F32)) for i in range(2)]
    wdstg = esC.enter_context(nc.sbuf_tensor("wdstg", [128, FF], F32))
    wdq_sb = esC.enter_context(nc.sbuf_tensor("wdq_sb", [128, FF], BF16))
    dum = esC.enter_context(nc.sbuf_tensor("dum", [128, 512], F32))

    wgT_tok = []
    wuT_tok = []
    for c in range(NCF):
        wgT_tok.append(p.op(
            "sync", lambda s, o=ts(wgT, c * FF, FF), i=wg_q.ap()[:, c * 128:(c + 1) * 128]:
                s.dma_start_transpose(o, i),
            after=[prep["stores"][NCI - 1], b_pe_done, bl[("x1st", NT - 1)],
                   scales_done], sig="sx", amt=16))
        wuT_tok.append(p.op(
            "sync", lambda s, o=ts(wuT, c * FF, FF), i=wu_q.ap()[:, c * 128:(c + 1) * 128]:
                s.dma_start_transpose(o, i),
            after=[prep["stores"][-1], b_pe_done, scales_done], sig="sx", amt=16))

    wdp = {"wstg_free": None, "wq_free": None, "stores": [],
           "extra": [b_pe_done, scales_done, bl[("x1st", NT - 1)]]}

    def emit_wd_prep(i):
        ld = p.op("sync", lambda s, o=wdstg[:, :], a=wdd.ap()[i * 128:(i + 1) * 128, :]:
                  s.dma_start(out=o, in_=a),
                  after=[wdp["wstg_free"]] + wdp.pop("extra", []), sig="sx", amt=16)
        rd = p.op("vector", lambda v, o=wds[:, i:i + 1], a=wdstg[:, :]:
                  v.reduce_max(o, a, axis=AX, apply_absolute_value=True),
                  after=[ld], sig="ve")
        sc = p.op("vector", lambda v, o=wds[:, i:i + 1], a=wds[:, i:i + 1]:
                  v.tensor_scalar(o, a, 1.0 / 127.0, None, op0=ALU.mult),
                  after=[rd], sig="ve")
        rc = p.op("vector", lambda v, o=tin2[:, 7:8], a=wds[:, i:i + 1]:
                  v.reciprocal(o, a), after=[sc], sig="ve")
        t1 = p.op("scalar", lambda s, o=wdstg[:, :], a=wdstg[:, :], sa=tin2[:, 7:8]:
                  s.activation(o, a, AF.Copy, bias=0.0, scale=sa), after=[rc], sig="ac")
        t2a = p.op("scalar", lambda s, o=wdstg[:, :], a=wdstg[:, :]:
                   s.activation(o, a, AF.Copy, bias=MAGIC, scale=1.0),
                   after=[t1], sig="ac")
        t2 = p.op("scalar", lambda s, o=wdstg[:, :], a=wdstg[:, :]:
                  s.activation(o, a, AF.Copy, bias=-MAGIC, scale=1.0),
                  after=[t2a], sig="ac")
        t3 = p.op("vector", lambda v, o=wdstg[:, :], a=wdstg[:, :]:
                  v.tensor_scalar(o, a, -128.0, 127.0, op0=ALU.max, op1=ALU.min),
                  after=[t2], sig="ve")
        t5 = p.op("scalar", lambda s, o=wdq_sb[:, :], a=wdstg[:, :], sa=wds[:, i:i + 1]:
                  s.activation(o, a, AF.Copy, bias=0.0, scale=sa),
                  after=[t3, wdp["wq_free"]], sig="ac")
        st = p.op("gpsimd", lambda g, o=wd_q.ap()[i * 128:(i + 1) * 128, :], a=wdq_sb[:, :]:
                  g.dma_start(out=o, in_=a), after=[t5], sig="sg", amt=16)
        wdp["wstg_free"] = t5
        wdp["wq_free"] = st
        wdp["stores"].append(st)

    p.op("scalar", lambda s: load_act_set(s, 18), sig="ac")
    cl = {"xfree": bl["xfree"]}
    hst_tok = {}
    for t in range(NT):
        xin = tok_ap(x1_hbm, t)
        ldx = p.op("sync", lambda s, o=sb3(x_t), a=xin: s.dma_start(out=o, in_=a),
                   after=[cl.get("xfree"), bl[("x1st", t)]], sig="sx", amt=16)

        xq_toks = []
        for g in range(NG):
            gc = t * NG + g
            q1 = p.op("scalar", lambda s, o=ts(scr, 0, D), a=ts(x_t, g * D, D),
                      sa=a2c[:, gc:gc + 1], ba=b2c[:, gc:gc + 1]:
                      s.activation(o, a, AF.Identity, bias=ba, scale=sa),
                      after=[ldx, e8, xq_toks[-1] if g else cl.get("scr_free"),
                             cl.get("xq_tm_free"),
                             b_pe_done if t == 0 and g == 0 else None], sig="ac")
            q2 = p.op("vector", lambda v, o=ts(scr, 0, D), a=ts(scr, 0, D):
                      v.tensor_scalar(o, a, -128.49999, 127.49998,
                                      op0=ALU.max, op1=ALU.min),
                      after=[q1], sig="ve")
            xq_toks.append(p.op("vector", lambda v, o=ts(xq_tm, g * D, D), a=ts(scr, 0, D):
                                v.tensor_scalar(o, a, MAGIC, -MAGIC,
                                                op0=ALU.add, op1=ALU.add),
                                after=[q2], sig="ve"))
        cl["xfree"] = xq_toks[-1]
        cl["scr_free"] = xq_toks[-1]

        evt = []
        for c in range(NCF):
            half = c % 2
            tl = None
            for g in range(NG):
                tl = p.op(
                    "tensor",
                    lambda te, o=pbf[:, half * 512 + g * 128:half * 512 + (g + 1) * 128],
                           a=ts(xq_tm, g * D + c * 128, 128):
                        te.transpose(o, a, ident[:, :]),
                    after=[xq_toks[-1], evt[c - 1] if c >= 1 else cl.get("pbf_free")],
                    sig="pe")
            evt.append(p.op(
                "vector",
                lambda v, o=ts(xq_fm, c * TILE, TILE),
                       a=pbf[:, half * 512:half * 512 + 512]:
                    v.tensor_copy(o, a), after=[tl], sig="ve"))
        cl["pbf_free"] = evt[-1]
        cl["xq_tm_free"] = p.now("pe")

        for i in range(NCI):
            bg = pb[2 * (i % 3)]
            bu = pb[2 * (i % 3) + 1]
            for c in range(NCF):
                p.op("tensor", lambda te, o=bg[:, :], l=ts(wgT, c * FF + i * 128, 128),
                     r=ts(xq_fm, c * TILE, TILE), st=(c == 0), sp=(c == NCF - 1):
                     te.matmul(o, l, r, start=st, stop=sp, skip_group_check=True),
                     after=[evt[c], wgT_tok[c], cl.get(("gps", i % 3))], sig="pe")
            gmm = p.now("pe")
            for c in range(NCF):
                p.op("tensor", lambda te, o=bu[:, :], l=ts(wuT, c * FF + i * 128, 128),
                     r=ts(xq_fm, c * TILE, TILE), st=(c == 0), sp=(c == NCF - 1):
                     te.matmul(o, l, r, start=st, stop=sp, skip_group_check=True),
                     after=[wuT_tok[c]], sig="pe")
            umm = p.now("pe")
            hb = h_sb[i % 2]
            gv = p.op("scalar", lambda s, o=g_sb[:, :], a=bg[:, :], sa=sg_all[:, i:i + 1]:
                      s.activation(o, a, AF.Silu, bias=0.0, scale=sa),
                      after=[gmm, e9, scales_done, cl.get("g_free")], sig="ac")
            uv = p.op("scalar", lambda s, o=u_sb[:, :], a=bu[:, :], sa=su_all[:, i:i + 1]:
                      s.activation(o, a, AF.Copy, bias=0.0, scale=sa),
                      after=[umm], sig="ac")
            hm = p.op("gpsimd", lambda g, o=hb[:, :], a=g_sb[:, :], b=u_sb[:, :]:
                      g.tensor_mul(o, a, b), after=[gv, uv, cl.get(("hfree", i % 2))],
                      sig="gp")
            cl[("gps", i % 3)] = hm
            cl["g_free"] = hm
            ht = p.op("vector", lambda v, o=tiny[:, 6:7], a=hb[:, :]:
                      v.reduce_max(o, a, axis=AX, apply_absolute_value=True),
                      after=[hm], sig="ve")
            p.op("vector", lambda v, o=habs[:, i:i + 1], a=habs[:, i:i + 1], b=tiny[:, 6:7]:
                 v.tensor_max(o, a, b), after=[ht], sig="ve")
            st = p.op("gpsimd", lambda g, o=h_hbm.ap()[i, :, t * TILE:(t + 1) * TILE],
                      a=hb[:, :]:
                      g.dma_start(out=o, in_=a), after=[hm], sig="sg", amt=16)
            cl[("hfree", i % 2)] = [st, ht]
            hst_tok[(i, t)] = st
        if t < NCF:
            emit_wd_prep(t)

    hmx = p.op("vector", lambda v: v.reduce_max(ts(tiny, 0, 1), habs[:, :], axis=AX),
               sig="ve")
    d4 = p.op("gpsimd", lambda g: g.dma_start(out=cc_m2_in.ap()[0:1, :], in_=ts(tiny, 0, 1)),
              after=[hmx], sig="sg", amt=16)
    cc3 = p.op("gpsimd", lambda g: g.collective_compute(
        "AllReduce", ALU.max, replica_groups=[list(range(8))],
        ins=[cc_m2_in.ap()], outs=[cc_m2_out.ap()]), after=[d4], sig="sc")
    bc2 = p.op("gpsimd", lambda g: g.dma_start(
        out=gmaxb[:, :], in_=bass.AP(cc_m2_out, 0, [[0, 128], [1, 128]])),
        after=[cc3], sig="sg", amt=16)
    gm2 = p.op("vector", lambda v: v.reduce_max(ts(tiny, 0, 1), gmaxb[:, :], axis=AX),
               after=[bc2], sig="ve")
    if dbg:
        p.op("gpsimd", lambda g: g.dma_start(out=dbg_m2.ap(), in_=cc_m2_out.ap()),
             after=[cc3], sig="sg", amt=16)
    bcd = p.op("gpsimd", lambda g: g.dma_start(
        out=ts(tiny, 1, 1), in_=bass.AP(clipd, 0, [[0, 128], [1, 1]])),
        sig="sg", amt=16)
    p.op("scalar", lambda s: load_act_set(s, 6), sig="ac")
    f1 = p.op("scalar", lambda s: s.activation(ts(tiny, 2, 1), ts(tiny, 1, 1),
                                               AF.Exp, bias=0.0, scale=-1.0),
              after=[bcd], sig="ac")
    f2 = p.op("vector", lambda v: v.tensor_scalar(ts(tiny, 2, 1), ts(tiny, 2, 1),
                                                  1.0, 127.0, op0=ALU.add, op1=ALU.mult),
              after=[f1], sig="ve")
    f3 = p.op("vector", lambda v: v.reciprocal(ts(tiny, 3, 1), ts(tiny, 0, 1)),
              after=[gm2], sig="ve")
    p.op("vector", lambda v: v.tensor_mul(scl[:, 2:3], ts(tiny, 2, 1), ts(tiny, 3, 1)),
         after=[f2, f3], sig="ve")
    f5 = p.op("vector", lambda v: v.reciprocal(ts(tiny, 3, 1), ts(tiny, 2, 1)), sig="ve")
    s2_done = p.op("vector", lambda v: v.tensor_mul(scl[:, 3:4], ts(tiny, 0, 1),
                                                    ts(tiny, 3, 1)),
                   after=[f5], sig="ve")

    c_pe_done = p.now("pe")
    c_ac_done = p.now("ac")
    esC.close()
    esD = ExitStack()
    wdT = esD.enter_context(nc.sbuf_tensor("wdT", [128, NCI * D], BF16))
    hq_sb = esD.enter_context(nc.sbuf_tensor("hq_sb", [128, NCI * 512], BF16))
    hstg = [esD.enter_context(nc.sbuf_tensor(f"hstg{i}", [128, 512], F32)) for i in range(2)]
    out_sb = [esD.enter_context(nc.sbuf_tensor(f"out_sb{i}", [128, NG * D], F32))
              for i in range(2)]

    wdT_tok = []
    for c in range(NCI):
        wdT_tok.append(p.op(
            "sync", lambda s, o=ts(wdT, c * D, D), i=wd_q.ap()[:, c * 128:(c + 1) * 128]:
                s.dma_start_transpose(o, i),
            after=[wdp["stores"][-1], c_pe_done, c_ac_done, s2_done,
                   hst_tok[(NCI - 1, NT - 1)]], sig="sx", amt=16))

    dl = {"xfree": cl["xfree"], "psum_free": c_ac_done}
    for t in range(NT):
        xin = tok_ap(x1_hbm, t)
        ldx1 = p.op("sync", lambda s, o=sb3(x_t), a=xin: s.dma_start(out=o, in_=a),
                    after=[dl.get("xfree")], sig="sx", amt=16)
        hq_toks = []
        for i in range(NCI):
            hb = hstg[i % 2]
            ld = p.op("sync", lambda s, o=hb[:, :],
                      a=h_hbm.ap()[i, :, t * TILE:(t + 1) * TILE]:
                      s.dma_start(out=o, in_=a),
                      after=[dl.get(("hfree", i % 2)), hst_tok[(i, t)],
                             c_pe_done if t == 0 and i == 0 else None],
                      sig="sx", amt=16)
            q1 = p.op("scalar", lambda s, o=ts(scr, 0, 512), a=hb[:, :], sa=scl[:, 2:3]:
                      s.activation(o, a, AF.Copy, bias=0.0, scale=sa),
                      after=[ld, s2_done,
                             hq_toks[-1] if i else dl.get("scr_free"),
                             dl.get("hq_free")], sig="ac")
            dl[("hfree", i % 2)] = q1
            q2 = p.op("vector", lambda v, o=ts(scr, 0, 512), a=ts(scr, 0, 512):
                      v.tensor_scalar(o, a, -128.49999, 127.49998,
                                      op0=ALU.max, op1=ALU.min),
                      after=[q1], sig="ve")
            hq_toks.append(p.op("vector", lambda v, o=ts(hq_sb, i * 512, 512),
                                a=ts(scr, 0, 512):
                                v.tensor_scalar(o, a, MAGIC, -MAGIC,
                                                op0=ALU.add, op1=ALU.add),
                                after=[q2], sig="ve"))
        dl["scr_free"] = hq_toks[-1]

        osb = out_sb[t % 2]
        wave_banks = ((pb[0], pb[1], pb[2], pb[3]), (pb[4], pb[5], pb[6], pb[3]))
        for w, groups in enumerate(((0, 1), (2, 3))):
            for i in range(NCI):
                for gi, g in enumerate(groups):
                    for n in range(2):
                        bank = wave_banks[w][2 * gi + n]
                        p.op("tensor", lambda te, o=bank[:, :],
                             l=ts(hq_sb, i * 512 + g * 128, 128),
                             r=ts(wdT, i * D + n * 512, 512),
                             st=(i == 0), sp=(i == NCI - 1):
                             te.matmul(o, l, r, start=st, stop=sp, skip_group_check=True),
                             after=[hq_toks[i], wdT_tok[i],
                                    dl.get(("dps", id(bank))), dl.get("psum_free")],
                             sig="pe")
            wave_done = p.now("pe")
            for gi, g in enumerate(groups):
                for n in range(2):
                    bank = wave_banks[w][2 * gi + n]
                    ev = p.op("vector", lambda v, o=ts(osb, g * D + n * 512, 512),
                              a=bank[:, :], b=ts(x_t, g * D + n * 512, 512):
                              v.scalar_tensor_tensor(o, a, scl[:, 3:4], b,
                                                     op0=ALU.mult, op1=ALU.add),
                              after=[wave_done, ldx1, s2_done,
                                     dl.get(("outfree", t % 2))], sig="ve")
                    dl[("dps", id(bank))] = ev
        dl["psum_free"] = None
        ev_done = p.now("ve")
        dl["xfree"] = ev_done
        dl["hq_free"] = wave_done
        yst = p.op("gpsimd", lambda g, o=tok_ap(y_out, t), a=sb3(osb):
                   g.dma_start(out=o, in_=a), after=[ev_done], sig="sg", amt=16)
        dl[("outfree", t % 2)] = yst
        dl[("yst", t)] = yst

    p.op("gpsimd", lambda g: g.nop(), after=[dl[("yst", NT - 1)]])

    with nc.Block() as block:
        @block.sync
        def _(eng):
            p.emit("sync", eng, sems)

        @block.tensor
        def _(eng):
            p.emit("tensor", eng, sems)

        @block.vector
        def _(eng):
            p.emit("vector", eng, sems)

        @block.scalar
        def _(eng):
            p.emit("scalar", eng, sems)

        @block.gpsimd
        def _(eng):
            p.emit("gpsimd", eng, sems)

    esD.close()
    es.close()
    return nc



def make_inputs(inputs):
    import ml_dtypes
    x = np.asarray(inputs["L0_attn_x"])
    flat = np.ascontiguousarray(x.reshape(8, NTOK, D))
    ident = np.eye(128, dtype=ml_dtypes.bfloat16)
    ones = np.ones((1, TILE), dtype=ml_dtypes.bfloat16)
    epair = np.zeros((16, NCF * 128), dtype=ml_dtypes.bfloat16)
    for pr in range(NCF):
        for j in range(128):
            epair[2 * pr + (j // 64), pr * 128 + j] = 1.0
    common = {
        "Wq": np.ascontiguousarray(inputs["L0_attn_Wq"], dtype=np.float32),
        "Wk": np.ascontiguousarray(inputs["L0_attn_Wk"], dtype=np.float32),
        "Wv": np.ascontiguousarray(inputs["L0_attn_Wv"], dtype=np.float32),
        "Wo": np.ascontiguousarray(inputs["L0_attn_Wo"], dtype=np.float32),
        "bq": np.ascontiguousarray(inputs["L0_attn_bq"], dtype=np.float32).reshape(1, D),
        "bk": np.ascontiguousarray(inputs["L0_attn_bk"], dtype=np.float32).reshape(1, D),
        "bv": np.ascontiguousarray(inputs["L0_attn_bv"], dtype=np.float32).reshape(1, D),
        "bo": np.ascontiguousarray(inputs["L0_attn_bo"], dtype=np.float32).reshape(1, D),
        "wg": np.ascontiguousarray(inputs["L0_ffn_w_gate"], dtype=np.float32),
        "wu": np.ascontiguousarray(inputs["L0_ffn_w_up"], dtype=np.float32),
        "wd": np.ascontiguousarray(inputs["L0_ffn_w_down"], dtype=np.float32),
        "clipx": np.ascontiguousarray(inputs["L0_ffn_clip_x"], dtype=np.float32).reshape(1, 1),
        "clipd": np.ascontiguousarray(inputs["L0_ffn_clip_down"], dtype=np.float32).reshape(1, 1),
        "ident": ident, "onesc": ones, "epair": epair,
    }
    out = []
    for c in range(8):
        sm = np.zeros((128, 4), dtype=np.float32)
        sm[:, c // 2] = 1.0
        out.append(dict(common, x=np.ascontiguousarray(flat[c]), smask=sm))
    return out


def kernel(**inputs):
    in_maps = make_inputs(inputs)
    nc = build_nc(dbg=False)
    res = run_bass_kernel_spmd(nc, in_maps, core_ids=list(range(8)))
    y = np.concatenate([np.asarray(r["y"]) for r in res.results], axis=0)
    return y.reshape(4, 8192, 1024)


# version 09
# speedup vs baseline: 2.3340x; 1.0647x over previous
import numpy as np
from contextlib import ExitStack

import concourse.bass as bass
import concourse.mybir as mybir
from concourse.bass_utils import run_bass_kernel_spmd

F32 = mybir.dt.float32
BF16 = mybir.dt.bfloat16
AF = mybir.ActivationFunctionType
ALU = mybir.AluOpType
AX = mybir.AxisListType.X

MAGIC = 12582912.0
D = 1024
NH = 16
HD = 64
FF = 2816
NTOK = 4096
TILE = 512
NT = NTOK // TILE
NG = TILE // 128
NCF = D // 128
NCI = FF // 128
LN_EPS = 1e-5
ATTN_EPS = 1e-6
VAGW = NH * 65 + 15

ENGINES = ("sync", "tensor", "vector", "scalar", "gpsimd")
SELF_SERIAL = {"vector": "ve", "scalar": "ac", "gpsimd": "gp"}


NPOOL = 8


class Prog:

    def __init__(self):
        self.items = {e: [] for e in ENGINES}
        self.cnt = {}
        self.pool_rr = {"sx": 0, "sg": 0}
        self.pool_last = {}
        self.last_sc = None

    def _flat(self, after):
        out = []
        for t in after:
            if t is None:
                continue
            if isinstance(t, list):
                out.extend(x for x in t if x is not None)
            else:
                out.append(t)
        return out

    def op(self, eng, fn, after=(), sig=None, amt=1):
        after = self._flat(after)
        if sig in ("sx", "sg"):
            i = self.pool_rr[sig] % NPOOL
            self.pool_rr[sig] += 1
            sem = f"{sig}{i}"
            prev = self.pool_last.get(sem)
            if prev is not None:
                after = after + [prev]
            self.cnt[sem] = self.cnt.get(sem, 0) + amt
            tok = (sem, self.cnt[sem])
            self.pool_last[sem] = tok
            self.items[eng].append((after, fn, sem, amt, None))
            return tok
        if sig == "sc":
            if self.last_sc is not None:
                after = after + [self.last_sc]
        own = SELF_SERIAL.get(eng)
        if own is not None and sig is None:
            sig = own
        self_wait = None
        if own is not None:
            self_wait = (own, self.cnt.get(own, 0))
        tok = None
        if sig is not None:
            self.cnt[sig] = self.cnt.get(sig, 0) + amt
            tok = (sig, self.cnt[sig])
        if sig == "sc":
            self.last_sc = tok
        self.items[eng].append((after, fn, sig, amt, self_wait))
        return tok

    def now(self, sig):
        if sig in ("sx", "sg"):
            return [self.pool_last[s] for s in
                    (f"{sig}{i}" for i in range(NPOOL)) if s in self.pool_last]
        return (sig, self.cnt.get(sig, 0))

    def emit(self, eng_name, eng, sems):
        waited = {}
        for waits, fn, sig, amt, self_wait in self.items[eng_name]:
            allw = list(waits)
            if self_wait is not None and self_wait[1] > 0:
                allw.append(self_wait)
            for (s, v) in allw:
                if waited.get(s, 0) >= v:
                    continue
                eng.wait_ge(sems[s], v)
                waited[s] = v
            inst = fn(eng)
            if sig is not None:
                inst.then_inc(sems[sig], amt)


def build_nc(dbg=False):
    nc = bass.Bass("TRN2", target_bir_lowering=False, detect_race_conditions=True)
    dkind = "ExternalOutput" if dbg else "Internal"

    x_in = nc.dram_tensor("x", [NTOK, D], F32, kind="ExternalInput")
    Wq = nc.dram_tensor("Wq", [D, D], F32, kind="ExternalInput")
    Wk = nc.dram_tensor("Wk", [D, D], F32, kind="ExternalInput")
    Wv = nc.dram_tensor("Wv", [D, D], F32, kind="ExternalInput")
    Wo = nc.dram_tensor("Wo", [D, D], F32, kind="ExternalInput")
    bqd = nc.dram_tensor("bq", [1, D], F32, kind="ExternalInput")
    bkd = nc.dram_tensor("bk", [1, D], F32, kind="ExternalInput")
    bvd = nc.dram_tensor("bv", [1, D], F32, kind="ExternalInput")
    bod = nc.dram_tensor("bo", [1, D], F32, kind="ExternalInput")
    wgd = nc.dram_tensor("wg", [FF, D], F32, kind="ExternalInput")
    wud = nc.dram_tensor("wu", [FF, D], F32, kind="ExternalInput")
    wdd = nc.dram_tensor("wd", [D, FF], F32, kind="ExternalInput")
    clipx = nc.dram_tensor("clipx", [1, 1], F32, kind="ExternalInput")
    clipd = nc.dram_tensor("clipd", [1, 1], F32, kind="ExternalInput")
    ident_in = nc.dram_tensor("ident", [128, 128], BF16, kind="ExternalInput")
    ones_in = nc.dram_tensor("onesc", [1, TILE], BF16, kind="ExternalInput")
    epair_in = nc.dram_tensor("epair", [16, NCF * 128], BF16, kind="ExternalInput")
    smask_in = nc.dram_tensor("smask", [128, 4], F32, kind="ExternalInput")

    y_out = nc.dram_tensor("y", [NTOK, D], F32, kind="ExternalOutput")

    w_bf = {n: nc.dram_tensor(f"{n}_bf", [D, D], BF16, kind="Internal")
            for n in ("wq", "wk", "wv", "wo")}
    wg_q = nc.dram_tensor("wg_q", [FF, D], BF16, kind="Internal")
    wu_q = nc.dram_tensor("wu_q", [FF, D], BF16, kind="Internal")
    wd_q = nc.dram_tensor("wd_q", [D, FF], BF16, kind="Internal")
    x1_hbm = nc.dram_tensor("x1_hbm", [NTOK, D], F32, kind=dkind)
    h_hbm = nc.dram_tensor("h_hbm", [NCI, 128, NTOK], F32, kind="Internal")
    cc_kv_in = nc.dram_tensor("cc_kv_in", [128, 4 * 520], F32, kind="Internal")
    cc_kv_out = nc.dram_tensor("cc_kv_out", [128, 4 * 520], F32, kind="Internal")
    cc_m1_in = nc.dram_tensor("cc_m1_in", [1, 128], F32, kind="Internal")
    cc_m1_out = nc.dram_tensor("cc_m1_out", [1, 128], F32, kind="Internal")
    cc_m2_in = nc.dram_tensor("cc_m2_in", [1, 128], F32, kind="Internal")
    cc_m2_out = nc.dram_tensor("cc_m2_out", [1, 128], F32, kind="Internal")
    if dbg:
        dbg_kv = nc.dram_tensor("dbg_kv", [128, 4 * 520], F32, kind="ExternalOutput")
        dbg_m1 = nc.dram_tensor("dbg_m1", [1, 128], F32, kind="ExternalOutput")
        dbg_m2 = nc.dram_tensor("dbg_m2", [1, 128], F32, kind="ExternalOutput")
        dbg_kvloc = nc.dram_tensor("dbg_kvloc", [128, 520], F32, kind="ExternalOutput")
        dbg_phiq = nc.dram_tensor("dbg_phiq", [128, NCF * NTOK], BF16, kind="ExternalOutput")
        dbg_ktm = nc.dram_tensor("dbg_ktm", [128, 2 * D], BF16, kind="ExternalOutput")
        dbg_vag = nc.dram_tensor("dbg_vag", [128, 2 * VAGW], BF16, kind="ExternalOutput")
        dbg_xnfm = nc.dram_tensor("dbg_xnfm", [128, NCF * TILE], BF16, kind="ExternalOutput")
        dbg_wkt = nc.dram_tensor("dbg_wkt", [128, NCF * D], BF16, kind="ExternalOutput")
        dbg_brow = nc.dram_tensor("dbg_brow", [1, 4 * D], BF16, kind="ExternalOutput")
        dbg_vag0 = nc.dram_tensor("dbg_vag0", [128, 2 * VAGW], BF16, kind="ExternalOutput")
        dbg_xntm = nc.dram_tensor("dbg_xntm", [128, NG * D], BF16, kind="ExternalOutput")
        dbg_rstd = nc.dram_tensor("dbg_rstd", [128, NT * NG], F32, kind="ExternalOutput")
        dbg_mv1 = nc.dram_tensor("dbg_mv1", [128, 2 * NT * NG], F32, kind="ExternalOutput")

    p = Prog()
    es = ExitStack()
    sem_names = ["sc", "pe", "ve", "ac", "gp"] + \
        [f"sx{i}" for i in range(NPOOL)] + [f"sg{i}" for i in range(NPOOL)]
    sems = {n: es.enter_context(nc.semaphore(n)) for n in sem_names}

    ident = es.enter_context(nc.sbuf_tensor("identt", [128, 128], BF16))
    onesc = es.enter_context(nc.sbuf_tensor("onesct", [1, TILE], BF16))
    brow = {n: es.enter_context(nc.sbuf_tensor(f"brow_{n}", [1, D], BF16))
            for n in ("q", "k", "v", "o")}
    mv1 = es.enter_context(nc.sbuf_tensor("mv1", [128, 2 * NT * NG], F32))
    rstd1 = es.enter_context(nc.sbuf_tensor("rstd1", [128, NT * NG], F32))
    mb1 = es.enter_context(nc.sbuf_tensor("mb1", [128, NT * NG], F32))
    mv2 = es.enter_context(nc.sbuf_tensor("mv2", [128, 2 * NT * NG], F32))
    xm2 = es.enter_context(nc.sbuf_tensor("xm2", [128, NT * NG], F32))
    a2c = es.enter_context(nc.sbuf_tensor("a2c", [128, NT * NG], F32))
    b2c = es.enter_context(nc.sbuf_tensor("b2c", [128, NT * NG], F32))
    wgs = es.enter_context(nc.sbuf_tensor("wgs", [128, NCI], F32))
    wus = es.enter_context(nc.sbuf_tensor("wus", [128, NCI], F32))
    wds = es.enter_context(nc.sbuf_tensor("wds", [128, NCF], F32))
    sg_all = es.enter_context(nc.sbuf_tensor("sg_all", [128, NCI], F32))
    su_all = es.enter_context(nc.sbuf_tensor("su_all", [128, NCI], F32))
    scl = es.enter_context(nc.sbuf_tensor("scl", [128, 4], F32))
    gmaxb = es.enter_context(nc.sbuf_tensor("gmaxb", [128, 128], F32))
    kv_sb = es.enter_context(nc.sbuf_tensor("kv_sb", [128, 520], F32))
    habs = es.enter_context(nc.sbuf_tensor("habs", [128, NCI], F32))
    tiny = es.enter_context(nc.sbuf_tensor("tiny", [128, 8], F32))
    tin2 = es.enter_context(nc.sbuf_tensor("tin2", [128, 8], F32))
    smask = es.enter_context(nc.sbuf_tensor("smaskt", [128, 4], F32))
    zrow = es.enter_context(nc.sbuf_tensor("zrow", [1, 512], BF16))
    x_t = es.enter_context(nc.sbuf_tensor("x_t", [128, NG * D], F32))
    scr = es.enter_context(nc.sbuf_tensor("scr", [128, 2560], F32))
    gsc = es.enter_context(nc.sbuf_tensor("gsc", [128, 1024], F32))
    bqfm = es.enter_context(nc.sbuf_tensor("bqfm", [128, NCF], F32))
    bkb = es.enter_context(nc.sbuf_tensor("bkb", [128, D], BF16))
    bvb = es.enter_context(nc.sbuf_tensor("bvb", [128, D], BF16))

    kps = es.enter_context(nc.psum_tensor("kps", [128, 1024], F32))
    vps = es.enter_context(nc.psum_tensor("vps", [128, 1024], F32))
    pb4t = es.enter_context(nc.psum_tensor("pb4t", [128, 512], F32))
    pb5t = es.enter_context(nc.psum_tensor("pb5t", [128, 512], F32))
    pb6t = es.enter_context(nc.psum_tensor("pb6t", [128, 512], F32))
    pbf = es.enter_context(nc.psum_tensor("pbf", [128, 1024], BF16))
    pb = [kps[:, 0:512], kps[:, 512:1024], vps[:, 0:512], vps[:, 512:1024],
          pb4t[:, :], pb5t[:, :], pb6t[:, :]]

    def ts(tensor, c0, w):
        return tensor[:, c0:c0 + w]

    def tok_ap(dram, t):
        return bass.AP(dram, t * TILE * D, [[D, 128], [128 * D, NG], [1, D]])

    def sb3(sb):
        return sb.ap().rearrange("p (g f) -> p g f", f=D)

    cast_tok = {}
    for n, W in (("wk", Wk), ("wv", Wv), ("wq", Wq), ("wo", Wo)):
        cast_tok[n] = p.op("gpsimd",
                           lambda g, a=w_bf[n].ap(), b=W.ap(): g.dma_start(out=a, in_=b),
                           sig="sg", amt=16)
    for n, B in (("q", bqd), ("k", bkd), ("v", bvd), ("o", bod)):
        p.op("gpsimd", lambda g, a=brow[n][:, :], b=B.ap(): g.dma_start(out=a, in_=b),
             sig="sg", amt=16)
    brow_tok = p.now("sg")
    bq_tok = p.op("sync", lambda s: s.dma_start(
        out=bqfm[:, :], in_=bass.AP(bqd, 0, [[1, 128], [128, NCF]]),
        allow_slow_non_contiguous=True), sig="sx", amt=16)

    p.op("sync", lambda s: s.dma_start(out=ident[:, :], in_=ident_in.ap()), sig="sx", amt=16)
    onesc_tok = p.op("sync", lambda s: s.dma_start(out=onesc[:, :], in_=ones_in.ap()),
                     sig="sx", amt=16)
    smask_tok = p.op("sync", lambda s: s.dma_start(out=smask[:, :], in_=smask_in.ap()),
                     sig="sx", amt=16)

    esA = ExitStack()
    WkT = esA.enter_context(nc.sbuf_tensor("WkT", [128, NCF * D], BF16))
    WvT = esA.enter_context(nc.sbuf_tensor("WvT", [128, NCF * D], BF16))
    WqT = esA.enter_context(nc.sbuf_tensor("WqT", [128, NCF * D], BF16))
    xn_tm = esA.enter_context(nc.sbuf_tensor("xn_tm", [128, NG * D], BF16))
    xn_fm = esA.enter_context(nc.sbuf_tensor("xn_fm", [128, NCF * TILE], BF16))
    k_tm = esA.enter_context(nc.sbuf_tensor("k_tm", [128, 2 * D], BF16))
    v_ag = esA.enter_context(nc.sbuf_tensor("v_ag", [128, 2 * VAGW], BF16))
    wstg = esA.enter_context(nc.sbuf_tensor("wstg", [128, D], F32))
    wqnt = esA.enter_context(nc.sbuf_tensor("wqnt", [128, D], BF16))
    esWo = ExitStack()
    WoT = esWo.enter_context(nc.sbuf_tensor("WoT", [128, NCF * D], BF16, side="right"))
    esPhi = ExitStack()
    phiq = esPhi.enter_context(nc.sbuf_tensor("phiq", [128, NCF * NTOK], BF16, side="right"))

    wT_tok = {}
    for n, t_ in (("wk", WkT), ("wv", WvT), ("wq", WqT), ("wo", WoT)):
        for c in range(NCF):
            wT_tok[n] = p.op(
                "sync",
                lambda s, o=t_[:, c * D:(c + 1) * D],
                       i=w_bf[n].ap()[:, c * 128:(c + 1) * 128]:
                    s.dma_start_transpose(o, i),
                after=[cast_tok[n]], sig="sx", amt=16)

    p.op("vector", lambda v: v.memset(v_ag[:, :], 0.0), sig="ve")
    for sl in range(2):
        for h in range(NH):
            cpos = sl * VAGW + h * 65 + 64
            p.op("vector", lambda v, c=cpos: v.memset(v_ag[:, c:c + 1], 1.0), sig="ve")
    p.op("vector", lambda v: v.memset(habs[:, :], 0.0), sig="ve")

    prep_jobs = [(wgd, wg_q, wgs, i) for i in range(NCI)] + \
                [(wud, wu_q, wus, i) for i in range(NCI)]
    prep = {"ld": None, "wstg_free": None, "wqnt_free": None, "stores": []}

    def emit_prep(j):
        src, dst, wsc, i = prep_jobs[j]
        ld = p.op("sync", lambda s, o=wstg[:, :], a=src.ap()[i * 128:(i + 1) * 128, :]:
                  s.dma_start(out=o, in_=a), after=[prep["wstg_free"]], sig="sx", amt=16)
        rd = p.op("vector", lambda v, o=wsc[:, i:i + 1], a=wstg[:, :]:
                  v.reduce_max(o, a, axis=AX, apply_absolute_value=True),
                  after=[ld], sig="ve")
        sc = p.op("vector", lambda v, o=wsc[:, i:i + 1], a=wsc[:, i:i + 1]:
                  v.tensor_scalar(o, a, 1.0 / 127.0, None, op0=ALU.mult),
                  after=[rd], sig="ve")
        rc = p.op("vector", lambda v, o=tin2[:, 6:7], a=wsc[:, i:i + 1]:
                  v.reciprocal(o, a), after=[sc], sig="ve")
        t1 = p.op("scalar", lambda s, o=wstg[:, :], a=wstg[:, :], sa=tin2[:, 6:7]:
                  s.activation(o, a, AF.Copy, bias=0.0, scale=sa),
                  after=[rc], sig="ac")
        t2a = p.op("scalar", lambda s, o=wstg[:, :], a=wstg[:, :]:
                   s.activation(o, a, AF.Copy, bias=MAGIC, scale=1.0),
                   after=[t1], sig="ac")
        t2 = p.op("scalar", lambda s, o=wstg[:, :], a=wstg[:, :]:
                  s.activation(o, a, AF.Copy, bias=-MAGIC, scale=1.0),
                  after=[t2a], sig="ac")
        t3 = p.op("vector", lambda v, o=wqnt[:, :], a=wstg[:, :]:
                  v.tensor_scalar(o, a, -128.0, 127.0, op0=ALU.max, op1=ALU.min),
                  after=[t2, prep["wqnt_free"]], sig="ve")
        st = p.op("gpsimd", lambda g, o=dst.ap()[i * 128:(i + 1) * 128, :], a=wqnt[:, :]:
                  g.dma_start(out=o, in_=a), after=[t3], sig="sg", amt=16)
        prep["wstg_free"] = t3
        prep["wqnt_free"] = st
        prep["stores"].append(st)

    def load_act_set(eng, set_id):
        return eng.add_instruction(mybir.InstLoadActFuncSet(
            name=eng.bass.get_next_instruction_name(),
            act_func_set_id=set_id, ins=[], outs=[]))
    p.op("scalar", lambda s: load_act_set(s, 6), sig="ac")
    zr0 = p.op("vector", lambda v: v.memset(zrow[:, :], 0.0), sig="ve")
    for bank in (pb[5], pb[6]):
        p.op("tensor", lambda te, o=bank[:, :], l=zrow[0:1, 0:128], r=zrow[0:1, 0:512]:
             te.matmul(o, l, r, start=True, stop=False, skip_group_check=True),
             after=[zr0], sig="pe")
    al = {}
    phiq_ps_tok = {}
    prep_done = 0
    phiq_tile_tok = {}
    lnst = {}

    bias_build_toks = []
    for row, dst in ((brow["k"], bkb), (brow["v"], bvb)):
        mmt = None
        for n in range(2):
            mmt = p.op("tensor", lambda te, o=kps[:, n * 512:(n + 1) * 512],
                       l=onesc[0:1, 0:128], r=row[0:1, n * 512:(n + 1) * 512]:
                       te.matmul(o, l, r, start=True, stop=True, skip_group_check=True),
                       after=[brow_tok, zr0, onesc_tok,
                              bias_build_toks[-1] if bias_build_toks else None],
                       sig="pe")
        bias_build_toks.append(p.op("vector", lambda v, o=dst[:, :], a=kps[:, 0:1024]:
                                    v.tensor_copy(o, a), after=[mmt], sig="ve"))
    bias_built = bias_build_toks[-1]

    def emit_ln(t):
        ldx = p.op("sync", lambda s, o=sb3(x_t), a=tok_ap(x_in, t):
                   s.dma_start(out=o, in_=a), after=[lnst.get("xn_last")],
                   sig="sx", amt=16)
        aggr_last = None
        for g in range(NG):
            col = 2 * (t * NG + g)
            p.op("scalar", lambda s, o=gsc[:, :], a=ts(x_t, g * D, D),
                 xc=tin2[:, 0:1]:
                 s.activation(o, a, AF.Copy, accum_out=xc),
                 after=[ldx, aggr_last if g else lnst.get("aggr_prev")], sig="ac")
            s2t = p.op("scalar", lambda s, o=gsc[:, :], a=ts(x_t, g * D, D),
                       xc=tin2[:, 1:2]:
                       s.activation(o, a, AF.Square, accum_out=xc), sig="ac")
            p.op("vector", lambda v, o=mv1[:, col:col + 1], a=tin2[:, 0:1]:
                 v.tensor_scalar(o, a, 1.0 / D, None, op0=ALU.mult),
                 after=[s2t], sig="ve")
            p.op("vector", lambda v, o=tin2[:, 2:3], a=tin2[:, 1:2]:
                 v.tensor_scalar(o, a, 1.0 / D, None, op0=ALU.mult), sig="ve")
            p.op("vector", lambda v, o=tin2[:, 3:4], a=mv1[:, col:col + 1]:
                 v.tensor_mul(o, a, a), sig="ve")
            aggr_last = p.op("vector", lambda v, o=mv1[:, col + 1:col + 2],
                             a=tin2[:, 2:3], b=tin2[:, 3:4]:
                             v.tensor_sub(o, a, b), sig="ve")
        c0 = t * NG
        lnst["aggr_last"] = aggr_last
        lnst["aggr_prev"] = aggr_last
        t_eps = p.op("vector", lambda v, o=ts(tiny, 0, 4),
                     a=mv1[:, 2 * c0 + 1:2 * (c0 + NG):2]:
                     v.tensor_scalar(o, a, LN_EPS, None, op0=ALU.add),
                     after=[aggr_last, al.get("lnfree")], sig="ve")
        t_ln = p.op("scalar", lambda s, o=ts(tiny, 4, 4), a=ts(tiny, 0, 4):
                    s.activation(o, a, AF.Ln), after=[t_eps], sig="ac")
        al["lnfree"] = t_ln
        t_rs = p.op("scalar", lambda s, o=rstd1[:, c0:c0 + NG], a=ts(tiny, 4, 4):
                    s.activation(o, a, AF.Exp, bias=0.0, scale=-0.5), sig="ac")
        t_mb = p.op("vector", lambda v, o=mb1[:, c0:c0 + NG],
                    a=mv1[:, 2 * c0:2 * (c0 + NG):2], b=rstd1[:, c0:c0 + NG]:
                    v.scalar_tensor_tensor(o, a, -1.0, b, op0=ALU.mult, op1=ALU.mult),
                    after=[t_rs], sig="ve")
        xn_last = None
        for g in range(NG):
            gc = c0 + g
            xn_last = p.op(
                "scalar",
                lambda s, o=ts(xn_tm, g * D, D), a=ts(x_t, g * D, D),
                       sa=rstd1[:, gc:gc + 1], ba=mb1[:, gc:gc + 1]:
                    s.activation(o, a, AF.Identity, bias=ba, scale=sa),
                after=[t_mb, al.get("xn_tm_free")], sig="ac")
        lnst["xn_last"] = xn_last
        lnst[("xn_done", t)] = xn_last

    emit_ln(0)
    for t in range(NT):
        xn_done_t = lnst[("xn_done", t)]

        evt = []
        for c in range(NCF):
            half = c % 2
            tl = None
            for g in range(NG):
                tl = p.op(
                    "tensor",
                    lambda te, o=pbf[:, half * 512 + g * 128:half * 512 + (g + 1) * 128],
                           a=ts(xn_tm, g * D + c * 128, 128):
                        te.transpose(o, a, ident[:, :]),
                    after=[xn_done_t, evt[c - 1] if c >= 1 else al.get("pbf_free")],
                    sig="pe")
            evt.append(p.op(
                "vector",
                lambda v, o=ts(xn_fm, c * TILE, TILE),
                       a=pbf[:, half * 512:half * 512 + 512]:
                    v.tensor_copy(o, a), after=[tl], sig="ve"))
        al["pbf_free"] = evt[-1]
        al["xn_tm_free"] = p.now("pe")
        xn_fm_done = evt[-1]

        def emit_k(g):
            for n in range(2):
                for c in range(NCF):
                    p.op("tensor",
                         lambda te, o=kps[:, n * 512:(n + 1) * 512],
                                l=ts(xn_fm, c * TILE + g * 128, 128),
                                r=ts(WkT, c * D + n * 512, 512),
                                st=(c == 0), sp=(c == NCF - 1):
                             te.matmul(o, l, r, start=st, stop=sp,
                                       skip_group_check=True),
                         after=[evt[c], al.get("kps_free")], sig="pe")
            return p.now("pe")

        def emit_v(g):
            for n in range(2):
                for c in range(NCF):
                    p.op("tensor",
                         lambda te, o=vps[:, n * 512:(n + 1) * 512],
                                l=ts(xn_fm, c * TILE + g * 128, 128),
                                r=ts(WvT, c * D + n * 512, 512),
                                st=(c == 0), sp=(c == NCF - 1):
                             te.matmul(o, l, r, start=st, stop=sp,
                                       skip_group_check=True),
                         after=[evt[c], al.get("vps_free"),
                                phiq_ps_tok.get(("q", 1))], sig="pe")
            return p.now("pe")

        def emit_q(cq):
            if cq % 2 == 0:
                bank, key, extra = pb4t[:, :], ("q", 0), None
            else:
                bank, key, extra = vps[:, 0:512], ("q", 1), al.get("vev_last")
            for cp in range(NCF):
                p.op("tensor",
                     lambda te, o=bank, l=ts(WqT, cp * D + cq * 128, 128),
                            r=ts(xn_fm, cp * TILE, TILE), st=(cp == 0),
                            sp=(cp == NCF - 1):
                         te.matmul(o, l, r, start=st, stop=sp,
                                   skip_group_check=True),
                     after=[xn_fm_done, phiq_ps_tok.get(key), extra, bias_built],
                     sig="pe")
            return bank, key, p.now("pe")

        def emit_phik(g, k_done):
            ksl = g % 2
            kk = k_tm[:, ksl * D:(ksl + 1) * D]
            tt = p.op("vector", lambda v, o=ts(scr, 0, D), a=kps[:, 0:1024], b=bkb[:, :]:
                      v.scalar_tensor_tensor(o, a, 1.0, b, op0=ALU.mult, op1=ALU.add),
                      after=[k_done, bias_built], sig="ve")
            al["kps_free"] = tt
            p.op("vector", lambda v, o=ts(scr, D, D), a=ts(scr, 0, D):
                 v.tensor_scalar(o, a, 0.0, None, op0=ALU.max), sig="ve")
            m_t = p.op("vector", lambda v, o=ts(scr, 0, D), a=ts(scr, 0, D):
                       v.tensor_scalar(o, a, 0.0, None, op0=ALU.min), sig="ve")
            e_t = p.op("scalar", lambda s, o=ts(scr, 0, D), a=ts(scr, 0, D):
                       s.activation(o, a, AF.Exp), after=[m_t], sig="ac")
            kphi = p.op("vector", lambda v, o=kk, a=ts(scr, D, D), b=ts(scr, 0, D):
                        v.tensor_add(o, a, b),
                        after=[e_t, al.get(("kv_done", ksl))], sig="ve")
            return kk, kphi

        def emit_vev(g, v_done):
            ksl = g % 2
            vsl = v_ag[:, ksl * VAGW:ksl * VAGW + NH * 65]
            vev = p.op(
                "vector",
                lambda v, o=vsl.rearrange("p (h e) -> p h e", e=65)[:, :, 0:64],
                       a=vps[:, 0:1024].rearrange("p (h e) -> p h e", e=64),
                       b=bvb[:, :].rearrange("p (h e) -> p h e", e=64):
                    v.scalar_tensor_tensor(o, a, 1.0, b, op0=ALU.mult, op1=ALU.add),
                after=[v_done, bias_built, al.get(("kv_done", ksl))], sig="ve")
            al["vps_free"] = vev
            al["vev_last"] = vev
            return vsl, vev

        def emit_phiq(qbank, key, q_tok, cqq):
            p.op("vector", lambda v, o=ts(scr, 1024, 512), a=qbank,
                 sa=bqfm[:, cqq:cqq + 1]:
                 v.tensor_scalar(o, a, sa, 0.0, op0=ALU.add, op1=ALU.max),
                 after=[q_tok, bq_tok], sig="ve")
            m2_ = p.op("vector", lambda v, o=ts(scr, 1536, 512), a=qbank,
                       sa=bqfm[:, cqq:cqq + 1]:
                       v.tensor_scalar(o, a, sa, 0.0, op0=ALU.add, op1=ALU.min),
                       sig="ve")
            e2_ = p.op("scalar", lambda s, o=ts(scr, 1536, 512), a=ts(scr, 1536, 512):
                       s.activation(o, a, AF.Exp), after=[m2_], sig="ac")
            ph = p.op("vector", lambda v, o=ts(phiq, cqq * NTOK + t * TILE, TILE),
                      a=ts(scr, 1024, 512), b=ts(scr, 1536, 512):
                      v.tensor_add(o, a, b), after=[e2_], sig="ve")
            phiq_ps_tok[key] = ph

        def emit_kv(g, kk, vsl, kphi, vev):
            for h in range(NH):
                bank = pb5t if h < 8 else pb6t
                q_ = (h % 8) // 2
                par = (h % 2) * 64
                p.op("tensor",
                     lambda te, o=bank[par:par + 64, q_ * 65:q_ * 65 + 65],
                            l=kk[:, h * HD:(h + 1) * HD],
                            r=vsl[:, h * 65:(h + 1) * 65],
                            sp=(t == NT - 1 and g == NG - 1),
                            tp=(0, par):
                         te.matmul(o, l, r, start=False, stop=sp, tile_position=tp,
                                   skip_group_check=True),
                     after=[kphi, vev], sig="pe")
            al[("kv_done", g % 2)] = p.now("pe")

        def maybe_prep(k):
            nonlocal prep_done
            for _ in range(k):
                if prep_done < len(prep_jobs):
                    emit_prep(prep_done)
                    prep_done += 1

        pend = {}
        qi = 0
        for g in range(NG):
            kd = emit_k(g)
            vd = emit_v(g)
            kk, kphi = emit_phik(g, kd)
            vsl, vev = emit_vev(g, vd)
            qb, qk, qd = emit_q(qi)
            emit_phiq(qb, qk, qd, qi)
            qi += 1
            pend[g] = (kk, vsl, kphi, vev)
            if g >= 1:
                kk2, vsl2, kphi2, vev2 = pend.pop(g - 1)
                emit_kv(g - 1, kk2, vsl2, kphi2, vev2)
            if g == 1 and t + 1 < NT:
                emit_ln(t + 1)
            maybe_prep(1 if g < 3 else 0)
        for _ in range(NG):
            qb, qk, qd = emit_q(qi)
            emit_phiq(qb, qk, qd, qi)
            qi += 1
            maybe_prep(1)
            if qi == 6:
                kk2, vsl2, kphi2, vev2 = pend.pop(NG - 1)
                emit_kv(NG - 1, kk2, vsl2, kphi2, vev2)
        if t == NT - 1:
            maybe_prep(len(prep_jobs) - prep_done)
        phiq_tile_tok[t] = p.now("ve")

    kv_done = p.now("pe")
    ev1 = p.op("vector", lambda v: v.tensor_copy(kv_sb[:, 0:260], pb[5][:, 0:260]),
               after=[kv_done], sig="ve")
    ev2 = p.op("vector", lambda v: v.tensor_copy(kv_sb[:, 260:520], pb[6][:, 0:260]),
               sig="ve")
    if dbg:
        p.op("gpsimd", lambda g: g.dma_start(out=dbg_kvloc.ap(), in_=kv_sb[:, :]),
             after=[ev1, ev2], sig="sg", amt=16)
        p.op("gpsimd", lambda g: g.dma_start(out=dbg_phiq.ap(), in_=phiq[:, :]),
             after=[phiq_tile_tok[NT - 1]], sig="sg", amt=16)
        p.op("gpsimd", lambda g: g.dma_start(out=dbg_ktm.ap(), in_=k_tm[:, :]),
             after=[kv_done], sig="sg", amt=16)
        p.op("gpsimd", lambda g: g.dma_start(out=dbg_vag.ap(), in_=v_ag[:, :]),
             after=[kv_done], sig="sg", amt=16)
        p.op("gpsimd", lambda g: g.dma_start(out=dbg_xnfm.ap(), in_=xn_fm[:, :]),
             after=[kv_done], sig="sg", amt=16)
        p.op("gpsimd", lambda g: g.dma_start(out=dbg_xntm.ap(), in_=xn_tm[:, :]),
             after=[kv_done], sig="sg", amt=16)
        p.op("gpsimd", lambda g: g.dma_start(out=dbg_rstd.ap(), in_=rstd1[:, :]),
             after=[kv_done], sig="sg", amt=16)
        p.op("gpsimd", lambda g: g.dma_start(out=dbg_mv1.ap(), in_=mv1[:, :]),
             after=[kv_done], sig="sg", amt=16)
        for bi, bn in enumerate(("q", "k", "v", "o")):
            p.op("gpsimd", lambda g, o=dbg_brow.ap()[0:1, bi * D:(bi + 1) * D],
                 a=brow[bn][:, :]: g.dma_start(out=o, in_=a),
                 after=[kv_done], sig="sg", amt=16)
        p.op("gpsimd", lambda g: g.dma_start(out=dbg_wkt.ap(), in_=WkT[:, :]),
             after=[kv_done], sig="sg", amt=16)
    dbg_dump_tok = p.now("sg")
    mtoks = []
    for s_ in range(4):
        mtoks.append(p.op(
            "vector", lambda v, o=ts(scr, s_ * 520, 520), m=smask[:, s_:s_ + 1]:
                v.tensor_scalar(o, kv_sb[:, :], m, None, op0=ALU.mult),
            after=[ev1, ev2, smask_tok], sig="ve"))
    d1 = p.op("gpsimd", lambda g: g.dma_start(out=cc_kv_in.ap(), in_=ts(scr, 0, 2080)),
              after=[mtoks[-1]], sig="sg", amt=16)
    cc1 = p.op("gpsimd", lambda g: g.collective_compute(
        "AllReduce", ALU.add, replica_groups=[list(range(8))],
        ins=[cc_kv_in.ap()], outs=[cc_kv_out.ap()]), after=[d1], sig="sc")
    d2a = p.op("gpsimd", lambda g: g.dma_start(out=ts(scr, 0, 2080), in_=cc_kv_out.ap()),
               after=[cc1], sig="sg", amt=16)
    d2 = p.op("vector", lambda v: v.tensor_scalar(kv_sb[:, :], ts(scr, 0, 520),
                                                  smask[:, 0:1], None, op0=ALU.mult),
              after=[d2a], sig="ve")
    for s_ in range(1, 4):
        d2 = p.op("vector", lambda v, a=ts(scr, s_ * 520, 520), m=smask[:, s_:s_ + 1]:
                  v.scalar_tensor_tensor(kv_sb[:, :], a, m, kv_sb[:, :],
                                         op0=ALU.mult, op1=ALU.add),
                  after=[d2], sig="ve")

    esA.close()
    esB = ExitStack()
    epair = esB.enter_context(nc.sbuf_tensor("epairt", [16, NCF * 128], BF16))
    kv_bd = esB.enter_context(nc.sbuf_tensor("kv_bd", [128, 8 * 128], BF16))
    ksum2 = esB.enter_context(nc.sbuf_tensor("ksum2", [128, NCF * 16], BF16))
    nrm_sb = esB.enter_context(nc.sbuf_tensor("nrm_sb", [16, TILE], F32))
    rec_sb = esB.enter_context(nc.sbuf_tensor("rec_sb", [16, TILE], F32))
    rec_bf = esB.enter_context(nc.sbuf_tensor("rec_bf", [16, TILE], BF16))
    rbc_sb = esB.enter_context(nc.sbuf_tensor("rbc_sb", [128, TILE], F32))
    qkv_fm_sb = esB.enter_context(nc.sbuf_tensor("qkv_fm", [128, NCF * TILE], BF16))
    x1_sb = esB.enter_context(nc.sbuf_tensor("x1_sb", [128, NG * D], F32))
    epair_tok = p.op("sync", lambda s: s.dma_start(out=epair[:, :], in_=epair_in.ap()),
                     after=[kv_done, prep["stores"][-1], d2, dbg_dump_tok], sig="sx", amt=16)

    if dbg:
        p.op("gpsimd", lambda g: g.dma_start(out=dbg_kv.ap(), in_=cc_kv_out.ap()),
             after=[cc1], sig="sg", amt=16)
    bd0 = p.op("vector", lambda v: v.memset(kv_bd[:, :], 0.0),
               after=[d2, kv_done, prep["stores"][-1], dbg_dump_tok], sig="ve")
    p.op("vector", lambda v: v.memset(ksum2[:, :], 0.0), sig="ve")
    kvbd_done = None
    for par in (0, 64):
        for b in range(2):
            p.op("vector",
                 lambda v, o=kv_bd[par:par + 64, 512 * b:512 * b + 512]
                          .rearrange("p (q e) -> p q e", e=128)[:, :, par:par + 64],
                        a=kv_sb[par:par + 64, 260 * b:260 * b + 260]
                          .rearrange("p (q e) -> p q e", e=65)[:, :, 0:64]:
                     v.tensor_copy(o, a), after=[bd0], sig="ve")
    for par, off in ((0, 0), (64, 1)):
        for b in range(2):
            kvbd_done = p.op(
                "vector",
                lambda v, o=bass.AP(ksum2, 72 * b + off, [[NCF * 16, 64], [18, 4]],
                                    base_partition=par)
                       if False else
                       ksum2[par:par + 64, 72 * b + off:72 * b + off + 55:18],
                       a=kv_sb[par:par + 64, 260 * b + 64:260 * b + 260:65]:
                    v.tensor_copy(o, a), sig="ve")

    bl = {"xfree": al.get("xfree"), "tin2_free": lnst.get("aggr_last")}
    for t in range(NT):
        xin = tok_ap(x_in, t)
        ldx = p.op("sync", lambda s, o=sb3(x_t), a=xin: s.dma_start(out=o, in_=a),
                   after=[bl.get("xfree")], sig="sx", amt=16)

        for c in range(NCF):
            p.op("tensor", lambda te, o=pb[4][0:16, :], l=ts(ksum2, c * 16, 16),
                 r=ts(phiq, c * NTOK + t * TILE, TILE), st=(c == 0), sp=(c == NCF - 1):
                 te.matmul(o, l, r, start=st, stop=sp, skip_group_check=True),
                 after=[kvbd_done, bl.get("nrm_ps"), phiq_tile_tok[t],
                        phiq_ps_tok.get(id(pb[4])) if t == 0 else None], sig="pe")
        nm = p.now("pe")
        n1 = p.op("vector", lambda v: v.tensor_scalar(nrm_sb[:, :], pb[4][0:16, :],
                                                      ATTN_EPS, None, op0=ALU.max),
                  after=[nm], sig="ve")
        bl["nrm_ps"] = n1
        n2 = p.op("vector", lambda v: v.reciprocal(rec_sb[:, :], nrm_sb[:, :]),
                  after=[n1, bl.get("rec_free")], sig="ve")
        n2 = p.op("vector", lambda v: v.tensor_copy(rec_bf[:, :], rec_sb[:, :]),
                  after=[n2], sig="ve")

        mm2 = None
        for pr in range(NCF):
            bank = pb[pr % 3]
            mm1 = p.op("tensor", lambda te, o=bank[:, :], l=ts(kv_bd, pr * 128, 128),
                       r=ts(phiq, pr * NTOK + t * TILE, TILE):
                       te.matmul(o, l, r, start=True, stop=True, skip_group_check=True),
                       after=[kvbd_done, bl.get(("qps", pr % 3)),
                              phiq_ps_tok.get(id(pb[0])) if t == 0 and pr % 3 == 0 else None],
                       sig="pe")
            mm2 = p.op("tensor", lambda te, o=pb[3][:, :], l=ts(epair, pr * 128, 128),
                       r=rec_bf[:, :]:
                       te.matmul(o, l, r, start=True, stop=True, skip_group_check=True),
                       after=[n2, bl.get("bc_ps"),
                              epair_tok if t == 0 and pr == 0 else None], sig="pe")
            bc = p.op("scalar", lambda s, o=rbc_sb[:, :], a=pb[3][:, :]:
                      s.activation(o, a, AF.Copy),
                      after=[mm2, bl.get("rbc_free")], sig="ac")
            bl["bc_ps"] = bc
            ev = p.op("vector", lambda v, o=ts(qkv_fm_sb, pr * TILE, TILE), a=bank[:, :],
                      b=rbc_sb[:, :]:
                      v.scalar_tensor_tensor(o, a, 1.0, b, op0=ALU.mult, op1=ALU.mult),
                      after=[mm1, bc], sig="ve")
            bl[("qps", pr % 3)] = ev
            bl["rbc_free"] = ev
        bl["rec_free"] = mm2
        qkv_done = p.now("ve")

        evbank = {}
        for g in range(NG):
            for n in range(2):
                bank = pb[(2 * g + n) % 4]
                for c in range(NCF):
                    p.op("tensor", lambda te, o=bank[:, :],
                         l=ts(qkv_fm_sb, c * TILE + g * 128, 128),
                         r=ts(WoT, c * D + n * 512, 512), st=(c == 0):
                         te.matmul(o, l, r, start=st, stop=False, skip_group_check=True),
                         after=[qkv_done, wT_tok["wo"], bl.get(("ops", (2 * g + n) % 4))],
                         sig="pe")
                om = p.op("tensor", lambda te, o=bank[:, :], l=onesc[0:1, 0:128],
                          r=brow["o"][0:1, n * 512:(n + 1) * 512]:
                          te.matmul(o, l, r, start=False, stop=True, skip_group_check=True),
                          after=[brow_tok], sig="pe")
                ev = p.op("vector", lambda v, o=ts(x1_sb, g * D + n * 512, 512),
                          a=bank[:, :], b=ts(x_t, g * D + n * 512, 512):
                          v.scalar_tensor_tensor(o, a, 1.0, b, op0=ALU.mult, op1=ALU.add),
                          after=[om, ldx, bl.get("x1free")], sig="ve")
                bl[("ops", (2 * g + n) % 4)] = ev
                evbank[(g, n)] = ev
        bl["xfree"] = p.now("ve")
        for b in range(3):
            bl[("qps", b)] = p.now("ve")

        for g in range(NG):
            col = 2 * (t * NG + g)
            p.op("scalar", lambda s, o=gsc[:, :], a=ts(x1_sb, g * D, D),
                 xc=tin2[:, 0:1]:
                 s.activation(o, a, AF.Copy, accum_out=xc),
                 after=[evbank[(g, 0)], evbank[(g, 1)], bl.get("tin2_free")], sig="ac")
            s2t = p.op("scalar", lambda s, o=gsc[:, :], a=ts(x1_sb, g * D, D),
                       xc=tin2[:, 1:2]:
                       s.activation(o, a, AF.Square, accum_out=xc), sig="ac")
            p.op("vector", lambda v, o=mv2[:, col:col + 1], a=tin2[:, 0:1]:
                 v.tensor_scalar(o, a, 1.0 / D, None, op0=ALU.mult),
                 after=[s2t], sig="ve")
            p.op("vector", lambda v, o=tin2[:, 2:3], a=tin2[:, 1:2]:
                 v.tensor_scalar(o, a, 1.0 / D, None, op0=ALU.mult), sig="ve")
            p.op("vector", lambda v, o=tin2[:, 3:4], a=mv2[:, col:col + 1]:
                 v.tensor_mul(o, a, a), sig="ve")
            aggr_last = p.op("vector", lambda v, o=mv2[:, col + 1:col + 2],
                             a=tin2[:, 2:3], b=tin2[:, 3:4]:
                 v.tensor_sub(o, a, b), sig="ve")
            bl["tin2_free"] = aggr_last
            ag = aggr_last
            sb_ = p.op("vector", lambda v, o=ts(scr, 0, D), a=ts(x1_sb, g * D, D),
                       sa=mv2[:, col:col + 1]:
                       v.tensor_scalar(o, a, sa, None, op0=ALU.subtract),
                       after=[ag], sig="ve")
            p.op("vector", lambda v, o=xm2[:, t * NG + g:t * NG + g + 1],
                 a=ts(scr, 0, D):
                 v.reduce_max(o, a, axis=AX, apply_absolute_value=True),
                 after=[sb_], sig="ve")
        xm_done = p.now("ve")
        st = p.op("gpsimd", lambda g, o=tok_ap(x1_hbm, t), a=sb3(x1_sb):
                  g.dma_start(out=o, in_=a), after=[xm_done], sig="sg", amt=16)
        bl["x1free"] = st
        bl[("x1st", t)] = st

    t_eps = p.op("vector", lambda v, o=ts(scr, 0, 32), a=mv2[:, 1:64:2]:
                 v.tensor_scalar(o, a, LN_EPS, None, op0=ALU.add), sig="ve")
    t_ln = p.op("scalar", lambda s, o=ts(scr, 32, 32), a=ts(scr, 0, 32):
                s.activation(o, a, AF.Ln), after=[t_eps], sig="ac")
    t_rs2 = p.op("scalar", lambda s, o=ts(scr, 64, 32), a=ts(scr, 32, 32):
                 s.activation(o, a, AF.Exp, bias=0.0, scale=-0.5), sig="ac")
    t_xm = p.op("vector", lambda v, o=ts(scr, 96, 32), a=xm2[:, 0:32], b=ts(scr, 64, 32):
                v.tensor_mul(o, a, b), after=[t_rs2], sig="ve")
    t_mx = p.op("vector", lambda v, o=ts(tiny, 0, 1), a=ts(scr, 96, 32):
                v.reduce_max(o, a, axis=AX), after=[t_xm], sig="ve")

    d3 = p.op("gpsimd", lambda g: g.dma_start(out=cc_m1_in.ap()[0:1, :], in_=ts(tiny, 0, 1)),
              after=[t_mx], sig="sg", amt=16)
    cc2 = p.op("gpsimd", lambda g: g.collective_compute(
        "AllReduce", ALU.max, replica_groups=[list(range(8))],
        ins=[cc_m1_in.ap()], outs=[cc_m1_out.ap()]), after=[d3], sig="sc")
    bc1 = p.op("gpsimd", lambda g: g.dma_start(
        out=gmaxb[:, :], in_=bass.AP(cc_m1_out, 0, [[0, 128], [1, 128]])),
        after=[cc2], sig="sg", amt=16)
    gm1 = p.op("vector", lambda v: v.reduce_max(ts(tiny, 0, 1), gmaxb[:, :], axis=AX),
               after=[bc1], sig="ve")
    if dbg:
        p.op("gpsimd", lambda g: g.dma_start(out=dbg_m1.ap(), in_=cc_m1_out.ap()),
             after=[cc2], sig="sg", amt=16)
    bcx = p.op("gpsimd", lambda g: g.dma_start(
        out=ts(tiny, 1, 1), in_=bass.AP(clipx, 0, [[0, 128], [1, 1]])),
        sig="sg", amt=16)
    e1 = p.op("scalar", lambda s: s.activation(ts(tiny, 2, 1), ts(tiny, 1, 1),
                                               AF.Exp, bias=0.0, scale=-1.0),
              after=[bcx], sig="ac")
    e2 = p.op("vector", lambda v: v.tensor_scalar(ts(tiny, 2, 1), ts(tiny, 2, 1),
                                                  1.0, 127.0, op0=ALU.add, op1=ALU.mult),
              after=[e1], sig="ve")
    e3 = p.op("vector", lambda v: v.reciprocal(ts(tiny, 3, 1), ts(tiny, 0, 1)),
              after=[gm1], sig="ve")
    e4 = p.op("vector", lambda v: v.tensor_mul(scl[:, 0:1], ts(tiny, 2, 1), ts(tiny, 3, 1)),
              after=[e2, e3], sig="ve")
    e5 = p.op("vector", lambda v: v.reciprocal(ts(tiny, 3, 1), ts(tiny, 2, 1)), sig="ve")
    e6 = p.op("vector", lambda v: v.tensor_mul(scl[:, 1:2], ts(tiny, 0, 1), ts(tiny, 3, 1)),
              sig="ve")
    e7 = p.op("vector", lambda v: v.tensor_scalar(a2c[:, 0:32], ts(scr, 64, 32),
                                                  scl[:, 0:1], None, op0=ALU.mult),
              after=[e4], sig="ve")
    e8 = p.op("vector", lambda v: v.scalar_tensor_tensor(b2c[:, 0:32], mv2[:, 0:64:2],
                                                         -1.0, a2c[:, 0:32],
                                                         op0=ALU.mult, op1=ALU.mult),
              after=[e7], sig="ve")
    e9 = p.op("vector", lambda v: v.tensor_scalar(sg_all[:, :], wgs[:, :], scl[:, 1:2],
                                                  None, op0=ALU.mult),
              after=[e6, prep["stores"][NCI - 1]], sig="ve")
    scales_done = p.op("vector", lambda v: v.tensor_scalar(su_all[:, :], wus[:, :],
                                                           scl[:, 1:2], None, op0=ALU.mult),
                       after=[prep["stores"][-1]], sig="ve")

    b_pe_done = p.now("pe")
    esB.close()
    esPhi.close()
    esWo.close()
    esC = ExitStack()
    wgT = esC.enter_context(nc.sbuf_tensor("wgT", [128, NCF * FF], BF16))
    wuT = esC.enter_context(nc.sbuf_tensor("wuT", [128, NCF * FF], BF16))
    xq_tm = esC.enter_context(nc.sbuf_tensor("xq_tm", [128, NG * D], BF16))
    xq_fm = esC.enter_context(nc.sbuf_tensor("xq_fm", [128, NCF * TILE], BF16))
    g_sb = esC.enter_context(nc.sbuf_tensor("g_sb", [128, 512], F32))
    u_sb = esC.enter_context(nc.sbuf_tensor("u_sb", [128, 512], F32))
    h_sb = [esC.enter_context(nc.sbuf_tensor(f"h_sb{i}", [128, 512], F32)) for i in range(2)]
    wdstg = esC.enter_context(nc.sbuf_tensor("wdstg", [128, FF], F32))
    wdq_sb = esC.enter_context(nc.sbuf_tensor("wdq_sb", [128, FF], BF16))
    dum = esC.enter_context(nc.sbuf_tensor("dum", [128, 512], F32))

    wgT_tok = []
    wuT_tok = []
    for c in range(NCF):
        wgT_tok.append(p.op(
            "sync", lambda s, o=ts(wgT, c * FF, FF), i=wg_q.ap()[:, c * 128:(c + 1) * 128]:
                s.dma_start_transpose(o, i),
            after=[prep["stores"][NCI - 1], b_pe_done, bl[("x1st", NT - 1)],
                   scales_done], sig="sx", amt=16))
        wuT_tok.append(p.op(
            "sync", lambda s, o=ts(wuT, c * FF, FF), i=wu_q.ap()[:, c * 128:(c + 1) * 128]:
                s.dma_start_transpose(o, i),
            after=[prep["stores"][-1], b_pe_done, scales_done], sig="sx", amt=16))

    wdp = {"wstg_free": None, "wq_free": None, "stores": [],
           "extra": [b_pe_done, scales_done, bl[("x1st", NT - 1)]]}

    def emit_wd_prep(i):
        ld = p.op("sync", lambda s, o=wdstg[:, :], a=wdd.ap()[i * 128:(i + 1) * 128, :]:
                  s.dma_start(out=o, in_=a),
                  after=[wdp["wstg_free"]] + wdp.pop("extra", []), sig="sx", amt=16)
        rd = p.op("vector", lambda v, o=wds[:, i:i + 1], a=wdstg[:, :]:
                  v.reduce_max(o, a, axis=AX, apply_absolute_value=True),
                  after=[ld], sig="ve")
        sc = p.op("vector", lambda v, o=wds[:, i:i + 1], a=wds[:, i:i + 1]:
                  v.tensor_scalar(o, a, 1.0 / 127.0, None, op0=ALU.mult),
                  after=[rd], sig="ve")
        rc = p.op("vector", lambda v, o=tin2[:, 7:8], a=wds[:, i:i + 1]:
                  v.reciprocal(o, a), after=[sc], sig="ve")
        t1 = p.op("scalar", lambda s, o=wdstg[:, :], a=wdstg[:, :], sa=tin2[:, 7:8]:
                  s.activation(o, a, AF.Copy, bias=0.0, scale=sa), after=[rc], sig="ac")
        t2a = p.op("scalar", lambda s, o=wdstg[:, :], a=wdstg[:, :]:
                   s.activation(o, a, AF.Copy, bias=MAGIC, scale=1.0),
                   after=[t1], sig="ac")
        t2 = p.op("scalar", lambda s, o=wdstg[:, :], a=wdstg[:, :]:
                  s.activation(o, a, AF.Copy, bias=-MAGIC, scale=1.0),
                  after=[t2a], sig="ac")
        t3 = p.op("vector", lambda v, o=wdstg[:, :], a=wdstg[:, :]:
                  v.tensor_scalar(o, a, -128.0, 127.0, op0=ALU.max, op1=ALU.min),
                  after=[t2], sig="ve")
        t5 = p.op("scalar", lambda s, o=wdq_sb[:, :], a=wdstg[:, :], sa=wds[:, i:i + 1]:
                  s.activation(o, a, AF.Copy, bias=0.0, scale=sa),
                  after=[t3, wdp["wq_free"]], sig="ac")
        st = p.op("gpsimd", lambda g, o=wd_q.ap()[i * 128:(i + 1) * 128, :], a=wdq_sb[:, :]:
                  g.dma_start(out=o, in_=a), after=[t5], sig="sg", amt=16)
        wdp["wstg_free"] = t5
        wdp["wq_free"] = st
        wdp["stores"].append(st)

    p.op("scalar", lambda s: load_act_set(s, 18), sig="ac")
    cl = {"xfree": bl["xfree"]}
    hst_tok = {}
    for t in range(NT):
        xin = tok_ap(x1_hbm, t)
        ldx = p.op("sync", lambda s, o=sb3(x_t), a=xin: s.dma_start(out=o, in_=a),
                   after=[cl.get("xfree"), bl[("x1st", t)]], sig="sx", amt=16)

        xq_toks = []
        for g in range(NG):
            gc = t * NG + g
            q1 = p.op("scalar", lambda s, o=ts(scr, 0, D), a=ts(x_t, g * D, D),
                      sa=a2c[:, gc:gc + 1], ba=b2c[:, gc:gc + 1]:
                      s.activation(o, a, AF.Identity, bias=ba, scale=sa),
                      after=[ldx, e8, xq_toks[-1] if g else cl.get("scr_free"),
                             cl.get("xq_tm_free"),
                             b_pe_done if t == 0 and g == 0 else None], sig="ac")
            q2 = p.op("vector", lambda v, o=ts(scr, 0, D), a=ts(scr, 0, D):
                      v.tensor_scalar(o, a, -128.49999, 127.49998,
                                      op0=ALU.max, op1=ALU.min),
                      after=[q1], sig="ve")
            xq_toks.append(p.op("vector", lambda v, o=ts(xq_tm, g * D, D), a=ts(scr, 0, D):
                                v.tensor_scalar(o, a, MAGIC, -MAGIC,
                                                op0=ALU.add, op1=ALU.add),
                                after=[q2], sig="ve"))
        cl["xfree"] = xq_toks[-1]
        cl["scr_free"] = xq_toks[-1]

        evt = []
        for c in range(NCF):
            half = c % 2
            tl = None
            for g in range(NG):
                tl = p.op(
                    "tensor",
                    lambda te, o=pbf[:, half * 512 + g * 128:half * 512 + (g + 1) * 128],
                           a=ts(xq_tm, g * D + c * 128, 128):
                        te.transpose(o, a, ident[:, :]),
                    after=[xq_toks[-1], evt[c - 1] if c >= 1 else cl.get("pbf_free")],
                    sig="pe")
            evt.append(p.op(
                "vector",
                lambda v, o=ts(xq_fm, c * TILE, TILE),
                       a=pbf[:, half * 512:half * 512 + 512]:
                    v.tensor_copy(o, a), after=[tl], sig="ve"))
        cl["pbf_free"] = evt[-1]
        cl["xq_tm_free"] = p.now("pe")

        for i in range(NCI):
            bg = pb[2 * (i % 3)]
            bu = pb[2 * (i % 3) + 1]
            for c in range(NCF):
                p.op("tensor", lambda te, o=bg[:, :], l=ts(wgT, c * FF + i * 128, 128),
                     r=ts(xq_fm, c * TILE, TILE), st=(c == 0), sp=(c == NCF - 1):
                     te.matmul(o, l, r, start=st, stop=sp, skip_group_check=True),
                     after=[evt[c], wgT_tok[c], cl.get(("gps", i % 3))], sig="pe")
            gmm = p.now("pe")
            for c in range(NCF):
                p.op("tensor", lambda te, o=bu[:, :], l=ts(wuT, c * FF + i * 128, 128),
                     r=ts(xq_fm, c * TILE, TILE), st=(c == 0), sp=(c == NCF - 1):
                     te.matmul(o, l, r, start=st, stop=sp, skip_group_check=True),
                     after=[wuT_tok[c]], sig="pe")
            umm = p.now("pe")
            hb = h_sb[i % 2]
            gv = p.op("scalar", lambda s, o=g_sb[:, :], a=bg[:, :], sa=sg_all[:, i:i + 1]:
                      s.activation(o, a, AF.Silu, bias=0.0, scale=sa),
                      after=[gmm, e9, scales_done, cl.get("g_free")], sig="ac")
            uv = p.op("scalar", lambda s, o=u_sb[:, :], a=bu[:, :], sa=su_all[:, i:i + 1]:
                      s.activation(o, a, AF.Copy, bias=0.0, scale=sa),
                      after=[umm], sig="ac")
            hm = p.op("gpsimd", lambda g, o=hb[:, :], a=g_sb[:, :], b=u_sb[:, :]:
                      g.tensor_mul(o, a, b), after=[gv, uv, cl.get(("hfree", i % 2))],
                      sig="gp")
            cl[("gps", i % 3)] = hm
            cl["g_free"] = hm
            ht = p.op("vector", lambda v, o=tiny[:, 6:7], a=hb[:, :]:
                      v.reduce_max(o, a, axis=AX, apply_absolute_value=True),
                      after=[hm], sig="ve")
            p.op("vector", lambda v, o=habs[:, i:i + 1], a=habs[:, i:i + 1], b=tiny[:, 6:7]:
                 v.tensor_max(o, a, b), after=[ht], sig="ve")
            st = p.op("gpsimd", lambda g, o=h_hbm.ap()[i, :, t * TILE:(t + 1) * TILE],
                      a=hb[:, :]:
                      g.dma_start(out=o, in_=a), after=[hm], sig="sg", amt=16)
            cl[("hfree", i % 2)] = [st, ht]
            hst_tok[(i, t)] = st
        if t < NCF:
            emit_wd_prep(t)

    hmx = p.op("vector", lambda v: v.reduce_max(ts(tiny, 0, 1), habs[:, :], axis=AX),
               sig="ve")
    d4 = p.op("gpsimd", lambda g: g.dma_start(out=cc_m2_in.ap()[0:1, :], in_=ts(tiny, 0, 1)),
              after=[hmx], sig="sg", amt=16)
    cc3 = p.op("gpsimd", lambda g: g.collective_compute(
        "AllReduce", ALU.max, replica_groups=[list(range(8))],
        ins=[cc_m2_in.ap()], outs=[cc_m2_out.ap()]), after=[d4], sig="sc")
    bc2 = p.op("gpsimd", lambda g: g.dma_start(
        out=gmaxb[:, :], in_=bass.AP(cc_m2_out, 0, [[0, 128], [1, 128]])),
        after=[cc3], sig="sg", amt=16)
    gm2 = p.op("vector", lambda v: v.reduce_max(ts(tiny, 0, 1), gmaxb[:, :], axis=AX),
               after=[bc2], sig="ve")
    if dbg:
        p.op("gpsimd", lambda g: g.dma_start(out=dbg_m2.ap(), in_=cc_m2_out.ap()),
             after=[cc3], sig="sg", amt=16)
    bcd = p.op("gpsimd", lambda g: g.dma_start(
        out=ts(tiny, 1, 1), in_=bass.AP(clipd, 0, [[0, 128], [1, 1]])),
        sig="sg", amt=16)
    p.op("scalar", lambda s: load_act_set(s, 6), sig="ac")
    f1 = p.op("scalar", lambda s: s.activation(ts(tiny, 2, 1), ts(tiny, 1, 1),
                                               AF.Exp, bias=0.0, scale=-1.0),
              after=[bcd], sig="ac")
    f2 = p.op("vector", lambda v: v.tensor_scalar(ts(tiny, 2, 1), ts(tiny, 2, 1),
                                                  1.0, 127.0, op0=ALU.add, op1=ALU.mult),
              after=[f1], sig="ve")
    f3 = p.op("vector", lambda v: v.reciprocal(ts(tiny, 3, 1), ts(tiny, 0, 1)),
              after=[gm2], sig="ve")
    p.op("vector", lambda v: v.tensor_mul(scl[:, 2:3], ts(tiny, 2, 1), ts(tiny, 3, 1)),
         after=[f2, f3], sig="ve")
    f5 = p.op("vector", lambda v: v.reciprocal(ts(tiny, 3, 1), ts(tiny, 2, 1)), sig="ve")
    s2_done = p.op("vector", lambda v: v.tensor_mul(scl[:, 3:4], ts(tiny, 0, 1),
                                                    ts(tiny, 3, 1)),
                   after=[f5], sig="ve")

    c_pe_done = p.now("pe")
    c_ac_done = p.now("ac")
    esC.close()
    esD = ExitStack()
    wdT = esD.enter_context(nc.sbuf_tensor("wdT", [128, NCI * D], BF16))
    hq_sb = esD.enter_context(nc.sbuf_tensor("hq_sb", [128, NCI * 512], BF16))
    hstg = [esD.enter_context(nc.sbuf_tensor(f"hstg{i}", [128, 512], F32)) for i in range(2)]
    out_sb = [esD.enter_context(nc.sbuf_tensor(f"out_sb{i}", [128, NG * D], F32))
              for i in range(2)]

    wdT_tok = []
    for c in range(NCI):
        wdT_tok.append(p.op(
            "sync", lambda s, o=ts(wdT, c * D, D), i=wd_q.ap()[:, c * 128:(c + 1) * 128]:
                s.dma_start_transpose(o, i),
            after=[wdp["stores"][-1], c_pe_done, c_ac_done, s2_done,
                   hst_tok[(NCI - 1, NT - 1)]], sig="sx", amt=16))

    dl = {"xfree": cl["xfree"], "psum_free": c_ac_done}
    for t in range(NT):
        xin = tok_ap(x1_hbm, t)
        ldx1 = p.op("sync", lambda s, o=sb3(x_t), a=xin: s.dma_start(out=o, in_=a),
                    after=[dl.get("xfree")], sig="sx", amt=16)
        hq_toks = []
        for i in range(NCI):
            hb = hstg[i % 2]
            ld = p.op("sync", lambda s, o=hb[:, :],
                      a=h_hbm.ap()[i, :, t * TILE:(t + 1) * TILE]:
                      s.dma_start(out=o, in_=a),
                      after=[dl.get(("hfree", i % 2)), hst_tok[(i, t)],
                             c_pe_done if t == 0 and i == 0 else None],
                      sig="sx", amt=16)
            q1 = p.op("scalar", lambda s, o=ts(scr, 0, 512), a=hb[:, :], sa=scl[:, 2:3]:
                      s.activation(o, a, AF.Copy, bias=0.0, scale=sa),
                      after=[ld, s2_done,
                             hq_toks[-1] if i else dl.get("scr_free"),
                             dl.get("hq_free")], sig="ac")
            dl[("hfree", i % 2)] = q1
            q2 = p.op("vector", lambda v, o=ts(scr, 0, 512), a=ts(scr, 0, 512):
                      v.tensor_scalar(o, a, -128.49999, 127.49998,
                                      op0=ALU.max, op1=ALU.min),
                      after=[q1], sig="ve")
            hq_toks.append(p.op("vector", lambda v, o=ts(hq_sb, i * 512, 512),
                                a=ts(scr, 0, 512):
                                v.tensor_scalar(o, a, MAGIC, -MAGIC,
                                                op0=ALU.add, op1=ALU.add),
                                after=[q2], sig="ve"))
        dl["scr_free"] = hq_toks[-1]

        osb = out_sb[t % 2]
        wave_banks = ((pb[0], pb[1], pb[2], pb[3]), (pb[4], pb[5], pb[6], pb[3]))
        for w, groups in enumerate(((0, 1), (2, 3))):
            for i in range(NCI):
                for gi, g in enumerate(groups):
                    for n in range(2):
                        bank = wave_banks[w][2 * gi + n]
                        p.op("tensor", lambda te, o=bank[:, :],
                             l=ts(hq_sb, i * 512 + g * 128, 128),
                             r=ts(wdT, i * D + n * 512, 512),
                             st=(i == 0), sp=(i == NCI - 1):
                             te.matmul(o, l, r, start=st, stop=sp, skip_group_check=True),
                             after=[hq_toks[i], wdT_tok[i],
                                    dl.get(("dps", id(bank))), dl.get("psum_free")],
                             sig="pe")
            wave_done = p.now("pe")
            for gi, g in enumerate(groups):
                for n in range(2):
                    bank = wave_banks[w][2 * gi + n]
                    ev = p.op("vector", lambda v, o=ts(osb, g * D + n * 512, 512),
                              a=bank[:, :], b=ts(x_t, g * D + n * 512, 512):
                              v.scalar_tensor_tensor(o, a, scl[:, 3:4], b,
                                                     op0=ALU.mult, op1=ALU.add),
                              after=[wave_done, ldx1, s2_done,
                                     dl.get(("outfree", t % 2))], sig="ve")
                    dl[("dps", id(bank))] = ev
        dl["psum_free"] = None
        ev_done = p.now("ve")
        dl["xfree"] = ev_done
        dl["hq_free"] = wave_done
        yst = p.op("gpsimd", lambda g, o=tok_ap(y_out, t), a=sb3(osb):
                   g.dma_start(out=o, in_=a), after=[ev_done], sig="sg", amt=16)
        dl[("outfree", t % 2)] = yst
        dl[("yst", t)] = yst

    p.op("gpsimd", lambda g: g.nop(), after=[dl[("yst", NT - 1)]])

    with nc.Block() as block:
        @block.sync
        def _(eng):
            p.emit("sync", eng, sems)

        @block.tensor
        def _(eng):
            p.emit("tensor", eng, sems)

        @block.vector
        def _(eng):
            p.emit("vector", eng, sems)

        @block.scalar
        def _(eng):
            p.emit("scalar", eng, sems)

        @block.gpsimd
        def _(eng):
            p.emit("gpsimd", eng, sems)

    esD.close()
    es.close()
    return nc



def make_inputs(inputs):
    import ml_dtypes
    x = np.asarray(inputs["L0_attn_x"])
    flat = np.ascontiguousarray(x.reshape(8, NTOK, D))
    ident = np.eye(128, dtype=ml_dtypes.bfloat16)
    ones = np.ones((1, TILE), dtype=ml_dtypes.bfloat16)
    epair = np.zeros((16, NCF * 128), dtype=ml_dtypes.bfloat16)
    for pr in range(NCF):
        for j in range(128):
            epair[2 * pr + (j // 64), pr * 128 + j] = 1.0
    common = {
        "Wq": np.ascontiguousarray(inputs["L0_attn_Wq"], dtype=np.float32),
        "Wk": np.ascontiguousarray(inputs["L0_attn_Wk"], dtype=np.float32),
        "Wv": np.ascontiguousarray(inputs["L0_attn_Wv"], dtype=np.float32),
        "Wo": np.ascontiguousarray(inputs["L0_attn_Wo"], dtype=np.float32),
        "bq": np.ascontiguousarray(inputs["L0_attn_bq"], dtype=np.float32).reshape(1, D),
        "bk": np.ascontiguousarray(inputs["L0_attn_bk"], dtype=np.float32).reshape(1, D),
        "bv": np.ascontiguousarray(inputs["L0_attn_bv"], dtype=np.float32).reshape(1, D),
        "bo": np.ascontiguousarray(inputs["L0_attn_bo"], dtype=np.float32).reshape(1, D),
        "wg": np.ascontiguousarray(inputs["L0_ffn_w_gate"], dtype=np.float32),
        "wu": np.ascontiguousarray(inputs["L0_ffn_w_up"], dtype=np.float32),
        "wd": np.ascontiguousarray(inputs["L0_ffn_w_down"], dtype=np.float32),
        "clipx": np.ascontiguousarray(inputs["L0_ffn_clip_x"], dtype=np.float32).reshape(1, 1),
        "clipd": np.ascontiguousarray(inputs["L0_ffn_clip_down"], dtype=np.float32).reshape(1, 1),
        "ident": ident, "onesc": ones, "epair": epair,
    }
    out = []
    for c in range(8):
        sm = np.zeros((128, 4), dtype=np.float32)
        sm[:, c // 2] = 1.0
        out.append(dict(common, x=np.ascontiguousarray(flat[c]), smask=sm))
    return out


def kernel(**inputs):
    in_maps = make_inputs(inputs)
    nc = build_nc(dbg=False)
    res = run_bass_kernel_spmd(nc, in_maps, core_ids=list(range(8)))
    y = np.concatenate([np.asarray(r["y"]) for r in res.results], axis=0)
    return y.reshape(4, 8192, 1024)


# version 12
# speedup vs baseline: 2.4104x; 1.0327x over previous
import numpy as np
from contextlib import ExitStack

import concourse.bass as bass
import concourse.mybir as mybir
from concourse.bass_utils import run_bass_kernel_spmd

F32 = mybir.dt.float32
BF16 = mybir.dt.bfloat16
AF = mybir.ActivationFunctionType
ALU = mybir.AluOpType
AX = mybir.AxisListType.X

MAGIC = 12582912.0
D = 1024
NH = 16
HD = 64
FF = 2816
NTOK = 4096
TILE = 512
NT = NTOK // TILE
NG = TILE // 128
NCF = D // 128
NCI = FF // 128
LN_EPS = 1e-5
ATTN_EPS = 1e-6
VAGW = NH * 65 + 15

ENGINES = ("sync", "tensor", "vector", "scalar", "gpsimd")
SELF_SERIAL = {"vector": "ve", "scalar": "ac", "gpsimd": "gp"}


NPOOL = 8


class Prog:

    def __init__(self):
        self.items = {e: [] for e in ENGINES}
        self.cnt = {}
        self.pool_rr = {"sx": 0, "sg": 0}
        self.pool_last = {}
        self.last_sc = None

    def _flat(self, after):
        out = []
        for t in after:
            if t is None:
                continue
            if isinstance(t, list):
                out.extend(x for x in t if x is not None)
            else:
                out.append(t)
        return out

    def op(self, eng, fn, after=(), sig=None, amt=1):
        after = self._flat(after)
        if sig in ("sx", "sg"):
            i = self.pool_rr[sig] % NPOOL
            self.pool_rr[sig] += 1
            sem = f"{sig}{i}"
            prev = self.pool_last.get(sem)
            if prev is not None:
                after = after + [prev]
            self.cnt[sem] = self.cnt.get(sem, 0) + amt
            tok = (sem, self.cnt[sem])
            self.pool_last[sem] = tok
            self.items[eng].append((after, fn, sem, amt, None))
            return tok
        if sig == "sc":
            if self.last_sc is not None:
                after = after + [self.last_sc]
        own = SELF_SERIAL.get(eng)
        if own is not None and sig is None:
            sig = own
        self_wait = None
        if own is not None:
            self_wait = (own, self.cnt.get(own, 0))
        tok = None
        if sig is not None:
            self.cnt[sig] = self.cnt.get(sig, 0) + amt
            tok = (sig, self.cnt[sig])
        if sig == "sc":
            self.last_sc = tok
        self.items[eng].append((after, fn, sig, amt, self_wait))
        return tok

    def now(self, sig):
        if sig in ("sx", "sg"):
            return [self.pool_last[s] for s in
                    (f"{sig}{i}" for i in range(NPOOL)) if s in self.pool_last]
        return (sig, self.cnt.get(sig, 0))

    def emit(self, eng_name, eng, sems):
        waited = {}
        for waits, fn, sig, amt, self_wait in self.items[eng_name]:
            allw = list(waits)
            if self_wait is not None and self_wait[1] > 0:
                allw.append(self_wait)
            for (s, v) in allw:
                if waited.get(s, 0) >= v:
                    continue
                eng.wait_ge(sems[s], v)
                waited[s] = v
            inst = fn(eng)
            if sig is not None:
                inst.then_inc(sems[sig], amt)


def build_nc(dbg=False):
    nc = bass.Bass("TRN2", target_bir_lowering=False, detect_race_conditions=True)
    dkind = "ExternalOutput" if dbg else "Internal"

    x_in = nc.dram_tensor("x", [NTOK, D], F32, kind="ExternalInput")
    Wq = nc.dram_tensor("Wq", [D, D], F32, kind="ExternalInput")
    Wk = nc.dram_tensor("Wk", [D, D], F32, kind="ExternalInput")
    Wv = nc.dram_tensor("Wv", [D, D], F32, kind="ExternalInput")
    Wo = nc.dram_tensor("Wo", [D, D], F32, kind="ExternalInput")
    bqd = nc.dram_tensor("bq", [1, D], F32, kind="ExternalInput")
    bkd = nc.dram_tensor("bk", [1, D], F32, kind="ExternalInput")
    bvd = nc.dram_tensor("bv", [1, D], F32, kind="ExternalInput")
    bod = nc.dram_tensor("bo", [1, D], F32, kind="ExternalInput")
    wgd = nc.dram_tensor("wg", [FF, D], F32, kind="ExternalInput")
    wud = nc.dram_tensor("wu", [FF, D], F32, kind="ExternalInput")
    wdd = nc.dram_tensor("wd", [D, FF], F32, kind="ExternalInput")
    clipx = nc.dram_tensor("clipx", [1, 1], F32, kind="ExternalInput")
    clipd = nc.dram_tensor("clipd", [1, 1], F32, kind="ExternalInput")
    ident_in = nc.dram_tensor("ident", [128, 128], BF16, kind="ExternalInput")
    ones_in = nc.dram_tensor("onesc", [1, TILE], BF16, kind="ExternalInput")
    epair_in = nc.dram_tensor("epair", [16, NCF * 128], BF16, kind="ExternalInput")
    smask_in = nc.dram_tensor("smask", [128, 4], F32, kind="ExternalInput")

    y_out = nc.dram_tensor("y", [NTOK, D], F32, kind="ExternalOutput")

    w_bf = {n: nc.dram_tensor(f"{n}_bf", [D, D], BF16, kind="Internal")
            for n in ("wq", "wk", "wv", "wo")}
    wg_q = nc.dram_tensor("wg_q", [FF, D], BF16, kind="Internal")
    wu_q = nc.dram_tensor("wu_q", [FF, D], BF16, kind="Internal")
    wd_q = nc.dram_tensor("wd_q", [D, FF], BF16, kind="Internal")
    x1_hbm = nc.dram_tensor("x1_hbm", [NTOK, D], F32, kind=dkind)
    h_hbm = nc.dram_tensor("h_hbm", [NCI, 128, NTOK], F32, kind="Internal")
    cc_kv_in = nc.dram_tensor("cc_kv_in", [128, 4 * 520], F32, kind="Internal")
    cc_kv_out = nc.dram_tensor("cc_kv_out", [128, 4 * 520], F32, kind="Internal")
    cc_m1_in = nc.dram_tensor("cc_m1_in", [1, 128], F32, kind="Internal")
    cc_m1_out = nc.dram_tensor("cc_m1_out", [1, 128], F32, kind="Internal")
    cc_m2_in = nc.dram_tensor("cc_m2_in", [1, 128], F32, kind="Internal")
    cc_m2_out = nc.dram_tensor("cc_m2_out", [1, 128], F32, kind="Internal")
    if dbg:
        dbg_kv = nc.dram_tensor("dbg_kv", [128, 4 * 520], F32, kind="ExternalOutput")
        dbg_m1 = nc.dram_tensor("dbg_m1", [1, 128], F32, kind="ExternalOutput")
        dbg_m2 = nc.dram_tensor("dbg_m2", [1, 128], F32, kind="ExternalOutput")
        dbg_kvloc = nc.dram_tensor("dbg_kvloc", [128, 520], F32, kind="ExternalOutput")
        dbg_phiq = nc.dram_tensor("dbg_phiq", [128, NCF * NTOK], BF16, kind="ExternalOutput")
        dbg_ktm = nc.dram_tensor("dbg_ktm", [128, 2 * D], BF16, kind="ExternalOutput")
        dbg_vag = nc.dram_tensor("dbg_vag", [128, 2 * VAGW], BF16, kind="ExternalOutput")
        dbg_xnfm = nc.dram_tensor("dbg_xnfm", [128, NCF * TILE], BF16, kind="ExternalOutput")
        dbg_wkt = nc.dram_tensor("dbg_wkt", [128, NCF * D], BF16, kind="ExternalOutput")
        dbg_brow = nc.dram_tensor("dbg_brow", [1, 4 * D], BF16, kind="ExternalOutput")
        dbg_vag0 = nc.dram_tensor("dbg_vag0", [128, 2 * VAGW], BF16, kind="ExternalOutput")
        dbg_xntm = nc.dram_tensor("dbg_xntm", [128, NG * D], BF16, kind="ExternalOutput")
        dbg_rstd = nc.dram_tensor("dbg_rstd", [128, NT * NG], F32, kind="ExternalOutput")
        dbg_mv1 = nc.dram_tensor("dbg_mv1", [128, 2 * NT * NG], F32, kind="ExternalOutput")

    p = Prog()
    es = ExitStack()
    sem_names = ["sc", "pe", "ve", "ac", "gp"] + \
        [f"sx{i}" for i in range(NPOOL)] + [f"sg{i}" for i in range(NPOOL)]
    sems = {n: es.enter_context(nc.semaphore(n)) for n in sem_names}

    ident = es.enter_context(nc.sbuf_tensor("identt", [128, 128], BF16))
    onesc = es.enter_context(nc.sbuf_tensor("onesct", [1, TILE], BF16))
    brow = {n: es.enter_context(nc.sbuf_tensor(f"brow_{n}", [1, D], BF16))
            for n in ("q", "k", "v", "o")}
    mv1 = es.enter_context(nc.sbuf_tensor("mv1", [128, 2 * NT * NG], F32))
    rstd1 = es.enter_context(nc.sbuf_tensor("rstd1", [128, NT * NG], F32))
    mb1 = es.enter_context(nc.sbuf_tensor("mb1", [128, NT * NG], F32))
    mv2 = es.enter_context(nc.sbuf_tensor("mv2", [128, 2 * NT * NG], F32))
    xm2 = es.enter_context(nc.sbuf_tensor("xm2", [128, NT * NG], F32))
    a2c = es.enter_context(nc.sbuf_tensor("a2c", [128, NT * NG], F32))
    b2c = es.enter_context(nc.sbuf_tensor("b2c", [128, NT * NG], F32))
    wgs = es.enter_context(nc.sbuf_tensor("wgs", [128, NCI], F32))
    wus = es.enter_context(nc.sbuf_tensor("wus", [128, NCI], F32))
    wds = es.enter_context(nc.sbuf_tensor("wds", [128, NCF], F32))
    sg_all = es.enter_context(nc.sbuf_tensor("sg_all", [128, NCI], F32))
    su_all = es.enter_context(nc.sbuf_tensor("su_all", [128, NCI], F32))
    scl = es.enter_context(nc.sbuf_tensor("scl", [128, 4], F32))
    gmaxb = es.enter_context(nc.sbuf_tensor("gmaxb", [128, 128], F32))
    kv_sb = es.enter_context(nc.sbuf_tensor("kv_sb", [128, 520], F32))
    habs = es.enter_context(nc.sbuf_tensor("habs", [128, NCI], F32))
    tiny = es.enter_context(nc.sbuf_tensor("tiny", [128, 8], F32))
    tin2 = es.enter_context(nc.sbuf_tensor("tin2", [128, 8], F32))
    smask = es.enter_context(nc.sbuf_tensor("smaskt", [128, 4], F32))
    zrow = es.enter_context(nc.sbuf_tensor("zrow", [1, 512], BF16))
    x_t = es.enter_context(nc.sbuf_tensor("x_t", [128, NG * D], F32))
    scr = es.enter_context(nc.sbuf_tensor("scr", [128, 2560], F32))
    gsc = es.enter_context(nc.sbuf_tensor("gsc", [128, 1024], F32))
    bqfm = es.enter_context(nc.sbuf_tensor("bqfm", [128, NCF], F32))
    nbqfm = es.enter_context(nc.sbuf_tensor("nbqfm", [128, NCF], F32))
    bvb = es.enter_context(nc.sbuf_tensor("bvb", [128, D], BF16))

    kps = es.enter_context(nc.psum_tensor("kps", [128, 1024], F32))
    vps = es.enter_context(nc.psum_tensor("vps", [128, 1024], F32))
    pb4t = es.enter_context(nc.psum_tensor("pb4t", [128, 512], F32))
    pb5t = es.enter_context(nc.psum_tensor("pb5t", [128, 512], F32))
    pb6t = es.enter_context(nc.psum_tensor("pb6t", [128, 512], F32))
    pbf = es.enter_context(nc.psum_tensor("pbf", [128, 1024], BF16))
    pb = [kps[:, 0:512], kps[:, 512:1024], vps[:, 0:512], vps[:, 512:1024],
          pb4t[:, :], pb5t[:, :], pb6t[:, :]]

    def ts(tensor, c0, w):
        return tensor[:, c0:c0 + w]

    def tok_ap(dram, t):
        return bass.AP(dram, t * TILE * D, [[D, 128], [128 * D, NG], [1, D]])

    def sb3(sb):
        return sb.ap().rearrange("p (g f) -> p g f", f=D)

    cast_tok = {}
    for n, W in (("wk", Wk), ("wv", Wv), ("wq", Wq), ("wo", Wo)):
        cast_tok[n] = p.op("gpsimd",
                           lambda g, a=w_bf[n].ap(), b=W.ap(): g.dma_start(out=a, in_=b),
                           sig="sg", amt=16)
    for n, B in (("q", bqd), ("k", bkd), ("v", bvd), ("o", bod)):
        p.op("gpsimd", lambda g, a=brow[n][:, :], b=B.ap(): g.dma_start(out=a, in_=b),
             sig="sg", amt=16)
    brow_tok = p.now("sg")
    bq_tok = p.op("sync", lambda s: s.dma_start(
        out=bqfm[:, :], in_=bass.AP(bqd, 0, [[1, 128], [128, NCF]]),
        allow_slow_non_contiguous=True), sig="sx", amt=16)
    nbq_tok = p.op("vector", lambda v: v.tensor_scalar(nbqfm[:, :], bqfm[:, :], -1.0,
                                                       None, op0=ALU.mult),
                   after=[bq_tok], sig="ve")

    p.op("sync", lambda s: s.dma_start(out=ident[:, :], in_=ident_in.ap()), sig="sx", amt=16)
    onesc_tok = p.op("sync", lambda s: s.dma_start(out=onesc[:, :], in_=ones_in.ap()),
                     sig="sx", amt=16)
    smask_tok = p.op("sync", lambda s: s.dma_start(out=smask[:, :], in_=smask_in.ap()),
                     sig="sx", amt=16)

    esA = ExitStack()
    WkT = esA.enter_context(nc.sbuf_tensor("WkT", [128, NCF * D], BF16))
    WvT = esA.enter_context(nc.sbuf_tensor("WvT", [128, NCF * D], BF16))
    WqT = esA.enter_context(nc.sbuf_tensor("WqT", [128, NCF * D], BF16))
    xn_tm = esA.enter_context(nc.sbuf_tensor("xn_tm", [128, NG * D], BF16))
    xn_fm = esA.enter_context(nc.sbuf_tensor("xn_fm", [128, NCF * TILE], BF16))
    k_tm = esA.enter_context(nc.sbuf_tensor("k_tm", [128, 2 * D], BF16))
    v_ag = esA.enter_context(nc.sbuf_tensor("v_ag", [128, 2 * VAGW], BF16))
    wstg = esA.enter_context(nc.sbuf_tensor("wstg", [128, D], F32))
    wqnt = esA.enter_context(nc.sbuf_tensor("wqnt", [128, D], BF16))
    esWo = ExitStack()
    WoT = esWo.enter_context(nc.sbuf_tensor("WoT", [128, NCF * D], BF16, side="right"))
    esPhi = ExitStack()
    phiq = esPhi.enter_context(nc.sbuf_tensor("phiq", [128, NCF * NTOK], BF16, side="right"))

    wT_tok = {}
    for n, t_ in (("wk", WkT), ("wv", WvT), ("wq", WqT), ("wo", WoT)):
        for c in range(NCF):
            wT_tok[n] = p.op(
                "sync",
                lambda s, o=t_[:, c * D:(c + 1) * D],
                       i=w_bf[n].ap()[:, c * 128:(c + 1) * 128]:
                    s.dma_start_transpose(o, i),
                after=[cast_tok[n]], sig="sx", amt=16)

    p.op("vector", lambda v: v.memset(v_ag[:, :], 0.0), sig="ve")
    for sl in range(2):
        for h in range(NH):
            cpos = sl * VAGW + h * 65 + 64
            p.op("vector", lambda v, c=cpos: v.memset(v_ag[:, c:c + 1], 1.0), sig="ve")
    p.op("vector", lambda v: v.memset(habs[:, :], 0.0), sig="ve")

    prep_jobs = [(wgd, wg_q, wgs, i) for i in range(NCI)] + \
                [(wud, wu_q, wus, i) for i in range(NCI)]
    prep = {"ld": None, "wstg_free": None, "wqnt_free": None, "stores": []}

    def emit_prep(j):
        src, dst, wsc, i = prep_jobs[j]
        ld = p.op("sync", lambda s, o=wstg[:, :], a=src.ap()[i * 128:(i + 1) * 128, :]:
                  s.dma_start(out=o, in_=a), after=[prep["wstg_free"]], sig="sx", amt=16)
        rd = p.op("vector", lambda v, o=wsc[:, i:i + 1], a=wstg[:, :]:
                  v.reduce_max(o, a, axis=AX, apply_absolute_value=True),
                  after=[ld], sig="ve")
        sc = p.op("vector", lambda v, o=wsc[:, i:i + 1], a=wsc[:, i:i + 1]:
                  v.tensor_scalar(o, a, 1.0 / 127.0, None, op0=ALU.mult),
                  after=[rd], sig="ve")
        rc = p.op("vector", lambda v, o=tin2[:, 6:7], a=wsc[:, i:i + 1]:
                  v.reciprocal(o, a), after=[sc], sig="ve")
        t1 = p.op("scalar", lambda s, o=wstg[:, :], a=wstg[:, :], sa=tin2[:, 6:7]:
                  s.activation(o, a, AF.Copy, bias=MAGIC, scale=sa),
                  after=[rc], sig="ac")
        t3 = p.op("scalar", lambda s, o=wqnt[:, :], a=wstg[:, :]:
                  s.activation(o, a, AF.Copy, bias=-MAGIC, scale=1.0),
                  after=[t1, prep["wqnt_free"]], sig="ac")
        st = p.op("gpsimd", lambda g, o=dst.ap()[i * 128:(i + 1) * 128, :], a=wqnt[:, :]:
                  g.dma_start(out=o, in_=a), after=[t3], sig="sg", amt=16)
        prep["wstg_free"] = t3
        prep["wqnt_free"] = st
        prep["stores"].append(st)

    def load_act_set(eng, set_id):
        return eng.add_instruction(mybir.InstLoadActFuncSet(
            name=eng.bass.get_next_instruction_name(),
            act_func_set_id=set_id, ins=[], outs=[]))
    p.op("scalar", lambda s: load_act_set(s, 6), sig="ac")
    zr0 = p.op("vector", lambda v: v.memset(zrow[:, :], 0.0), sig="ve")
    for bank in (pb[5], pb[6]):
        p.op("tensor", lambda te, o=bank[:, :], l=zrow[0:1, 0:128], r=zrow[0:1, 0:512]:
             te.matmul(o, l, r, start=True, stop=False, skip_group_check=True),
             after=[zr0], sig="pe")
    al = {}
    phiq_ps_tok = {}
    prep_done = 0
    phiq_tile_tok = {}
    lnst = {}

    bias_build_toks = []
    for row, dst in ((brow["v"], bvb),):
        mmt = None
        for n in range(2):
            mmt = p.op("tensor", lambda te, o=kps[:, n * 512:(n + 1) * 512],
                       l=onesc[0:1, 0:128], r=row[0:1, n * 512:(n + 1) * 512]:
                       te.matmul(o, l, r, start=True, stop=True, skip_group_check=True),
                       after=[brow_tok, zr0, onesc_tok,
                              bias_build_toks[-1] if bias_build_toks else None],
                       sig="pe")
        bias_build_toks.append(p.op("vector", lambda v, o=dst[:, :], a=kps[:, 0:1024]:
                                    v.tensor_copy(o, a), after=[mmt], sig="ve"))
    bias_built = bias_build_toks[-1]

    def emit_ln(t):
        ldx = p.op("sync", lambda s, o=sb3(x_t), a=tok_ap(x_in, t):
                   s.dma_start(out=o, in_=a), after=[lnst.get("xn_last")],
                   sig="sx", amt=16)
        aggr_last = None
        for g in range(NG):
            col = 2 * (t * NG + g)
            s2t = p.op("scalar", lambda s, o=ts(xn_tm, g * D, D), a=ts(x_t, g * D, D):
                       s.activation(o, a, AF.Square),
                       after=[ldx, aggr_last if g else lnst.get("aggr_prev"),
                              al.get("xn_tm_free")], sig="ac")
            p.op("vector", lambda v, o=tin2[:, 0:1], a=ts(x_t, g * D, D):
                 v.reduce_sum(o, a, axis=AX), after=[ldx], sig="ve")
            p.op("vector", lambda v, o=tin2[:, 1:2], a=ts(xn_tm, g * D, D):
                 v.reduce_sum(o, a, axis=AX), after=[s2t], sig="ve")
            p.op("vector", lambda v, o=mv1[:, col:col + 1], a=tin2[:, 0:1]:
                 v.tensor_scalar(o, a, 1.0 / D, None, op0=ALU.mult), sig="ve")
            p.op("vector", lambda v, o=tin2[:, 2:3], a=tin2[:, 1:2]:
                 v.tensor_scalar(o, a, 1.0 / D, None, op0=ALU.mult), sig="ve")
            p.op("vector", lambda v, o=tin2[:, 3:4], a=mv1[:, col:col + 1]:
                 v.tensor_mul(o, a, a), sig="ve")
            aggr_last = p.op("vector", lambda v, o=mv1[:, col + 1:col + 2],
                             a=tin2[:, 2:3], b=tin2[:, 3:4]:
                             v.tensor_sub(o, a, b), sig="ve")
        c0 = t * NG
        lnst["aggr_last"] = aggr_last
        lnst["aggr_prev"] = aggr_last
        t_eps = p.op("vector", lambda v, o=ts(tiny, 0, 4),
                     a=mv1[:, 2 * c0 + 1:2 * (c0 + NG):2]:
                     v.tensor_scalar(o, a, LN_EPS, None, op0=ALU.add),
                     after=[aggr_last, al.get("lnfree")], sig="ve")
        t_ln = p.op("scalar", lambda s, o=ts(tiny, 4, 4), a=ts(tiny, 0, 4):
                    s.activation(o, a, AF.Ln), after=[t_eps], sig="ac")
        al["lnfree"] = t_ln
        t_rs = p.op("scalar", lambda s, o=rstd1[:, c0:c0 + NG], a=ts(tiny, 4, 4):
                    s.activation(o, a, AF.Exp, bias=0.0, scale=-0.5), sig="ac")
        t_mb = p.op("vector", lambda v, o=mb1[:, c0:c0 + NG],
                    a=mv1[:, 2 * c0:2 * (c0 + NG):2], b=rstd1[:, c0:c0 + NG]:
                    v.scalar_tensor_tensor(o, a, -1.0, b, op0=ALU.mult, op1=ALU.mult),
                    after=[t_rs], sig="ve")
        xn_last = None
        for g in range(NG):
            gc = c0 + g
            xn_last = p.op(
                "vector",
                lambda v, o=ts(xn_tm, g * D, D), a=ts(x_t, g * D, D),
                       sa=rstd1[:, gc:gc + 1], ba=mb1[:, gc:gc + 1]:
                    v.tensor_scalar(o, a, sa, ba, op0=ALU.mult, op1=ALU.add),
                after=[t_mb, al.get("xn_tm_free")], sig="ve")
        lnst["xn_last"] = xn_last
        lnst[("xn_done", t)] = xn_last

    emit_ln(0)
    for t in range(NT):
        xn_done_t = lnst[("xn_done", t)]

        evt = []
        for c in range(NCF):
            half = c % 2
            tl = None
            for g in range(NG):
                tl = p.op(
                    "tensor",
                    lambda te, o=pbf[:, half * 512 + g * 128:half * 512 + (g + 1) * 128],
                           a=ts(xn_tm, g * D + c * 128, 128):
                        te.transpose(o, a, ident[:, :]),
                    after=[xn_done_t, evt[c - 1] if c >= 1 else al.get("pbf_free")],
                    sig="pe")
            evt.append(p.op(
                "vector",
                lambda v, o=ts(xn_fm, c * TILE, TILE),
                       a=pbf[:, half * 512:half * 512 + 512]:
                    v.tensor_copy(o, a), after=[tl], sig="ve"))
        al["pbf_free"] = evt[-1]
        al["xn_tm_free"] = p.now("pe")
        xn_fm_done = evt[-1]

        def emit_k(g):
            for n in range(2):
                for c in range(NCF):
                    p.op("tensor",
                         lambda te, o=kps[:, n * 512:(n + 1) * 512],
                                l=ts(xn_fm, c * TILE + g * 128, 128),
                                r=ts(WkT, c * D + n * 512, 512),
                                st=(c == 0), sp=False:
                             te.matmul(o, l, r, start=st, stop=sp,
                                       skip_group_check=True),
                         after=[evt[c], al.get("kps_free")], sig="pe")
                p.op("tensor",
                     lambda te, o=kps[:, n * 512:(n + 1) * 512], l=onesc[0:1, 0:128],
                            r=brow["k"][0:1, n * 512:(n + 1) * 512]:
                         te.matmul(o, l, r, start=False, stop=True,
                                   skip_group_check=True),
                     after=[brow_tok, onesc_tok], sig="pe")
            return p.now("pe")

        def emit_v(g):
            for n in range(2):
                for c in range(NCF):
                    p.op("tensor",
                         lambda te, o=vps[:, n * 512:(n + 1) * 512],
                                l=ts(xn_fm, c * TILE + g * 128, 128),
                                r=ts(WvT, c * D + n * 512, 512),
                                st=(c == 0), sp=(c == NCF - 1):
                             te.matmul(o, l, r, start=st, stop=sp,
                                       skip_group_check=True),
                         after=[evt[c], al.get("vps_free"),
                                phiq_ps_tok.get(("q", 1))], sig="pe")
            return p.now("pe")

        def emit_q(cq):
            if cq % 2 == 0:
                bank, key, extra = pb4t[:, :], ("q", 0), None
            else:
                bank, key, extra = vps[:, 0:512], ("q", 1), al.get("vev_last")
            for cp in range(NCF):
                p.op("tensor",
                     lambda te, o=bank, l=ts(WqT, cp * D + cq * 128, 128),
                            r=ts(xn_fm, cp * TILE, TILE), st=(cp == 0),
                            sp=(cp == NCF - 1):
                         te.matmul(o, l, r, start=st, stop=sp,
                                   skip_group_check=True),
                     after=[xn_fm_done, phiq_ps_tok.get(key), extra, bias_built],
                     sig="pe")
            return bank, key, p.now("pe")

        def emit_phik(g, k_done):
            ksl = g % 2
            kk = k_tm[:, ksl * D:(ksl + 1) * D]
            rv = p.op("vector", lambda v, o=ts(scr, 0, D), a=kps[:, 0:1024]:
                      v.tensor_scalar(o, a, 0.0, None, op0=ALU.max),
                      after=[k_done], sig="ve")
            m_t = p.op("vector", lambda v, o=ts(scr, D, D), a=kps[:, 0:1024]:
                       v.tensor_scalar(o, a, 0.0, None, op0=ALU.min),
                       after=[k_done, al.get("phik_free")], sig="ve")
            al["kps_free"] = [rv, m_t]
            e_t = p.op("scalar", lambda s, o=ts(scr, D, D), a=ts(scr, D, D):
                       s.activation(o, a, AF.Exp), after=[m_t], sig="ac")
            kphi = p.op("vector", lambda v, o=kk, a=ts(scr, 0, D), b=ts(scr, D, D):
                        v.tensor_add(o, a, b),
                        after=[e_t, al.get(("kv_done", ksl))], sig="ve")
            al["phik_free"] = kphi
            return kk, kphi

        def emit_vev(g, v_done):
            ksl = g % 2
            vsl = v_ag[:, ksl * VAGW:ksl * VAGW + NH * 65]
            vev = p.op(
                "vector",
                lambda v, o=vsl.rearrange("p (h e) -> p h e", e=65)[:, :, 0:64],
                       a=vps[:, 0:1024].rearrange("p (h e) -> p h e", e=64),
                       b=bvb[:, :].rearrange("p (h e) -> p h e", e=64):
                    v.scalar_tensor_tensor(o, a, 1.0, b, op0=ALU.mult, op1=ALU.add),
                after=[v_done, bias_built, al.get(("kv_done", ksl))], sig="ve")
            al["vps_free"] = vev
            al["vev_last"] = vev
            return vsl, vev

        def emit_phiq(qbank, key, q_tok, cqq):
            sl = (cqq % 2) * 512
            rv = p.op("vector", lambda v, o=ts(scr, 2048, 512), a=qbank,
                      sa=bqfm[:, cqq:cqq + 1]:
                      v.tensor_scalar(o, a, sa, 0.0, op0=ALU.add, op1=ALU.max),
                      after=[q_tok, bq_tok, al.get("phiq_r_free")], sig="ve")
            m2_ = p.op("vector", lambda v, o=ts(gsc, sl, 512), a=qbank,
                       sa=bqfm[:, cqq:cqq + 1]:
                       v.tensor_scalar(o, a, sa, 0.0, op0=ALU.add, op1=ALU.min),
                       after=[q_tok, bq_tok, al.get(("phiq_e_free", cqq % 2))],
                       sig="ve")
            e2_ = p.op("scalar", lambda s, o=ts(gsc, sl, 512), a=ts(gsc, sl, 512):
                       s.activation(o, a, AF.Exp), after=[m2_], sig="ac")
            ph = p.op("vector", lambda v, o=ts(phiq, cqq * NTOK + t * TILE, TILE),
                      a=ts(scr, 2048, 512), b=ts(gsc, sl, 512):
                      v.tensor_add(o, a, b), after=[e2_], sig="ve")
            al["phiq_r_free"] = ph
            al[("phiq_e_free", cqq % 2)] = ph
            phiq_ps_tok[key] = ph

        def emit_kv(g, kk, vsl, kphi, vev):
            for h in range(NH):
                bank = pb5t if h < 8 else pb6t
                q_ = (h % 8) // 2
                par = (h % 2) * 64
                p.op("tensor",
                     lambda te, o=bank[par:par + 64, q_ * 65:q_ * 65 + 65],
                            l=kk[:, h * HD:(h + 1) * HD],
                            r=vsl[:, h * 65:(h + 1) * 65],
                            sp=(t == NT - 1 and g == NG - 1),
                            tp=(0, par):
                         te.matmul(o, l, r, start=False, stop=sp, tile_position=tp,
                                   skip_group_check=True),
                     after=[kphi, vev], sig="pe")
            al[("kv_done", g % 2)] = p.now("pe")

        def maybe_prep(k):
            nonlocal prep_done
            for _ in range(k):
                if prep_done < len(prep_jobs):
                    emit_prep(prep_done)
                    prep_done += 1

        pend = {}
        qi = 0
        for g in range(NG):
            kd = emit_k(g)
            vd = emit_v(g)
            kk, kphi = emit_phik(g, kd)
            vsl, vev = emit_vev(g, vd)
            qb, qk, qd = emit_q(qi)
            emit_phiq(qb, qk, qd, qi)
            qi += 1
            pend[g] = (kk, vsl, kphi, vev)
            if g >= 1:
                kk2, vsl2, kphi2, vev2 = pend.pop(g - 1)
                emit_kv(g - 1, kk2, vsl2, kphi2, vev2)
            if g == 1 and t + 1 < NT:
                emit_ln(t + 1)
            maybe_prep(1 if g < 3 else 0)
        for _ in range(NG):
            qb, qk, qd = emit_q(qi)
            emit_phiq(qb, qk, qd, qi)
            qi += 1
            maybe_prep(1)
            if qi == 6:
                kk2, vsl2, kphi2, vev2 = pend.pop(NG - 1)
                emit_kv(NG - 1, kk2, vsl2, kphi2, vev2)
        if t == NT - 1:
            maybe_prep(len(prep_jobs) - prep_done)
        phiq_tile_tok[t] = p.now("ve")

    kv_done = p.now("pe")
    ev1 = p.op("vector", lambda v: v.tensor_copy(kv_sb[:, 0:260], pb[5][:, 0:260]),
               after=[kv_done], sig="ve")
    ev2 = p.op("vector", lambda v: v.tensor_copy(kv_sb[:, 260:520], pb[6][:, 0:260]),
               sig="ve")
    if dbg:
        p.op("gpsimd", lambda g: g.dma_start(out=dbg_kvloc.ap(), in_=kv_sb[:, :]),
             after=[ev1, ev2], sig="sg", amt=16)
        p.op("gpsimd", lambda g: g.dma_start(out=dbg_phiq.ap(), in_=phiq[:, :]),
             after=[phiq_tile_tok[NT - 1]], sig="sg", amt=16)
        p.op("gpsimd", lambda g: g.dma_start(out=dbg_ktm.ap(), in_=k_tm[:, :]),
             after=[kv_done], sig="sg", amt=16)
        p.op("gpsimd", lambda g: g.dma_start(out=dbg_vag.ap(), in_=v_ag[:, :]),
             after=[kv_done], sig="sg", amt=16)
        p.op("gpsimd", lambda g: g.dma_start(out=dbg_xnfm.ap(), in_=xn_fm[:, :]),
             after=[kv_done], sig="sg", amt=16)
        p.op("gpsimd", lambda g: g.dma_start(out=dbg_xntm.ap(), in_=xn_tm[:, :]),
             after=[kv_done], sig="sg", amt=16)
        p.op("gpsimd", lambda g: g.dma_start(out=dbg_rstd.ap(), in_=rstd1[:, :]),
             after=[kv_done], sig="sg", amt=16)
        p.op("gpsimd", lambda g: g.dma_start(out=dbg_mv1.ap(), in_=mv1[:, :]),
             after=[kv_done], sig="sg", amt=16)
        for bi, bn in enumerate(("q", "k", "v", "o")):
            p.op("gpsimd", lambda g, o=dbg_brow.ap()[0:1, bi * D:(bi + 1) * D],
                 a=brow[bn][:, :]: g.dma_start(out=o, in_=a),
                 after=[kv_done], sig="sg", amt=16)
        p.op("gpsimd", lambda g: g.dma_start(out=dbg_wkt.ap(), in_=WkT[:, :]),
             after=[kv_done], sig="sg", amt=16)
    dbg_dump_tok = p.now("sg")
    mtoks = []
    for s_ in range(4):
        mtoks.append(p.op(
            "vector", lambda v, o=ts(scr, s_ * 520, 520), m=smask[:, s_:s_ + 1]:
                v.tensor_scalar(o, kv_sb[:, :], m, None, op0=ALU.mult),
            after=[ev1, ev2, smask_tok], sig="ve"))
    d1 = p.op("gpsimd", lambda g: g.dma_start(out=cc_kv_in.ap(), in_=ts(scr, 0, 2080)),
              after=[mtoks[-1]], sig="sg", amt=16)
    cc1 = p.op("gpsimd", lambda g: g.collective_compute(
        "AllReduce", ALU.add, replica_groups=[list(range(8))],
        ins=[cc_kv_in.ap()], outs=[cc_kv_out.ap()]), after=[d1], sig="sc")
    d2a = p.op("gpsimd", lambda g: g.dma_start(out=ts(scr, 0, 2080), in_=cc_kv_out.ap()),
               after=[cc1], sig="sg", amt=16)
    d2 = p.op("vector", lambda v: v.tensor_scalar(kv_sb[:, :], ts(scr, 0, 520),
                                                  smask[:, 0:1], None, op0=ALU.mult),
              after=[d2a], sig="ve")
    for s_ in range(1, 4):
        d2 = p.op("vector", lambda v, a=ts(scr, s_ * 520, 520), m=smask[:, s_:s_ + 1]:
                  v.scalar_tensor_tensor(kv_sb[:, :], a, m, kv_sb[:, :],
                                         op0=ALU.mult, op1=ALU.add),
                  after=[d2], sig="ve")

    esA.close()
    esB = ExitStack()
    epair = esB.enter_context(nc.sbuf_tensor("epairt", [16, NCF * 128], BF16))
    kv_bd = esB.enter_context(nc.sbuf_tensor("kv_bd", [128, 8 * 128], BF16))
    ksum2 = esB.enter_context(nc.sbuf_tensor("ksum2", [128, NCF * 16], BF16))
    nrm_sb = esB.enter_context(nc.sbuf_tensor("nrm_sb", [16, TILE], F32))
    rec_sb = esB.enter_context(nc.sbuf_tensor("rec_sb", [16, TILE], F32))
    rec_bf = esB.enter_context(nc.sbuf_tensor("rec_bf", [16, TILE], BF16))
    rbc_sb = esB.enter_context(nc.sbuf_tensor("rbc_sb", [128, TILE], F32))
    qkv_fm_sb = esB.enter_context(nc.sbuf_tensor("qkv_fm", [128, NCF * TILE], BF16))
    x1_sb = esB.enter_context(nc.sbuf_tensor("x1_sb", [128, NG * D], F32))
    epair_tok = p.op("sync", lambda s: s.dma_start(out=epair[:, :], in_=epair_in.ap()),
                     after=[kv_done, prep["stores"][-1], d2, dbg_dump_tok], sig="sx", amt=16)

    if dbg:
        p.op("gpsimd", lambda g: g.dma_start(out=dbg_kv.ap(), in_=cc_kv_out.ap()),
             after=[cc1], sig="sg", amt=16)
    bd0 = p.op("vector", lambda v: v.memset(kv_bd[:, :], 0.0),
               after=[d2, kv_done, prep["stores"][-1], dbg_dump_tok], sig="ve")
    p.op("vector", lambda v: v.memset(ksum2[:, :], 0.0), sig="ve")
    kvbd_done = None
    for par in (0, 64):
        for b in range(2):
            p.op("vector",
                 lambda v, o=kv_bd[par:par + 64, 512 * b:512 * b + 512]
                          .rearrange("p (q e) -> p q e", e=128)[:, :, par:par + 64],
                        a=kv_sb[par:par + 64, 260 * b:260 * b + 260]
                          .rearrange("p (q e) -> p q e", e=65)[:, :, 0:64]:
                     v.tensor_copy(o, a), after=[bd0], sig="ve")
    for par, off in ((0, 0), (64, 1)):
        for b in range(2):
            kvbd_done = p.op(
                "vector",
                lambda v, o=bass.AP(ksum2, 72 * b + off, [[NCF * 16, 64], [18, 4]],
                                    base_partition=par)
                       if False else
                       ksum2[par:par + 64, 72 * b + off:72 * b + off + 55:18],
                       a=kv_sb[par:par + 64, 260 * b + 64:260 * b + 260:65]:
                    v.tensor_copy(o, a), sig="ve")

    bl = {"xfree": al.get("xfree"), "tin2_free": lnst.get("aggr_last")}
    for t in range(NT):
        xin = tok_ap(x_in, t)
        ldx = p.op("sync", lambda s, o=sb3(x_t), a=xin: s.dma_start(out=o, in_=a),
                   after=[bl.get("xfree")], sig="sx", amt=16)

        for c in range(NCF):
            p.op("tensor", lambda te, o=pb[4][0:16, :], l=ts(ksum2, c * 16, 16),
                 r=ts(phiq, c * NTOK + t * TILE, TILE), st=(c == 0), sp=(c == NCF - 1):
                 te.matmul(o, l, r, start=st, stop=sp, skip_group_check=True),
                 after=[kvbd_done, bl.get("nrm_ps"), phiq_tile_tok[t],
                        phiq_ps_tok.get(id(pb[4])) if t == 0 else None], sig="pe")
        nm = p.now("pe")
        n1 = p.op("vector", lambda v: v.tensor_scalar(nrm_sb[:, :], pb[4][0:16, :],
                                                      ATTN_EPS, None, op0=ALU.max),
                  after=[nm], sig="ve")
        bl["nrm_ps"] = n1
        n2 = p.op("vector", lambda v: v.reciprocal(rec_sb[:, :], nrm_sb[:, :]),
                  after=[n1, bl.get("rec_free")], sig="ve")
        n2 = p.op("vector", lambda v: v.tensor_copy(rec_bf[:, :], rec_sb[:, :]),
                  after=[n2], sig="ve")

        mm2 = None
        for pr in range(NCF):
            bank = pb[pr % 3]
            mm1 = p.op("tensor", lambda te, o=bank[:, :], l=ts(kv_bd, pr * 128, 128),
                       r=ts(phiq, pr * NTOK + t * TILE, TILE):
                       te.matmul(o, l, r, start=True, stop=True, skip_group_check=True),
                       after=[kvbd_done, bl.get(("qps", pr % 3)),
                              phiq_ps_tok.get(id(pb[0])) if t == 0 and pr % 3 == 0 else None],
                       sig="pe")
            mm2 = p.op("tensor", lambda te, o=pb[3][:, :], l=ts(epair, pr * 128, 128),
                       r=rec_bf[:, :]:
                       te.matmul(o, l, r, start=True, stop=True, skip_group_check=True),
                       after=[n2, bl.get("bc_ps"),
                              epair_tok if t == 0 and pr == 0 else None], sig="pe")
            bc = p.op("scalar", lambda s, o=rbc_sb[:, :], a=pb[3][:, :]:
                      s.activation(o, a, AF.Copy),
                      after=[mm2, bl.get("rbc_free")], sig="ac")
            bl["bc_ps"] = bc
            ev = p.op("vector", lambda v, o=ts(qkv_fm_sb, pr * TILE, TILE), a=bank[:, :],
                      b=rbc_sb[:, :]:
                      v.scalar_tensor_tensor(o, a, 1.0, b, op0=ALU.mult, op1=ALU.mult),
                      after=[mm1, bc], sig="ve")
            bl[("qps", pr % 3)] = ev
            bl["rbc_free"] = ev
        bl["rec_free"] = mm2
        qkv_done = p.now("ve")

        evbank = {}
        for g in range(NG):
            for n in range(2):
                bank = pb[(2 * g + n) % 4]
                for c in range(NCF):
                    p.op("tensor", lambda te, o=bank[:, :],
                         l=ts(qkv_fm_sb, c * TILE + g * 128, 128),
                         r=ts(WoT, c * D + n * 512, 512), st=(c == 0):
                         te.matmul(o, l, r, start=st, stop=False, skip_group_check=True),
                         after=[qkv_done, wT_tok["wo"], bl.get(("ops", (2 * g + n) % 4))],
                         sig="pe")
                om = p.op("tensor", lambda te, o=bank[:, :], l=onesc[0:1, 0:128],
                          r=brow["o"][0:1, n * 512:(n + 1) * 512]:
                          te.matmul(o, l, r, start=False, stop=True, skip_group_check=True),
                          after=[brow_tok], sig="pe")
                ev = p.op("vector", lambda v, o=ts(x1_sb, g * D + n * 512, 512),
                          a=bank[:, :], b=ts(x_t, g * D + n * 512, 512):
                          v.scalar_tensor_tensor(o, a, 1.0, b, op0=ALU.mult, op1=ALU.add),
                          after=[om, ldx, bl.get("x1free")], sig="ve")
                bl[("ops", (2 * g + n) % 4)] = ev
                evbank[(g, n)] = ev
        bl["xfree"] = p.now("ve")
        for b in range(3):
            bl[("qps", b)] = p.now("ve")

        for g in range(NG):
            col = 2 * (t * NG + g)
            s2t = p.op("scalar", lambda s, o=gsc[:, 0:D], a=ts(x1_sb, g * D, D):
                       s.activation(o, a, AF.Square),
                       after=[evbank[(g, 0)], evbank[(g, 1)], bl.get("tin2_free"),
                              bl.get("gsc_free")], sig="ac")
            p.op("vector", lambda v, o=tin2[:, 0:1], a=ts(x1_sb, g * D, D):
                 v.reduce_sum(o, a, axis=AX),
                 after=[evbank[(g, 0)], evbank[(g, 1)]], sig="ve")
            gf = p.op("vector", lambda v, o=tin2[:, 1:2], a=gsc[:, 0:D]:
                      v.reduce_sum(o, a, axis=AX), after=[s2t], sig="ve")
            bl["gsc_free"] = gf
            p.op("vector", lambda v, o=mv2[:, col:col + 1], a=tin2[:, 0:1]:
                 v.tensor_scalar(o, a, 1.0 / D, None, op0=ALU.mult), sig="ve")
            p.op("vector", lambda v, o=tin2[:, 2:3], a=tin2[:, 1:2]:
                 v.tensor_scalar(o, a, 1.0 / D, None, op0=ALU.mult), sig="ve")
            p.op("vector", lambda v, o=tin2[:, 3:4], a=mv2[:, col:col + 1]:
                 v.tensor_mul(o, a, a), sig="ve")
            aggr_last = p.op("vector", lambda v, o=mv2[:, col + 1:col + 2],
                             a=tin2[:, 2:3], b=tin2[:, 3:4]:
                 v.tensor_sub(o, a, b), sig="ve")
            bl["tin2_free"] = aggr_last
            ag = aggr_last
            sb_ = p.op("vector", lambda v, o=ts(scr, 0, D), a=ts(x1_sb, g * D, D),
                       sa=mv2[:, col:col + 1]:
                       v.tensor_scalar(o, a, sa, None, op0=ALU.subtract),
                       after=[ag], sig="ve")
            p.op("vector", lambda v, o=xm2[:, t * NG + g:t * NG + g + 1],
                 a=ts(scr, 0, D):
                 v.reduce_max(o, a, axis=AX, apply_absolute_value=True),
                 after=[sb_], sig="ve")
        xm_done = p.now("ve")
        st = p.op("gpsimd", lambda g, o=tok_ap(x1_hbm, t), a=sb3(x1_sb):
                  g.dma_start(out=o, in_=a), after=[xm_done], sig="sg", amt=16)
        bl["x1free"] = st
        bl[("x1st", t)] = st

    t_eps = p.op("vector", lambda v, o=ts(scr, 0, 32), a=mv2[:, 1:64:2]:
                 v.tensor_scalar(o, a, LN_EPS, None, op0=ALU.add), sig="ve")
    t_ln = p.op("scalar", lambda s, o=ts(scr, 32, 32), a=ts(scr, 0, 32):
                s.activation(o, a, AF.Ln), after=[t_eps], sig="ac")
    t_rs2 = p.op("scalar", lambda s, o=ts(scr, 64, 32), a=ts(scr, 32, 32):
                 s.activation(o, a, AF.Exp, bias=0.0, scale=-0.5), sig="ac")
    t_xm = p.op("vector", lambda v, o=ts(scr, 96, 32), a=xm2[:, 0:32], b=ts(scr, 64, 32):
                v.tensor_mul(o, a, b), after=[t_rs2], sig="ve")
    t_mx = p.op("vector", lambda v, o=ts(tiny, 0, 1), a=ts(scr, 96, 32):
                v.reduce_max(o, a, axis=AX), after=[t_xm], sig="ve")

    d3 = p.op("gpsimd", lambda g: g.dma_start(out=cc_m1_in.ap()[0:1, :], in_=ts(tiny, 0, 1)),
              after=[t_mx], sig="sg", amt=16)
    cc2 = p.op("gpsimd", lambda g: g.collective_compute(
        "AllReduce", ALU.max, replica_groups=[list(range(8))],
        ins=[cc_m1_in.ap()], outs=[cc_m1_out.ap()]), after=[d3], sig="sc")
    bc1 = p.op("gpsimd", lambda g: g.dma_start(
        out=gmaxb[:, :], in_=bass.AP(cc_m1_out, 0, [[0, 128], [1, 128]])),
        after=[cc2], sig="sg", amt=16)
    gm1 = p.op("vector", lambda v: v.reduce_max(ts(tiny, 0, 1), gmaxb[:, :], axis=AX),
               after=[bc1], sig="ve")
    if dbg:
        p.op("gpsimd", lambda g: g.dma_start(out=dbg_m1.ap(), in_=cc_m1_out.ap()),
             after=[cc2], sig="sg", amt=16)
    bcx = p.op("gpsimd", lambda g: g.dma_start(
        out=ts(tiny, 1, 1), in_=bass.AP(clipx, 0, [[0, 128], [1, 1]])),
        sig="sg", amt=16)
    e1 = p.op("scalar", lambda s: s.activation(ts(tiny, 2, 1), ts(tiny, 1, 1),
                                               AF.Exp, bias=0.0, scale=-1.0),
              after=[bcx], sig="ac")
    e2 = p.op("vector", lambda v: v.tensor_scalar(ts(tiny, 2, 1), ts(tiny, 2, 1),
                                                  1.0, 127.0, op0=ALU.add, op1=ALU.mult),
              after=[e1], sig="ve")
    e3 = p.op("vector", lambda v: v.reciprocal(ts(tiny, 3, 1), ts(tiny, 0, 1)),
              after=[gm1], sig="ve")
    e4 = p.op("vector", lambda v: v.tensor_mul(scl[:, 0:1], ts(tiny, 2, 1), ts(tiny, 3, 1)),
              after=[e2, e3], sig="ve")
    e5 = p.op("vector", lambda v: v.reciprocal(ts(tiny, 3, 1), ts(tiny, 2, 1)), sig="ve")
    e6 = p.op("vector", lambda v: v.tensor_mul(scl[:, 1:2], ts(tiny, 0, 1), ts(tiny, 3, 1)),
              sig="ve")
    e7 = p.op("vector", lambda v: v.tensor_scalar(a2c[:, 0:32], ts(scr, 64, 32),
                                                  scl[:, 0:1], None, op0=ALU.mult),
              after=[e4], sig="ve")
    e8 = p.op("vector", lambda v: v.scalar_tensor_tensor(b2c[:, 0:32], mv2[:, 0:64:2],
                                                         -1.0, a2c[:, 0:32],
                                                         op0=ALU.mult, op1=ALU.mult),
              after=[e7], sig="ve")
    e9 = p.op("vector", lambda v: v.tensor_scalar(sg_all[:, :], wgs[:, :], scl[:, 1:2],
                                                  None, op0=ALU.mult),
              after=[e6, prep["stores"][NCI - 1]], sig="ve")
    scales_done = p.op("vector", lambda v: v.tensor_scalar(su_all[:, :], wus[:, :],
                                                           scl[:, 1:2], None, op0=ALU.mult),
                       after=[prep["stores"][-1]], sig="ve")

    b_pe_done = p.now("pe")
    esB.close()
    esPhi.close()
    esWo.close()
    esC = ExitStack()
    wgT = esC.enter_context(nc.sbuf_tensor("wgT", [128, NCF * FF], BF16))
    wuT = esC.enter_context(nc.sbuf_tensor("wuT", [128, NCF * FF], BF16))
    xq_tm = esC.enter_context(nc.sbuf_tensor("xq_tm", [128, NG * D], BF16))
    xq_fm = esC.enter_context(nc.sbuf_tensor("xq_fm", [128, NCF * TILE], BF16))
    g_sb = esC.enter_context(nc.sbuf_tensor("g_sb", [128, 512], F32))
    u_sb = esC.enter_context(nc.sbuf_tensor("u_sb", [128, 512], F32))
    h_sb = [esC.enter_context(nc.sbuf_tensor(f"h_sb{i}", [128, 512], F32)) for i in range(2)]
    wdstg = esC.enter_context(nc.sbuf_tensor("wdstg", [128, FF], F32))
    wdq_sb = esC.enter_context(nc.sbuf_tensor("wdq_sb", [128, FF], BF16))
    dum = esC.enter_context(nc.sbuf_tensor("dum", [128, 512], F32))

    wgT_tok = []
    wuT_tok = []
    for c in range(NCF):
        wgT_tok.append(p.op(
            "sync", lambda s, o=ts(wgT, c * FF, FF), i=wg_q.ap()[:, c * 128:(c + 1) * 128]:
                s.dma_start_transpose(o, i),
            after=[prep["stores"][NCI - 1], b_pe_done, bl[("x1st", NT - 1)],
                   scales_done], sig="sx", amt=16))
        wuT_tok.append(p.op(
            "sync", lambda s, o=ts(wuT, c * FF, FF), i=wu_q.ap()[:, c * 128:(c + 1) * 128]:
                s.dma_start_transpose(o, i),
            after=[prep["stores"][-1], b_pe_done, scales_done], sig="sx", amt=16))

    wdp = {"wstg_free": None, "wq_free": None, "stores": [],
           "extra": [b_pe_done, scales_done, bl[("x1st", NT - 1)]]}

    def emit_wd_prep(i):
        ld = p.op("sync", lambda s, o=wdstg[:, :], a=wdd.ap()[i * 128:(i + 1) * 128, :]:
                  s.dma_start(out=o, in_=a),
                  after=[wdp["wstg_free"]] + wdp.pop("extra", []), sig="sx", amt=16)
        rd = p.op("vector", lambda v, o=wds[:, i:i + 1], a=wdstg[:, :]:
                  v.reduce_max(o, a, axis=AX, apply_absolute_value=True),
                  after=[ld], sig="ve")
        sc = p.op("vector", lambda v, o=wds[:, i:i + 1], a=wds[:, i:i + 1]:
                  v.tensor_scalar(o, a, 1.0 / 127.0, None, op0=ALU.mult),
                  after=[rd], sig="ve")
        rc = p.op("vector", lambda v, o=tin2[:, 7:8], a=wds[:, i:i + 1]:
                  v.reciprocal(o, a), after=[sc], sig="ve")
        t1 = p.op("scalar", lambda s, o=wdstg[:, :], a=wdstg[:, :], sa=tin2[:, 7:8]:
                  s.activation(o, a, AF.Copy, bias=MAGIC, scale=sa), after=[rc], sig="ac")
        t3 = p.op("scalar", lambda s, o=wdstg[:, :], a=wdstg[:, :]:
                  s.activation(o, a, AF.Copy, bias=-MAGIC, scale=1.0),
                  after=[t1], sig="ac")
        t5 = p.op("scalar", lambda s, o=wdq_sb[:, :], a=wdstg[:, :], sa=wds[:, i:i + 1]:
                  s.activation(o, a, AF.Copy, bias=0.0, scale=sa),
                  after=[t3, wdp["wq_free"]], sig="ac")
        st = p.op("gpsimd", lambda g, o=wd_q.ap()[i * 128:(i + 1) * 128, :], a=wdq_sb[:, :]:
                  g.dma_start(out=o, in_=a), after=[t5], sig="sg", amt=16)
        wdp["wstg_free"] = t5
        wdp["wq_free"] = st
        wdp["stores"].append(st)

    p.op("scalar", lambda s: load_act_set(s, 18), sig="ac")
    cl = {"xfree": bl["xfree"]}
    hst_tok = {}
    for t in range(NT):
        xin = tok_ap(x1_hbm, t)
        ldx = p.op("sync", lambda s, o=sb3(x_t), a=xin: s.dma_start(out=o, in_=a),
                   after=[cl.get("xfree"), bl[("x1st", t)]], sig="sx", amt=16)

        xq_toks = []
        for g in range(NG):
            gc = t * NG + g
            q1 = p.op("scalar", lambda s, o=ts(scr, 0, D), a=ts(x_t, g * D, D),
                      sa=a2c[:, gc:gc + 1], ba=b2c[:, gc:gc + 1]:
                      s.activation(o, a, AF.Identity, bias=ba, scale=sa),
                      after=[ldx, e8, xq_toks[-1] if g else cl.get("scr_free"),
                             cl.get("xq_tm_free"),
                             b_pe_done if t == 0 and g == 0 else None], sig="ac")
            q2 = p.op("vector", lambda v, o=ts(scr, 0, D), a=ts(scr, 0, D):
                      v.tensor_scalar(o, a, -128.49999, 127.49998,
                                      op0=ALU.max, op1=ALU.min),
                      after=[q1], sig="ve")
            xq_toks.append(p.op("vector", lambda v, o=ts(xq_tm, g * D, D), a=ts(scr, 0, D):
                                v.tensor_scalar(o, a, MAGIC, -MAGIC,
                                                op0=ALU.add, op1=ALU.add),
                                after=[q2], sig="ve"))
        cl["xfree"] = xq_toks[-1]
        cl["scr_free"] = xq_toks[-1]

        evt = []
        for c in range(NCF):
            half = c % 2
            tl = None
            for g in range(NG):
                tl = p.op(
                    "tensor",
                    lambda te, o=pbf[:, half * 512 + g * 128:half * 512 + (g + 1) * 128],
                           a=ts(xq_tm, g * D + c * 128, 128):
                        te.transpose(o, a, ident[:, :]),
                    after=[xq_toks[-1], evt[c - 1] if c >= 1 else cl.get("pbf_free")],
                    sig="pe")
            evt.append(p.op(
                "vector",
                lambda v, o=ts(xq_fm, c * TILE, TILE),
                       a=pbf[:, half * 512:half * 512 + 512]:
                    v.tensor_copy(o, a), after=[tl], sig="ve"))
        cl["pbf_free"] = evt[-1]
        cl["xq_tm_free"] = p.now("pe")

        for i in range(NCI):
            bg = pb[2 * (i % 3)]
            bu = pb[2 * (i % 3) + 1]
            for c in range(NCF):
                p.op("tensor", lambda te, o=bg[:, :], l=ts(wgT, c * FF + i * 128, 128),
                     r=ts(xq_fm, c * TILE, TILE), st=(c == 0), sp=(c == NCF - 1):
                     te.matmul(o, l, r, start=st, stop=sp, skip_group_check=True),
                     after=[evt[c], wgT_tok[c], cl.get(("gps", i % 3))], sig="pe")
            gmm = p.now("pe")
            for c in range(NCF):
                p.op("tensor", lambda te, o=bu[:, :], l=ts(wuT, c * FF + i * 128, 128),
                     r=ts(xq_fm, c * TILE, TILE), st=(c == 0), sp=(c == NCF - 1):
                     te.matmul(o, l, r, start=st, stop=sp, skip_group_check=True),
                     after=[wuT_tok[c]], sig="pe")
            umm = p.now("pe")
            hb = h_sb[i % 2]
            gv = p.op("scalar", lambda s, o=g_sb[:, :], a=bg[:, :], sa=sg_all[:, i:i + 1]:
                      s.activation(o, a, AF.Silu, bias=0.0, scale=sa),
                      after=[gmm, e9, scales_done, cl.get("g_free")], sig="ac")
            uv = p.op("scalar", lambda s, o=u_sb[:, :], a=bu[:, :], sa=su_all[:, i:i + 1]:
                      s.activation(o, a, AF.Copy, bias=0.0, scale=sa),
                      after=[umm], sig="ac")
            hm = p.op("gpsimd", lambda g, o=hb[:, :], a=g_sb[:, :], b=u_sb[:, :]:
                      g.tensor_mul(o, a, b), after=[gv, uv, cl.get(("hfree", i % 2))],
                      sig="gp")
            cl[("gps", i % 3)] = hm
            cl["g_free"] = hm
            ht = p.op("vector", lambda v, o=tiny[:, 6:7], a=hb[:, :]:
                      v.reduce_max(o, a, axis=AX, apply_absolute_value=True),
                      after=[hm], sig="ve")
            p.op("vector", lambda v, o=habs[:, i:i + 1], a=habs[:, i:i + 1], b=tiny[:, 6:7]:
                 v.tensor_max(o, a, b), after=[ht], sig="ve")
            st = p.op("gpsimd", lambda g, o=h_hbm.ap()[i, :, t * TILE:(t + 1) * TILE],
                      a=hb[:, :]:
                      g.dma_start(out=o, in_=a), after=[hm], sig="sg", amt=16)
            cl[("hfree", i % 2)] = [st, ht]
            hst_tok[(i, t)] = st
        if t < NCF:
            emit_wd_prep(t)

    hmx = p.op("vector", lambda v: v.reduce_max(ts(tiny, 0, 1), habs[:, :], axis=AX),
               sig="ve")
    d4 = p.op("gpsimd", lambda g: g.dma_start(out=cc_m2_in.ap()[0:1, :], in_=ts(tiny, 0, 1)),
              after=[hmx], sig="sg", amt=16)
    cc3 = p.op("gpsimd", lambda g: g.collective_compute(
        "AllReduce", ALU.max, replica_groups=[list(range(8))],
        ins=[cc_m2_in.ap()], outs=[cc_m2_out.ap()]), after=[d4], sig="sc")
    bc2 = p.op("gpsimd", lambda g: g.dma_start(
        out=gmaxb[:, :], in_=bass.AP(cc_m2_out, 0, [[0, 128], [1, 128]])),
        after=[cc3], sig="sg", amt=16)
    gm2 = p.op("vector", lambda v: v.reduce_max(ts(tiny, 0, 1), gmaxb[:, :], axis=AX),
               after=[bc2], sig="ve")
    if dbg:
        p.op("gpsimd", lambda g: g.dma_start(out=dbg_m2.ap(), in_=cc_m2_out.ap()),
             after=[cc3], sig="sg", amt=16)
    bcd = p.op("gpsimd", lambda g: g.dma_start(
        out=ts(tiny, 1, 1), in_=bass.AP(clipd, 0, [[0, 128], [1, 1]])),
        sig="sg", amt=16)
    p.op("scalar", lambda s: load_act_set(s, 6), sig="ac")
    f1 = p.op("scalar", lambda s: s.activation(ts(tiny, 2, 1), ts(tiny, 1, 1),
                                               AF.Exp, bias=0.0, scale=-1.0),
              after=[bcd], sig="ac")
    f2 = p.op("vector", lambda v: v.tensor_scalar(ts(tiny, 2, 1), ts(tiny, 2, 1),
                                                  1.0, 127.0, op0=ALU.add, op1=ALU.mult),
              after=[f1], sig="ve")
    f3 = p.op("vector", lambda v: v.reciprocal(ts(tiny, 3, 1), ts(tiny, 0, 1)),
              after=[gm2], sig="ve")
    p.op("vector", lambda v: v.tensor_mul(scl[:, 2:3], ts(tiny, 2, 1), ts(tiny, 3, 1)),
         after=[f2, f3], sig="ve")
    f5 = p.op("vector", lambda v: v.reciprocal(ts(tiny, 3, 1), ts(tiny, 2, 1)), sig="ve")
    s2_done = p.op("vector", lambda v: v.tensor_mul(scl[:, 3:4], ts(tiny, 0, 1),
                                                    ts(tiny, 3, 1)),
                   after=[f5], sig="ve")

    c_pe_done = p.now("pe")
    c_ac_done = p.now("ac")
    esC.close()
    esD = ExitStack()
    wdT = esD.enter_context(nc.sbuf_tensor("wdT", [128, NCI * D], BF16))
    hq_sb = esD.enter_context(nc.sbuf_tensor("hq_sb", [128, NCI * 512], BF16))
    hstg = [esD.enter_context(nc.sbuf_tensor(f"hstg{i}", [128, 512], F32)) for i in range(2)]
    out_sb = [esD.enter_context(nc.sbuf_tensor(f"out_sb{i}", [128, NG * D], F32))
              for i in range(2)]

    wdT_tok = []
    for c in range(NCI):
        wdT_tok.append(p.op(
            "sync", lambda s, o=ts(wdT, c * D, D), i=wd_q.ap()[:, c * 128:(c + 1) * 128]:
                s.dma_start_transpose(o, i),
            after=[wdp["stores"][-1], c_pe_done, c_ac_done, s2_done,
                   hst_tok[(NCI - 1, NT - 1)]], sig="sx", amt=16))

    dl = {"xfree": cl["xfree"], "psum_free": c_ac_done}
    for t in range(NT):
        xin = tok_ap(x1_hbm, t)
        ldx1 = p.op("sync", lambda s, o=sb3(x_t), a=xin: s.dma_start(out=o, in_=a),
                    after=[dl.get("xfree")], sig="sx", amt=16)
        hq_toks = []
        for i in range(NCI):
            hb = hstg[i % 2]
            ld = p.op("sync", lambda s, o=hb[:, :],
                      a=h_hbm.ap()[i, :, t * TILE:(t + 1) * TILE]:
                      s.dma_start(out=o, in_=a),
                      after=[dl.get(("hfree", i % 2)), hst_tok[(i, t)],
                             c_pe_done if t == 0 and i == 0 else None],
                      sig="sx", amt=16)
            q1 = p.op("scalar", lambda s, o=ts(scr, 0, 512), a=hb[:, :], sa=scl[:, 2:3]:
                      s.activation(o, a, AF.Copy, bias=0.0, scale=sa),
                      after=[ld, s2_done,
                             hq_toks[-1] if i else dl.get("scr_free"),
                             dl.get("hq_free")], sig="ac")
            dl[("hfree", i % 2)] = q1
            q2 = p.op("vector", lambda v, o=ts(scr, 0, 512), a=ts(scr, 0, 512):
                      v.tensor_scalar(o, a, -128.49999, 127.49998,
                                      op0=ALU.max, op1=ALU.min),
                      after=[q1], sig="ve")
            hq_toks.append(p.op("vector", lambda v, o=ts(hq_sb, i * 512, 512),
                                a=ts(scr, 0, 512):
                                v.tensor_scalar(o, a, MAGIC, -MAGIC,
                                                op0=ALU.add, op1=ALU.add),
                                after=[q2], sig="ve"))
        dl["scr_free"] = hq_toks[-1]

        osb = out_sb[t % 2]
        wave_banks = ((pb[0], pb[1], pb[2], pb[3]), (pb[4], pb[5], pb[6], pb[3]))
        for w, groups in enumerate(((0, 1), (2, 3))):
            for i in range(NCI):
                for gi, g in enumerate(groups):
                    for n in range(2):
                        bank = wave_banks[w][2 * gi + n]
                        p.op("tensor", lambda te, o=bank[:, :],
                             l=ts(hq_sb, i * 512 + g * 128, 128),
                             r=ts(wdT, i * D + n * 512, 512),
                             st=(i == 0), sp=(i == NCI - 1):
                             te.matmul(o, l, r, start=st, stop=sp, skip_group_check=True),
                             after=[hq_toks[i], wdT_tok[i],
                                    dl.get(("dps", id(bank))), dl.get("psum_free")],
                             sig="pe")
            wave_done = p.now("pe")
            for gi, g in enumerate(groups):
                for n in range(2):
                    bank = wave_banks[w][2 * gi + n]
                    ev = p.op("vector", lambda v, o=ts(osb, g * D + n * 512, 512),
                              a=bank[:, :], b=ts(x_t, g * D + n * 512, 512):
                              v.scalar_tensor_tensor(o, a, scl[:, 3:4], b,
                                                     op0=ALU.mult, op1=ALU.add),
                              after=[wave_done, ldx1, s2_done,
                                     dl.get(("outfree", t % 2))], sig="ve")
                    dl[("dps", id(bank))] = ev
        dl["psum_free"] = None
        ev_done = p.now("ve")
        dl["xfree"] = ev_done
        dl["hq_free"] = wave_done
        yst = p.op("gpsimd", lambda g, o=tok_ap(y_out, t), a=sb3(osb):
                   g.dma_start(out=o, in_=a), after=[ev_done], sig="sg", amt=16)
        dl[("outfree", t % 2)] = yst
        dl[("yst", t)] = yst

    p.op("gpsimd", lambda g: g.nop(), after=[dl[("yst", NT - 1)]])

    with nc.Block() as block:
        @block.sync
        def _(eng):
            p.emit("sync", eng, sems)

        @block.tensor
        def _(eng):
            p.emit("tensor", eng, sems)

        @block.vector
        def _(eng):
            p.emit("vector", eng, sems)

        @block.scalar
        def _(eng):
            p.emit("scalar", eng, sems)

        @block.gpsimd
        def _(eng):
            p.emit("gpsimd", eng, sems)

    esD.close()
    es.close()
    return nc



def make_inputs(inputs):
    import ml_dtypes
    x = np.asarray(inputs["L0_attn_x"])
    flat = np.ascontiguousarray(x.reshape(8, NTOK, D))
    ident = np.eye(128, dtype=ml_dtypes.bfloat16)
    ones = np.ones((1, TILE), dtype=ml_dtypes.bfloat16)
    epair = np.zeros((16, NCF * 128), dtype=ml_dtypes.bfloat16)
    for pr in range(NCF):
        for j in range(128):
            epair[2 * pr + (j // 64), pr * 128 + j] = 1.0
    common = {
        "Wq": np.ascontiguousarray(inputs["L0_attn_Wq"], dtype=np.float32),
        "Wk": np.ascontiguousarray(inputs["L0_attn_Wk"], dtype=np.float32),
        "Wv": np.ascontiguousarray(inputs["L0_attn_Wv"], dtype=np.float32),
        "Wo": np.ascontiguousarray(inputs["L0_attn_Wo"], dtype=np.float32),
        "bq": np.ascontiguousarray(inputs["L0_attn_bq"], dtype=np.float32).reshape(1, D),
        "bk": np.ascontiguousarray(inputs["L0_attn_bk"], dtype=np.float32).reshape(1, D),
        "bv": np.ascontiguousarray(inputs["L0_attn_bv"], dtype=np.float32).reshape(1, D),
        "bo": np.ascontiguousarray(inputs["L0_attn_bo"], dtype=np.float32).reshape(1, D),
        "wg": np.ascontiguousarray(inputs["L0_ffn_w_gate"], dtype=np.float32),
        "wu": np.ascontiguousarray(inputs["L0_ffn_w_up"], dtype=np.float32),
        "wd": np.ascontiguousarray(inputs["L0_ffn_w_down"], dtype=np.float32),
        "clipx": np.ascontiguousarray(inputs["L0_ffn_clip_x"], dtype=np.float32).reshape(1, 1),
        "clipd": np.ascontiguousarray(inputs["L0_ffn_clip_down"], dtype=np.float32).reshape(1, 1),
        "ident": ident, "onesc": ones, "epair": epair,
    }
    out = []
    for c in range(8):
        sm = np.zeros((128, 4), dtype=np.float32)
        sm[:, c // 2] = 1.0
        out.append(dict(common, x=np.ascontiguousarray(flat[c]), smask=sm))
    return out


def kernel(**inputs):
    in_maps = make_inputs(inputs)
    nc = build_nc(dbg=False)
    res = run_bass_kernel_spmd(nc, in_maps, core_ids=list(range(8)))
    y = np.concatenate([np.asarray(r["y"]) for r in res.results], axis=0)
    return y.reshape(4, 8192, 1024)


# version 13
# speedup vs baseline: 2.4161x; 1.0024x over previous
import numpy as np
from contextlib import ExitStack

import concourse.bass as bass
import concourse.mybir as mybir
from concourse.bass_utils import run_bass_kernel_spmd

F32 = mybir.dt.float32
BF16 = mybir.dt.bfloat16
AF = mybir.ActivationFunctionType
ALU = mybir.AluOpType
AX = mybir.AxisListType.X

MAGIC = 12582912.0
D = 1024
NH = 16
HD = 64
FF = 2816
NTOK = 4096
TILE = 512
NT = NTOK // TILE
NG = TILE // 128
NCF = D // 128
NCI = FF // 128
LN_EPS = 1e-5
ATTN_EPS = 1e-6
VAGW = NH * 65 + 15

ENGINES = ("sync", "tensor", "vector", "scalar", "gpsimd")
SELF_SERIAL = {"vector": "ve", "scalar": "ac", "gpsimd": "gp"}


NPOOL = 8


class Prog:

    def __init__(self):
        self.items = {e: [] for e in ENGINES}
        self.cnt = {}
        self.pool_rr = {"sx": 0, "sg": 0}
        self.pool_last = {}
        self.last_sc = None

    def _flat(self, after):
        out = []
        for t in after:
            if t is None:
                continue
            if isinstance(t, list):
                out.extend(x for x in t if x is not None)
            else:
                out.append(t)
        return out

    def op(self, eng, fn, after=(), sig=None, amt=1):
        after = self._flat(after)
        if sig in ("sx", "sg"):
            i = self.pool_rr[sig] % NPOOL
            self.pool_rr[sig] += 1
            sem = f"{sig}{i}"
            prev = self.pool_last.get(sem)
            if prev is not None:
                after = after + [prev]
            self.cnt[sem] = self.cnt.get(sem, 0) + amt
            tok = (sem, self.cnt[sem])
            self.pool_last[sem] = tok
            self.items[eng].append((after, fn, sem, amt, None))
            return tok
        if sig == "sc":
            if self.last_sc is not None:
                after = after + [self.last_sc]
        own = SELF_SERIAL.get(eng)
        if own is not None and sig is None:
            sig = own
        self_wait = None
        if own is not None:
            self_wait = (own, self.cnt.get(own, 0))
        tok = None
        if sig is not None:
            self.cnt[sig] = self.cnt.get(sig, 0) + amt
            tok = (sig, self.cnt[sig])
        if sig == "sc":
            self.last_sc = tok
        self.items[eng].append((after, fn, sig, amt, self_wait))
        return tok

    def now(self, sig):
        if sig in ("sx", "sg"):
            return [self.pool_last[s] for s in
                    (f"{sig}{i}" for i in range(NPOOL)) if s in self.pool_last]
        return (sig, self.cnt.get(sig, 0))

    def emit(self, eng_name, eng, sems):
        waited = {}
        for waits, fn, sig, amt, self_wait in self.items[eng_name]:
            allw = list(waits)
            if self_wait is not None and self_wait[1] > 0:
                allw.append(self_wait)
            for (s, v) in allw:
                if waited.get(s, 0) >= v:
                    continue
                eng.wait_ge(sems[s], v)
                waited[s] = v
            inst = fn(eng)
            if sig is not None:
                inst.then_inc(sems[sig], amt)


def build_nc(dbg=False):
    nc = bass.Bass("TRN2", target_bir_lowering=False, detect_race_conditions=True)
    dkind = "ExternalOutput" if dbg else "Internal"

    x_in = nc.dram_tensor("x", [NTOK, D], F32, kind="ExternalInput")
    Wq = nc.dram_tensor("Wq", [D, D], F32, kind="ExternalInput")
    Wk = nc.dram_tensor("Wk", [D, D], F32, kind="ExternalInput")
    Wv = nc.dram_tensor("Wv", [D, D], F32, kind="ExternalInput")
    Wo = nc.dram_tensor("Wo", [D, D], F32, kind="ExternalInput")
    bqd = nc.dram_tensor("bq", [1, D], F32, kind="ExternalInput")
    bkd = nc.dram_tensor("bk", [1, D], F32, kind="ExternalInput")
    bvd = nc.dram_tensor("bv", [1, D], F32, kind="ExternalInput")
    bod = nc.dram_tensor("bo", [1, D], F32, kind="ExternalInput")
    wgd = nc.dram_tensor("wg", [FF, D], F32, kind="ExternalInput")
    wud = nc.dram_tensor("wu", [FF, D], F32, kind="ExternalInput")
    wdd = nc.dram_tensor("wd", [D, FF], F32, kind="ExternalInput")
    clipx = nc.dram_tensor("clipx", [1, 1], F32, kind="ExternalInput")
    clipd = nc.dram_tensor("clipd", [1, 1], F32, kind="ExternalInput")
    ident_in = nc.dram_tensor("ident", [128, 128], BF16, kind="ExternalInput")
    ones_in = nc.dram_tensor("onesc", [1, TILE], BF16, kind="ExternalInput")
    epair_in = nc.dram_tensor("epair", [16, NCF * 128], BF16, kind="ExternalInput")
    smask_in = nc.dram_tensor("smask", [128, 4], F32, kind="ExternalInput")

    y_out = nc.dram_tensor("y", [NTOK, D], F32, kind="ExternalOutput")

    w_bf = {n: nc.dram_tensor(f"{n}_bf", [D, D], BF16, kind="Internal")
            for n in ("wq", "wk", "wv", "wo")}
    wg_q = nc.dram_tensor("wg_q", [FF, D], BF16, kind="Internal")
    wu_q = nc.dram_tensor("wu_q", [FF, D], BF16, kind="Internal")
    wd_q = nc.dram_tensor("wd_q", [D, FF], BF16, kind="Internal")
    x1_hbm = nc.dram_tensor("x1_hbm", [NTOK, D], F32, kind=dkind)
    h_hbm = nc.dram_tensor("h_hbm", [NCI, 128, NTOK], F32, kind="Internal")
    cc_kv_in = nc.dram_tensor("cc_kv_in", [128, 4 * 520], F32, kind="Internal")
    cc_kv_out = nc.dram_tensor("cc_kv_out", [128, 4 * 520], F32, kind="Internal")
    cc_m1_in = nc.dram_tensor("cc_m1_in", [1, 128], F32, kind="Internal")
    cc_m1_out = nc.dram_tensor("cc_m1_out", [1, 128], F32, kind="Internal")
    cc_m2_in = nc.dram_tensor("cc_m2_in", [1, 128], F32, kind="Internal")
    cc_m2_out = nc.dram_tensor("cc_m2_out", [1, 128], F32, kind="Internal")
    if dbg:
        dbg_kv = nc.dram_tensor("dbg_kv", [128, 4 * 520], F32, kind="ExternalOutput")
        dbg_m1 = nc.dram_tensor("dbg_m1", [1, 128], F32, kind="ExternalOutput")
        dbg_m2 = nc.dram_tensor("dbg_m2", [1, 128], F32, kind="ExternalOutput")
        dbg_kvloc = nc.dram_tensor("dbg_kvloc", [128, 520], F32, kind="ExternalOutput")
        dbg_phiq = nc.dram_tensor("dbg_phiq", [128, NCF * NTOK], BF16, kind="ExternalOutput")
        dbg_ktm = nc.dram_tensor("dbg_ktm", [128, 2 * D], BF16, kind="ExternalOutput")
        dbg_vag = nc.dram_tensor("dbg_vag", [128, 2 * VAGW], BF16, kind="ExternalOutput")
        dbg_xnfm = nc.dram_tensor("dbg_xnfm", [128, NCF * TILE], BF16, kind="ExternalOutput")
        dbg_wkt = nc.dram_tensor("dbg_wkt", [128, NCF * D], BF16, kind="ExternalOutput")
        dbg_brow = nc.dram_tensor("dbg_brow", [1, 4 * D], BF16, kind="ExternalOutput")
        dbg_vag0 = nc.dram_tensor("dbg_vag0", [128, 2 * VAGW], BF16, kind="ExternalOutput")
        dbg_xntm = nc.dram_tensor("dbg_xntm", [128, NG * D], BF16, kind="ExternalOutput")
        dbg_rstd = nc.dram_tensor("dbg_rstd", [128, NT * NG], F32, kind="ExternalOutput")
        dbg_mv1 = nc.dram_tensor("dbg_mv1", [128, 2 * NT * NG], F32, kind="ExternalOutput")

    p = Prog()
    es = ExitStack()
    sem_names = ["sc", "pe", "ve", "ac", "gp"] + \
        [f"sx{i}" for i in range(NPOOL)] + [f"sg{i}" for i in range(NPOOL)]
    sems = {n: es.enter_context(nc.semaphore(n)) for n in sem_names}

    ident = es.enter_context(nc.sbuf_tensor("identt", [128, 128], BF16))
    onesc = es.enter_context(nc.sbuf_tensor("onesct", [1, TILE], BF16))
    brow = {n: es.enter_context(nc.sbuf_tensor(f"brow_{n}", [1, D], BF16))
            for n in ("q", "k", "v", "o")}
    mv1 = es.enter_context(nc.sbuf_tensor("mv1", [128, 2 * NT * NG], F32))
    rstd1 = es.enter_context(nc.sbuf_tensor("rstd1", [128, NT * NG], F32))
    mb1 = es.enter_context(nc.sbuf_tensor("mb1", [128, NT * NG], F32))
    mv2 = es.enter_context(nc.sbuf_tensor("mv2", [128, 2 * NT * NG], F32))
    xm2 = es.enter_context(nc.sbuf_tensor("xm2", [128, NT * NG], F32))
    a2c = es.enter_context(nc.sbuf_tensor("a2c", [128, NT * NG], F32))
    b2c = es.enter_context(nc.sbuf_tensor("b2c", [128, NT * NG], F32))
    wgs = es.enter_context(nc.sbuf_tensor("wgs", [128, NCI], F32))
    wus = es.enter_context(nc.sbuf_tensor("wus", [128, NCI], F32))
    wds = es.enter_context(nc.sbuf_tensor("wds", [128, NCF], F32))
    sg_all = es.enter_context(nc.sbuf_tensor("sg_all", [128, NCI], F32))
    su_all = es.enter_context(nc.sbuf_tensor("su_all", [128, NCI], F32))
    scl = es.enter_context(nc.sbuf_tensor("scl", [128, 4], F32))
    gmaxb = es.enter_context(nc.sbuf_tensor("gmaxb", [128, 128], F32))
    kv_sb = es.enter_context(nc.sbuf_tensor("kv_sb", [128, 520], F32))
    habs = es.enter_context(nc.sbuf_tensor("habs", [128, NCI], F32))
    tiny = es.enter_context(nc.sbuf_tensor("tiny", [128, 8], F32))
    tin2 = es.enter_context(nc.sbuf_tensor("tin2", [128, 8], F32))
    smask = es.enter_context(nc.sbuf_tensor("smaskt", [128, 4], F32))
    zrow = es.enter_context(nc.sbuf_tensor("zrow", [1, 512], BF16))
    x_t = es.enter_context(nc.sbuf_tensor("x_t", [128, NG * D], F32))
    scr = es.enter_context(nc.sbuf_tensor("scr", [128, 2560], F32))
    gsc = es.enter_context(nc.sbuf_tensor("gsc", [128, 1024], F32))
    bqfm = es.enter_context(nc.sbuf_tensor("bqfm", [128, NCF], F32))
    nbqfm = es.enter_context(nc.sbuf_tensor("nbqfm", [128, NCF], F32))
    bvb = es.enter_context(nc.sbuf_tensor("bvb", [128, D], BF16))

    kps = es.enter_context(nc.psum_tensor("kps", [128, 1024], F32))
    vps = es.enter_context(nc.psum_tensor("vps", [128, 1024], F32))
    pb4t = es.enter_context(nc.psum_tensor("pb4t", [128, 512], F32))
    pb5t = es.enter_context(nc.psum_tensor("pb5t", [128, 512], F32))
    pb6t = es.enter_context(nc.psum_tensor("pb6t", [128, 512], F32))
    pbf = es.enter_context(nc.psum_tensor("pbf", [128, 1024], BF16))
    pb = [kps[:, 0:512], kps[:, 512:1024], vps[:, 0:512], vps[:, 512:1024],
          pb4t[:, :], pb5t[:, :], pb6t[:, :]]

    def ts(tensor, c0, w):
        return tensor[:, c0:c0 + w]

    def tok_ap(dram, t):
        return bass.AP(dram, t * TILE * D, [[D, 128], [128 * D, NG], [1, D]])

    def sb3(sb):
        return sb.ap().rearrange("p (g f) -> p g f", f=D)

    cast_tok = {}
    for n, W in (("wk", Wk), ("wv", Wv), ("wq", Wq), ("wo", Wo)):
        cast_tok[n] = p.op("gpsimd",
                           lambda g, a=w_bf[n].ap(), b=W.ap(): g.dma_start(out=a, in_=b),
                           sig="sg", amt=16)
    for n, B in (("q", bqd), ("k", bkd), ("v", bvd), ("o", bod)):
        p.op("gpsimd", lambda g, a=brow[n][:, :], b=B.ap(): g.dma_start(out=a, in_=b),
             sig="sg", amt=16)
    brow_tok = p.now("sg")
    bq_tok = p.op("sync", lambda s: s.dma_start(
        out=bqfm[:, :], in_=bass.AP(bqd, 0, [[1, 128], [128, NCF]]),
        allow_slow_non_contiguous=True), sig="sx", amt=16)
    nbq_tok = p.op("vector", lambda v: v.tensor_scalar(nbqfm[:, :], bqfm[:, :], -1.0,
                                                       None, op0=ALU.mult),
                   after=[bq_tok], sig="ve")

    p.op("sync", lambda s: s.dma_start(out=ident[:, :], in_=ident_in.ap()), sig="sx", amt=16)
    onesc_tok = p.op("sync", lambda s: s.dma_start(out=onesc[:, :], in_=ones_in.ap()),
                     sig="sx", amt=16)
    smask_tok = p.op("sync", lambda s: s.dma_start(out=smask[:, :], in_=smask_in.ap()),
                     sig="sx", amt=16)

    esA = ExitStack()
    WkT = esA.enter_context(nc.sbuf_tensor("WkT", [128, NCF * D], BF16))
    WvT = esA.enter_context(nc.sbuf_tensor("WvT", [128, NCF * D], BF16))
    WqT = esA.enter_context(nc.sbuf_tensor("WqT", [128, NCF * D], BF16))
    xn_tm = esA.enter_context(nc.sbuf_tensor("xn_tm", [128, NG * D], BF16))
    xn_fm = esA.enter_context(nc.sbuf_tensor("xn_fm", [128, NCF * TILE], BF16))
    k_tm = esA.enter_context(nc.sbuf_tensor("k_tm", [128, 2 * D], BF16))
    v_ag = esA.enter_context(nc.sbuf_tensor("v_ag", [128, 2 * VAGW], BF16))
    wstg = esA.enter_context(nc.sbuf_tensor("wstg", [128, D], F32))
    wqnt = esA.enter_context(nc.sbuf_tensor("wqnt", [128, D], BF16))
    esWo = ExitStack()
    WoT = esWo.enter_context(nc.sbuf_tensor("WoT", [128, NCF * D], BF16, side="right"))
    esPhi = ExitStack()
    phiq = esPhi.enter_context(nc.sbuf_tensor("phiq", [128, NCF * NTOK], BF16, side="right"))

    wT_tok = {}
    for n, t_ in (("wk", WkT), ("wv", WvT), ("wq", WqT), ("wo", WoT)):
        for c in range(NCF):
            wT_tok[n] = p.op(
                "sync",
                lambda s, o=t_[:, c * D:(c + 1) * D],
                       i=w_bf[n].ap()[:, c * 128:(c + 1) * 128]:
                    s.dma_start_transpose(o, i),
                after=[cast_tok[n]], sig="sx", amt=16)

    p.op("vector", lambda v: v.memset(v_ag[:, :], 0.0), sig="ve")
    for sl in range(2):
        for h in range(NH):
            cpos = sl * VAGW + h * 65 + 64
            p.op("vector", lambda v, c=cpos: v.memset(v_ag[:, c:c + 1], 1.0), sig="ve")
    p.op("vector", lambda v: v.memset(habs[:, :], 0.0), sig="ve")

    prep_jobs = [(wgd, wg_q, wgs, i) for i in range(NCI)] + \
                [(wud, wu_q, wus, i) for i in range(NCI)]
    prep = {"ld": None, "wstg_free": None, "wqnt_free": None, "stores": []}

    def emit_prep(j):
        src, dst, wsc, i = prep_jobs[j]
        ld = p.op("sync", lambda s, o=wstg[:, :], a=src.ap()[i * 128:(i + 1) * 128, :]:
                  s.dma_start(out=o, in_=a), after=[prep["wstg_free"]], sig="sx", amt=16)
        rd = p.op("vector", lambda v, o=wsc[:, i:i + 1], a=wstg[:, :]:
                  v.reduce_max(o, a, axis=AX, apply_absolute_value=True),
                  after=[ld], sig="ve")
        sc = p.op("vector", lambda v, o=wsc[:, i:i + 1], a=wsc[:, i:i + 1]:
                  v.tensor_scalar(o, a, 1.0 / 127.0, None, op0=ALU.mult),
                  after=[rd], sig="ve")
        rc = p.op("vector", lambda v, o=tin2[:, 6:7], a=wsc[:, i:i + 1]:
                  v.reciprocal(o, a), after=[sc], sig="ve")
        t1 = p.op("scalar", lambda s, o=wstg[:, :], a=wstg[:, :], sa=tin2[:, 6:7]:
                  s.activation(o, a, AF.Copy, bias=MAGIC, scale=sa),
                  after=[rc], sig="ac")
        t3 = p.op("scalar", lambda s, o=wqnt[:, :], a=wstg[:, :]:
                  s.activation(o, a, AF.Copy, bias=-MAGIC, scale=1.0),
                  after=[t1, prep["wqnt_free"]], sig="ac")
        st = p.op("gpsimd", lambda g, o=dst.ap()[i * 128:(i + 1) * 128, :], a=wqnt[:, :]:
                  g.dma_start(out=o, in_=a), after=[t3], sig="sg", amt=16)
        prep["wstg_free"] = t3
        prep["wqnt_free"] = st
        prep["stores"].append(st)

    def load_act_set(eng, set_id):
        return eng.add_instruction(mybir.InstLoadActFuncSet(
            name=eng.bass.get_next_instruction_name(),
            act_func_set_id=set_id, ins=[], outs=[]))
    p.op("scalar", lambda s: load_act_set(s, 6), sig="ac")
    zr0 = p.op("vector", lambda v: v.memset(zrow[:, :], 0.0), sig="ve")
    for bank in (pb[5], pb[6]):
        p.op("tensor", lambda te, o=bank[:, :], l=zrow[0:1, 0:128], r=zrow[0:1, 0:512]:
             te.matmul(o, l, r, start=True, stop=False, skip_group_check=True),
             after=[zr0], sig="pe")
    al = {}
    phiq_ps_tok = {}
    prep_done = 0
    phiq_tile_tok = {}
    lnst = {}

    bias_build_toks = []
    for row, dst in ((brow["v"], bvb),):
        mmt = None
        for n in range(2):
            mmt = p.op("tensor", lambda te, o=kps[:, n * 512:(n + 1) * 512],
                       l=onesc[0:1, 0:128], r=row[0:1, n * 512:(n + 1) * 512]:
                       te.matmul(o, l, r, start=True, stop=True, skip_group_check=True),
                       after=[brow_tok, zr0, onesc_tok,
                              bias_build_toks[-1] if bias_build_toks else None],
                       sig="pe")
        bias_build_toks.append(p.op("vector", lambda v, o=dst[:, :], a=kps[:, 0:1024]:
                                    v.tensor_copy(o, a), after=[mmt], sig="ve"))
    bias_built = bias_build_toks[-1]

    def emit_ln(t):
        ldx = p.op("sync", lambda s, o=sb3(x_t), a=tok_ap(x_in, t):
                   s.dma_start(out=o, in_=a), after=[lnst.get("xn_last")],
                   sig="sx", amt=16)
        aggr_last = None
        for g in range(NG):
            col = 2 * (t * NG + g)
            p.op("scalar", lambda s, o=ts(xn_tm, g * D, D), a=ts(x_t, g * D, D),
                 xc=tin2[:, 0:1]:
                 s.activation(o, a, AF.Copy, accum_out=xc),
                 after=[ldx, aggr_last if g else lnst.get("aggr_prev"),
                        al.get("xn_tm_free")], sig="ac")
            s2t = p.op("scalar", lambda s, o=ts(xn_tm, g * D, D), a=ts(x_t, g * D, D),
                       xc=tin2[:, 1:2]:
                       s.activation(o, a, AF.Square, accum_out=xc), sig="ac")
            p.op("vector", lambda v, o=mv1[:, col:col + 1], a=tin2[:, 0:1]:
                 v.tensor_scalar(o, a, 1.0 / D, None, op0=ALU.mult),
                 after=[s2t], sig="ve")
            p.op("vector", lambda v, o=tin2[:, 2:3], a=tin2[:, 1:2]:
                 v.tensor_scalar(o, a, 1.0 / D, None, op0=ALU.mult), sig="ve")
            p.op("vector", lambda v, o=tin2[:, 3:4], a=mv1[:, col:col + 1]:
                 v.tensor_mul(o, a, a), sig="ve")
            aggr_last = p.op("vector", lambda v, o=mv1[:, col + 1:col + 2],
                             a=tin2[:, 2:3], b=tin2[:, 3:4]:
                             v.tensor_sub(o, a, b), sig="ve")
        c0 = t * NG
        lnst["aggr_last"] = aggr_last
        lnst["aggr_prev"] = aggr_last
        t_eps = p.op("vector", lambda v, o=ts(tiny, 0, 4),
                     a=mv1[:, 2 * c0 + 1:2 * (c0 + NG):2]:
                     v.tensor_scalar(o, a, LN_EPS, None, op0=ALU.add),
                     after=[aggr_last, al.get("lnfree")], sig="ve")
        t_ln = p.op("scalar", lambda s, o=ts(tiny, 4, 4), a=ts(tiny, 0, 4):
                    s.activation(o, a, AF.Ln), after=[t_eps], sig="ac")
        al["lnfree"] = t_ln
        t_rs = p.op("scalar", lambda s, o=rstd1[:, c0:c0 + NG], a=ts(tiny, 4, 4):
                    s.activation(o, a, AF.Exp, bias=0.0, scale=-0.5), sig="ac")
        t_mb = p.op("vector", lambda v, o=mb1[:, c0:c0 + NG],
                    a=mv1[:, 2 * c0:2 * (c0 + NG):2], b=rstd1[:, c0:c0 + NG]:
                    v.scalar_tensor_tensor(o, a, -1.0, b, op0=ALU.mult, op1=ALU.mult),
                    after=[t_rs], sig="ve")
        xn_last = None
        for g in range(NG):
            gc = c0 + g
            xn_last = p.op(
                "scalar",
                lambda s, o=ts(xn_tm, g * D, D), a=ts(x_t, g * D, D),
                       sa=rstd1[:, gc:gc + 1], ba=mb1[:, gc:gc + 1]:
                    s.activation(o, a, AF.Identity, bias=ba, scale=sa),
                after=[t_mb, al.get("xn_tm_free")], sig="ac")
        lnst["xn_last"] = xn_last
        lnst[("xn_done", t)] = xn_last

    emit_ln(0)
    for t in range(NT):
        xn_done_t = lnst[("xn_done", t)]

        evt = []
        for c in range(NCF):
            half = c % 2
            tl = None
            for g in range(NG):
                tl = p.op(
                    "tensor",
                    lambda te, o=pbf[:, half * 512 + g * 128:half * 512 + (g + 1) * 128],
                           a=ts(xn_tm, g * D + c * 128, 128):
                        te.transpose(o, a, ident[:, :]),
                    after=[xn_done_t, evt[c - 1] if c >= 1 else al.get("pbf_free")],
                    sig="pe")
            if c % 2 == 0:
                evt.append(p.op(
                    "vector",
                    lambda v, o=ts(xn_fm, c * TILE, TILE),
                           a=pbf[:, half * 512:half * 512 + 512]:
                        v.tensor_copy(o, a), after=[tl], sig="ve"))
            else:
                evt.append(p.op(
                    "scalar",
                    lambda s, o=ts(xn_fm, c * TILE, TILE),
                           a=pbf[:, half * 512:half * 512 + 512]:
                        s.activation(o, a, AF.Copy), after=[tl], sig="ac"))
        al["pbf_free"] = evt[-1]
        al["xn_tm_free"] = p.now("pe")
        xn_fm_done = evt[-1]

        def emit_k(g):
            for n in range(2):
                for c in range(NCF):
                    p.op("tensor",
                         lambda te, o=kps[:, n * 512:(n + 1) * 512],
                                l=ts(xn_fm, c * TILE + g * 128, 128),
                                r=ts(WkT, c * D + n * 512, 512),
                                st=(c == 0), sp=False:
                             te.matmul(o, l, r, start=st, stop=sp,
                                       skip_group_check=True),
                         after=[evt[c], al.get("kps_free")], sig="pe")
                p.op("tensor",
                     lambda te, o=kps[:, n * 512:(n + 1) * 512], l=onesc[0:1, 0:128],
                            r=brow["k"][0:1, n * 512:(n + 1) * 512]:
                         te.matmul(o, l, r, start=False, stop=True,
                                   skip_group_check=True),
                     after=[brow_tok, onesc_tok], sig="pe")
            return p.now("pe")

        def emit_v(g):
            for n in range(2):
                for c in range(NCF):
                    p.op("tensor",
                         lambda te, o=vps[:, n * 512:(n + 1) * 512],
                                l=ts(xn_fm, c * TILE + g * 128, 128),
                                r=ts(WvT, c * D + n * 512, 512),
                                st=(c == 0), sp=(c == NCF - 1):
                             te.matmul(o, l, r, start=st, stop=sp,
                                       skip_group_check=True),
                         after=[evt[c], al.get("vps_free"),
                                phiq_ps_tok.get(("q", 1))], sig="pe")
            return p.now("pe")

        def emit_q(cq):
            if cq % 2 == 0:
                bank, key, extra = pb4t[:, :], ("q", 0), None
            else:
                bank, key, extra = vps[:, 0:512], ("q", 1), al.get("vev_last")
            for cp in range(NCF):
                p.op("tensor",
                     lambda te, o=bank, l=ts(WqT, cp * D + cq * 128, 128),
                            r=ts(xn_fm, cp * TILE, TILE), st=(cp == 0),
                            sp=(cp == NCF - 1):
                         te.matmul(o, l, r, start=st, stop=sp,
                                   skip_group_check=True),
                     after=[xn_fm_done, phiq_ps_tok.get(key), extra, bias_built],
                     sig="pe")
            return bank, key, p.now("pe")

        def emit_phik(g, k_done):
            ksl = g % 2
            kk = k_tm[:, ksl * D:(ksl + 1) * D]
            rv = p.op("vector", lambda v, o=ts(scr, 0, D), a=kps[:, 0:1024]:
                      v.tensor_scalar(o, a, 0.0, None, op0=ALU.max),
                      after=[k_done], sig="ve")
            m_t = p.op("vector", lambda v, o=ts(scr, D, D), a=kps[:, 0:1024]:
                       v.tensor_scalar(o, a, 0.0, None, op0=ALU.min),
                       after=[k_done, al.get("phik_free")], sig="ve")
            al["kps_free"] = [rv, m_t]
            e_t = p.op("scalar", lambda s, o=ts(scr, D, D), a=ts(scr, D, D):
                       s.activation(o, a, AF.Exp), after=[m_t], sig="ac")
            kphi = p.op("vector", lambda v, o=kk, a=ts(scr, 0, D), b=ts(scr, D, D):
                        v.tensor_add(o, a, b),
                        after=[e_t, al.get(("kv_done", ksl))], sig="ve")
            al["phik_free"] = kphi
            return kk, kphi

        def emit_vev(g, v_done):
            ksl = g % 2
            vsl = v_ag[:, ksl * VAGW:ksl * VAGW + NH * 65]
            vev = p.op(
                "vector",
                lambda v, o=vsl.rearrange("p (h e) -> p h e", e=65)[:, :, 0:64],
                       a=vps[:, 0:1024].rearrange("p (h e) -> p h e", e=64),
                       b=bvb[:, :].rearrange("p (h e) -> p h e", e=64):
                    v.scalar_tensor_tensor(o, a, 1.0, b, op0=ALU.mult, op1=ALU.add),
                after=[v_done, bias_built, al.get(("kv_done", ksl))], sig="ve")
            al["vps_free"] = vev
            al["vev_last"] = vev
            return vsl, vev

        def emit_phiq(qbank, key, q_tok, cqq):
            sl = (cqq % 2) * 512
            rv = p.op("vector", lambda v, o=ts(scr, 2048, 512), a=qbank,
                      sa=bqfm[:, cqq:cqq + 1]:
                      v.tensor_scalar(o, a, sa, 0.0, op0=ALU.add, op1=ALU.max),
                      after=[q_tok, bq_tok, al.get("phiq_r_free")], sig="ve")
            m2_ = p.op("vector", lambda v, o=ts(gsc, sl, 512), a=qbank,
                       sa=bqfm[:, cqq:cqq + 1]:
                       v.tensor_scalar(o, a, sa, 0.0, op0=ALU.add, op1=ALU.min),
                       after=[q_tok, bq_tok, al.get(("phiq_e_free", cqq % 2))],
                       sig="ve")
            e2_ = p.op("scalar", lambda s, o=ts(gsc, sl, 512), a=ts(gsc, sl, 512):
                       s.activation(o, a, AF.Exp), after=[m2_], sig="ac")
            ph = p.op("vector", lambda v, o=ts(phiq, cqq * NTOK + t * TILE, TILE),
                      a=ts(scr, 2048, 512), b=ts(gsc, sl, 512):
                      v.tensor_add(o, a, b), after=[e2_], sig="ve")
            al["phiq_r_free"] = ph
            al[("phiq_e_free", cqq % 2)] = ph
            phiq_ps_tok[key] = ph

        def emit_kv(g, kk, vsl, kphi, vev):
            for h in range(NH):
                bank = pb5t if h < 8 else pb6t
                q_ = (h % 8) // 2
                par = (h % 2) * 64
                p.op("tensor",
                     lambda te, o=bank[par:par + 64, q_ * 65:q_ * 65 + 65],
                            l=kk[:, h * HD:(h + 1) * HD],
                            r=vsl[:, h * 65:(h + 1) * 65],
                            sp=(t == NT - 1 and g == NG - 1),
                            tp=(0, par):
                         te.matmul(o, l, r, start=False, stop=sp, tile_position=tp,
                                   skip_group_check=True),
                     after=[kphi, vev], sig="pe")
            al[("kv_done", g % 2)] = p.now("pe")

        def maybe_prep(k):
            nonlocal prep_done
            for _ in range(k):
                if prep_done < len(prep_jobs):
                    emit_prep(prep_done)
                    prep_done += 1

        pend = {}
        qi = 0
        for g in range(NG):
            kd = emit_k(g)
            vd = emit_v(g)
            kk, kphi = emit_phik(g, kd)
            vsl, vev = emit_vev(g, vd)
            qb, qk, qd = emit_q(qi)
            emit_phiq(qb, qk, qd, qi)
            qi += 1
            pend[g] = (kk, vsl, kphi, vev)
            if g >= 1:
                kk2, vsl2, kphi2, vev2 = pend.pop(g - 1)
                emit_kv(g - 1, kk2, vsl2, kphi2, vev2)
            if g == 1 and t + 1 < NT:
                emit_ln(t + 1)
            maybe_prep(1 if g < 3 else 0)
        for _ in range(NG):
            qb, qk, qd = emit_q(qi)
            emit_phiq(qb, qk, qd, qi)
            qi += 1
            maybe_prep(1)
            if qi == 6:
                kk2, vsl2, kphi2, vev2 = pend.pop(NG - 1)
                emit_kv(NG - 1, kk2, vsl2, kphi2, vev2)
        if t == NT - 1:
            maybe_prep(len(prep_jobs) - prep_done)
        phiq_tile_tok[t] = p.now("ve")

    kv_done = p.now("pe")
    ev1 = p.op("vector", lambda v: v.tensor_copy(kv_sb[:, 0:260], pb[5][:, 0:260]),
               after=[kv_done], sig="ve")
    ev2 = p.op("vector", lambda v: v.tensor_copy(kv_sb[:, 260:520], pb[6][:, 0:260]),
               sig="ve")
    if dbg:
        p.op("gpsimd", lambda g: g.dma_start(out=dbg_kvloc.ap(), in_=kv_sb[:, :]),
             after=[ev1, ev2], sig="sg", amt=16)
        p.op("gpsimd", lambda g: g.dma_start(out=dbg_phiq.ap(), in_=phiq[:, :]),
             after=[phiq_tile_tok[NT - 1]], sig="sg", amt=16)
        p.op("gpsimd", lambda g: g.dma_start(out=dbg_ktm.ap(), in_=k_tm[:, :]),
             after=[kv_done], sig="sg", amt=16)
        p.op("gpsimd", lambda g: g.dma_start(out=dbg_vag.ap(), in_=v_ag[:, :]),
             after=[kv_done], sig="sg", amt=16)
        p.op("gpsimd", lambda g: g.dma_start(out=dbg_xnfm.ap(), in_=xn_fm[:, :]),
             after=[kv_done], sig="sg", amt=16)
        p.op("gpsimd", lambda g: g.dma_start(out=dbg_xntm.ap(), in_=xn_tm[:, :]),
             after=[kv_done], sig="sg", amt=16)
        p.op("gpsimd", lambda g: g.dma_start(out=dbg_rstd.ap(), in_=rstd1[:, :]),
             after=[kv_done], sig="sg", amt=16)
        p.op("gpsimd", lambda g: g.dma_start(out=dbg_mv1.ap(), in_=mv1[:, :]),
             after=[kv_done], sig="sg", amt=16)
        for bi, bn in enumerate(("q", "k", "v", "o")):
            p.op("gpsimd", lambda g, o=dbg_brow.ap()[0:1, bi * D:(bi + 1) * D],
                 a=brow[bn][:, :]: g.dma_start(out=o, in_=a),
                 after=[kv_done], sig="sg", amt=16)
        p.op("gpsimd", lambda g: g.dma_start(out=dbg_wkt.ap(), in_=WkT[:, :]),
             after=[kv_done], sig="sg", amt=16)
    dbg_dump_tok = p.now("sg")
    mtoks = []
    for s_ in range(4):
        mtoks.append(p.op(
            "vector", lambda v, o=ts(scr, s_ * 520, 520), m=smask[:, s_:s_ + 1]:
                v.tensor_scalar(o, kv_sb[:, :], m, None, op0=ALU.mult),
            after=[ev1, ev2, smask_tok], sig="ve"))
    d1 = p.op("gpsimd", lambda g: g.dma_start(out=cc_kv_in.ap(), in_=ts(scr, 0, 2080)),
              after=[mtoks[-1]], sig="sg", amt=16)
    cc1 = p.op("gpsimd", lambda g: g.collective_compute(
        "AllReduce", ALU.add, replica_groups=[list(range(8))],
        ins=[cc_kv_in.ap()], outs=[cc_kv_out.ap()]), after=[d1], sig="sc")
    d2a = p.op("gpsimd", lambda g: g.dma_start(out=ts(scr, 0, 2080), in_=cc_kv_out.ap()),
               after=[cc1], sig="sg", amt=16)
    d2 = p.op("vector", lambda v: v.tensor_scalar(kv_sb[:, :], ts(scr, 0, 520),
                                                  smask[:, 0:1], None, op0=ALU.mult),
              after=[d2a], sig="ve")
    for s_ in range(1, 4):
        d2 = p.op("vector", lambda v, a=ts(scr, s_ * 520, 520), m=smask[:, s_:s_ + 1]:
                  v.scalar_tensor_tensor(kv_sb[:, :], a, m, kv_sb[:, :],
                                         op0=ALU.mult, op1=ALU.add),
                  after=[d2], sig="ve")

    esA.close()
    esB = ExitStack()
    epair = esB.enter_context(nc.sbuf_tensor("epairt", [16, NCF * 128], BF16))
    kv_bd = esB.enter_context(nc.sbuf_tensor("kv_bd", [128, 8 * 128], BF16))
    ksum2 = esB.enter_context(nc.sbuf_tensor("ksum2", [128, NCF * 16], BF16))
    nrm_sb = esB.enter_context(nc.sbuf_tensor("nrm_sb", [16, TILE], F32))
    rec_sb = esB.enter_context(nc.sbuf_tensor("rec_sb", [16, TILE], F32))
    rec_bf = esB.enter_context(nc.sbuf_tensor("rec_bf", [16, TILE], BF16))
    rbc_sb = esB.enter_context(nc.sbuf_tensor("rbc_sb", [128, TILE], F32))
    qkv_fm_sb = esB.enter_context(nc.sbuf_tensor("qkv_fm", [128, NCF * TILE], BF16))
    x1_sb = esB.enter_context(nc.sbuf_tensor("x1_sb", [128, NG * D], F32))
    epair_tok = p.op("sync", lambda s: s.dma_start(out=epair[:, :], in_=epair_in.ap()),
                     after=[kv_done, prep["stores"][-1], d2, dbg_dump_tok], sig="sx", amt=16)

    if dbg:
        p.op("gpsimd", lambda g: g.dma_start(out=dbg_kv.ap(), in_=cc_kv_out.ap()),
             after=[cc1], sig="sg", amt=16)
    bd0 = p.op("vector", lambda v: v.memset(kv_bd[:, :], 0.0),
               after=[d2, kv_done, prep["stores"][-1], dbg_dump_tok], sig="ve")
    p.op("vector", lambda v: v.memset(ksum2[:, :], 0.0), sig="ve")
    kvbd_done = None
    for par in (0, 64):
        for b in range(2):
            p.op("vector",
                 lambda v, o=kv_bd[par:par + 64, 512 * b:512 * b + 512]
                          .rearrange("p (q e) -> p q e", e=128)[:, :, par:par + 64],
                        a=kv_sb[par:par + 64, 260 * b:260 * b + 260]
                          .rearrange("p (q e) -> p q e", e=65)[:, :, 0:64]:
                     v.tensor_copy(o, a), after=[bd0], sig="ve")
    for par, off in ((0, 0), (64, 1)):
        for b in range(2):
            kvbd_done = p.op(
                "vector",
                lambda v, o=bass.AP(ksum2, 72 * b + off, [[NCF * 16, 64], [18, 4]],
                                    base_partition=par)
                       if False else
                       ksum2[par:par + 64, 72 * b + off:72 * b + off + 55:18],
                       a=kv_sb[par:par + 64, 260 * b + 64:260 * b + 260:65]:
                    v.tensor_copy(o, a), sig="ve")

    bl = {"xfree": al.get("xfree"), "tin2_free": lnst.get("aggr_last")}
    for t in range(NT):
        xin = tok_ap(x_in, t)
        ldx = p.op("sync", lambda s, o=sb3(x_t), a=xin: s.dma_start(out=o, in_=a),
                   after=[bl.get("xfree")], sig="sx", amt=16)

        for c in range(NCF):
            p.op("tensor", lambda te, o=pb[4][0:16, :], l=ts(ksum2, c * 16, 16),
                 r=ts(phiq, c * NTOK + t * TILE, TILE), st=(c == 0), sp=(c == NCF - 1):
                 te.matmul(o, l, r, start=st, stop=sp, skip_group_check=True),
                 after=[kvbd_done, bl.get("nrm_ps"), phiq_tile_tok[t],
                        phiq_ps_tok.get(id(pb[4])) if t == 0 else None], sig="pe")
        nm = p.now("pe")
        n1 = p.op("vector", lambda v: v.tensor_scalar(nrm_sb[:, :], pb[4][0:16, :],
                                                      ATTN_EPS, None, op0=ALU.max),
                  after=[nm], sig="ve")
        bl["nrm_ps"] = n1
        n2 = p.op("vector", lambda v: v.reciprocal(rec_sb[:, :], nrm_sb[:, :]),
                  after=[n1, bl.get("rec_free")], sig="ve")
        n2 = p.op("vector", lambda v: v.tensor_copy(rec_bf[:, :], rec_sb[:, :]),
                  after=[n2], sig="ve")

        mm2 = None
        for pr in range(NCF):
            bank = pb[pr % 3]
            mm1 = p.op("tensor", lambda te, o=bank[:, :], l=ts(kv_bd, pr * 128, 128),
                       r=ts(phiq, pr * NTOK + t * TILE, TILE):
                       te.matmul(o, l, r, start=True, stop=True, skip_group_check=True),
                       after=[kvbd_done, bl.get(("qps", pr % 3)),
                              phiq_ps_tok.get(id(pb[0])) if t == 0 and pr % 3 == 0 else None],
                       sig="pe")
            mm2 = p.op("tensor", lambda te, o=pb[3][:, :], l=ts(epair, pr * 128, 128),
                       r=rec_bf[:, :]:
                       te.matmul(o, l, r, start=True, stop=True, skip_group_check=True),
                       after=[n2, bl.get("bc_ps"),
                              epair_tok if t == 0 and pr == 0 else None], sig="pe")
            bc = p.op("scalar", lambda s, o=rbc_sb[:, :], a=pb[3][:, :]:
                      s.activation(o, a, AF.Copy),
                      after=[mm2, bl.get("rbc_free")], sig="ac")
            bl["bc_ps"] = bc
            ev = p.op("vector", lambda v, o=ts(qkv_fm_sb, pr * TILE, TILE), a=bank[:, :],
                      b=rbc_sb[:, :]:
                      v.scalar_tensor_tensor(o, a, 1.0, b, op0=ALU.mult, op1=ALU.mult),
                      after=[mm1, bc], sig="ve")
            bl[("qps", pr % 3)] = ev
            bl["rbc_free"] = ev
        bl["rec_free"] = mm2
        qkv_done = p.now("ve")

        evbank = {}
        for g in range(NG):
            for n in range(2):
                bank = pb[(2 * g + n) % 4]
                for c in range(NCF):
                    p.op("tensor", lambda te, o=bank[:, :],
                         l=ts(qkv_fm_sb, c * TILE + g * 128, 128),
                         r=ts(WoT, c * D + n * 512, 512), st=(c == 0):
                         te.matmul(o, l, r, start=st, stop=False, skip_group_check=True),
                         after=[qkv_done, wT_tok["wo"], bl.get(("ops", (2 * g + n) % 4))],
                         sig="pe")
                om = p.op("tensor", lambda te, o=bank[:, :], l=onesc[0:1, 0:128],
                          r=brow["o"][0:1, n * 512:(n + 1) * 512]:
                          te.matmul(o, l, r, start=False, stop=True, skip_group_check=True),
                          after=[brow_tok], sig="pe")
                ev = p.op("vector", lambda v, o=ts(x1_sb, g * D + n * 512, 512),
                          a=bank[:, :], b=ts(x_t, g * D + n * 512, 512):
                          v.scalar_tensor_tensor(o, a, 1.0, b, op0=ALU.mult, op1=ALU.add),
                          after=[om, ldx, bl.get("x1free")], sig="ve")
                bl[("ops", (2 * g + n) % 4)] = ev
                evbank[(g, n)] = ev
        bl["xfree"] = p.now("ve")
        for b in range(3):
            bl[("qps", b)] = p.now("ve")

        for g in range(NG):
            col = 2 * (t * NG + g)
            p.op("scalar", lambda s, o=gsc[:, 0:D], a=ts(x1_sb, g * D, D),
                 xc=tin2[:, 0:1]:
                 s.activation(o, a, AF.Copy, accum_out=xc),
                 after=[evbank[(g, 0)], evbank[(g, 1)], bl.get("tin2_free")], sig="ac")
            s2t = p.op("scalar", lambda s, o=gsc[:, 0:D], a=ts(x1_sb, g * D, D),
                       xc=tin2[:, 1:2]:
                       s.activation(o, a, AF.Square, accum_out=xc), sig="ac")
            p.op("vector", lambda v, o=mv2[:, col:col + 1], a=tin2[:, 0:1]:
                 v.tensor_scalar(o, a, 1.0 / D, None, op0=ALU.mult),
                 after=[s2t], sig="ve")
            p.op("vector", lambda v, o=tin2[:, 2:3], a=tin2[:, 1:2]:
                 v.tensor_scalar(o, a, 1.0 / D, None, op0=ALU.mult), sig="ve")
            p.op("vector", lambda v, o=tin2[:, 3:4], a=mv2[:, col:col + 1]:
                 v.tensor_mul(o, a, a), sig="ve")
            aggr_last = p.op("vector", lambda v, o=mv2[:, col + 1:col + 2],
                             a=tin2[:, 2:3], b=tin2[:, 3:4]:
                 v.tensor_sub(o, a, b), sig="ve")
            bl["tin2_free"] = aggr_last
            ag = aggr_last
            sb_ = p.op("vector", lambda v, o=ts(scr, 0, D), a=ts(x1_sb, g * D, D),
                       sa=mv2[:, col:col + 1]:
                       v.tensor_scalar(o, a, sa, None, op0=ALU.subtract),
                       after=[ag], sig="ve")
            p.op("vector", lambda v, o=xm2[:, t * NG + g:t * NG + g + 1],
                 a=ts(scr, 0, D):
                 v.reduce_max(o, a, axis=AX, apply_absolute_value=True),
                 after=[sb_], sig="ve")
        xm_done = p.now("ve")
        st = p.op("gpsimd", lambda g, o=tok_ap(x1_hbm, t), a=sb3(x1_sb):
                  g.dma_start(out=o, in_=a), after=[xm_done], sig="sg", amt=16)
        bl["x1free"] = st
        bl[("x1st", t)] = st

    t_eps = p.op("vector", lambda v, o=ts(scr, 0, 32), a=mv2[:, 1:64:2]:
                 v.tensor_scalar(o, a, LN_EPS, None, op0=ALU.add), sig="ve")
    t_ln = p.op("scalar", lambda s, o=ts(scr, 32, 32), a=ts(scr, 0, 32):
                s.activation(o, a, AF.Ln), after=[t_eps], sig="ac")
    t_rs2 = p.op("scalar", lambda s, o=ts(scr, 64, 32), a=ts(scr, 32, 32):
                 s.activation(o, a, AF.Exp, bias=0.0, scale=-0.5), sig="ac")
    t_xm = p.op("vector", lambda v, o=ts(scr, 96, 32), a=xm2[:, 0:32], b=ts(scr, 64, 32):
                v.tensor_mul(o, a, b), after=[t_rs2], sig="ve")
    t_mx = p.op("vector", lambda v, o=ts(tiny, 0, 1), a=ts(scr, 96, 32):
                v.reduce_max(o, a, axis=AX), after=[t_xm], sig="ve")

    d3 = p.op("gpsimd", lambda g: g.dma_start(out=cc_m1_in.ap()[0:1, :], in_=ts(tiny, 0, 1)),
              after=[t_mx], sig="sg", amt=16)
    cc2 = p.op("gpsimd", lambda g: g.collective_compute(
        "AllReduce", ALU.max, replica_groups=[list(range(8))],
        ins=[cc_m1_in.ap()], outs=[cc_m1_out.ap()]), after=[d3], sig="sc")
    bc1 = p.op("gpsimd", lambda g: g.dma_start(
        out=gmaxb[:, :], in_=bass.AP(cc_m1_out, 0, [[0, 128], [1, 128]])),
        after=[cc2], sig="sg", amt=16)
    gm1 = p.op("vector", lambda v: v.reduce_max(ts(tiny, 0, 1), gmaxb[:, :], axis=AX),
               after=[bc1], sig="ve")
    if dbg:
        p.op("gpsimd", lambda g: g.dma_start(out=dbg_m1.ap(), in_=cc_m1_out.ap()),
             after=[cc2], sig="sg", amt=16)
    bcx = p.op("gpsimd", lambda g: g.dma_start(
        out=ts(tiny, 1, 1), in_=bass.AP(clipx, 0, [[0, 128], [1, 1]])),
        sig="sg", amt=16)
    e1 = p.op("scalar", lambda s: s.activation(ts(tiny, 2, 1), ts(tiny, 1, 1),
                                               AF.Exp, bias=0.0, scale=-1.0),
              after=[bcx], sig="ac")
    e2 = p.op("vector", lambda v: v.tensor_scalar(ts(tiny, 2, 1), ts(tiny, 2, 1),
                                                  1.0, 127.0, op0=ALU.add, op1=ALU.mult),
              after=[e1], sig="ve")
    e3 = p.op("vector", lambda v: v.reciprocal(ts(tiny, 3, 1), ts(tiny, 0, 1)),
              after=[gm1], sig="ve")
    e4 = p.op("vector", lambda v: v.tensor_mul(scl[:, 0:1], ts(tiny, 2, 1), ts(tiny, 3, 1)),
              after=[e2, e3], sig="ve")
    e5 = p.op("vector", lambda v: v.reciprocal(ts(tiny, 3, 1), ts(tiny, 2, 1)), sig="ve")
    e6 = p.op("vector", lambda v: v.tensor_mul(scl[:, 1:2], ts(tiny, 0, 1), ts(tiny, 3, 1)),
              sig="ve")
    e7 = p.op("vector", lambda v: v.tensor_scalar(a2c[:, 0:32], ts(scr, 64, 32),
                                                  scl[:, 0:1], None, op0=ALU.mult),
              after=[e4], sig="ve")
    e8 = p.op("vector", lambda v: v.scalar_tensor_tensor(b2c[:, 0:32], mv2[:, 0:64:2],
                                                         -1.0, a2c[:, 0:32],
                                                         op0=ALU.mult, op1=ALU.mult),
              after=[e7], sig="ve")
    e9 = p.op("vector", lambda v: v.tensor_scalar(sg_all[:, :], wgs[:, :], scl[:, 1:2],
                                                  None, op0=ALU.mult),
              after=[e6, prep["stores"][NCI - 1]], sig="ve")
    scales_done = p.op("vector", lambda v: v.tensor_scalar(su_all[:, :], wus[:, :],
                                                           scl[:, 1:2], None, op0=ALU.mult),
                       after=[prep["stores"][-1]], sig="ve")

    b_pe_done = p.now("pe")
    esB.close()
    esPhi.close()
    esWo.close()
    esC = ExitStack()
    wgT = esC.enter_context(nc.sbuf_tensor("wgT", [128, NCF * FF], BF16))
    wuT = esC.enter_context(nc.sbuf_tensor("wuT", [128, NCF * FF], BF16))
    xq_tm = esC.enter_context(nc.sbuf_tensor("xq_tm", [128, NG * D], BF16))
    xq_fm = esC.enter_context(nc.sbuf_tensor("xq_fm", [128, NCF * TILE], BF16))
    g_sb = esC.enter_context(nc.sbuf_tensor("g_sb", [128, 512], F32))
    u_sb = esC.enter_context(nc.sbuf_tensor("u_sb", [128, 512], F32))
    h_sb = [esC.enter_context(nc.sbuf_tensor(f"h_sb{i}", [128, 512], F32)) for i in range(2)]
    wdstg = esC.enter_context(nc.sbuf_tensor("wdstg", [128, FF], F32))
    wdq_sb = esC.enter_context(nc.sbuf_tensor("wdq_sb", [128, FF], BF16))
    dum = esC.enter_context(nc.sbuf_tensor("dum", [128, 512], F32))

    wgT_tok = []
    wuT_tok = []
    for c in range(NCF):
        wgT_tok.append(p.op(
            "sync", lambda s, o=ts(wgT, c * FF, FF), i=wg_q.ap()[:, c * 128:(c + 1) * 128]:
                s.dma_start_transpose(o, i),
            after=[prep["stores"][NCI - 1], b_pe_done, bl[("x1st", NT - 1)],
                   scales_done], sig="sx", amt=16))
        wuT_tok.append(p.op(
            "sync", lambda s, o=ts(wuT, c * FF, FF), i=wu_q.ap()[:, c * 128:(c + 1) * 128]:
                s.dma_start_transpose(o, i),
            after=[prep["stores"][-1], b_pe_done, scales_done], sig="sx", amt=16))

    wdp = {"wstg_free": None, "wq_free": None, "stores": [],
           "extra": [b_pe_done, scales_done, bl[("x1st", NT - 1)]]}

    def emit_wd_prep(i):
        ld = p.op("sync", lambda s, o=wdstg[:, :], a=wdd.ap()[i * 128:(i + 1) * 128, :]:
                  s.dma_start(out=o, in_=a),
                  after=[wdp["wstg_free"]] + wdp.pop("extra", []), sig="sx", amt=16)
        rd = p.op("vector", lambda v, o=wds[:, i:i + 1], a=wdstg[:, :]:
                  v.reduce_max(o, a, axis=AX, apply_absolute_value=True),
                  after=[ld], sig="ve")
        sc = p.op("vector", lambda v, o=wds[:, i:i + 1], a=wds[:, i:i + 1]:
                  v.tensor_scalar(o, a, 1.0 / 127.0, None, op0=ALU.mult),
                  after=[rd], sig="ve")
        rc = p.op("vector", lambda v, o=tin2[:, 7:8], a=wds[:, i:i + 1]:
                  v.reciprocal(o, a), after=[sc], sig="ve")
        t1 = p.op("scalar", lambda s, o=wdstg[:, :], a=wdstg[:, :], sa=tin2[:, 7:8]:
                  s.activation(o, a, AF.Copy, bias=MAGIC, scale=sa), after=[rc], sig="ac")
        t3 = p.op("scalar", lambda s, o=wdstg[:, :], a=wdstg[:, :]:
                  s.activation(o, a, AF.Copy, bias=-MAGIC, scale=1.0),
                  after=[t1], sig="ac")
        t5 = p.op("scalar", lambda s, o=wdq_sb[:, :], a=wdstg[:, :], sa=wds[:, i:i + 1]:
                  s.activation(o, a, AF.Copy, bias=0.0, scale=sa),
                  after=[t3, wdp["wq_free"]], sig="ac")
        st = p.op("gpsimd", lambda g, o=wd_q.ap()[i * 128:(i + 1) * 128, :], a=wdq_sb[:, :]:
                  g.dma_start(out=o, in_=a), after=[t5], sig="sg", amt=16)
        wdp["wstg_free"] = t5
        wdp["wq_free"] = st
        wdp["stores"].append(st)

    p.op("scalar", lambda s: load_act_set(s, 18), sig="ac")
    cl = {"xfree": bl["xfree"]}
    hst_tok = {}
    for t in range(NT):
        xin = tok_ap(x1_hbm, t)
        ldx = p.op("sync", lambda s, o=sb3(x_t), a=xin: s.dma_start(out=o, in_=a),
                   after=[cl.get("xfree"), bl[("x1st", t)]], sig="sx", amt=16)

        xq_toks = []
        for g in range(NG):
            gc = t * NG + g
            q1 = p.op("scalar", lambda s, o=ts(scr, 0, D), a=ts(x_t, g * D, D),
                      sa=a2c[:, gc:gc + 1], ba=b2c[:, gc:gc + 1]:
                      s.activation(o, a, AF.Identity, bias=ba, scale=sa),
                      after=[ldx, e8, xq_toks[-1] if g else cl.get("scr_free"),
                             cl.get("xq_tm_free"),
                             b_pe_done if t == 0 and g == 0 else None], sig="ac")
            q2 = p.op("vector", lambda v, o=ts(scr, 0, D), a=ts(scr, 0, D):
                      v.tensor_scalar(o, a, -128.49999, 127.49998,
                                      op0=ALU.max, op1=ALU.min),
                      after=[q1], sig="ve")
            xq_toks.append(p.op("vector", lambda v, o=ts(xq_tm, g * D, D), a=ts(scr, 0, D):
                                v.tensor_scalar(o, a, MAGIC, -MAGIC,
                                                op0=ALU.add, op1=ALU.add),
                                after=[q2], sig="ve"))
        cl["xfree"] = xq_toks[-1]
        cl["scr_free"] = xq_toks[-1]

        evt = []
        for c in range(NCF):
            half = c % 2
            tl = None
            for g in range(NG):
                tl = p.op(
                    "tensor",
                    lambda te, o=pbf[:, half * 512 + g * 128:half * 512 + (g + 1) * 128],
                           a=ts(xq_tm, g * D + c * 128, 128):
                        te.transpose(o, a, ident[:, :]),
                    after=[xq_toks[-1], evt[c - 1] if c >= 1 else cl.get("pbf_free")],
                    sig="pe")
            evt.append(p.op(
                "vector",
                lambda v, o=ts(xq_fm, c * TILE, TILE),
                       a=pbf[:, half * 512:half * 512 + 512]:
                    v.tensor_copy(o, a), after=[tl], sig="ve"))
        cl["pbf_free"] = evt[-1]
        cl["xq_tm_free"] = p.now("pe")

        for i in range(NCI):
            bg = pb[2 * (i % 3)]
            bu = pb[2 * (i % 3) + 1]
            for c in range(NCF):
                p.op("tensor", lambda te, o=bg[:, :], l=ts(wgT, c * FF + i * 128, 128),
                     r=ts(xq_fm, c * TILE, TILE), st=(c == 0), sp=(c == NCF - 1):
                     te.matmul(o, l, r, start=st, stop=sp, skip_group_check=True),
                     after=[evt[c], wgT_tok[c], cl.get(("gps", i % 3))], sig="pe")
            gmm = p.now("pe")
            for c in range(NCF):
                p.op("tensor", lambda te, o=bu[:, :], l=ts(wuT, c * FF + i * 128, 128),
                     r=ts(xq_fm, c * TILE, TILE), st=(c == 0), sp=(c == NCF - 1):
                     te.matmul(o, l, r, start=st, stop=sp, skip_group_check=True),
                     after=[wuT_tok[c]], sig="pe")
            umm = p.now("pe")
            hb = h_sb[i % 2]
            gv = p.op("scalar", lambda s, o=g_sb[:, :], a=bg[:, :], sa=sg_all[:, i:i + 1]:
                      s.activation(o, a, AF.Silu, bias=0.0, scale=sa),
                      after=[gmm, e9, scales_done, cl.get("g_free")], sig="ac")
            uv = p.op("scalar", lambda s, o=u_sb[:, :], a=bu[:, :], sa=su_all[:, i:i + 1]:
                      s.activation(o, a, AF.Copy, bias=0.0, scale=sa),
                      after=[umm], sig="ac")
            hm = p.op("gpsimd", lambda g, o=hb[:, :], a=g_sb[:, :], b=u_sb[:, :]:
                      g.tensor_mul(o, a, b), after=[gv, uv, cl.get(("hfree", i % 2))],
                      sig="gp")
            cl[("gps", i % 3)] = hm
            cl["g_free"] = hm
            ht = p.op("vector", lambda v, o=tiny[:, 6:7], a=hb[:, :]:
                      v.reduce_max(o, a, axis=AX, apply_absolute_value=True),
                      after=[hm], sig="ve")
            p.op("vector", lambda v, o=habs[:, i:i + 1], a=habs[:, i:i + 1], b=tiny[:, 6:7]:
                 v.tensor_max(o, a, b), after=[ht], sig="ve")
            st = p.op("gpsimd", lambda g, o=h_hbm.ap()[i, :, t * TILE:(t + 1) * TILE],
                      a=hb[:, :]:
                      g.dma_start(out=o, in_=a), after=[hm], sig="sg", amt=16)
            cl[("hfree", i % 2)] = [st, ht]
            hst_tok[(i, t)] = st
        if t < NCF:
            emit_wd_prep(t)

    hmx = p.op("vector", lambda v: v.reduce_max(ts(tiny, 0, 1), habs[:, :], axis=AX),
               sig="ve")
    d4 = p.op("gpsimd", lambda g: g.dma_start(out=cc_m2_in.ap()[0:1, :], in_=ts(tiny, 0, 1)),
              after=[hmx], sig="sg", amt=16)
    cc3 = p.op("gpsimd", lambda g: g.collective_compute(
        "AllReduce", ALU.max, replica_groups=[list(range(8))],
        ins=[cc_m2_in.ap()], outs=[cc_m2_out.ap()]), after=[d4], sig="sc")
    bc2 = p.op("gpsimd", lambda g: g.dma_start(
        out=gmaxb[:, :], in_=bass.AP(cc_m2_out, 0, [[0, 128], [1, 128]])),
        after=[cc3], sig="sg", amt=16)
    gm2 = p.op("vector", lambda v: v.reduce_max(ts(tiny, 0, 1), gmaxb[:, :], axis=AX),
               after=[bc2], sig="ve")
    if dbg:
        p.op("gpsimd", lambda g: g.dma_start(out=dbg_m2.ap(), in_=cc_m2_out.ap()),
             after=[cc3], sig="sg", amt=16)
    bcd = p.op("gpsimd", lambda g: g.dma_start(
        out=ts(tiny, 1, 1), in_=bass.AP(clipd, 0, [[0, 128], [1, 1]])),
        sig="sg", amt=16)
    p.op("scalar", lambda s: load_act_set(s, 6), sig="ac")
    f1 = p.op("scalar", lambda s: s.activation(ts(tiny, 2, 1), ts(tiny, 1, 1),
                                               AF.Exp, bias=0.0, scale=-1.0),
              after=[bcd], sig="ac")
    f2 = p.op("vector", lambda v: v.tensor_scalar(ts(tiny, 2, 1), ts(tiny, 2, 1),
                                                  1.0, 127.0, op0=ALU.add, op1=ALU.mult),
              after=[f1], sig="ve")
    f3 = p.op("vector", lambda v: v.reciprocal(ts(tiny, 3, 1), ts(tiny, 0, 1)),
              after=[gm2], sig="ve")
    p.op("vector", lambda v: v.tensor_mul(scl[:, 2:3], ts(tiny, 2, 1), ts(tiny, 3, 1)),
         after=[f2, f3], sig="ve")
    f5 = p.op("vector", lambda v: v.reciprocal(ts(tiny, 3, 1), ts(tiny, 2, 1)), sig="ve")
    s2_done = p.op("vector", lambda v: v.tensor_mul(scl[:, 3:4], ts(tiny, 0, 1),
                                                    ts(tiny, 3, 1)),
                   after=[f5], sig="ve")

    c_pe_done = p.now("pe")
    c_ac_done = p.now("ac")
    esC.close()
    esD = ExitStack()
    wdT = esD.enter_context(nc.sbuf_tensor("wdT", [128, NCI * D], BF16))
    hq_sb = esD.enter_context(nc.sbuf_tensor("hq_sb", [128, NCI * 512], BF16))
    hstg = [esD.enter_context(nc.sbuf_tensor(f"hstg{i}", [128, 512], F32)) for i in range(2)]
    out_sb = [esD.enter_context(nc.sbuf_tensor(f"out_sb{i}", [128, NG * D], F32))
              for i in range(2)]

    wdT_tok = []
    for c in range(NCI):
        wdT_tok.append(p.op(
            "sync", lambda s, o=ts(wdT, c * D, D), i=wd_q.ap()[:, c * 128:(c + 1) * 128]:
                s.dma_start_transpose(o, i),
            after=[wdp["stores"][-1], c_pe_done, c_ac_done, s2_done,
                   hst_tok[(NCI - 1, NT - 1)]], sig="sx", amt=16))

    dl = {"xfree": cl["xfree"], "psum_free": c_ac_done}
    for t in range(NT):
        xin = tok_ap(x1_hbm, t)
        ldx1 = p.op("sync", lambda s, o=sb3(x_t), a=xin: s.dma_start(out=o, in_=a),
                    after=[dl.get("xfree")], sig="sx", amt=16)
        hq_toks = []
        for i in range(NCI):
            hb = hstg[i % 2]
            ld = p.op("sync", lambda s, o=hb[:, :],
                      a=h_hbm.ap()[i, :, t * TILE:(t + 1) * TILE]:
                      s.dma_start(out=o, in_=a),
                      after=[dl.get(("hfree", i % 2)), hst_tok[(i, t)],
                             c_pe_done if t == 0 and i == 0 else None],
                      sig="sx", amt=16)
            q1 = p.op("scalar", lambda s, o=ts(scr, 0, 512), a=hb[:, :], sa=scl[:, 2:3]:
                      s.activation(o, a, AF.Copy, bias=0.0, scale=sa),
                      after=[ld, s2_done,
                             hq_toks[-1] if i else dl.get("scr_free"),
                             dl.get("hq_free")], sig="ac")
            dl[("hfree", i % 2)] = q1
            q2 = p.op("vector", lambda v, o=ts(scr, 0, 512), a=ts(scr, 0, 512):
                      v.tensor_scalar(o, a, -128.49999, 127.49998,
                                      op0=ALU.max, op1=ALU.min),
                      after=[q1], sig="ve")
            hq_toks.append(p.op("vector", lambda v, o=ts(hq_sb, i * 512, 512),
                                a=ts(scr, 0, 512):
                                v.tensor_scalar(o, a, MAGIC, -MAGIC,
                                                op0=ALU.add, op1=ALU.add),
                                after=[q2], sig="ve"))
        dl["scr_free"] = hq_toks[-1]

        osb = out_sb[t % 2]
        wave_banks = ((pb[0], pb[1], pb[2], pb[3]), (pb[4], pb[5], pb[6], pb[3]))
        for w, groups in enumerate(((0, 1), (2, 3))):
            for i in range(NCI):
                for gi, g in enumerate(groups):
                    for n in range(2):
                        bank = wave_banks[w][2 * gi + n]
                        p.op("tensor", lambda te, o=bank[:, :],
                             l=ts(hq_sb, i * 512 + g * 128, 128),
                             r=ts(wdT, i * D + n * 512, 512),
                             st=(i == 0), sp=(i == NCI - 1):
                             te.matmul(o, l, r, start=st, stop=sp, skip_group_check=True),
                             after=[hq_toks[i], wdT_tok[i],
                                    dl.get(("dps", id(bank))), dl.get("psum_free")],
                             sig="pe")
            wave_done = p.now("pe")
            for gi, g in enumerate(groups):
                for n in range(2):
                    bank = wave_banks[w][2 * gi + n]
                    ev = p.op("vector", lambda v, o=ts(osb, g * D + n * 512, 512),
                              a=bank[:, :], b=ts(x_t, g * D + n * 512, 512):
                              v.scalar_tensor_tensor(o, a, scl[:, 3:4], b,
                                                     op0=ALU.mult, op1=ALU.add),
                              after=[wave_done, ldx1, s2_done,
                                     dl.get(("outfree", t % 2))], sig="ve")
                    dl[("dps", id(bank))] = ev
        dl["psum_free"] = None
        ev_done = p.now("ve")
        dl["xfree"] = ev_done
        dl["hq_free"] = wave_done
        yst = p.op("gpsimd", lambda g, o=tok_ap(y_out, t), a=sb3(osb):
                   g.dma_start(out=o, in_=a), after=[ev_done], sig="sg", amt=16)
        dl[("outfree", t % 2)] = yst
        dl[("yst", t)] = yst

    p.op("gpsimd", lambda g: g.nop(), after=[dl[("yst", NT - 1)]])

    with nc.Block() as block:
        @block.sync
        def _(eng):
            p.emit("sync", eng, sems)

        @block.tensor
        def _(eng):
            p.emit("tensor", eng, sems)

        @block.vector
        def _(eng):
            p.emit("vector", eng, sems)

        @block.scalar
        def _(eng):
            p.emit("scalar", eng, sems)

        @block.gpsimd
        def _(eng):
            p.emit("gpsimd", eng, sems)

    esD.close()
    es.close()
    return nc



def make_inputs(inputs):
    import ml_dtypes
    x = np.asarray(inputs["L0_attn_x"])
    flat = np.ascontiguousarray(x.reshape(8, NTOK, D))
    ident = np.eye(128, dtype=ml_dtypes.bfloat16)
    ones = np.ones((1, TILE), dtype=ml_dtypes.bfloat16)
    epair = np.zeros((16, NCF * 128), dtype=ml_dtypes.bfloat16)
    for pr in range(NCF):
        for j in range(128):
            epair[2 * pr + (j // 64), pr * 128 + j] = 1.0
    common = {
        "Wq": np.ascontiguousarray(inputs["L0_attn_Wq"], dtype=np.float32),
        "Wk": np.ascontiguousarray(inputs["L0_attn_Wk"], dtype=np.float32),
        "Wv": np.ascontiguousarray(inputs["L0_attn_Wv"], dtype=np.float32),
        "Wo": np.ascontiguousarray(inputs["L0_attn_Wo"], dtype=np.float32),
        "bq": np.ascontiguousarray(inputs["L0_attn_bq"], dtype=np.float32).reshape(1, D),
        "bk": np.ascontiguousarray(inputs["L0_attn_bk"], dtype=np.float32).reshape(1, D),
        "bv": np.ascontiguousarray(inputs["L0_attn_bv"], dtype=np.float32).reshape(1, D),
        "bo": np.ascontiguousarray(inputs["L0_attn_bo"], dtype=np.float32).reshape(1, D),
        "wg": np.ascontiguousarray(inputs["L0_ffn_w_gate"], dtype=np.float32),
        "wu": np.ascontiguousarray(inputs["L0_ffn_w_up"], dtype=np.float32),
        "wd": np.ascontiguousarray(inputs["L0_ffn_w_down"], dtype=np.float32),
        "clipx": np.ascontiguousarray(inputs["L0_ffn_clip_x"], dtype=np.float32).reshape(1, 1),
        "clipd": np.ascontiguousarray(inputs["L0_ffn_clip_down"], dtype=np.float32).reshape(1, 1),
        "ident": ident, "onesc": ones, "epair": epair,
    }
    out = []
    for c in range(8):
        sm = np.zeros((128, 4), dtype=np.float32)
        sm[:, c // 2] = 1.0
        out.append(dict(common, x=np.ascontiguousarray(flat[c]), smask=sm))
    return out


def kernel(**inputs):
    in_maps = make_inputs(inputs)
    nc = build_nc(dbg=False)
    res = run_bass_kernel_spmd(nc, in_maps, core_ids=list(range(8)))
    y = np.concatenate([np.asarray(r["y"]) for r in res.results], axis=0)
    return y.reshape(4, 8192, 1024)


# version 15
# speedup vs baseline: 2.7496x; 1.1380x over previous
import numpy as np
from contextlib import ExitStack

import concourse.bass as bass
import concourse.mybir as mybir
from concourse.bass_utils import run_bass_kernel_spmd

F32 = mybir.dt.float32
BF16 = mybir.dt.bfloat16
AF = mybir.ActivationFunctionType
ALU = mybir.AluOpType
AX = mybir.AxisListType.X

MAGIC = 12582912.0
D = 1024
NH = 16
HD = 64
FF = 2816
NTOK = 4096
TILE = 512
NT = NTOK // TILE
NG = TILE // 128
NCF = D // 128
NCI = FF // 128
LN_EPS = 1e-5
ATTN_EPS = 1e-6
VAGW = NH * 65 + 15

ENGINES = ("sync", "tensor", "vector", "scalar", "gpsimd")
SELF_SERIAL = {"vector": "ve", "scalar": "ac", "gpsimd": "gp"}


NPOOL = 8


class Prog:

    def __init__(self):
        self.items = {e: [] for e in ENGINES}
        self.cnt = {}
        self.pool_rr = {"sx": 0, "sg": 0}
        self.pool_last = {}
        self.last_sc = None

    def _flat(self, after):
        out = []
        for t in after:
            if t is None:
                continue
            if isinstance(t, list):
                out.extend(self._flat(t))
            else:
                out.append(t)
        return out

    def op(self, eng, fn, after=(), sig=None, amt=1):
        after = self._flat(after)
        if sig in ("sx", "sg"):
            i = self.pool_rr[sig] % NPOOL
            self.pool_rr[sig] += 1
            sem = f"{sig}{i}"
            prev = self.pool_last.get(sem)
            if prev is not None:
                after = after + [prev]
            self.cnt[sem] = self.cnt.get(sem, 0) + amt
            tok = (sem, self.cnt[sem])
            self.pool_last[sem] = tok
            self.items[eng].append((after, fn, sem, amt, None))
            return tok
        if sig == "sc":
            if self.last_sc is not None:
                after = after + [self.last_sc]
        own = SELF_SERIAL.get(eng)
        if own is not None and sig is None:
            sig = own
        self_wait = None
        if own is not None:
            self_wait = (own, self.cnt.get(own, 0))
        tok = None
        if sig is not None:
            self.cnt[sig] = self.cnt.get(sig, 0) + amt
            tok = (sig, self.cnt[sig])
        if sig == "sc":
            self.last_sc = tok
        self.items[eng].append((after, fn, sig, amt, self_wait))
        return tok

    def now(self, sig):
        if sig in ("sx", "sg"):
            return [self.pool_last[s] for s in
                    (f"{sig}{i}" for i in range(NPOOL)) if s in self.pool_last]
        return (sig, self.cnt.get(sig, 0))

    def emit(self, eng_name, eng, sems):
        waited = {}
        for waits, fn, sig, amt, self_wait in self.items[eng_name]:
            allw = list(waits)
            if self_wait is not None and self_wait[1] > 0:
                allw.append(self_wait)
            for (s, v) in allw:
                if waited.get(s, 0) >= v:
                    continue
                eng.wait_ge(sems[s], v)
                waited[s] = v
            inst = fn(eng)
            if sig is not None:
                inst.then_inc(sems[sig], amt)


def build_nc(dbg=False):
    nc = bass.Bass("TRN2", target_bir_lowering=False, detect_race_conditions=True)
    dkind = "ExternalOutput" if dbg else "Internal"

    x_in = nc.dram_tensor("x", [NTOK, D], F32, kind="ExternalInput")
    Wq = nc.dram_tensor("Wq", [D, D], F32, kind="ExternalInput")
    Wk = nc.dram_tensor("Wk", [D, D], F32, kind="ExternalInput")
    Wv = nc.dram_tensor("Wv", [D, D], F32, kind="ExternalInput")
    Wo = nc.dram_tensor("Wo", [D, D], F32, kind="ExternalInput")
    bqd = nc.dram_tensor("bq", [1, D], F32, kind="ExternalInput")
    bkd = nc.dram_tensor("bk", [1, D], F32, kind="ExternalInput")
    bvd = nc.dram_tensor("bv", [1, D], F32, kind="ExternalInput")
    bod = nc.dram_tensor("bo", [1, D], F32, kind="ExternalInput")
    wgd = nc.dram_tensor("wg", [FF, D], F32, kind="ExternalInput")
    wud = nc.dram_tensor("wu", [FF, D], F32, kind="ExternalInput")
    wdd = nc.dram_tensor("wd", [D, FF], F32, kind="ExternalInput")
    clipx = nc.dram_tensor("clipx", [1, 1], F32, kind="ExternalInput")
    clipd = nc.dram_tensor("clipd", [1, 1], F32, kind="ExternalInput")
    ident_in = nc.dram_tensor("ident", [128, 128], BF16, kind="ExternalInput")
    ones_in = nc.dram_tensor("onesc", [1, TILE], BF16, kind="ExternalInput")
    epair_in = nc.dram_tensor("epair", [16, NCF * 128], BF16, kind="ExternalInput")
    smask_in = nc.dram_tensor("smask", [128, 4], F32, kind="ExternalInput")

    y_out = nc.dram_tensor("y", [NTOK, D], F32, kind="ExternalOutput")

    w_bf = {n: nc.dram_tensor(f"{n}_bf", [D, D], BF16, kind="Internal")
            for n in ("wq", "wk", "wv", "wo")}
    wg_q = nc.dram_tensor("wg_q", [FF, D], BF16, kind="Internal")
    wu_q = nc.dram_tensor("wu_q", [FF, D], BF16, kind="Internal")
    wd_q = nc.dram_tensor("wd_q", [D, FF], BF16, kind="Internal")
    x1_hbm = nc.dram_tensor("x1_hbm", [NTOK, D], F32, kind=dkind)
    h_hbm = nc.dram_tensor("h_hbm", [NCI, 128, NTOK], F32, kind="Internal")
    cc_kv_in = nc.dram_tensor("cc_kv_in", [128, 4 * 520], F32, kind="Internal")
    cc_kv_out = nc.dram_tensor("cc_kv_out", [128, 4 * 520], F32, kind="Internal")
    cc_m1_in = nc.dram_tensor("cc_m1_in", [1, 128], F32, kind="Internal")
    cc_m1_out = nc.dram_tensor("cc_m1_out", [1, 128], F32, kind="Internal")
    cc_m2_in = nc.dram_tensor("cc_m2_in", [1, 128], F32, kind="Internal")
    cc_m2_out = nc.dram_tensor("cc_m2_out", [1, 128], F32, kind="Internal")
    if dbg:
        dbg_kv = nc.dram_tensor("dbg_kv", [128, 4 * 520], F32, kind="ExternalOutput")
        dbg_m1 = nc.dram_tensor("dbg_m1", [1, 128], F32, kind="ExternalOutput")
        dbg_m2 = nc.dram_tensor("dbg_m2", [1, 128], F32, kind="ExternalOutput")
        dbg_kvloc = nc.dram_tensor("dbg_kvloc", [128, 520], F32, kind="ExternalOutput")
        dbg_phiq = nc.dram_tensor("dbg_phiq", [128, NCF * NTOK], BF16, kind="ExternalOutput")
        dbg_ktm = nc.dram_tensor("dbg_ktm", [128, 2 * D], BF16, kind="ExternalOutput")
        dbg_vag = nc.dram_tensor("dbg_vag", [128, 2 * VAGW], BF16, kind="ExternalOutput")
        dbg_xnfm = nc.dram_tensor("dbg_xnfm", [128, NCF * TILE], BF16, kind="ExternalOutput")
        dbg_wkt = nc.dram_tensor("dbg_wkt", [128, NCF * D], BF16, kind="ExternalOutput")
        dbg_brow = nc.dram_tensor("dbg_brow", [1, 4 * D], BF16, kind="ExternalOutput")
        dbg_vag0 = nc.dram_tensor("dbg_vag0", [128, 2 * VAGW], BF16, kind="ExternalOutput")
        dbg_xntm = nc.dram_tensor("dbg_xntm", [128, NG * D], BF16, kind="ExternalOutput")
        dbg_rstd = nc.dram_tensor("dbg_rstd", [128, NT * NG], F32, kind="ExternalOutput")
        dbg_mv1 = nc.dram_tensor("dbg_mv1", [128, 2 * NT * NG], F32, kind="ExternalOutput")

    p = Prog()
    es = ExitStack()
    sem_names = ["sc", "pe", "ve", "ac", "gp"] + \
        [f"sx{i}" for i in range(NPOOL)] + [f"sg{i}" for i in range(NPOOL)]
    sems = {n: es.enter_context(nc.semaphore(n)) for n in sem_names}

    ident = es.enter_context(nc.sbuf_tensor("identt", [128, 128], BF16))
    onesc = es.enter_context(nc.sbuf_tensor("onesct", [1, TILE], BF16))
    brow = {n: es.enter_context(nc.sbuf_tensor(f"brow_{n}", [1, D], BF16))
            for n in ("q", "k", "v", "o")}
    mv1 = es.enter_context(nc.sbuf_tensor("mv1", [128, 2 * NT * NG], F32))
    rstd1 = es.enter_context(nc.sbuf_tensor("rstd1", [128, NT * NG], F32))
    mb1 = es.enter_context(nc.sbuf_tensor("mb1", [128, NT * NG], F32))
    mv2 = es.enter_context(nc.sbuf_tensor("mv2", [128, 2 * NT * NG], F32))
    xm2 = es.enter_context(nc.sbuf_tensor("xm2", [128, NT * NG], F32))
    nmv2 = es.enter_context(nc.sbuf_tensor("nmv2", [128, NT * NG], F32))
    a2c = es.enter_context(nc.sbuf_tensor("a2c", [128, NT * NG], F32))
    b2c = es.enter_context(nc.sbuf_tensor("b2c", [128, NT * NG], F32))
    wgs = es.enter_context(nc.sbuf_tensor("wgs", [128, NCI], F32))
    wus = es.enter_context(nc.sbuf_tensor("wus", [128, NCI], F32))
    wds = es.enter_context(nc.sbuf_tensor("wds", [128, NCF], F32))
    sg_all = es.enter_context(nc.sbuf_tensor("sg_all", [128, NCI], F32))
    su_all = es.enter_context(nc.sbuf_tensor("su_all", [128, NCI], F32))
    scl = es.enter_context(nc.sbuf_tensor("scl", [128, 4], F32))
    gmaxb = es.enter_context(nc.sbuf_tensor("gmaxb", [128, 128], F32))
    kv_sb = es.enter_context(nc.sbuf_tensor("kv_sb", [128, 520], F32))
    habs = es.enter_context(nc.sbuf_tensor("habs", [128, NCI], F32))
    tiny = es.enter_context(nc.sbuf_tensor("tiny", [128, 8], F32))
    tin2 = es.enter_context(nc.sbuf_tensor("tin2", [128, 12], F32))
    smask = es.enter_context(nc.sbuf_tensor("smaskt", [128, 4], F32))
    zrow = es.enter_context(nc.sbuf_tensor("zrow", [1, 512], BF16))
    x_t = es.enter_context(nc.sbuf_tensor("x_t", [128, NG * D], F32))
    scr = es.enter_context(nc.sbuf_tensor("scr", [128, 2560], F32))
    gsc = es.enter_context(nc.sbuf_tensor("gsc", [128, 1024], F32))
    bqfm = es.enter_context(nc.sbuf_tensor("bqfm", [128, NCF], F32))
    nbqfm = es.enter_context(nc.sbuf_tensor("nbqfm", [128, NCF], F32))
    bvb = es.enter_context(nc.sbuf_tensor("bvb", [128, D], BF16))

    kps = es.enter_context(nc.psum_tensor("kps", [128, 1024], F32))
    vps = es.enter_context(nc.psum_tensor("vps", [128, 1024], F32))
    pb4t = es.enter_context(nc.psum_tensor("pb4t", [128, 512], F32))
    pb5t = es.enter_context(nc.psum_tensor("pb5t", [128, 512], F32))
    pb6t = es.enter_context(nc.psum_tensor("pb6t", [128, 512], F32))
    pbf = es.enter_context(nc.psum_tensor("pbf", [128, 1024], BF16))
    pb = [kps[:, 0:512], kps[:, 512:1024], vps[:, 0:512], vps[:, 512:1024],
          pb4t[:, :], pb5t[:, :], pb6t[:, :]]

    def ts(tensor, c0, w):
        return tensor[:, c0:c0 + w]

    def tok_ap(dram, t):
        return bass.AP(dram, t * TILE * D, [[D, 128], [128 * D, NG], [1, D]])

    def sb3(sb):
        return sb.ap().rearrange("p (g f) -> p g f", f=D)

    cast_tok = {}
    for n, W in (("wk", Wk), ("wv", Wv), ("wq", Wq), ("wo", Wo)):
        cast_tok[n] = p.op("gpsimd",
                           lambda g, a=w_bf[n].ap(), b=W.ap(): g.dma_start(out=a, in_=b),
                           sig="sg", amt=16)
    for n, B in (("q", bqd), ("k", bkd), ("v", bvd), ("o", bod)):
        p.op("gpsimd", lambda g, a=brow[n][:, :], b=B.ap(): g.dma_start(out=a, in_=b),
             sig="sg", amt=16)
    brow_tok = p.now("sg")
    bq_tok = p.op("sync", lambda s: s.dma_start(
        out=bqfm[:, :], in_=bass.AP(bqd, 0, [[1, 128], [128, NCF]]),
        allow_slow_non_contiguous=True), sig="sx", amt=16)
    nbq_tok = p.op("vector", lambda v: v.tensor_scalar(nbqfm[:, :], bqfm[:, :], -1.0,
                                                       None, op0=ALU.mult),
                   after=[bq_tok], sig="ve")

    p.op("sync", lambda s: s.dma_start(out=ident[:, :], in_=ident_in.ap()), sig="sx", amt=16)
    onesc_tok = p.op("sync", lambda s: s.dma_start(out=onesc[:, :], in_=ones_in.ap()),
                     sig="sx", amt=16)
    smask_tok = p.op("sync", lambda s: s.dma_start(out=smask[:, :], in_=smask_in.ap()),
                     sig="sx", amt=16)

    esA = ExitStack()
    WkT = esA.enter_context(nc.sbuf_tensor("WkT", [128, NCF * D], BF16))
    WvT = esA.enter_context(nc.sbuf_tensor("WvT", [128, NCF * D], BF16))
    WqT = esA.enter_context(nc.sbuf_tensor("WqT", [128, NCF * D], BF16))
    xn_tm = esA.enter_context(nc.sbuf_tensor("xn_tm", [128, NG * D], BF16))
    xn_fm = esA.enter_context(nc.sbuf_tensor("xn_fm", [128, NCF * TILE], BF16))
    k_tm = esA.enter_context(nc.sbuf_tensor("k_tm", [128, 2 * D], BF16))
    v_ag = esA.enter_context(nc.sbuf_tensor("v_ag", [128, 2 * VAGW], BF16))
    wstg = esA.enter_context(nc.sbuf_tensor("wstg", [128, D], F32))
    wqnt = esA.enter_context(nc.sbuf_tensor("wqnt", [128, D], BF16))
    esWo = ExitStack()
    WoT = esWo.enter_context(nc.sbuf_tensor("WoT", [128, NCF * D], BF16, side="right"))
    esPhi = ExitStack()
    phiq = esPhi.enter_context(nc.sbuf_tensor("phiq", [128, NCF * NTOK], BF16, side="right"))

    wT_tok = {}
    for n, t_ in (("wk", WkT), ("wv", WvT), ("wq", WqT), ("wo", WoT)):
        for c in range(NCF):
            wT_tok[n] = p.op(
                "sync",
                lambda s, o=t_[:, c * D:(c + 1) * D],
                       i=w_bf[n].ap()[:, c * 128:(c + 1) * 128]:
                    s.dma_start_transpose(o, i),
                after=[cast_tok[n]], sig="sx", amt=16)

    p.op("vector", lambda v: v.memset(v_ag[:, :], 0.0), sig="ve")
    for sl in range(2):
        for h in range(NH):
            cpos = sl * VAGW + h * 65 + 64
            p.op("vector", lambda v, c=cpos: v.memset(v_ag[:, c:c + 1], 1.0), sig="ve")
    p.op("vector", lambda v: v.memset(habs[:, :], 0.0), sig="ve")

    prep_jobs = [(wgd, wg_q, wgs, i) for i in range(NCI)] + \
                [(wud, wu_q, wus, i) for i in range(NCI)]
    prep = {"ld": None, "wstg_free": None, "wqnt_free": None, "stores": []}

    def emit_prep(j):
        src, dst, wsc, i = prep_jobs[j]
        ld = p.op("sync", lambda s, o=wstg[:, :], a=src.ap()[i * 128:(i + 1) * 128, :]:
                  s.dma_start(out=o, in_=a), after=[prep["wstg_free"]], sig="sx", amt=16)
        rd = p.op("vector", lambda v, o=wsc[:, i:i + 1], a=wstg[:, :]:
                  v.reduce_max(o, a, axis=AX, apply_absolute_value=True),
                  after=[ld], sig="ve")
        sc = p.op("vector", lambda v, o=wsc[:, i:i + 1], a=wsc[:, i:i + 1]:
                  v.tensor_scalar(o, a, 1.0 / 127.0, None, op0=ALU.mult),
                  after=[rd], sig="ve")
        rc = p.op("vector", lambda v, o=tin2[:, 6:7], a=wsc[:, i:i + 1]:
                  v.reciprocal(o, a), after=[sc], sig="ve")
        t1 = p.op("scalar", lambda s, o=wstg[:, :], a=wstg[:, :], sa=tin2[:, 6:7]:
                  s.activation(o, a, AF.Copy, bias=MAGIC, scale=sa),
                  after=[rc], sig="ac")
        t3 = p.op("scalar", lambda s, o=wqnt[:, :], a=wstg[:, :]:
                  s.activation(o, a, AF.Copy, bias=-MAGIC, scale=1.0),
                  after=[t1, prep["wqnt_free"]], sig="ac")
        st = p.op("gpsimd", lambda g, o=dst.ap()[i * 128:(i + 1) * 128, :], a=wqnt[:, :]:
                  g.dma_start(out=o, in_=a), after=[t3], sig="sg", amt=16)
        prep["wstg_free"] = t3
        prep["wqnt_free"] = st
        prep["stores"].append(st)

    def load_act_set(eng, set_id):
        return eng.add_instruction(mybir.InstLoadActFuncSet(
            name=eng.bass.get_next_instruction_name(),
            act_func_set_id=set_id, ins=[], outs=[]))
    p.op("scalar", lambda s: load_act_set(s, 6), sig="ac")
    zr0 = p.op("vector", lambda v: v.memset(zrow[:, :], 0.0), sig="ve")
    for bank in (pb[5], pb[6]):
        p.op("tensor", lambda te, o=bank[:, :], l=zrow[0:1, 0:128], r=zrow[0:1, 0:512]:
             te.matmul(o, l, r, start=True, stop=False, skip_group_check=True),
             after=[zr0], sig="pe")
    al = {}
    phiq_ps_tok = {}
    prep_done = 0
    phiq_tile_tok = {}
    lnst = {}

    bias_build_toks = []
    for row, dst in ((brow["v"], bvb),):
        mmt = None
        for n in range(2):
            mmt = p.op("tensor", lambda te, o=kps[:, n * 512:(n + 1) * 512],
                       l=onesc[0:1, 0:128], r=row[0:1, n * 512:(n + 1) * 512]:
                       te.matmul(o, l, r, start=True, stop=True, skip_group_check=True),
                       after=[brow_tok, zr0, onesc_tok,
                              bias_build_toks[-1] if bias_build_toks else None],
                       sig="pe")
        bias_build_toks.append(p.op("vector", lambda v, o=dst[:, :], a=kps[:, 0:1024]:
                                    v.tensor_copy(o, a), after=[mmt], sig="ve"))
    bias_built = bias_build_toks[-1]

    def emit_ln(t):
        ldx = p.op("sync", lambda s, o=sb3(x_t), a=tok_ap(x_in, t):
                   s.dma_start(out=o, in_=a), after=[lnst.get("xn_last")],
                   sig="sx", amt=16)
        aggr_last = None
        for g in range(NG):
            col = 2 * (t * NG + g)
            p.op("scalar", lambda s, o=ts(xn_tm, g * D, D), a=ts(x_t, g * D, D),
                 xc=tin2[:, 0:1]:
                 s.activation(o, a, AF.Copy, accum_out=xc),
                 after=[ldx, aggr_last if g else lnst.get("aggr_prev"),
                        al.get("xn_tm_free")], sig="ac")
            s2t = p.op("scalar", lambda s, o=ts(xn_tm, g * D, D), a=ts(x_t, g * D, D),
                       xc=tin2[:, 1:2]:
                       s.activation(o, a, AF.Square, accum_out=xc), sig="ac")
            p.op("vector", lambda v, o=mv1[:, col:col + 1], a=tin2[:, 0:1]:
                 v.tensor_scalar(o, a, 1.0 / D, None, op0=ALU.mult),
                 after=[s2t], sig="ve")
            p.op("vector", lambda v, o=tin2[:, 2:3], a=tin2[:, 1:2]:
                 v.tensor_scalar(o, a, 1.0 / D, None, op0=ALU.mult), sig="ve")
            p.op("vector", lambda v, o=tin2[:, 3:4], a=mv1[:, col:col + 1]:
                 v.tensor_mul(o, a, a), sig="ve")
            aggr_last = p.op("vector", lambda v, o=mv1[:, col + 1:col + 2],
                             a=tin2[:, 2:3], b=tin2[:, 3:4]:
                             v.tensor_sub(o, a, b), sig="ve")
        c0 = t * NG
        lnst["aggr_last"] = aggr_last
        lnst["aggr_prev"] = aggr_last
        t_eps = p.op("vector", lambda v, o=ts(tiny, 0, 4),
                     a=mv1[:, 2 * c0 + 1:2 * (c0 + NG):2]:
                     v.tensor_scalar(o, a, LN_EPS, None, op0=ALU.add),
                     after=[aggr_last, al.get("lnfree")], sig="ve")
        t_ln = p.op("scalar", lambda s, o=ts(tiny, 4, 4), a=ts(tiny, 0, 4):
                    s.activation(o, a, AF.Ln), after=[t_eps], sig="ac")
        al["lnfree"] = t_ln
        t_rs = p.op("scalar", lambda s, o=rstd1[:, c0:c0 + NG], a=ts(tiny, 4, 4):
                    s.activation(o, a, AF.Exp, bias=0.0, scale=-0.5), sig="ac")
        t_mb = p.op("vector", lambda v, o=mb1[:, c0:c0 + NG],
                    a=mv1[:, 2 * c0:2 * (c0 + NG):2], b=rstd1[:, c0:c0 + NG]:
                    v.scalar_tensor_tensor(o, a, -1.0, b, op0=ALU.mult, op1=ALU.mult),
                    after=[t_rs], sig="ve")
        xn_last = None
        for g in range(NG):
            gc = c0 + g
            xn_last = p.op(
                "scalar",
                lambda s, o=ts(xn_tm, g * D, D), a=ts(x_t, g * D, D),
                       sa=rstd1[:, gc:gc + 1], ba=mb1[:, gc:gc + 1]:
                    s.activation(o, a, AF.Identity, bias=ba, scale=sa),
                after=[t_mb, al.get("xn_tm_free")], sig="ac")
        lnst["xn_last"] = xn_last
        lnst[("xn_done", t)] = xn_last

    emit_ln(0)
    for t in range(NT):
        xn_done_t = lnst[("xn_done", t)]

        evt = []
        for c in range(NCF):
            half = c % 2
            tl = None
            for g in range(NG):
                tl = p.op(
                    "tensor",
                    lambda te, o=pbf[:, half * 512 + g * 128:half * 512 + (g + 1) * 128],
                           a=ts(xn_tm, g * D + c * 128, 128):
                        te.transpose(o, a, ident[:, :]),
                    after=[xn_done_t, evt[c - 1] if c >= 1 else al.get("pbf_free")],
                    sig="pe")
            if c % 2 == 0:
                evt.append(p.op(
                    "vector",
                    lambda v, o=ts(xn_fm, c * TILE, TILE),
                           a=pbf[:, half * 512:half * 512 + 512]:
                        v.tensor_copy(o, a), after=[tl], sig="ve"))
            else:
                evt.append(p.op(
                    "scalar",
                    lambda s, o=ts(xn_fm, c * TILE, TILE),
                           a=pbf[:, half * 512:half * 512 + 512]:
                        s.activation(o, a, AF.Copy), after=[tl], sig="ac"))
        al["pbf_free"] = evt[-1]
        al["xn_tm_free"] = p.now("pe")
        xn_fm_done = evt[-1]

        def emit_k(g):
            for n in range(2):
                for c in range(NCF):
                    p.op("tensor",
                         lambda te, o=kps[:, n * 512:(n + 1) * 512],
                                l=ts(xn_fm, c * TILE + g * 128, 128),
                                r=ts(WkT, c * D + n * 512, 512),
                                st=(c == 0), sp=False:
                             te.matmul(o, l, r, start=st, stop=sp,
                                       skip_group_check=True),
                         after=[evt[c], al.get("kps_free")], sig="pe")
                p.op("tensor",
                     lambda te, o=kps[:, n * 512:(n + 1) * 512], l=onesc[0:1, 0:128],
                            r=brow["k"][0:1, n * 512:(n + 1) * 512]:
                         te.matmul(o, l, r, start=False, stop=True,
                                   skip_group_check=True),
                     after=[brow_tok, onesc_tok], sig="pe")
            return p.now("pe")

        def emit_v(g):
            for n in range(2):
                for c in range(NCF):
                    p.op("tensor",
                         lambda te, o=vps[:, n * 512:(n + 1) * 512],
                                l=ts(xn_fm, c * TILE + g * 128, 128),
                                r=ts(WvT, c * D + n * 512, 512),
                                st=(c == 0), sp=(c == NCF - 1):
                             te.matmul(o, l, r, start=st, stop=sp,
                                       skip_group_check=True),
                         after=[evt[c], al.get("vps_free"),
                                phiq_ps_tok.get(("q", 1))], sig="pe")
            return p.now("pe")

        def emit_q(cq):
            if cq % 2 == 0:
                bank, key, extra = pb4t[:, :], ("q", 0), None
            else:
                bank, key, extra = vps[:, 0:512], ("q", 1), al.get("vev_last")
            for cp in range(NCF):
                p.op("tensor",
                     lambda te, o=bank, l=ts(WqT, cp * D + cq * 128, 128),
                            r=ts(xn_fm, cp * TILE, TILE), st=(cp == 0),
                            sp=(cp == NCF - 1):
                         te.matmul(o, l, r, start=st, stop=sp,
                                   skip_group_check=True),
                     after=[xn_fm_done, phiq_ps_tok.get(key), extra, bias_built],
                     sig="pe")
            return bank, key, p.now("pe")

        def emit_phik(g, k_done):
            ksl = g % 2
            kk = k_tm[:, ksl * D:(ksl + 1) * D]
            rv = p.op("vector", lambda v, o=ts(scr, 0, D), a=kps[:, 0:1024]:
                      v.tensor_scalar(o, a, 0.0, None, op0=ALU.max),
                      after=[k_done], sig="ve")
            m_t = p.op("vector", lambda v, o=ts(scr, D, D), a=kps[:, 0:1024]:
                       v.tensor_scalar(o, a, 0.0, None, op0=ALU.min),
                       after=[k_done, al.get("phik_free")], sig="ve")
            al["kps_free"] = [rv, m_t]
            e_t = p.op("scalar", lambda s, o=ts(scr, D, D), a=ts(scr, D, D):
                       s.activation(o, a, AF.Exp), after=[m_t], sig="ac")
            kphi = p.op("vector", lambda v, o=kk, a=ts(scr, 0, D), b=ts(scr, D, D):
                        v.tensor_add(o, a, b),
                        after=[e_t, al.get(("kv_done", ksl))], sig="ve")
            al["phik_free"] = kphi
            return kk, kphi

        def emit_vev(g, v_done):
            ksl = g % 2
            vsl = v_ag[:, ksl * VAGW:ksl * VAGW + NH * 65]
            vev = p.op(
                "vector",
                lambda v, o=vsl.rearrange("p (h e) -> p h e", e=65)[:, :, 0:64],
                       a=vps[:, 0:1024].rearrange("p (h e) -> p h e", e=64),
                       b=bvb[:, :].rearrange("p (h e) -> p h e", e=64):
                    v.scalar_tensor_tensor(o, a, 1.0, b, op0=ALU.mult, op1=ALU.add),
                after=[v_done, bias_built, al.get(("kv_done", ksl))], sig="ve")
            al["vps_free"] = vev
            al["vev_last"] = vev
            return vsl, vev

        def emit_phiq(qbank, key, q_tok, cqq):
            sl = (cqq % 2) * 512
            rv = p.op("vector", lambda v, o=ts(scr, 2048, 512), a=qbank,
                      sa=bqfm[:, cqq:cqq + 1]:
                      v.tensor_scalar(o, a, sa, 0.0, op0=ALU.add, op1=ALU.max),
                      after=[q_tok, bq_tok, al.get("phiq_r_free")], sig="ve")
            m2_ = p.op("vector", lambda v, o=ts(gsc, sl, 512), a=qbank,
                       sa=bqfm[:, cqq:cqq + 1]:
                       v.tensor_scalar(o, a, sa, 0.0, op0=ALU.add, op1=ALU.min),
                       after=[q_tok, bq_tok, al.get(("phiq_e_free", cqq % 2))],
                       sig="ve")
            e2_ = p.op("scalar", lambda s, o=ts(gsc, sl, 512), a=ts(gsc, sl, 512):
                       s.activation(o, a, AF.Exp), after=[m2_], sig="ac")
            ph = p.op("vector", lambda v, o=ts(phiq, cqq * NTOK + t * TILE, TILE),
                      a=ts(scr, 2048, 512), b=ts(gsc, sl, 512):
                      v.tensor_add(o, a, b), after=[e2_], sig="ve")
            al["phiq_r_free"] = ph
            al[("phiq_e_free", cqq % 2)] = ph
            phiq_ps_tok[key] = ph

        def emit_kv(g, kk, vsl, kphi, vev):
            for h in range(NH):
                bank = pb5t if h < 8 else pb6t
                q_ = (h % 8) // 2
                par = (h % 2) * 64
                p.op("tensor",
                     lambda te, o=bank[par:par + 64, q_ * 65:q_ * 65 + 65],
                            l=kk[:, h * HD:(h + 1) * HD],
                            r=vsl[:, h * 65:(h + 1) * 65],
                            sp=(t == NT - 1 and g == NG - 1),
                            tp=(0, par):
                         te.matmul(o, l, r, start=False, stop=sp, tile_position=tp,
                                   skip_group_check=True),
                     after=[kphi, vev], sig="pe")
            al[("kv_done", g % 2)] = p.now("pe")

        def maybe_prep(k):
            nonlocal prep_done
            for _ in range(k):
                if prep_done < len(prep_jobs):
                    emit_prep(prep_done)
                    prep_done += 1

        pend = {}
        qi = 0
        for g in range(NG):
            kd = emit_k(g)
            vd = emit_v(g)
            kk, kphi = emit_phik(g, kd)
            vsl, vev = emit_vev(g, vd)
            qb, qk, qd = emit_q(qi)
            emit_phiq(qb, qk, qd, qi)
            qi += 1
            pend[g] = (kk, vsl, kphi, vev)
            if g >= 1:
                kk2, vsl2, kphi2, vev2 = pend.pop(g - 1)
                emit_kv(g - 1, kk2, vsl2, kphi2, vev2)
            if g == 1 and t + 1 < NT:
                emit_ln(t + 1)
            maybe_prep(1 if g < 3 else 0)
        for _ in range(NG):
            qb, qk, qd = emit_q(qi)
            emit_phiq(qb, qk, qd, qi)
            qi += 1
            maybe_prep(1)
            if qi == 6:
                kk2, vsl2, kphi2, vev2 = pend.pop(NG - 1)
                emit_kv(NG - 1, kk2, vsl2, kphi2, vev2)
        if t == NT - 1:
            maybe_prep(len(prep_jobs) - prep_done)
        phiq_tile_tok[t] = p.now("ve")

    kv_done = p.now("pe")
    ev1 = p.op("vector", lambda v: v.tensor_copy(kv_sb[:, 0:260], pb[5][:, 0:260]),
               after=[kv_done], sig="ve")
    ev2 = p.op("vector", lambda v: v.tensor_copy(kv_sb[:, 260:520], pb[6][:, 0:260]),
               sig="ve")
    if dbg:
        p.op("gpsimd", lambda g: g.dma_start(out=dbg_kvloc.ap(), in_=kv_sb[:, :]),
             after=[ev1, ev2], sig="sg", amt=16)
        p.op("gpsimd", lambda g: g.dma_start(out=dbg_phiq.ap(), in_=phiq[:, :]),
             after=[phiq_tile_tok[NT - 1]], sig="sg", amt=16)
        p.op("gpsimd", lambda g: g.dma_start(out=dbg_ktm.ap(), in_=k_tm[:, :]),
             after=[kv_done], sig="sg", amt=16)
        p.op("gpsimd", lambda g: g.dma_start(out=dbg_vag.ap(), in_=v_ag[:, :]),
             after=[kv_done], sig="sg", amt=16)
        p.op("gpsimd", lambda g: g.dma_start(out=dbg_xnfm.ap(), in_=xn_fm[:, :]),
             after=[kv_done], sig="sg", amt=16)
        p.op("gpsimd", lambda g: g.dma_start(out=dbg_xntm.ap(), in_=xn_tm[:, :]),
             after=[kv_done], sig="sg", amt=16)
        p.op("gpsimd", lambda g: g.dma_start(out=dbg_rstd.ap(), in_=rstd1[:, :]),
             after=[kv_done], sig="sg", amt=16)
        p.op("gpsimd", lambda g: g.dma_start(out=dbg_mv1.ap(), in_=mv1[:, :]),
             after=[kv_done], sig="sg", amt=16)
        for bi, bn in enumerate(("q", "k", "v", "o")):
            p.op("gpsimd", lambda g, o=dbg_brow.ap()[0:1, bi * D:(bi + 1) * D],
                 a=brow[bn][:, :]: g.dma_start(out=o, in_=a),
                 after=[kv_done], sig="sg", amt=16)
        p.op("gpsimd", lambda g: g.dma_start(out=dbg_wkt.ap(), in_=WkT[:, :]),
             after=[kv_done], sig="sg", amt=16)
    dbg_dump_tok = p.now("sg")
    mtoks = []
    for s_ in range(4):
        mtoks.append(p.op(
            "vector", lambda v, o=ts(scr, s_ * 520, 520), m=smask[:, s_:s_ + 1]:
                v.tensor_scalar(o, kv_sb[:, :], m, None, op0=ALU.mult),
            after=[ev1, ev2, smask_tok], sig="ve"))
    d1 = p.op("gpsimd", lambda g: g.dma_start(out=cc_kv_in.ap(), in_=ts(scr, 0, 2080)),
              after=[mtoks[-1]], sig="sg", amt=16)
    cc1 = p.op("gpsimd", lambda g: g.collective_compute(
        "AllReduce", ALU.add, replica_groups=[list(range(8))],
        ins=[cc_kv_in.ap()], outs=[cc_kv_out.ap()]), after=[d1], sig="sc")
    d2a = p.op("gpsimd", lambda g: g.dma_start(out=ts(scr, 0, 2080), in_=cc_kv_out.ap()),
               after=[cc1], sig="sg", amt=16)
    d2 = p.op("vector", lambda v: v.tensor_scalar(kv_sb[:, :], ts(scr, 0, 520),
                                                  smask[:, 0:1], None, op0=ALU.mult),
              after=[d2a], sig="ve")
    for s_ in range(1, 4):
        d2 = p.op("vector", lambda v, a=ts(scr, s_ * 520, 520), m=smask[:, s_:s_ + 1]:
                  v.scalar_tensor_tensor(kv_sb[:, :], a, m, kv_sb[:, :],
                                         op0=ALU.mult, op1=ALU.add),
                  after=[d2], sig="ve")

    a1_sg = p.now("sg")
    esA.close()
    esB = ExitStack()
    epair = esB.enter_context(nc.sbuf_tensor("epairt", [16, NCF * 128], BF16))
    kv_bd = esB.enter_context(nc.sbuf_tensor("kv_bd", [128, 8 * 128], BF16))
    ksum2 = esB.enter_context(nc.sbuf_tensor("ksum2", [128, NCF * 16], BF16))
    nrm_sb = esB.enter_context(nc.sbuf_tensor("nrm_sb", [16, TILE], F32))
    rec_sb = esB.enter_context(nc.sbuf_tensor("rec_sb", [16, TILE], F32))
    rec_bf = esB.enter_context(nc.sbuf_tensor("rec_bf", [16, TILE], BF16))
    rbc_sb = esB.enter_context(nc.sbuf_tensor("rbc_sb", [128, TILE], F32))
    qkv_fm_sb = esB.enter_context(nc.sbuf_tensor("qkv_fm", [128, NCF * TILE], BF16))
    x1_sb = esB.enter_context(nc.sbuf_tensor("x1_sb", [128, NG * D], F32))
    x_t2 = esB.enter_context(nc.sbuf_tensor("x_t2", [128, NG * D], F32))
    epair_tok = p.op("sync", lambda s: s.dma_start(out=epair[:, :], in_=epair_in.ap()),
                     after=[kv_done, prep["stores"][-1], d2, dbg_dump_tok], sig="sx", amt=16)

    if dbg:
        p.op("gpsimd", lambda g: g.dma_start(out=dbg_kv.ap(), in_=cc_kv_out.ap()),
             after=[cc1], sig="sg", amt=16)
    bd0 = p.op("vector", lambda v: v.memset(kv_bd[:, :], 0.0),
               after=[d2, kv_done, prep["stores"][-1], dbg_dump_tok], sig="ve")
    p.op("vector", lambda v: v.memset(ksum2[:, :], 0.0), sig="ve")
    kvbd_done = None
    for par in (0, 64):
        for b in range(2):
            p.op("vector",
                 lambda v, o=kv_bd[par:par + 64, 512 * b:512 * b + 512]
                          .rearrange("p (q e) -> p q e", e=128)[:, :, par:par + 64],
                        a=kv_sb[par:par + 64, 260 * b:260 * b + 260]
                          .rearrange("p (q e) -> p q e", e=65)[:, :, 0:64]:
                     v.tensor_copy(o, a), after=[bd0], sig="ve")
    for par, off in ((0, 0), (64, 1)):
        for b in range(2):
            kvbd_done = p.op(
                "vector",
                lambda v, o=bass.AP(ksum2, 72 * b + off, [[NCF * 16, 64], [18, 4]],
                                    base_partition=par)
                       if False else
                       ksum2[par:par + 64, 72 * b + off:72 * b + off + 55:18],
                       a=kv_sb[par:par + 64, 260 * b + 64:260 * b + 260:65]:
                    v.tensor_copy(o, a), sig="ve")

    bl = {"xfree": al.get("xfree"), "tin2_free": lnst.get("aggr_last")}
    xB = [x_t, x_t2]
    ldx_tok = {}
    ldx_tok[0] = p.op("sync", lambda s, o=sb3(x_t), a=tok_ap(x_in, 0):
                      s.dma_start(out=o, in_=a), after=[bl.get("xfree")],
                      sig="sx", amt=16)

    def emit_nrm(t):
        for c in range(NCF):
            p.op("tensor", lambda te, o=pb[4][0:16, :], l=ts(ksum2, c * 16, 16),
                 r=ts(phiq, c * NTOK + t * TILE, TILE), st=(c == 0), sp=(c == NCF - 1):
                 te.matmul(o, l, r, start=st, stop=sp, skip_group_check=True),
                 after=[kvbd_done, bl.get("nrm_ps"), phiq_tile_tok[t],
                        phiq_ps_tok.get(id(pb[4])) if t == 0 else None], sig="pe")
        nm = p.now("pe")
        n1 = p.op("vector", lambda v: v.tensor_scalar(nrm_sb[:, :], pb[4][0:16, :],
                                                      ATTN_EPS, None, op0=ALU.max),
                  after=[nm], sig="ve")
        bl["nrm_ps"] = n1
        n2 = p.op("vector", lambda v: v.reciprocal(rec_sb[:, :], nrm_sb[:, :]),
                  after=[n1, bl.get("rec_free")], sig="ve")
        bl["rec_done"] = p.op("vector", lambda v: v.tensor_copy(rec_bf[:, :],
                                                                rec_sb[:, :]),
                              after=[n2], sig="ve")

    emit_nrm(0)
    for t in range(NT):
        if t + 1 < NT:
            ldx_tok[t + 1] = p.op(
                "sync", lambda s, o=sb3(xB[(t + 1) % 2]), a=tok_ap(x_in, t + 1):
                    s.dma_start(out=o, in_=a),
                after=[bl.get(("xfree", (t + 1) % 2)),
                       [kv_done, a1_sg, d2] if t == 0 else None],
                sig="sx", amt=16)
        n2 = bl["rec_done"]
        mm2 = None
        for pr in range(NCF):
            bank = pb[pr % 3]
            mm1 = p.op("tensor", lambda te, o=bank[:, :], l=ts(kv_bd, pr * 128, 128),
                       r=ts(phiq, pr * NTOK + t * TILE, TILE):
                       te.matmul(o, l, r, start=True, stop=True, skip_group_check=True),
                       after=[kvbd_done, bl.get(("qps", pr % 3)),
                              phiq_ps_tok.get(id(pb[0])) if t == 0 and pr % 3 == 0 else None],
                       sig="pe")
            mm2 = p.op("tensor", lambda te, o=pb[3][:, :], l=ts(epair, pr * 128, 128),
                       r=rec_bf[:, :]:
                       te.matmul(o, l, r, start=True, stop=True, skip_group_check=True),
                       after=[n2, bl.get("bc_ps"),
                              epair_tok if t == 0 and pr == 0 else None], sig="pe")
            bc = p.op("scalar", lambda s, o=rbc_sb[:, :], a=pb[3][:, :]:
                      s.activation(o, a, AF.Copy),
                      after=[mm2, bl.get("rbc_free")], sig="ac")
            bl["bc_ps"] = bc
            ev = p.op("vector", lambda v, o=ts(qkv_fm_sb, pr * TILE, TILE), a=bank[:, :],
                      b=rbc_sb[:, :]:
                      v.scalar_tensor_tensor(o, a, 1.0, b, op0=ALU.mult, op1=ALU.mult),
                      after=[mm1, bc], sig="ve")
            bl[("qps", pr % 3)] = ev
            bl["rbc_free"] = ev
        bl["rec_free"] = mm2
        qkv_done = p.now("ve")
        if t + 1 < NT:
            emit_nrm(t + 1)

        evbank = {}
        for g in range(NG):
            for n in range(2):
                bank = pb[(2 * g + n) % 4]
                for c in range(NCF):
                    p.op("tensor", lambda te, o=bank[:, :],
                         l=ts(qkv_fm_sb, c * TILE + g * 128, 128),
                         r=ts(WoT, c * D + n * 512, 512), st=(c == 0):
                         te.matmul(o, l, r, start=st, stop=False, skip_group_check=True),
                         after=[qkv_done, wT_tok["wo"], bl.get(("ops", (2 * g + n) % 4))],
                         sig="pe")
                om = p.op("tensor", lambda te, o=bank[:, :], l=onesc[0:1, 0:128],
                          r=brow["o"][0:1, n * 512:(n + 1) * 512]:
                          te.matmul(o, l, r, start=False, stop=True, skip_group_check=True),
                          after=[brow_tok], sig="pe")
                ev = p.op("vector", lambda v, o=ts(x1_sb, g * D + n * 512, 512),
                          a=bank[:, :], b=ts(xB[t % 2], g * D + n * 512, 512):
                          v.scalar_tensor_tensor(o, a, 1.0, b, op0=ALU.mult, op1=ALU.add),
                          after=[om, ldx_tok[t], bl.get("x1free")], sig="ve")
                bl[("ops", (2 * g + n) % 4)] = ev
                evbank[(g, n)] = ev
        bl[("xfree", t % 2)] = p.now("ve")
        for b in range(3):
            bl[("qps", b)] = p.now("ve")

        acc = None
        for g in range(NG):
            p.op("scalar", lambda s, o=gsc[:, 0:D], a=ts(x1_sb, g * D, D),
                 xc=tin2[:, g:g + 1]:
                 s.activation(o, a, AF.Copy, accum_out=xc),
                 after=[evbank[(g, 0)], evbank[(g, 1)], bl.get("tin2_free")], sig="ac")
            acc = p.op("scalar", lambda s, o=gsc[:, 0:D], a=ts(x1_sb, g * D, D),
                       xc=tin2[:, 4 + g:5 + g]:
                       s.activation(o, a, AF.Square, accum_out=xc), sig="ac")
        p.op("vector", lambda v, o=mv2[:, 8 * t:8 * t + 8:2], a=tin2[:, 0:4]:
             v.tensor_scalar(o, a, 1.0 / D, None, op0=ALU.mult),
             after=[acc], sig="ve")
        nm2t = p.op("vector", lambda v, o=nmv2[:, 4 * t:4 * t + 4], a=tin2[:, 0:4]:
                    v.tensor_scalar(o, a, -1.0 / D, None, op0=ALU.mult), sig="ve")
        p.op("vector", lambda v, o=tin2[:, 8:12], a=mv2[:, 8 * t:8 * t + 8:2]:
             v.tensor_mul(o, a, a), sig="ve")
        vr = p.op("vector", lambda v, o=mv2[:, 8 * t + 1:8 * t + 8:2],
                  a=tin2[:, 4:8], b=tin2[:, 8:12]:
                  v.scalar_tensor_tensor(o, a, 1.0 / D, b,
                                         op0=ALU.mult, op1=ALU.subtract), sig="ve")
        bl["tin2_free"] = vr
        rd = nm2t
        for g in range(NG):
            sb_ = p.op("scalar", lambda s, o=ts(scr, 0, D), a=ts(x1_sb, g * D, D),
                       nm=nmv2[:, 4 * t + g:4 * t + g + 1]:
                       s.activation(o, a, AF.Abs, bias=nm, scale=1.0),
                       after=[rd], sig="ac")
            rd = p.op("vector", lambda v, o=xm2[:, t * NG + g:t * NG + g + 1],
                      a=ts(scr, 0, D):
                      v.reduce_max(o, a, axis=AX), after=[sb_], sig="ve")
        xm_done = p.now("ve")
        st = p.op("gpsimd", lambda g, o=tok_ap(x1_hbm, t), a=sb3(x1_sb):
                  g.dma_start(out=o, in_=a), after=[xm_done], sig="sg", amt=16)
        bl["x1free"] = st
        bl[("x1st", t)] = st

    t_eps = p.op("vector", lambda v, o=ts(scr, 0, 32), a=mv2[:, 1:64:2]:
                 v.tensor_scalar(o, a, LN_EPS, None, op0=ALU.add), sig="ve")
    t_ln = p.op("scalar", lambda s, o=ts(scr, 32, 32), a=ts(scr, 0, 32):
                s.activation(o, a, AF.Ln), after=[t_eps], sig="ac")
    t_rs2 = p.op("scalar", lambda s, o=ts(scr, 64, 32), a=ts(scr, 32, 32):
                 s.activation(o, a, AF.Exp, bias=0.0, scale=-0.5), sig="ac")
    t_xm = p.op("vector", lambda v, o=ts(scr, 96, 32), a=xm2[:, 0:32], b=ts(scr, 64, 32):
                v.tensor_mul(o, a, b), after=[t_rs2], sig="ve")
    t_mx = p.op("vector", lambda v, o=ts(tiny, 0, 1), a=ts(scr, 96, 32):
                v.reduce_max(o, a, axis=AX), after=[t_xm], sig="ve")

    b_ve_done = p.now("ve")
    b_ac_done = p.now("ac")
    b_sg_done = p.now("sg")
    d3 = p.op("gpsimd", lambda g: g.dma_start(out=cc_m1_in.ap()[0:1, :], in_=ts(tiny, 0, 1)),
              after=[t_mx], sig="sg", amt=16)
    cc2 = p.op("gpsimd", lambda g: g.collective_compute(
        "AllReduce", ALU.max, replica_groups=[list(range(8))],
        ins=[cc_m1_in.ap()], outs=[cc_m1_out.ap()]), after=[d3], sig="sc")
    bc1 = p.op("gpsimd", lambda g: g.dma_start(
        out=gmaxb[:, :], in_=bass.AP(cc_m1_out, 0, [[0, 128], [1, 128]])),
        after=[cc2], sig="sg", amt=16)
    gm1 = p.op("vector", lambda v: v.reduce_max(ts(tiny, 0, 1), gmaxb[:, :], axis=AX),
               after=[bc1], sig="ve")
    if dbg:
        p.op("gpsimd", lambda g: g.dma_start(out=dbg_m1.ap(), in_=cc_m1_out.ap()),
             after=[cc2], sig="sg", amt=16)
    bcx = p.op("gpsimd", lambda g: g.dma_start(
        out=ts(tiny, 1, 1), in_=bass.AP(clipx, 0, [[0, 128], [1, 1]])),
        sig="sg", amt=16)
    e1 = p.op("scalar", lambda s: s.activation(ts(tiny, 2, 1), ts(tiny, 1, 1),
                                               AF.Exp, bias=0.0, scale=-1.0),
              after=[bcx], sig="ac")
    e2 = p.op("vector", lambda v: v.tensor_scalar(ts(tiny, 2, 1), ts(tiny, 2, 1),
                                                  1.0, 127.0, op0=ALU.add, op1=ALU.mult),
              after=[e1], sig="ve")
    e3 = p.op("vector", lambda v: v.reciprocal(ts(tiny, 3, 1), ts(tiny, 0, 1)),
              after=[gm1], sig="ve")
    e4 = p.op("vector", lambda v: v.tensor_mul(scl[:, 0:1], ts(tiny, 2, 1), ts(tiny, 3, 1)),
              after=[e2, e3], sig="ve")
    e5 = p.op("vector", lambda v: v.reciprocal(ts(tiny, 3, 1), ts(tiny, 2, 1)), sig="ve")
    e6 = p.op("vector", lambda v: v.tensor_mul(scl[:, 1:2], ts(tiny, 0, 1), ts(tiny, 3, 1)),
              sig="ve")
    e7 = p.op("vector", lambda v: v.tensor_scalar(a2c[:, 0:32], ts(scr, 64, 32),
                                                  scl[:, 0:1], None, op0=ALU.mult),
              after=[e4], sig="ve")
    e8 = p.op("vector", lambda v: v.scalar_tensor_tensor(b2c[:, 0:32], mv2[:, 0:64:2],
                                                         -1.0, a2c[:, 0:32],
                                                         op0=ALU.mult, op1=ALU.mult),
              after=[e7], sig="ve")
    e9 = p.op("vector", lambda v: v.tensor_scalar(sg_all[:, :], wgs[:, :], scl[:, 1:2],
                                                  None, op0=ALU.mult),
              after=[e6, prep["stores"][NCI - 1]], sig="ve")
    scales_done = p.op("vector", lambda v: v.tensor_scalar(su_all[:, :], wus[:, :],
                                                           scl[:, 1:2], None, op0=ALU.mult),
                       after=[prep["stores"][-1]], sig="ve")

    b_pe_done = p.now("pe")
    esB.close()
    esPhi.close()
    esWo.close()
    esC = ExitStack()
    wgT = esC.enter_context(nc.sbuf_tensor("wgT", [128, NCF * FF], BF16))
    wuT = esC.enter_context(nc.sbuf_tensor("wuT", [128, NCF * FF], BF16))
    xq_tm = esC.enter_context(nc.sbuf_tensor("xq_tm", [128, NG * D], BF16))
    xq_fm = [esC.enter_context(nc.sbuf_tensor(f"xq_fm{i}", [128, NCF * TILE], BF16))
             for i in range(2)]
    g_sb = esC.enter_context(nc.sbuf_tensor("g_sb", [128, 512], F32))
    u_sb = esC.enter_context(nc.sbuf_tensor("u_sb", [128, 512], F32))
    h_sb = [esC.enter_context(nc.sbuf_tensor(f"h_sb{i}", [128, 1024], F32)) for i in range(2)]
    wdstg = esC.enter_context(nc.sbuf_tensor("wdstg", [128, FF], F32))
    wdq_sb = esC.enter_context(nc.sbuf_tensor("wdq_sb", [128, FF], BF16))
    dum = esC.enter_context(nc.sbuf_tensor("dum", [128, 512], F32))

    wgT_tok = []
    wuT_tok = []
    for c in range(NCF):
        wgT_tok.append(p.op(
            "sync", lambda s, o=ts(wgT, c * FF, FF), i=wg_q.ap()[:, c * 128:(c + 1) * 128]:
                s.dma_start_transpose(o, i),
            after=[b_sg_done, b_pe_done, b_ve_done, b_ac_done], sig="sx", amt=16))
        wuT_tok.append(p.op(
            "sync", lambda s, o=ts(wuT, c * FF, FF), i=wu_q.ap()[:, c * 128:(c + 1) * 128]:
                s.dma_start_transpose(o, i),
            after=[b_sg_done, b_pe_done, b_ve_done, b_ac_done],
            sig="sx", amt=16))

    wdp = {"wstg_free": None, "wq_free": None, "stores": [],
           "extra": [b_pe_done, scales_done, b_sg_done]}

    def emit_wd_prep(i):
        ld = p.op("sync", lambda s, o=wdstg[:, :], a=wdd.ap()[i * 128:(i + 1) * 128, :]:
                  s.dma_start(out=o, in_=a),
                  after=[wdp["wstg_free"]] + wdp.pop("extra", []), sig="sx", amt=16)
        rd = p.op("vector", lambda v, o=wds[:, i:i + 1], a=wdstg[:, :]:
                  v.reduce_max(o, a, axis=AX, apply_absolute_value=True),
                  after=[ld], sig="ve")
        sc = p.op("vector", lambda v, o=wds[:, i:i + 1], a=wds[:, i:i + 1]:
                  v.tensor_scalar(o, a, 1.0 / 127.0, None, op0=ALU.mult),
                  after=[rd], sig="ve")
        rc = p.op("vector", lambda v, o=tin2[:, 7:8], a=wds[:, i:i + 1]:
                  v.reciprocal(o, a), after=[sc], sig="ve")
        t1 = p.op("scalar", lambda s, o=wdstg[:, :], a=wdstg[:, :], sa=tin2[:, 7:8]:
                  s.activation(o, a, AF.Copy, bias=MAGIC, scale=sa), after=[rc], sig="ac")
        t3 = p.op("scalar", lambda s, o=wdstg[:, :], a=wdstg[:, :]:
                  s.activation(o, a, AF.Copy, bias=-MAGIC, scale=1.0),
                  after=[t1], sig="ac")
        t5 = p.op("scalar", lambda s, o=wdq_sb[:, :], a=wdstg[:, :], sa=wds[:, i:i + 1]:
                  s.activation(o, a, AF.Copy, bias=0.0, scale=sa),
                  after=[t3, wdp["wq_free"]], sig="ac")
        st = p.op("gpsimd", lambda g, o=wd_q.ap()[i * 128:(i + 1) * 128, :], a=wdq_sb[:, :]:
                  g.dma_start(out=o, in_=a), after=[t5], sig="sg", amt=16)
        wdp["wstg_free"] = t5
        wdp["wq_free"] = st
        wdp["stores"].append(st)

    p.op("scalar", lambda s: load_act_set(s, 18), sig="ac")
    cl = {"xfree": bl[("xfree", 0)]}
    hst_tok = {}
    qd = {}

    def emit_cldx(t):
        cl[("ldx", t)] = p.op(
            "sync", lambda s, o=sb3(x_t), a=tok_ap(x1_hbm, t): s.dma_start(out=o, in_=a),
            after=[cl.get("xfree"), bl[("x1st", t)]], sig="sx", amt=16)

    def emit_cquant(t):
        xq_toks = []
        for g in range(NG):
            gc = t * NG + g
            q1 = p.op("scalar", lambda s, o=ts(scr, 0, D), a=ts(x_t, g * D, D),
                      sa=a2c[:, gc:gc + 1], ba=b2c[:, gc:gc + 1]:
                      s.activation(o, a, AF.Identity, bias=ba, scale=sa),
                      after=[cl[("ldx", t)], e8, xq_toks[-1] if g else cl.get("scr_free"),
                             cl.get("xq_tm_free"),
                             b_pe_done if t == 0 and g == 0 else None], sig="ac")
            q2 = p.op("vector", lambda v, o=ts(scr, 0, D), a=ts(scr, 0, D):
                      v.tensor_scalar(o, a, -128.49999, 127.49998,
                                      op0=ALU.max, op1=ALU.min),
                      after=[q1], sig="ve")
            xq_toks.append(p.op("vector", lambda v, o=ts(xq_tm, g * D, D), a=ts(scr, 0, D):
                                v.tensor_scalar(o, a, MAGIC, -MAGIC,
                                                op0=ALU.add, op1=ALU.add),
                                after=[q2], sig="ve"))
        cl["xfree"] = xq_toks[-1]
        cl["scr_free"] = xq_toks[-1]
        cl[("q3", t)] = xq_toks[-1]

    def emit_ctrans(t):
        buf = xq_fm[t % 2]
        evt = []
        for c in range(NCF):
            half = c % 2
            tl = None
            for g in range(NG):
                tl = p.op(
                    "tensor",
                    lambda te, o=pbf[:, half * 512 + g * 128:half * 512 + (g + 1) * 128],
                           a=ts(xq_tm, g * D + c * 128, 128):
                        te.transpose(o, a, ident[:, :]),
                    after=[cl[("q3", t)], evt[c - 1] if c >= 1 else cl.get("pbf_free")],
                    sig="pe")
            evt.append(p.op(
                "vector",
                lambda v, o=ts(buf, c * TILE, TILE),
                       a=pbf[:, half * 512:half * 512 + 512]:
                    v.tensor_copy(o, a),
                after=[tl, cl.get(("fmfree", t % 2))], sig="ve"))
        cl["pbf_free"] = evt[-1]
        cl["xq_tm_free"] = p.now("pe")
        qd[t] = evt

    emit_cldx(0)
    emit_cquant(0)
    emit_ctrans(0)
    for t in range(NT):
        if t + 1 < NT:
            emit_cldx(t + 1)
        evt = qd[t]
        buf = xq_fm[t % 2]
        for i in range(NCI):
            bg = pb[2 * (i % 3)]
            bu = pb[2 * (i % 3) + 1]
            for c in range(NCF):
                p.op("tensor", lambda te, o=bg[:, :], l=ts(wgT, c * FF + i * 128, 128),
                     r=ts(buf, c * TILE, TILE), st=(c == 0), sp=(c == NCF - 1):
                     te.matmul(o, l, r, start=st, stop=sp, skip_group_check=True),
                     after=[evt[c], wgT_tok[c], cl.get(("gps", i % 3))], sig="pe")
            gmm = p.now("pe")
            for c in range(NCF):
                p.op("tensor", lambda te, o=bu[:, :], l=ts(wuT, c * FF + i * 128, 128),
                     r=ts(buf, c * TILE, TILE), st=(c == 0), sp=(c == NCF - 1):
                     te.matmul(o, l, r, start=st, stop=sp, skip_group_check=True),
                     after=[wuT_tok[c]], sig="pe")
            umm = p.now("pe")
            slot = (i // 2) % 2
            half = i % 2
            hb = h_sb[slot][:, half * 512:(half + 1) * 512]
            gv = p.op("scalar", lambda s, o=g_sb[:, :], a=bg[:, :], sa=sg_all[:, i:i + 1]:
                      s.activation(o, a, AF.Silu, bias=0.0, scale=sa),
                      after=[gmm, e9, scales_done, cl.get("g_free")], sig="ac")
            uv = p.op("scalar", lambda s, o=u_sb[:, :], a=bu[:, :], sa=su_all[:, i:i + 1]:
                      s.activation(o, a, AF.Copy, bias=0.0, scale=sa),
                      after=[umm], sig="ac")
            hm = p.op("gpsimd", lambda g, o=hb, a=g_sb[:, :], b=u_sb[:, :]:
                      g.tensor_mul(o, a, b),
                      after=[gv, uv, cl.get(("hst", slot, half)),
                             cl.get(("htok", slot))], sig="gp")
            cl[("gps", i % 3)] = hm
            cl["g_free"] = hm
            if half == 1:
                ht = p.op("vector", lambda v, o=tiny[:, 6:7], a=h_sb[slot][:, :]:
                          v.reduce_max(o, a, axis=AX, apply_absolute_value=True),
                          after=[hm, cl.get(("hm_even", slot))], sig="ve")
                p.op("vector", lambda v, o=habs[:, i // 2:i // 2 + 1],
                     a=habs[:, i // 2:i // 2 + 1], b=tiny[:, 6:7]:
                     v.tensor_max(o, a, b), after=[ht], sig="ve")
                cl[("htok", slot)] = ht
            else:
                cl[("hm_even", slot)] = hm
            st = p.op("gpsimd", lambda g, o=h_hbm.ap()[i, :, t * TILE:(t + 1) * TILE],
                      a=hb:
                      g.dma_start(out=o, in_=a), after=[hm], sig="sg", amt=16)
            cl[("hst", slot, half)] = st
            hst_tok[(i, t)] = st
            if i == 10 and t + 1 < NT:
                emit_cquant(t + 1)
        cl[("fmfree", t % 2)] = p.now("pe")
        if t + 1 < NT:
            emit_ctrans(t + 1)
        if t < NCF:
            emit_wd_prep(t)

    c_ve_done = p.now("ve")
    c_sg_done = p.now("sg")
    hmx = p.op("vector", lambda v: v.reduce_max(ts(tiny, 0, 1), habs[:, :], axis=AX),
               sig="ve")
    d4 = p.op("gpsimd", lambda g: g.dma_start(out=cc_m2_in.ap()[0:1, :], in_=ts(tiny, 0, 1)),
              after=[hmx], sig="sg", amt=16)
    cc3 = p.op("gpsimd", lambda g: g.collective_compute(
        "AllReduce", ALU.max, replica_groups=[list(range(8))],
        ins=[cc_m2_in.ap()], outs=[cc_m2_out.ap()]), after=[d4], sig="sc")
    bc2 = p.op("gpsimd", lambda g: g.dma_start(
        out=gmaxb[:, :], in_=bass.AP(cc_m2_out, 0, [[0, 128], [1, 128]])),
        after=[cc3], sig="sg", amt=16)
    gm2 = p.op("vector", lambda v: v.reduce_max(ts(tiny, 0, 1), gmaxb[:, :], axis=AX),
               after=[bc2], sig="ve")
    if dbg:
        p.op("gpsimd", lambda g: g.dma_start(out=dbg_m2.ap(), in_=cc_m2_out.ap()),
             after=[cc3], sig="sg", amt=16)
    bcd = p.op("gpsimd", lambda g: g.dma_start(
        out=ts(tiny, 1, 1), in_=bass.AP(clipd, 0, [[0, 128], [1, 1]])),
        sig="sg", amt=16)
    p.op("scalar", lambda s: load_act_set(s, 6), sig="ac")
    f1 = p.op("scalar", lambda s: s.activation(ts(tiny, 2, 1), ts(tiny, 1, 1),
                                               AF.Exp, bias=0.0, scale=-1.0),
              after=[bcd], sig="ac")
    f2 = p.op("vector", lambda v: v.tensor_scalar(ts(tiny, 2, 1), ts(tiny, 2, 1),
                                                  1.0, 127.0, op0=ALU.add, op1=ALU.mult),
              after=[f1], sig="ve")
    f3 = p.op("vector", lambda v: v.reciprocal(ts(tiny, 3, 1), ts(tiny, 0, 1)),
              after=[gm2], sig="ve")
    p.op("vector", lambda v: v.tensor_mul(scl[:, 2:3], ts(tiny, 2, 1), ts(tiny, 3, 1)),
         after=[f2, f3], sig="ve")
    f5 = p.op("vector", lambda v: v.reciprocal(ts(tiny, 3, 1), ts(tiny, 2, 1)), sig="ve")
    s2_done = p.op("vector", lambda v: v.tensor_mul(scl[:, 3:4], ts(tiny, 0, 1),
                                                    ts(tiny, 3, 1)),
                   after=[f5], sig="ve")

    c_pe_done = p.now("pe")
    c_ac_done = p.now("ac")
    esC.close()
    esD = ExitStack()
    wdT = esD.enter_context(nc.sbuf_tensor("wdT", [128, NCI * D], BF16))
    hq_sb = [esD.enter_context(nc.sbuf_tensor(f"hq_sb{i}", [128, NCI * 512], BF16))
             for i in range(2)]
    hstg = [esD.enter_context(nc.sbuf_tensor(f"hstg{i}", [128, 512], F32)) for i in range(4)]
    out_sb = [esD.enter_context(nc.sbuf_tensor(f"out_sb{i}", [128, NG * D], F32))
              for i in range(2)]
    xD2 = esD.enter_context(nc.sbuf_tensor("xD2", [128, NG * D], F32))
    xD = [x_t, xD2]

    wdT_tok = []
    for c in range(NCI):
        wdT_tok.append(p.op(
            "sync", lambda s, o=ts(wdT, c * D, D), i=wd_q.ap()[:, c * 128:(c + 1) * 128]:
                s.dma_start_transpose(o, i),
            after=[c_sg_done, c_pe_done, c_ac_done, c_ve_done], sig="sx", amt=16))

    dl = {"scr_free": cl["xfree"], "psum_free": c_ac_done}
    dldx = {}
    hqd = {}

    def emit_dldx(t):
        dldx[t] = p.op(
            "sync", lambda s, o=sb3(xD[t % 2]), a=tok_ap(x1_hbm, t):
                s.dma_start(out=o, in_=a),
            after=[dl.get(("xfree", t % 2)),
                   [cl["xfree"]] if t == 0 else None,
                   [c_ac_done, c_ve_done, c_sg_done] if t == 1 else None],
            sig="sx", amt=16)

    def emit_hq_part(t, lo, hi):
        buf = hq_sb[t % 2]
        toks = hqd.setdefault(t, [])
        for i in range(lo, hi):
            hb = hstg[i % 4]
            ld = p.op("sync", lambda s, o=hb[:, :],
                      a=h_hbm.ap()[i, :, t * TILE:(t + 1) * TILE]:
                      s.dma_start(out=o, in_=a),
                      after=[dl.get(("hstg_free", i % 4)), hst_tok[(i, t)],
                             c_pe_done if t == 0 and i < 4 else None],
                      sig="sx", amt=16)
            q1 = p.op("scalar", lambda s, o=ts(scr, 0, 512), a=hb[:, :], sa=scl[:, 2:3]:
                      s.activation(o, a, AF.Copy, bias=0.0, scale=sa),
                      after=[ld, s2_done, dl.get("scr_free"),
                             dl.get(("hq_free", t % 2)),
                             c_pe_done if t == 0 and i == 0 else None], sig="ac")
            dl[("hstg_free", i % 4)] = q1
            q2 = p.op("vector", lambda v, o=ts(scr, 0, 512), a=ts(scr, 0, 512):
                      v.tensor_scalar(o, a, -128.49999, 127.49998,
                                      op0=ALU.max, op1=ALU.min),
                      after=[q1], sig="ve")
            q3 = p.op("vector", lambda v, o=ts(buf, i * 512, 512),
                      a=ts(scr, 0, 512):
                      v.tensor_scalar(o, a, MAGIC, -MAGIC,
                                      op0=ALU.add, op1=ALU.add),
                      after=[q2], sig="ve")
            toks.append(q3)
            dl["scr_free"] = q3

    emit_dldx(0)
    emit_hq_part(0, 0, NCI)
    for t in range(NT):
        toks = hqd[t]
        buf = hq_sb[t % 2]
        if t + 1 < NT:
            emit_dldx(t + 1)
        osb = out_sb[t % 2]
        wave_banks = ((pb[0], pb[1], pb[2], pb[3]), (pb[4], pb[5], pb[6], pb[3]))
        for w, groups in enumerate(((0, 1), (2, 3))):
            for i in range(NCI):
                for gi, g in enumerate(groups):
                    for n in range(2):
                        bank = wave_banks[w][2 * gi + n]
                        p.op("tensor", lambda te, o=bank[:, :],
                             l=ts(buf, i * 512 + g * 128, 128),
                             r=ts(wdT, i * D + n * 512, 512),
                             st=(i == 0), sp=(i == NCI - 1):
                             te.matmul(o, l, r, start=st, stop=sp, skip_group_check=True),
                             after=[toks[i], wdT_tok[i],
                                    dl.get(("dps", id(bank))), dl.get("psum_free")],
                             sig="pe")
            wave_done = p.now("pe")
            for gi, g in enumerate(groups):
                for n in range(2):
                    bank = wave_banks[w][2 * gi + n]
                    ev = p.op("vector", lambda v, o=ts(osb, g * D + n * 512, 512),
                              a=bank[:, :], b=ts(xD[t % 2], g * D + n * 512, 512):
                              v.scalar_tensor_tensor(o, a, scl[:, 3:4], b,
                                                     op0=ALU.mult, op1=ALU.add),
                              after=[wave_done, dldx[t], s2_done,
                                     dl.get(("outfree", t % 2))], sig="ve")
                    dl[("dps", id(bank))] = ev
            if t + 1 < NT:
                emit_hq_part(t + 1, 0 if w == 0 else 11, 11 if w == 0 else NCI)
        dl["psum_free"] = None
        ev_done = p.now("ve")
        dl[("xfree", t % 2)] = ev_done
        dl[("hq_free", t % 2)] = p.now("pe")
        yst = p.op("gpsimd", lambda g, o=tok_ap(y_out, t), a=sb3(osb):
                   g.dma_start(out=o, in_=a), after=[ev_done], sig="sg", amt=16)
        dl[("outfree", t % 2)] = yst
        dl[("yst", t)] = yst

    p.op("gpsimd", lambda g: g.nop(), after=[dl[("yst", NT - 1)]])

    with nc.Block() as block:
        @block.sync
        def _(eng):
            p.emit("sync", eng, sems)

        @block.tensor
        def _(eng):
            p.emit("tensor", eng, sems)

        @block.vector
        def _(eng):
            p.emit("vector", eng, sems)

        @block.scalar
        def _(eng):
            p.emit("scalar", eng, sems)

        @block.gpsimd
        def _(eng):
            p.emit("gpsimd", eng, sems)

    esD.close()
    es.close()
    return nc



def make_inputs(inputs):
    import ml_dtypes
    x = np.asarray(inputs["L0_attn_x"])
    flat = np.ascontiguousarray(x.reshape(8, NTOK, D))
    ident = np.eye(128, dtype=ml_dtypes.bfloat16)
    ones = np.ones((1, TILE), dtype=ml_dtypes.bfloat16)
    epair = np.zeros((16, NCF * 128), dtype=ml_dtypes.bfloat16)
    for pr in range(NCF):
        for j in range(128):
            epair[2 * pr + (j // 64), pr * 128 + j] = 1.0
    common = {
        "Wq": np.ascontiguousarray(inputs["L0_attn_Wq"], dtype=np.float32),
        "Wk": np.ascontiguousarray(inputs["L0_attn_Wk"], dtype=np.float32),
        "Wv": np.ascontiguousarray(inputs["L0_attn_Wv"], dtype=np.float32),
        "Wo": np.ascontiguousarray(inputs["L0_attn_Wo"], dtype=np.float32),
        "bq": np.ascontiguousarray(inputs["L0_attn_bq"], dtype=np.float32).reshape(1, D),
        "bk": np.ascontiguousarray(inputs["L0_attn_bk"], dtype=np.float32).reshape(1, D),
        "bv": np.ascontiguousarray(inputs["L0_attn_bv"], dtype=np.float32).reshape(1, D),
        "bo": np.ascontiguousarray(inputs["L0_attn_bo"], dtype=np.float32).reshape(1, D),
        "wg": np.ascontiguousarray(inputs["L0_ffn_w_gate"], dtype=np.float32),
        "wu": np.ascontiguousarray(inputs["L0_ffn_w_up"], dtype=np.float32),
        "wd": np.ascontiguousarray(inputs["L0_ffn_w_down"], dtype=np.float32),
        "clipx": np.ascontiguousarray(inputs["L0_ffn_clip_x"], dtype=np.float32).reshape(1, 1),
        "clipd": np.ascontiguousarray(inputs["L0_ffn_clip_down"], dtype=np.float32).reshape(1, 1),
        "ident": ident, "onesc": ones, "epair": epair,
    }
    out = []
    for c in range(8):
        sm = np.zeros((128, 4), dtype=np.float32)
        sm[:, c // 2] = 1.0
        out.append(dict(common, x=np.ascontiguousarray(flat[c]), smask=sm))
    return out


def kernel(**inputs):
    in_maps = make_inputs(inputs)
    nc = build_nc(dbg=False)
    res = run_bass_kernel_spmd(nc, in_maps, core_ids=list(range(8)))
    y = np.concatenate([np.asarray(r["y"]) for r in res.results], axis=0)
    return y.reshape(4, 8192, 1024)
